# version 04
# speedup vs baseline: 1.4357x; 1.1075x over previous
import math
import numpy as np
import concourse.bass as bass
import concourse.mybir as mybir
from concourse.bass_utils import run_bass_kernel_spmd
from contextlib import ExitStack

F32 = mybir.dt.float32
BF16 = mybir.dt.bfloat16
AF = mybir.ActivationFunctionType
ALU = mybir.AluOpType

P = 128
S = 2048
D = 1024
KC = D // P
TT = S // P
TC = S // 512
NH = 16
HD = 64


def lhsT_tiles(w_io, mtile=128):
    din, dout = w_io.shape
    kc, mc = din // P, dout // mtile
    t = w_io.reshape(kc, P, mc, mtile).transpose(2, 1, 0, 3)
    return np.ascontiguousarray(t.reshape(mc, P, kc * mtile))

def moving_tiles(w_io):
    din, dout = w_io.shape
    return np.ascontiguousarray(w_io.reshape(din // P, P, dout))


class Sem:
    __slots__ = ("h", "n")
    def __init__(self, h):
        self.h = h
        self.n = 0
    def inc(self, k=1):
        self.n += k
        return self.n

class Kit:
    def __init__(self, nc, ctx, block):
        self.nc, self.ctx, self.block = nc, ctx, block
        self._sems = {}
        self._uid = 0
    def uid(self):
        self._uid += 1
        return self._uid
    def sem(self, name):
        if name not in self._sems:
            self._sems[name] = Sem(self.ctx.enter_context(self.nc.semaphore(name)))
        return self._sems[name]
    def sb(self, name, shape, dt=F32, stack=None):
        return (stack or self.ctx).enter_context(
            self.nc.sbuf_tensor(f"{name}_{self.uid()}", shape, dt))
    def ps(self, name, shape, dt=F32, stack=None):
        return (stack or self.ctx).enter_context(
            self.nc.psum_tensor(f"{name}_{self.uid()}", shape, dt))
    def barrier(self):
        bar = self.sem("bar")
        tgt = bar.n + 5
        bar.n = tgt
        for name in ("sync", "gpsimd", "vector", "scalar", "tensor"):
            def f(eng, tgt=tgt, name=name):
                if name == "sync":
                    eng.nop().then_inc(self._sems["bar"].h, 1)
                else:
                    eng.drain().then_inc(self._sems["bar"].h, 1)
                eng.wait_ge(self._sems["bar"].h, tgt)
            getattr(self.block, name)(f)

BLOCK_EMITTERS = {}

def on(k, eng_name, f):
    getattr(k.block, eng_name)(f)

def load_const(k, dst, src_ap, queue="sync"):
    s = k.sem("ldc" if queue == "sync" else "ldcg")
    t = s.inc(16)
    on(k, queue, lambda e: e.dma_start(out=dst[:], in_=src_ap).then_inc(s.h, 16))
    return s, t


def host_constants():
    import ml_dtypes
    bf = ml_dtypes.bfloat16
    c = {}
    c["c_ident_bf"] = np.eye(P, dtype=bf)
    c["c_ones_row_f32"] = np.ones((1, P), dtype=np.float32)
    c["c_ones_row_bf"] = np.ones((1, P), dtype=bf)
    c["c_ones_col_bf"] = np.ones((P, 1), dtype=bf)
    c["c_one_bf"] = np.ones((1, 1), dtype=bf)
    se = np.zeros((16, 16 * P), dtype=np.float32)
    for e in range(16):
        se[e, e * P:(e + 1) * P] = 1.0
    c["c_sel16e_bf"] = se.astype(bf)
    pos = np.arange(S, dtype=np.float32)[:, None]
    div = np.exp(np.arange(0, 32, 2, dtype=np.float32) * (-math.log(10000.0) / 32))
    ang = (pos * div).astype(np.float32)
    pe = np.zeros((S, 32), dtype=np.float32)
    pe[:, 0::2] = np.sin(ang)
    pe[:, 1::2] = np.cos(ang)
    c["c_peT"] = np.ascontiguousarray(pe.T)
    ss = np.zeros((KC, P, NH), dtype=np.float32)
    sb_ = np.zeros((KC, NH, P), dtype=np.float32)
    for kc in range(KC):
        for r in range(P):
            hh = 2 * kc + r // 64
            ss[kc, r, hh] = 1.0
            sb_[kc, hh, r] = 1.0
    c["c_sum64_bf"] = ss.astype(bf)
    c["c_bc64_bf"] = sb_.astype(bf)
    cnt = np.full((1, S), 2.0, dtype=np.float32)
    cnt[0, [0, 1, S - 2, S - 1]] = 1.0
    c["c_l1a_count"] = cnt.astype(bf)
    c["c_ident_f32"] = np.eye(P, dtype=np.float32)
    g64 = np.zeros((P, 2), dtype=np.float32)
    g64[0:64, 0] = 1.0
    g64[64:128, 1] = 1.0
    c["c_g64_bf"] = g64.astype(bf)
    c["c_g64T_bf"] = np.ascontiguousarray(g64.T).astype(bf)
    sump = np.zeros((4, P, 16), dtype=np.float32)
    bcp = np.zeros((4, 16, P), dtype=np.float32)
    for v in range(4):
        for r in range(P):
            hh = 4 * v + r // 32
            sump[v, r, hh] = 1.0
            bcp[v, hh, r] = 1.0
    c["c_sumP_bf"] = sump.astype(bf)
    c["c_bcP_bf"] = bcp.astype(bf)
    ar = np.arange(0, 64, 2, dtype=np.float32)
    inv_freq = 1.0 / 10000.0 ** (ar / 64)
    ramp = np.clip((ar / 64 - 0.25) / 0.75, 0.0, 1.0)
    inv_freq = inv_freq / (ramp * 3.0 + 1.0)
    freqs = np.arange(S, dtype=np.float32)[:, None] * inv_freq[None, :]
    cosT = np.cos(freqs)
    sinT = np.sin(freqs)
    ridx = np.arange(P) % 32
    c["c_l0cos_bf"] = np.ascontiguousarray(cosT[:, ridx].T).astype(bf)
    c["c_l0sin_bf"] = np.ascontiguousarray(sinT[:, ridx].T).astype(bf)
    ltm = (np.arange(P)[:, None] <= np.arange(P)[None, :]).astype(np.float32)
    c["c_ltmask_bf"] = ltm.astype(bf)
    return c

def declare_inputs(nc, host):
    aps = {}
    for name, arr in host.items():
        dt = {"float32": F32, "bfloat16": BF16}[str(arr.dtype)]
        aps[name] = nc.dram_tensor(name, list(arr.shape), dt, kind="ExternalInput").ap()
    return aps


def emit_ln(k, g):
    nc = k.nc
    st = ExitStack()
    resid = k.sb("lnresid", [P, TT, D], F32, stack=st)
    st6 = k.sb("lnst6", [P, TT, 12], F32, stack=st)
    mv = k.sb("lnmv", [P, TT, 2], F32, stack=st)
    rstd = k.sb("lnrstd", [P, TT], F32, stack=st)
    veps = k.sb("lnveps", [P, TT], F32, stack=st)
    tmb = k.sb("lntmb", [P, 2, D], BF16, stack=st)
    xln = g["xln"]
    ident = g["ident"]
    pst = g["ps_tr"]

    s_ld = k.sem("ln_ld")
    s_v2s = k.sem("ln_v2s")
    s_s2v = k.sem("ln_s2v")
    s_tm = k.sem("ln_tm")
    s_tp = k.sem("ln_tp")
    s_ev = k.sem("ln_ev")

    t_ld = s_ld.inc(16)
    on(k, "sync", lambda e: e.dma_start(
        out=resid[:], in_=g["resid_src"].rearrange("(t p) d -> p t d", p=P)).then_inc(s_ld.h, 16))

    base_tm, base_tp, base_ev = s_tm.n, s_tp.n, s_ev.n

    def v1(v):
        v.wait_ge(s_ld.h, t_ld)
        for t in range(TT):
            v.bn_stats(out=st6[:, t, 0:6], in_=resid[:, t, 0:512])
            v.bn_stats(out=st6[:, t, 6:12], in_=resid[:, t, 512:1024])
        v.drain()
        for t in range(TT):
            v.bn_aggr(out=mv[:, t, :], in_=st6[:, t, :])
        v.drain()
        v.tensor_scalar(out=veps[:], in0=mv[:, :, 1:2], scalar1=1e-5, scalar2=None, op0=ALU.add)
        v.drain().then_inc(s_v2s.h, 1)
    on(k, "vector", v1)
    tv = s_v2s.inc(1)

    def sc1(sc):
        sc.wait_ge(s_v2s.h, tv)
        sc.activation(out=veps[:], in_=veps[:], func=AF.Sqrt)
        sc.drain().then_inc(s_s2v.h, 1)
    on(k, "scalar", sc1)
    ts_ = s_s2v.inc(1)

    def v2(v):
        v.wait_ge(s_s2v.h, ts_)
        v.reciprocal(out=rstd[:], in_=veps[:])
        v.drain()
        for t in range(TT):
            if t >= 2:
                v.wait_ge(s_tp.h, base_tp + t - 1)
            v.tensor_scalar(out=tmb[:, t % 2, :], in0=resid[:, t, :],
                            scalar1=mv[:, t, 0:1], scalar2=rstd[:, t:t + 1],
                            op0=ALU.subtract, op1=ALU.mult).then_inc(s_tm.h, 1)
    on(k, "vector", v2)
    s_tm.n += TT

    def te(te_):
        for t in range(TT):
            te_.wait_ge(s_tm.h, base_tm + t + 1)
            if t >= 2:
                te_.wait_ge(s_ev.h, base_ev + t - 1)
            for kc in range(KC):
                ins = te_.transpose(out=pst[:, t % 2, kc * 128:(kc + 1) * 128],
                                    in_=tmb[:, t % 2, kc * 128:(kc + 1) * 128],
                                    identity=ident[:])
                if kc == KC - 1:
                    ins.then_inc(s_tp.h, 1)
    on(k, "tensor", te)
    s_tp.n += TT

    def sc2(sc):
        for t in range(TT):
            sc.wait_ge(s_tp.h, base_tp + t + 1)
            sc.activation(out=xln[:, :, t * 128:(t + 1) * 128],
                          in_=pst[:, t % 2, :].rearrange("p (kc c) -> p kc c", kc=KC),
                          func=AF.Copy).then_inc(s_ev.h, 1)
    on(k, "scalar", sc2)
    s_ev.n += TT
    k.barrier()
    st.close()


class ResidAdder:
    def __init__(self, k, g, stack=None):
        self.k, self.g = k, g
        self.buf = k.sb("radd", [P, 2, D], F32, stack=stack)
        self.s_ld = k.sem("ra_ld")
        self.s_add = k.sem("ra_add")
        self.s_st = k.sem("ra_st")
        self.b_ld = self.s_ld.n
        self.b_st = self.s_st.n

    def prepare(self, t):
        kk, g = self.k, self.g
        def f(e, t=t):
            if t >= 2:
                e.wait_ge(self.s_st.h, self.b_st + 16 * (t - 1))
            e.dma_start(out=self.buf[:, t % 2, :],
                        in_=g["resid_src"].rearrange("(t p) d -> t p d", p=P)[t]).then_inc(self.s_ld.h, 16)
        on(kk, "sync", f)
        self.s_ld.inc(16)

    def add(self, t, c0, c1, psum_ap, scale_ap=None, wait=None):
        kk = self.k
        ld_tgt = self.b_ld + 16 * (t + 1)
        def f(v, t=t, c0=c0, c1=c1, psum_ap=psum_ap, scale_ap=scale_ap, wait=wait):
            v.wait_ge(self.s_ld.h, ld_tgt)
            if wait is not None:
                v.wait_ge(wait[0].h, wait[1])
            dst = self.buf[:, t % 2, c0:c1]
            if scale_ap is None:
                v.tensor_tensor(out=dst, in0=psum_ap, in1=dst, op=ALU.add).then_inc(self.s_add.h, 1)
            else:
                v.scalar_tensor_tensor(out=dst, in0=psum_ap, scalar=scale_ap, in1=dst,
                                       op0=ALU.mult, op1=ALU.add).then_inc(self.s_add.h, 1)
        on(kk, "vector", f)
        self.s_add.inc(1)

    def flush(self, t):
        kk, g = self.k, self.g
        tgt = self.s_add.n
        def f(e, t=t, tgt=tgt):
            e.wait_ge(self.s_add.h, tgt)
            e.dma_start(out=g["resid_dst"].rearrange("(t p) d -> t p d", p=P)[t],
                        in_=self.buf[:, t % 2, :]).then_inc(self.s_st.h, 16)
        on(kk, "sync", f)
        self.s_st.inc(16)

    def done(self):
        tgt = self.s_st.n
        on(self.k, "sync", lambda e: e.wait_ge(self.s_st.h, tgt))


def emit_l2ffn(k, g, aps):
    st = ExitStack()
    xln = g["xln"]
    wsign = k.sb("l2fw", [P, 8, 1024], BF16, stack=st)
    psum = g["ps_mm"]
    s_wl = k.sem("wl")
    s_sg = k.sem("l2f_sg")
    s_pr = k.sem("pr")

    t_wl = s_wl.inc(16)
    on(k, "gpsimd", lambda e: e.dma_start(
        out=wsign[:], in_=aps["L2f_wt"].rearrange("kc p n -> p kc n")).then_inc(s_wl.h, 16))

    def sc(sc_):
        sc_.wait_ge(s_wl.h, t_wl)
        sc_.activation(out=wsign[:], in_=wsign[:], func=AF.Sign)
        sc_.drain().then_inc(s_sg.h, 1)
    on(k, "scalar", sc)
    t_sg = s_sg.inc(1)

    ra = ResidAdder(k, g, stack=st)
    base_pr, base_add = s_pr.n, ra.s_add.n
    def te(te_):
        te_.wait_ge(s_sg.h, t_sg)
        n = 0
        for t in range(TT):
            for half in range(2):
                n += 1
                if n > 2:
                    te_.wait_ge(ra.s_add.h, base_add + n - 2)
                for kc in range(KC):
                    ins = te_.matmul(out=psum[:, n % 2, :],
                                     lhsT=xln[:, kc, t * 128:(t + 1) * 128],
                                     rhs=wsign[:, kc, half * 512:(half + 1) * 512],
                                     start=(kc == 0), stop=(kc == KC - 1))
                    if kc == KC - 1:
                        ins.then_inc(s_pr.h, 1)
    on(k, "tensor", te)
    s_pr.n += TT * 2

    n = 0
    for t in range(TT):
        ra.prepare(t)
        for half in range(2):
            n += 1
            ra.add(t, half * 512, (half + 1) * 512, psum[:, n % 2, :], wait=(s_pr, base_pr + n))
        ra.flush(t)
    ra.done()
    k.barrier()
    st.close()


def emit_l3ffn(k, g, aps):
    st = ExitStack()
    xln = g["xln"]
    selW = k.sb("l3f_selw", [P, 8 * 16], BF16, stack=st)
    keysW = k.sb("l3f_keys", [P, 16, 1024], BF16, stack=st)
    valsW = k.sb("l3f_vals", [P, 16, 1024], BF16, stack=st)
    selS = k.sb("l3f_sel", [16, S], BF16, stack=st)
    scb = k.sb("l3f_scb", [P, 2, 256], BF16, stack=st)
    rbuf = k.sb("l3f_rl", [P, 2, 256], BF16, stack=st)
    ps_out = g["ps_mm"]
    pp = [k.ps("l3f_pp0", [P, 512], F32, stack=st),
          k.ps("l3f_pp1", [P, 512], F32, stack=st)]
    ps_sel = g["ps_mm"][0:16, 0, :]

    s_wl = k.sem("wl")
    b_wl = s_wl.n
    def gl(e):
        e.dma_start(out=selW[:], in_=aps["L3f_selt"].rearrange("a p m -> p (a m)")).then_inc(s_wl.h, 16)
        e.dma_start(out=keysW[:], in_=aps["L3f_keyst"].rearrange("e p n -> p e n")).then_inc(s_wl.h, 16)
        e.dma_start(out=valsW[:], in_=aps["L3f_valst"].rearrange("e p n -> p e n")).then_inc(s_wl.h, 16)
    on(k, "gpsimd", gl)
    s_wl.inc(48)

    s_pr = k.sem("pr")
    b_pr = s_pr.n
    s_ev = k.sem("ev")
    b_ev = s_ev.n
    def te1(te):
        te.wait_ge(s_wl.h, b_wl + 16)
        for tc in range(TC):
            if tc >= 1:
                te.wait_ge(s_ev.h, b_ev + tc)
            for kc in range(KC):
                ins = te.matmul(out=ps_sel, lhsT=selW[:, kc * 16:(kc + 1) * 16],
                                rhs=xln[:, kc, tc * 512:(tc + 1) * 512],
                                start=(kc == 0), stop=(kc == KC - 1))
            ins.then_inc(s_pr.h, 1)
    on(k, "tensor", te1)
    s_pr.n += TC

    def sc1(sc):
        for tc in range(TC):
            sc.wait_ge(s_pr.h, b_pr + tc + 1)
            sc.activation(out=selS[:, tc * 512:(tc + 1) * 512], in_=ps_sel,
                          func=AF.Sigmoid).then_inc(s_ev.h, 1)
    on(k, "scalar", sc1)
    s_ev.n += TC

    ra = ResidAdder(k, g, stack=st)
    s_sc = k.sem("l3f_sc")
    s_bc = k.sem("l3f_bc")
    s_rl = k.sem("l3f_rlu")
    s_ss = k.sem("l3f_ss")
    b_sc, b_bc, b_rl, b_ss = s_sc.n, s_bc.n, s_rl.n, s_ss.n
    b_add = ra.s_add.n
    NTG = 8
    E = 16
    b_pr2 = s_pr.n

    def out_of(te, tg, e):
        nsc = tg * E + e + 1
        par = nsc % 2
        te.wait_ge(s_ss.h, b_ss + nsc)
        for tt2 in range(2):
            for nn in range(2):
                te.matmul(out=ps_out[:, tt2 * 2 + nn, :],
                          lhsT=scb[:, par, tt2 * 128:(tt2 + 1) * 128],
                          rhs=valsW[:, e, nn * 512:(nn + 1) * 512],
                          start=(e == 0), stop=(e == E - 1),
                          skip_group_check=True).then_inc(s_pr.h, 1)

    for tg in range(NTG):
        def te2(te, tg=tg):
            if tg == 0:
                te.wait_ge(s_ev.h, b_ev + TC)
                te.wait_ge(s_wl.h, b_wl + 48)
            else:
                te.wait_ge(ra.s_add.h, b_add + tg * 4)
            for e in range(E):
                nsc = tg * E + e + 1
                if nsc > 2:
                    te.wait_ge(s_ss.h, b_ss + nsc - 2)
                par = nsc % 2
                te.matmul(out=pp[par][:, 256:512], lhsT=g["sel16e"][:, e * 128:(e + 1) * 128],
                          rhs=selS[:, tg * 256:(tg + 1) * 256],
                          start=True, stop=True, skip_group_check=True).then_inc(s_bc.h, 1)
                for kc in range(KC):
                    ins = te.matmul(out=pp[par][:, 0:256], lhsT=keysW[:, e, kc * 128:(kc + 1) * 128],
                                    rhs=xln[:, kc, tg * 256:(tg + 1) * 256],
                                    start=(kc == 0), stop=(kc == KC - 1))
                ins.then_inc(s_sc.h, 1)
                if e >= 1:
                    out_of(te, tg, e - 1)
            out_of(te, tg, E - 1)
        on(k, "tensor", te2)
        s_bc.n += E
        s_sc.n += E
        s_pr.n += E * 4

        def sc2(sc, tg=tg):
            for e in range(E):
                n = tg * E + e + 1
                par = n % 2
                sc.wait_ge(s_sc.h, b_sc + n)
                if n > 2:
                    sc.wait_ge(s_ss.h, b_ss + n - 2)
                sc.activation(out=rbuf[:, par, :], in_=pp[par][:, 0:256],
                              func=AF.Relu).then_inc(s_rl.h, 1)
        on(k, "scalar", sc2)
        s_rl.n += E

        def v2(v, tg=tg):
            for e in range(E):
                n = tg * E + e + 1
                par = n % 2
                v.wait_ge(s_rl.h, b_rl + n)
                v.wait_ge(s_bc.h, b_bc + n)
                if n > 2:
                    v.wait_ge(s_pr.h, b_pr2 + (n - 2) * 4)
                v.tensor_tensor(out=scb[:, par, :], in0=rbuf[:, par, :],
                                in1=pp[par][:, 256:512], op=ALU.mult).then_inc(s_ss.h, 1)
        on(k, "vector", v2)
        s_ss.n += E

        t0 = tg * 2
        ra.prepare(t0)
        ra.prepare(t0 + 1)
        n_end = b_pr2 + (tg + 1) * E * 4
        for tt2 in range(2):
            for nn in range(2):
                ra.add(t0 + tt2, nn * 512, (nn + 1) * 512, ps_out[:, tt2 * 2 + nn, :],
                       wait=(s_pr, n_end))
        ra.flush(t0)
        ra.flush(t0 + 1)
    ra.done()
    k.barrier()
    st.close()

BLOCK_EMITTERS["l3ffn"] = emit_l3ffn


def emit_l0ffn(k, g, aps):
    st = ExitStack()
    xln = g["xln"]
    a = k.sb("l0f_a", [P, 32, 1024], BF16, stack=st)
    b = k.sb("l0f_b", [P, 32, 1024], BF16, stack=st)
    outF = b[:, 0:8, :]
    tmp = k.sb("l0f_t", [P, 2, 512], BF16, stack=st)
    w1b = k.sb("l0f_w1", [P, 2, 1024], BF16, stack=st)
    w2b = k.sb("l0f_w2", [P, 2, 1024], BF16, stack=st)
    hb = k.sb("l0f_h", [P, 2, 4096], BF16, stack=st)
    scales = k.sb("l0f_s", [P, 4 * 32 + 8], F32, stack=st)
    ps = g["ps_mm"]
    pst = g["ps_tr"]
    ra = ResidAdder(k, g, stack=st)

    s_wl = k.sem("wl")
    s_wf = k.sem("l0f_wf")
    s_pr = k.sem("pr")
    s_ev = k.sem("ev")
    s_sl = k.sem("l0f_sl")
    s_tp = k.sem("ln_tp")

    t0_ = s_wl.inc(16)
    on(k, "gpsimd", lambda e: e.dma_start(out=scales[:], in_=aps["L0f_scales"][:]).then_inc(s_wl.h, 16))
    sc_s1 = scales[:, 0:32]
    sc_s2 = scales[:, 32:64]
    sc_h1 = scales[:, 64:96]
    sc_h2 = scales[:, 96:128]
    sc_s3 = scales[:, 128:136]

    for half in range(2):
        tok0 = half * 1024
        b_wl, b_wf, b_pr, b_ev, b_sl = s_wl.n, s_wf.n, s_pr.n, s_ev.n, s_sl.n
        def g1(e, b_wf=b_wf):
            for mc in range(32):
                if mc >= 2:
                    e.wait_ge(s_wf.h, b_wf + mc - 1)
                e.dma_start(out=w1b[:, mc % 2, :], in_=aps["L0f_w1t"][mc]).then_inc(s_wl.h, 16)
                e.dma_start(out=w2b[:, mc % 2, :], in_=aps["L0f_w2t"][mc]).then_inc(s_wl.h, 16)
        on(k, "gpsimd", g1)
        s_wl.n += 32 * 32

        def t1(te, b_wl=b_wl, b_ev=b_ev, tok0=tok0):
            te.wait_ge(s_wl.h, t0_)
            n = 0
            for mc in range(32):
                te.wait_ge(s_wl.h, b_wl + 32 * (mc + 1))
                for tcc in range(2):
                    n += 1
                    if n > 2:
                        te.wait_ge(s_ev.h, b_ev + n - 2)
                    q = n % 2
                    for kc in range(KC):
                        i1 = te.matmul(out=ps[:, q * 2, :], lhsT=w1b[:, mc % 2, kc * 128:(kc + 1) * 128],
                                       rhs=xln[:, kc, tok0 + tcc * 512: tok0 + (tcc + 1) * 512],
                                       start=(kc == 0), stop=(kc == KC - 1))
                    i1.then_inc(s_pr.h, 1)
                    for kc in range(KC):
                        i2 = te.matmul(out=ps[:, q * 2 + 1, :], lhsT=w2b[:, mc % 2, kc * 128:(kc + 1) * 128],
                                       rhs=xln[:, kc, tok0 + tcc * 512: tok0 + (tcc + 1) * 512],
                                       start=(kc == 0), stop=(kc == KC - 1))
                    i2.then_inc(s_pr.h, 1)
                te.drain().then_inc(s_wf.h, 1)
        on(k, "tensor", t1)
        s_pr.n += 128
        s_wf.n += 32

        def sc1(sc, b_pr=b_pr, b_sl=b_sl, b_ev=b_ev):
            n = 0
            for mc in range(32):
                for tcc in range(2):
                    n += 1
                    sc.wait_ge(s_pr.h, b_pr + 2 * n - 1)
                    if n > 2:
                        sc.wait_ge(s_ev.h, b_ev + n - 2)
                    sc.activation(out=tmp[:, n % 2, :], in_=ps[:, (n % 2) * 2, :],
                                  func=AF.Silu, scale=sc_s1[:, mc:mc + 1]).then_inc(s_sl.h, 1)
        on(k, "scalar", sc1)
        s_sl.n += 64

        def v1(v, b_pr=b_pr, b_sl=b_sl):
            n = 0
            for mc in range(32):
                for tcc in range(2):
                    n += 1
                    v.wait_ge(s_sl.h, b_sl + n)
                    v.wait_ge(s_pr.h, b_pr + 2 * n)
                    v.scalar_tensor_tensor(out=a[:, mc, tcc * 512:(tcc + 1) * 512],
                                           in0=ps[:, (n % 2) * 2 + 1, :], scalar=sc_s2[:, mc:mc + 1],
                                           in1=tmp[:, n % 2, :], op0=ALU.mult,
                                           op1=ALU.mult).then_inc(s_ev.h, 1)
        on(k, "vector", v1)
        s_ev.n += 64
        k.barrier()

        for stg, (wkey, scol, src, dst) in enumerate(
                [("L0f_h1t", sc_h1, a, b), ("L0f_h2t", sc_h2, b, a)]):
            b_wl, b_wf, b_pr, b_ev = s_wl.n, s_wf.n, s_pr.n, s_ev.n
            def g2(e, wkey=wkey, b_wf=b_wf):
                for mc in range(32):
                    if mc >= 2:
                        e.wait_ge(s_wf.h, b_wf + mc - 1)
                    e.dma_start(out=hb[:, mc % 2, :], in_=aps[wkey][mc]).then_inc(s_wl.h, 16)
            on(k, "gpsimd", g2)
            s_wl.n += 32 * 16

            def t2(te, b_wl=b_wl, b_ev=b_ev, src=src):
                n = 0
                for mc in range(32):
                    te.wait_ge(s_wl.h, b_wl + 16 * (mc + 1))
                    for tcc in range(2):
                        n += 1
                        if n > 2:
                            te.wait_ge(s_ev.h, b_ev + n - 2)
                        for kc in range(32):
                            ins = te.matmul(out=ps[:, n % 2, :],
                                            lhsT=hb[:, mc % 2, kc * 128:(kc + 1) * 128],
                                            rhs=src[:, kc, tcc * 512:(tcc + 1) * 512],
                                            start=(kc == 0), stop=(kc == 31))
                        ins.then_inc(s_pr.h, 1)
                    te.drain().then_inc(s_wf.h, 1)
            on(k, "tensor", t2)
            s_pr.n += 64
            s_wf.n += 32

            def sc2(sc, b_pr=b_pr, scol=scol, dst=dst):
                n = 0
                for mc in range(32):
                    for tcc in range(2):
                        n += 1
                        sc.wait_ge(s_pr.h, b_pr + n)
                        sc.activation(out=dst[:, mc, tcc * 512:(tcc + 1) * 512],
                                      in_=ps[:, n % 2, :], func=AF.Gelu,
                                      scale=scol[:, mc:mc + 1]).then_inc(s_ev.h, 1)
            on(k, "scalar", sc2)
            s_ev.n += 64
            k.barrier()

        b_wl, b_wf, b_pr, b_ev = s_wl.n, s_wf.n, s_pr.n, s_ev.n
        def g4(e, b_wf=b_wf):
            for mc in range(KC):
                if mc >= 2:
                    e.wait_ge(s_wf.h, b_wf + mc - 1)
                e.dma_start(out=hb[:, mc % 2, :], in_=aps["L0f_w3t"][mc]).then_inc(s_wl.h, 16)
        on(k, "gpsimd", g4)
        s_wl.n += KC * 16

        def t4(te, b_wl=b_wl, b_ev=b_ev):
            n = 0
            for mc in range(KC):
                te.wait_ge(s_wl.h, b_wl + 16 * (mc + 1))
                for tcc in range(2):
                    n += 1
                    if n > 2:
                        te.wait_ge(s_ev.h, b_ev + n - 2)
                    for kc in range(32):
                        ins = te.matmul(out=ps[:, n % 2, :],
                                        lhsT=hb[:, mc % 2, kc * 128:(kc + 1) * 128],
                                        rhs=a[:, kc, tcc * 512:(tcc + 1) * 512],
                                        start=(kc == 0), stop=(kc == 31))
                    ins.then_inc(s_pr.h, 1)
                te.drain().then_inc(s_wf.h, 1)
        on(k, "tensor", t4)
        s_pr.n += 16
        s_wf.n += KC

        def sc4(sc, b_pr=b_pr):
            n = 0
            for mc in range(KC):
                for tcc in range(2):
                    n += 1
                    sc.wait_ge(s_pr.h, b_pr + n)
                    sc.activation(out=outF[:, mc, tcc * 512:(tcc + 1) * 512],
                                  in_=ps[:, n % 2, :], func=AF.Copy,
                                  scale=sc_s3[:, mc:mc + 1]).then_inc(s_ev.h, 1)
        on(k, "scalar", sc4)
        s_ev.n += 16
        k.barrier()

        b_tp = s_tp.n
        def t5(te):
            for tt8 in range(8):
                if tt8 >= 2:
                    te.wait_ge(ra.s_add.h, ra.s_add.n - 0 + tt8 - 1 - 8)
                for kc in range(KC):
                    ins = te.transpose(out=pst[:, tt8 % 2, kc * 128:(kc + 1) * 128],
                                       in_=outF[:, kc, tt8 * 128:(tt8 + 1) * 128],
                                       identity=g["ident"][:])
                ins.then_inc(s_tp.h, 1)
        b_add0 = ra.s_add.n
        def t5b(te, b_tp=b_tp, b_add0=b_add0):
            for tt8 in range(8):
                if tt8 >= 2:
                    te.wait_ge(ra.s_add.h, b_add0 + tt8 - 1)
                for kc in range(KC):
                    ins = te.transpose(out=pst[:, tt8 % 2, kc * 128:(kc + 1) * 128],
                                       in_=outF[:, kc, tt8 * 128:(tt8 + 1) * 128],
                                       identity=g["ident"][:])
                ins.then_inc(s_tp.h, 1)
        on(k, "tensor", t5b)
        s_tp.n += 8

        for tt8 in range(8):
            t = half * 8 + tt8
            ra.prepare(t)
            ra.add(t, 0, D, pst[:, tt8 % 2, :], wait=(s_tp, b_tp + tt8 + 1))
            ra.flush(t)
        k.barrier()
    ra.done()
    k.barrier()
    st.close()

BLOCK_EMITTERS["l0ffn"] = emit_l0ffn


def emit_l2attn(k, g, aps, upto=99):
    phase = [0]
    def cut():
        phase[0] += 1
        return phase[0] >= upto
    st = ExitStack()
    xln = g["xln"]
    XB = k.sb("l2a_xb", [P, KC, S], BF16, stack=st)
    wbuf = k.sb("l2a_w", [P, KC, 1024], BF16, stack=st)
    qp = k.sb("l2a_qp", [P, 2, S], BF16, stack=st)
    kp = k.sb("l2a_kp", [P, 2, S], BF16, stack=st)
    ktok = k.sb("l2a_kt", [P, TT, 256], BF16, stack=st)
    vtok = k.sb("l2a_vt", [P, TT, 1024], BF16, stack=st)
    kvS = k.sb("l2a_kv", [P, 2, 1024], BF16, stack=st)
    zrow = k.sb("l2a_z", [1, S], F32, stack=st)
    zib = k.sb("l2a_zib", [1, S], BF16, stack=st)
    zinvT = k.sb("l2a_zt", [P, TT], F32, stack=st)
    fmt = k.sb("l2a_fmt", [P, 2, 512], F32, stack=st)
    fme = k.sb("l2a_fme", [P, 2, 512], F32, stack=st)
    ps = g["ps_mm"]
    pst = g["ps_tr"]
    ps_z = k.ps("l2a_pz", [1, 512], F32, stack=st)
    ps_zt = k.ps("l2a_pzt", [P, TT], F32, stack=st)
    ra = ResidAdder(k, g, stack=st)

    s_wl = k.sem("wl")
    s_pr = k.sem("pr")
    s_ev = k.sem("ev")
    s_fm = k.sem("l2a_fm")
    s_tp = k.sem("ln_tp")

    def loadw(key, mcN=KC):
        t = s_wl.inc(16)
        on(k, "gpsimd", lambda e: e.dma_start(
            out=wbuf[:, 0:mcN, :], in_=aps[key].rearrange("mc p n -> p mc n")).then_inc(s_wl.h, 16))
        return t

    def proj_into_XB(wkey):
        b_pr, b_ev = s_pr.n, s_ev.n
        twl = loadw(wkey)
        def te(te_):
            te_.wait_ge(s_wl.h, twl)
            n = 0
            for mc in range(KC):
                for tc in range(TC):
                    n += 1
                    if n > 2:
                        te_.wait_ge(s_ev.h, b_ev + n - 2)
                    for kc in range(KC):
                        ins = te_.matmul(out=ps[:, n % 2, :],
                                         lhsT=wbuf[:, mc, kc * 128:(kc + 1) * 128],
                                         rhs=xln[:, kc, tc * 512:(tc + 1) * 512],
                                         start=(kc == 0), stop=(kc == KC - 1))
                    ins.then_inc(s_pr.h, 1)
        on(k, "tensor", te)
        s_pr.n += KC * TC
        def sc(sc_):
            n = 0
            for mc in range(KC):
                for tc in range(TC):
                    n += 1
                    sc_.wait_ge(s_pr.h, b_pr + n)
                    sc_.activation(out=XB[:, mc, tc * 512:(tc + 1) * 512], in_=ps[:, n % 2, :],
                                   func=AF.Copy).then_inc(s_ev.h, 1)
        on(k, "scalar", sc)
        s_ev.n += KC * TC
        k.barrier()

    def featuremap_into(dst):
        s_fm2 = k.sem("l2a_fm2")
        b_pr, b_ev, b_fm, b_fm2 = s_pr.n, s_ev.n, s_fm.n, s_fm2.n
        twl = loadw("L2a_projt", mcN=2)
        def te(te_):
            te_.wait_ge(s_wl.h, twl)
            n = 0
            for mc in range(2):
                for tc in range(TC):
                    n += 1
                    if n > 2:
                        te_.wait_ge(s_ev.h, b_ev + n - 2)
                    for kc in range(KC):
                        ins = te_.matmul(out=ps[:, n % 2, :],
                                         lhsT=wbuf[:, mc, kc * 128:(kc + 1) * 128],
                                         rhs=XB[:, kc, tc * 512:(tc + 1) * 512],
                                         start=(kc == 0), stop=(kc == KC - 1))
                    ins.then_inc(s_pr.h, 1)
        on(k, "tensor", te)
        s_pr.n += 2 * TC
        NT = 2 * TC
        def stt_of(v, j, dst):
            mc, tc = (j - 1) // TC, (j - 1) % TC
            v.wait_ge(s_fm2.h, b_fm2 + j)
            v.scalar_tensor_tensor(out=dst[:, mc, tc * 512:(tc + 1) * 512],
                                   in0=ps[:, j % 2, :], scalar=0.0, in1=fme[:, j % 2, :],
                                   op0=ALU.max, op1=ALU.add).then_inc(s_ev.h, 1)
        def vv(v, dst=dst):
            for n in range(1, NT + 1):
                v.wait_ge(s_pr.h, b_pr + n)
                if n > 2:
                    v.wait_ge(s_fm2.h, b_fm2 + n - 2)
                v.tensor_scalar(out=fmt[:, n % 2, :], in0=ps[:, n % 2, :],
                                scalar1=0.0, scalar2=None, op0=ALU.min).then_inc(s_fm.h, 1)
                if n >= 2:
                    stt_of(v, n - 1, dst)
            stt_of(v, NT, dst)
        on(k, "vector", vv)
        def sc1(sc):
            for n in range(1, NT + 1):
                sc.wait_ge(s_fm.h, b_fm + n)
                if n > 2:
                    sc.wait_ge(s_ev.h, b_ev + n - 2)
                sc.activation(out=fme[:, n % 2, :], in_=fmt[:, n % 2, :],
                              func=AF.Exp).then_inc(s_fm2.h, 1)
        on(k, "scalar", sc1)
        s_fm.n += NT
        s_fm2.n += NT
        s_ev.n += NT
        k.barrier()

    proj_into_XB("L2a_wqt")
    if cut(): st_p34.close(); st_p2.close(); st_p1.close(); st_vb.close(); st_qk.close(); st.close(); return
    featuremap_into(qp)
    if cut(): st_p34.close(); st_p2.close(); st_p1.close(); st_vb.close(); st_qk.close(); st.close(); return
    proj_into_XB("L2a_wkt")
    featuremap_into(kp)
    if cut(): st_p34.close(); st_p2.close(); st_p1.close(); st_vb.close(); st_qk.close(); st.close(); return
    proj_into_XB("L2a_wvt")
    if cut(): st_p34.close(); st_p2.close(); st_p1.close(); st_vb.close(); st_qk.close(); st.close(); return

    b_pr, b_ev = s_pr.n, s_ev.n
    def tez(te):
        for tc in range(TC):
            if tc >= 1:
                te.wait_ge(s_ev.h, b_ev + tc)
            for c2 in range(2):
                ins = te.matmul(out=ps_z[:], lhsT=g["ones_col_bf"][:],
                                rhs=qp[:, c2, tc * 512:(tc + 1) * 512],
                                start=(c2 == 0), stop=(c2 == 1))
            ins.then_inc(s_pr.h, 1)
    on(k, "tensor", tez)
    s_pr.n += TC
    def vz(v):
        for tc in range(TC):
            v.wait_ge(s_pr.h, b_pr + tc + 1)
            v.tensor_copy(out=zrow[:, tc * 512:(tc + 1) * 512], in_=ps_z[:]).then_inc(s_ev.h, 1)
        v.tensor_scalar(out=zrow[:], in0=zrow[:], scalar1=1e-8, scalar2=None, op0=ALU.add)
        v.drain()
        v.reciprocal(out=zrow[:], in_=zrow[:])
        v.drain()
        v.tensor_copy(out=zib[:], in_=zrow[:])
        v.drain().then_inc(s_ev.h, 1)
    on(k, "vector", vz)
    s_ev.n += TC + 1
    k.barrier()
    if cut(): st_p34.close(); st_p2.close(); st_p1.close(); st_vb.close(); st_qk.close(); st.close(); return

    b_tp = s_tp.n
    def tt1(te):
        nb = 0
        for tt in range(TT):
            nb += 1
            if nb > 2:
                te.wait_ge(s_ev.h, s_ev.n + nb - 2 - 2 * TT)
    def tt_k(te, b_tp=b_tp):
        for tt in range(TT):
            if tt >= 2:
                te.wait_ge(s_ev.h, b_ev2 + tt - 1)
            for c2 in range(2):
                ins = te.transpose(out=pst[:, tt % 2, c2 * 128:(c2 + 1) * 128],
                                   in_=kp[:, c2, tt * 128:(tt + 1) * 128],
                                   identity=g["ident"][:])
            ins.then_inc(s_tp.h, 1)
    b_ev2 = s_ev.n
    on(k, "tensor", tt_k)
    s_tp.n += TT
    def sc_k(sc, b_tp=b_tp, b_ev2=b_ev2):
        for tt in range(TT):
            sc.wait_ge(s_tp.h, b_tp + tt + 1)
            sc.activation(out=ktok[:, tt, :], in_=pst[:, tt % 2, 0:256],
                          func=AF.Copy).then_inc(s_ev.h, 1)
    on(k, "scalar", sc_k)
    s_ev.n += TT

    b_tp2 = s_tp.n
    b_ev3 = s_ev.n
    def tt_v(te, b_tp2=b_tp2, b_ev3=b_ev3):
        for tt in range(TT):
            if tt < 2:
                te.wait_ge(s_ev.h, b_ev2 + 15 + tt)
            else:
                te.wait_ge(s_ev.h, b_ev3 + tt - 1)
            for kc in range(KC):
                ins = te.transpose(out=pst[:, tt % 2, kc * 128:(kc + 1) * 128],
                                   in_=XB[:, kc, tt * 128:(tt + 1) * 128],
                                   identity=g["ident"][:])
            ins.then_inc(s_tp.h, 1)
    on(k, "tensor", tt_v)
    s_tp.n += TT
    def sc_v(sc, b_tp2=b_tp2):
        for tt in range(TT):
            sc.wait_ge(s_tp.h, b_tp2 + tt + 1)
            sc.activation(out=vtok[:, tt, :], in_=pst[:, tt % 2, :],
                          func=AF.Copy).then_inc(s_ev.h, 1)
    on(k, "scalar", sc_v)
    s_ev.n += TT
    k.barrier()
    if cut(): st_p34.close(); st_p2.close(); st_p1.close(); st_vb.close(); st_qk.close(); st.close(); return

    b_pr, b_ev = s_pr.n, s_ev.n
    def tekv(te):
        n = 0
        for pc2 in range(2):
            for nn in range(2):
                n += 1
                if n > 2:
                    te.wait_ge(s_ev.h, b_ev + n - 2)
                for t in range(TT):
                    ins = te.matmul(out=ps[:, n % 2, :],
                                    lhsT=ktok[:, t, pc2 * 128:(pc2 + 1) * 128],
                                    rhs=vtok[:, t, nn * 512:(nn + 1) * 512],
                                    start=(t == 0), stop=(t == TT - 1))
                ins.then_inc(s_pr.h, 1)
    on(k, "tensor", tekv)
    s_pr.n += 4
    def sckv(sc):
        n = 0
        for pc2 in range(2):
            for nn in range(2):
                n += 1
                sc.wait_ge(s_pr.h, b_pr + n)
                sc.activation(out=kvS[:, pc2, nn * 512:(nn + 1) * 512], in_=ps[:, n % 2, :],
                              func=AF.Copy).then_inc(s_ev.h, 1)
    on(k, "scalar", sckv)
    s_ev.n += 4
    k.barrier()
    if cut(): st_p34.close(); st_p2.close(); st_p1.close(); st_vb.close(); st_qk.close(); st.close(); return

    b_pr, b_ev = s_pr.n, s_ev.n
    def tenum(te):
        n = 0
        for mc in range(KC):
            for tc in range(TC):
                n += 1
                if n > 2:
                    te.wait_ge(s_ev.h, b_ev + n - 2)
                for pc2 in range(2):
                    ins = te.matmul(out=ps[:, n % 2, :],
                                    lhsT=kvS[:, pc2, mc * 128:(mc + 1) * 128],
                                    rhs=qp[:, pc2, tc * 512:(tc + 1) * 512],
                                    start=(pc2 == 0), stop=(pc2 == 1))
                ins.then_inc(s_pr.h, 1)
        for tt in range(TT):
            te.matmul(out=ps_zt[:, tt:tt + 1], lhsT=zib[:, tt * 128:(tt + 1) * 128],
                      rhs=g["c_one"][:], start=True, stop=True)
        te.drain().then_inc(s_pr.h, 1)
    on(k, "tensor", tenum)
    s_pr.n += KC * TC + 1
    def scnum(sc):
        n = 0
        for mc in range(KC):
            for tc in range(TC):
                n += 1
                sc.wait_ge(s_pr.h, b_pr + n)
                sc.activation(out=XB[:, mc, tc * 512:(tc + 1) * 512], in_=ps[:, n % 2, :],
                              func=AF.Copy).then_inc(s_ev.h, 1)
        sc.wait_ge(s_pr.h, b_pr + KC * TC + 1)
        sc.activation(out=zinvT[:], in_=ps_zt[:], func=AF.Copy).then_inc(s_ev.h, 1)
    on(k, "scalar", scnum)
    s_ev.n += KC * TC + 1
    k.barrier()
    if cut(): st_p34.close(); st_p2.close(); st_p1.close(); st_vb.close(); st_qk.close(); st.close(); return

    b_tp3 = s_tp.n
    b_add = ra.s_add.n
    def ttn(te, b_tp3=b_tp3, b_add=b_add):
        for tt in range(TT):
            if tt >= 2:
                te.wait_ge(ra.s_add.h, b_add + tt - 1)
            for kc in range(KC):
                ins = te.transpose(out=pst[:, tt % 2, kc * 128:(kc + 1) * 128],
                                   in_=XB[:, kc, tt * 128:(tt + 1) * 128],
                                   identity=g["ident"][:])
            ins.then_inc(s_tp.h, 1)
    on(k, "tensor", ttn)
    s_tp.n += TT
    for tt in range(TT):
        ra.prepare(tt)
        ra.add(tt, 0, D, pst[:, tt % 2, :], scale_ap=zinvT[:, tt:tt + 1],
               wait=(s_tp, b_tp3 + tt + 1))
        ra.flush(tt)
    ra.done()
    k.barrier()
    st.close()

BLOCK_EMITTERS["l2attn"] = emit_l2attn


def emit_l1ffn(k, g, aps):
    st = ExitStack()
    xln = g["xln"]
    signsF = k.sb("l1f_sg", [32, 8], F32, stack=st)
    peT = k.sb("l1f_pe", [32, S], F32, stack=st)
    scT = k.sb("l1f_sc", [P, TT, 8], F32, stack=st)
    rmax = k.sb("l1f_rm", [P, TT], F32, stack=st)
    eq = k.sb("l1f_eq", [P, TT, 8], F32, stack=st)
    avail = k.sb("l1f_av", [P, TT], F32, stack=st)
    masks = k.sb("l1f_mk", [P, TT, 8], F32, stack=st)
    mb = k.sb("l1f_mb", [P, TT, 8], BF16, stack=st)
    maskRb = k.sb("l1f_mr", [8, S], BF16, stack=st)
    bB = k.sb("l1f_bb", [8, 1024], BF16, stack=st)
    WB = k.sb("l1f_wb", [P, 8, 8, 1024], BF16, stack=st)
    xm = k.sb("l1f_xm", [P, 2, 8, 256], BF16, stack=st)
    ps = g["ps_mm"]
    pst = g["ps_tr"]
    ps1 = [k.ps("l1f_ps1a", [P, 512], F32, stack=st),
           k.ps("l1f_ps1b", [P, 512], F32, stack=st)]
    ra = ResidAdder(k, g, stack=st)

    s_wl = k.sem("wl")
    s_pr = k.sem("pr")
    s_ev = k.sem("ev")
    s_tp = k.sem("ln_tp")
    s_bc = k.sem("l1f_bc")
    s_xm = k.sem("l1f_xm")
    s_xmf = k.sem("l1f_xmf")
    b_wl = s_wl.n

    def gl(e):
        e.dma_start(out=signsF[:], in_=aps["L1f_sigsT"][:]).then_inc(s_wl.h, 16)
        e.dma_start(out=peT[:], in_=aps["c_peT"][:]).then_inc(s_wl.h, 16)
        e.dma_start(out=bB[:], in_=aps["L1_ffn_b"][:]).then_inc(s_wl.h, 16)
        e.dma_start(out=WB[:], in_=aps["L1f_wt"].rearrange("u kc p n -> p u kc n")).then_inc(s_wl.h, 16)
    on(k, "gpsimd", gl)
    s_wl.inc(64)

    s_sg = k.sem("l1f_sgs")
    def sc0(sc):
        sc.wait_ge(s_wl.h, b_wl + 16)
        sc.activation(out=signsF[:], in_=signsF[:], func=AF.Sign)
        sc.drain().then_inc(s_sg.h, 1)
    on(k, "scalar", sc0)
    t_sg = s_sg.inc(1)

    b_pr = s_pr.n
    def te0(te):
        te.wait_ge(s_sg.h, t_sg)
        te.wait_ge(s_wl.h, b_wl + 32)
        for tt in range(TT):
            ins = te.matmul(out=ps1[0][:, tt * 8:(tt + 1) * 8],
                            lhsT=peT[:, tt * 128:(tt + 1) * 128],
                            rhs=signsF[:], start=True, stop=True, skip_group_check=True)
        ins.then_inc(s_pr.h, 1)
    on(k, "tensor", te0)
    s_pr.n += 1

    s_mk = k.sem("l1f_mk")
    def v0(v):
        v.wait_ge(s_pr.h, b_pr + 1)
        v.tensor_copy(out=scT[:], in_=ps1[0][:, 0:128].rearrange("p (t u) -> p t u", u=8))
        v.drain()
        v.tensor_reduce(out=rmax[:], in_=scT[:], axis=mybir.AxisListType.X, op=ALU.max)
        v.drain()
        for t in range(TT):
            v.tensor_scalar(out=eq[:, t, :], in0=scT[:, t, :], scalar1=rmax[:, t:t + 1],
                            scalar2=None, op0=ALU.is_ge)
        v.drain()
        v.tensor_copy(out=masks[:, :, 0:1], in_=eq[:, :, 0:1])
        v.tensor_scalar(out=avail[:], in0=eq[:, :, 0], scalar1=-1.0, scalar2=1.0,
                        op0=ALU.mult, op1=ALU.add)
        v.drain()
        for u in range(1, 8):
            v.tensor_tensor(out=masks[:, :, u], in0=avail[:], in1=eq[:, :, u], op=ALU.mult)
            v.drain()
            v.tensor_tensor(out=avail[:], in0=avail[:], in1=masks[:, :, u], op=ALU.subtract)
            v.drain()
        v.tensor_copy(out=mb[:], in_=masks[:])
        v.drain().then_inc(s_mk.h, 1)
    on(k, "vector", v0)
    t_mk = s_mk.inc(1)

    b_tp = s_tp.n
    def te1(te):
        te.wait_ge(s_mk.h, t_mk)
        for t in range(TT):
            ins = te.transpose(out=pst[0:8, t // 8, (t % 8) * 128:(t % 8 + 1) * 128],
                               in_=mb[:, t, :], identity=g["ident"][:])
        ins.then_inc(s_tp.h, 1)
    on(k, "tensor", te1)
    s_tp.n += 1

    s_mr = k.sem("l1f_mrs")
    def sc1(sc):
        sc.wait_ge(s_tp.h, b_tp + 1)
        sc.activation(out=maskRb[:, 0:1024], in_=pst[0:8, 0, :], func=AF.Copy)
        sc.activation(out=maskRb[:, 1024:2048], in_=pst[0:8, 1, :], func=AF.Copy)
        sc.drain().then_inc(s_mr.h, 1)
    on(k, "scalar", sc1)
    t_mr = s_mr.inc(1)
    k.barrier()

    b_bc, b_xm, b_xmf = s_bc.n, s_xm.n, s_xmf.n
    b_pr2 = s_pr.n
    b_add = ra.s_add.n
    NTG = 8

    def outu_of(te, tg, u):
        n = tg * 8 + u + 1
        par = n % 2
        te.wait_ge(s_xm.h, b_xm + n)
        for tt2 in range(2):
            for nn in range(2):
                ins = None
                for kc in range(KC):
                    ins = te.matmul(out=ps[:, tt2 * 2 + nn, :],
                                    lhsT=xm[:, par, kc, tt2 * 128:(tt2 + 1) * 128],
                                    rhs=WB[:, u, kc, nn * 512:(nn + 1) * 512],
                                    start=(u == 0 and kc == 0), stop=False,
                                    skip_group_check=True)
                if u == 7:
                    ins = te.matmul(out=ps[:, tt2 * 2 + nn, :],
                                    lhsT=maskRb[:, tg * 256 + tt2 * 128: tg * 256 + (tt2 + 1) * 128],
                                    rhs=bB[:, nn * 512:(nn + 1) * 512],
                                    start=False, stop=True, skip_group_check=True)
                    ins.then_inc(s_pr.h, 1)
        te.drain().then_inc(s_xmf.h, 1)

    for tg in range(NTG):
        def te2(te, tg=tg):
            if tg == 0:
                te.wait_ge(s_wl.h, b_wl + 64)
                te.wait_ge(s_mr.h, t_mr)
            else:
                te.wait_ge(ra.s_add.h, b_add + tg * 4)
            for u in range(8):
                n = tg * 8 + u + 1
                if n > 2:
                    te.wait_ge(s_xm.h, b_xm + n - 2)
                par = n % 2
                te.matmul(out=ps1[par][:, 0:256],
                          lhsT=g["sel16e"][0:8, u * 128:(u + 1) * 128],
                          rhs=maskRb[:, tg * 256:(tg + 1) * 256],
                          start=True, stop=True, skip_group_check=True).then_inc(s_bc.h, 1)
                if u >= 1:
                    outu_of(te, tg, u - 1)
            outu_of(te, tg, 7)
        on(k, "tensor", te2)
        s_bc.n += 8
        s_xmf.n += 8
        s_pr.n += 4

        def v2(v, tg=tg):
            for u in range(8):
                n = tg * 8 + u + 1
                par = n % 2
                v.wait_ge(s_bc.h, b_bc + n)
                if n > 2:
                    v.wait_ge(s_xmf.h, b_xmf + n - 2)
                for kc in range(KC):
                    ins = v.tensor_tensor(out=xm[:, par, kc, :], in0=xln[:, kc, tg * 256:(tg + 1) * 256],
                                          in1=ps1[par][:, 0:256], op=ALU.mult)
                ins.then_inc(s_xm.h, 1)
        on(k, "vector", v2)
        s_xm.n += 8

        t0 = tg * 2
        ra.prepare(t0)
        ra.prepare(t0 + 1)
        n_end = b_pr2 + (tg + 1) * 4
        for tt2 in range(2):
            for nn in range(2):
                ra.add(t0 + tt2, nn * 512, (nn + 1) * 512, ps[:, tt2 * 2 + nn, :],
                       wait=(s_pr, n_end))
        ra.flush(t0)
        ra.flush(t0 + 1)
    ra.done()
    k.barrier()
    st.close()

BLOCK_EMITTERS["l1ffn"] = emit_l1ffn


def emit_l1attn(k, g, aps):
    st = ExitStack()
    xln = g["xln"]
    W = 1023
    qkv = k.sb("l1a_qkv", [P, 24, S], BF16, stack=st)
    wbuf = k.sb("l1a_w", [P, 2, 1024], BF16, stack=st)
    prod = k.sb("l1a_pr", [P, 2, W], BF16, stack=st)
    sab4 = k.sb("l1a_s4", [16, 4, W], F32, stack=st)
    smax = k.sb("l1a_sm", [16, W], F32, stack=st)
    pstack = k.sb("l1a_pk", [16, 4, W], BF16, stack=st)
    tb = k.sb("l1a_tb", [P, W], BF16, stack=st)
    acc = k.sb("l1a_ac", [P, W], BF16, stack=st)
    oscat = g["xln"]
    sum64 = k.sb("l1a_s64", [P, KC, 16], BF16, stack=st)
    bc64 = k.sb("l1a_b64", [16, KC, P], BF16, stack=st)
    cntR = k.sb("l1a_cnt", [1, S], BF16, stack=st)
    boutB = k.sb("l1a_bo", [1, 1024], BF16, stack=st)
    ps = g["ps_mm"]
    ra = ResidAdder(k, g, stack=st)

    s_wl = k.sem("wl")
    s_pr = k.sem("pr")
    s_ev = k.sem("ev")
    b_wl = s_wl.n

    def gl(e):
        e.dma_start(out=sum64[:], in_=aps["c_sum64_bf"].rearrange("kc p m -> p kc m")).then_inc(s_wl.h, 16)
        e.dma_start(out=bc64[:], in_=aps["c_bc64_bf"].rearrange("kc m p -> m kc p")).then_inc(s_wl.h, 16)
        e.dma_start(out=cntR[:], in_=aps["c_l1a_count"][:]).then_inc(s_wl.h, 16)
        e.dma_start(out=boutB[:], in_=aps["L1a_bout"][:]).then_inc(s_wl.h, 16)
        e.nop().then_inc(s_wl.h, 1)
    on(k, "gpsimd", gl)
    s_wl.inc(65)

    b_pr, b_ev = s_pr.n, s_ev.n
    s_wf = k.sem("l1a_wf")
    b_wf = s_wf.n
    def ga(e):
        for mc in range(24):
            if mc >= 2:
                e.wait_ge(s_wf.h, b_wf + mc - 1)
            e.dma_start(out=wbuf[:, mc % 2, :], in_=aps["L1a_wqkvt"][mc]).then_inc(s_wl.h, 16)
    on(k, "gpsimd", ga)
    s_wl.n += 24 * 16

    def ta(te):
        n = 0
        for mc in range(24):
            te.wait_ge(s_wl.h, b_wl + 65 + 16 * (mc + 1))
            for tc in range(TC):
                n += 1
                if n > 2:
                    te.wait_ge(s_ev.h, b_ev + n - 2)
                for kc in range(KC):
                    ins = te.matmul(out=ps[:, n % 2, :], lhsT=wbuf[:, mc % 2, kc * 128:(kc + 1) * 128],
                                    rhs=xln[:, kc, tc * 512:(tc + 1) * 512],
                                    start=(kc == 0), stop=(kc == KC - 1))
                ins.then_inc(s_pr.h, 1)
            te.drain().then_inc(s_wf.h, 1)
    on(k, "tensor", ta)
    s_pr.n += 24 * TC
    s_wf.n += 24

    def sca(sc):
        n = 0
        for mc in range(24):
            for tc in range(TC):
                n += 1
                sc.wait_ge(s_pr.h, b_pr + n)
                sc.activation(out=qkv[:, mc, tc * 512:(tc + 1) * 512], in_=ps[:, n % 2, :],
                              func=AF.Copy).then_inc(s_ev.h, 1)
    on(k, "scalar", sca)
    s_ev.n += 24 * TC
    k.barrier()
    on(k, "vector", lambda v: v.memset(oscat[:], 0.0))

    s_pd = k.sem("l1a_pd")
    s_sb = k.sem("l1a_sb")
    s_sx = k.sem("l1a_sx")
    s_bc = k.sem("l1a_bc")
    s_tt = k.sem("l1a_tt")
    for a in range(4):
        b_pd, b_pr1, b_sb = s_pd.n, s_pr.n, s_sb.n
        def vb1b(v, a=a, b_pd=b_pd, b_pr1=b_pr1):
            n = 0
            for b in range(4):
                for kc in range(KC):
                    n += 1
                    if n > 2:
                        v.wait_ge(s_pr.h, b_pr1 + (n - 2) * 2)
                    v.tensor_tensor(out=prod[:, n % 2, :],
                                    in0=qkv[:, kc, a:a + 2 * W - 1:2],
                                    in1=qkv[:, 8 + kc, b:b + 2 * W - 1:2],
                                    op=ALU.mult).then_inc(s_pd.h, 1)
        on(k, "vector", vb1b)
        s_pd.n += 32

        def tb1(te, a=a, b_pd=b_pd, b_sb=b_sb):
            n = 0
            for b in range(4):
                slot = b % 2
                if b >= 2:
                    te.wait_ge(s_sb.h, b_sb + b - 1)
                for kc in range(KC):
                    n += 1
                    te.wait_ge(s_pd.h, b_pd + n)
                    te.matmul(out=ps[0:16, slot * 2, :], lhsT=sum64[:, kc, :],
                              rhs=prod[:, n % 2, 0:512], start=(kc == 0), stop=(kc == KC - 1),
                              skip_group_check=True).then_inc(s_pr.h, 1)
                    te.matmul(out=ps[0:16, slot * 2 + 1, 0:W - 512], lhsT=sum64[:, kc, :],
                              rhs=prod[:, n % 2, 512:W], start=(kc == 0), stop=(kc == KC - 1),
                              skip_group_check=True).then_inc(s_pr.h, 1)
        on(k, "tensor", tb1)
        s_pr.n += 64

        b_pr_end = s_pr.n
        def scb1(sc, a=a, b_pr1=b_pr1, b_sb=b_sb):
            for b in range(4):
                slot = b % 2
                sc.wait_ge(s_pr.h, b_pr1 + (b + 1) * 16)
                sc.activation(out=sab4[:, b, 0:512], in_=ps[0:16, slot * 2, :], func=AF.Copy)
                sc.activation(out=sab4[:, b, 512:W], in_=ps[0:16, slot * 2 + 1, 0:W - 512], func=AF.Copy)
                sc.drain().then_inc(s_sb.h, 1)
        on(k, "scalar", scb1)
        s_sb.n += 4

        b_sx = s_sx.n
        def vb2(v, a=a, b_sb=b_sb):
            v.wait_ge(s_sb.h, b_sb + 4)
            v.tensor_tensor(out=smax[:], in0=sab4[:, 0, :], in1=sab4[:, 1, :], op=ALU.max)
            v.drain()
            v.tensor_tensor(out=smax[:], in0=smax[:], in1=sab4[:, 2, :], op=ALU.max)
            v.drain()
            v.tensor_tensor(out=smax[:], in0=smax[:], in1=sab4[:, 3, :], op=ALU.max)
            v.drain()
            for b in range(4):
                v.tensor_tensor(out=sab4[:, b, :], in0=sab4[:, b, :], in1=smax[:], op=ALU.subtract)
                v.drain()
            v.nop().then_inc(s_sx.h, 1)
        on(k, "vector", vb2)
        t_sx1 = s_sx.inc(1)

        def scb2(sc, t_sx1=t_sx1):
            sc.wait_ge(s_sx.h, t_sx1)
            for b in range(4):
                sc.activation(out=sab4[:, b, :], in_=sab4[:, b, :], func=AF.Exp,
                              scale=1.0 / 32.0)
            sc.drain().then_inc(s_sx.h, 1)
        on(k, "scalar", scb2)
        t_sx2 = s_sx.inc(1)

        def vb3(v, t_sx2=t_sx2):
            v.wait_ge(s_sx.h, t_sx2)
            v.tensor_tensor(out=smax[:], in0=sab4[:, 0, :], in1=sab4[:, 1, :], op=ALU.add)
            v.drain()
            v.tensor_tensor(out=smax[:], in0=smax[:], in1=sab4[:, 2, :], op=ALU.add)
            v.drain()
            v.tensor_tensor(out=smax[:], in0=smax[:], in1=sab4[:, 3, :], op=ALU.add)
            v.drain()
            v.reciprocal(out=smax[:], in_=smax[:])
            v.drain()
            for b in range(4):
                v.tensor_tensor(out=pstack[:, b, :], in0=sab4[:, b, :], in1=smax[:], op=ALU.mult)
                v.drain()
            v.nop().then_inc(s_sx.h, 1)
        on(k, "vector", vb3)
        t_sx3 = s_sx.inc(1)

        b_bc, b_tt = s_bc.n, s_tt.n
        def tb3(te, a=a, t_sx3=t_sx3, b_tt=b_tt):
            te.wait_ge(s_sx.h, t_sx3)
            n = 0
            for kc in range(KC):
                for b in range(4):
                    n += 1
                    if n > 2:
                        te.wait_ge(s_tt.h, b_tt + n - 2)
                    slot = n % 2
                    te.matmul(out=ps[:, slot * 2, :], lhsT=bc64[:, kc, :],
                              rhs=pstack[:, b, 0:512], start=True, stop=True,
                              skip_group_check=True)
                    te.matmul(out=ps[:, slot * 2 + 1, 0:W - 512], lhsT=bc64[:, kc, :],
                              rhs=pstack[:, b, 512:W], start=True, stop=True,
                              skip_group_check=True).then_inc(s_bc.h, 1)
        on(k, "tensor", tb3)
        s_bc.n += 32

        def vb4(v, a=a, b_bc=b_bc, b_tt=b_tt):
            n = 0
            for kc in range(KC):
                for b in range(4):
                    n += 1
                    slot = n % 2
                    v.wait_ge(s_bc.h, b_bc + n)
                    v.tensor_tensor(out=tb[:, 0:512], in0=ps[:, slot * 2, :],
                                    in1=qkv[:, 16 + kc, b:b + 2 * W - 1:2][:, 0:512], op=ALU.mult)
                    v.tensor_tensor(out=tb[:, 512:W], in0=ps[:, slot * 2 + 1, 0:W - 512],
                                    in1=qkv[:, 16 + kc, b:b + 2 * W - 1:2][:, 512:W], op=ALU.mult)
                    v.drain()
                    if b == 0:
                        v.tensor_copy(out=acc[:], in_=tb[:])
                    else:
                        v.tensor_tensor(out=acc[:], in0=acc[:], in1=tb[:], op=ALU.add)
                    v.drain().then_inc(s_tt.h, 1)
                v.tensor_tensor(out=oscat[:, kc, a:a + 2 * W - 1:2],
                                in0=oscat[:, kc, a:a + 2 * W - 1:2], in1=acc[:], op=ALU.add)
                v.drain()
            v.nop().then_inc(s_tt.h, 1)
        on(k, "vector", vb4)
        s_tt.n += 33
        k.barrier()

    woutB = k.sb("l1a_wo", [P, KC, 1024], BF16, stack=st)
    b_wl3 = s_wl.n
    on(k, "gpsimd", lambda e: e.dma_start(
        out=woutB[:], in_=aps["L1a_woutt"].rearrange("kc p n -> p kc n")).then_inc(s_wl.h, 16))
    s_wl.inc(16)

    b_pr4 = s_pr.n
    b_add = ra.s_add.n
    def tc_(te):
        te.wait_ge(s_wl.h, b_wl3 + 16)
        te.wait_ge(s_wl.h, b_wl + 64)
        n = 0
        for t in range(TT):
            for nn in range(2):
                n += 1
                if n > 2:
                    te.wait_ge(ra.s_add.h, b_add + n - 2)
                for kc in range(KC):
                    te.matmul(out=ps[:, n % 2, :], lhsT=oscat[:, kc, t * 128:(t + 1) * 128],
                              rhs=woutB[:, kc, nn * 512:(nn + 1) * 512],
                              start=(kc == 0), stop=False, skip_group_check=True)
                te.matmul(out=ps[:, n % 2, :], lhsT=cntR[:, t * 128:(t + 1) * 128],
                          rhs=boutB[:, nn * 512:(nn + 1) * 512],
                          start=False, stop=True, skip_group_check=True).then_inc(s_pr.h, 1)
    on(k, "tensor", tc_)
    s_pr.n += TT * 2

    n = 0
    for t in range(TT):
        ra.prepare(t)
        for nn in range(2):
            n += 1
            ra.add(t, nn * 512, (nn + 1) * 512, ps[:, n % 2, :], wait=(s_pr, b_pr4 + n))
        ra.flush(t)
    ra.done()
    k.barrier()
    st.close()

BLOCK_EMITTERS["l1attn"] = emit_l1attn


def emit_l0attn(k, g, aps, upto=99):
    phase = [0]
    def cut():
        phase[0] += 1
        return phase[0] >= upto
    st = ExitStack()
    xln = g["xln"]
    ps = g["ps_mm"]
    pst = g["ps_tr"]

    st_qk = ExitStack()
    st_p1 = ExitStack()
    st_p2 = ExitStack()
    st_vb = ExitStack()
    st_p34 = ExitStack()
    wqs = k.sb("l0a_wq", [P, KC, 1024], BF16, stack=st)
    g64 = k.sb("l0a_g64", [P, 2], BF16, stack=st)
    g64T = k.sb("l0a_g64T", [2, P], BF16, stack=st)
    sumP = k.sb("l0a_sump", [P, 4, 16], BF16, stack=st)
    bcP = k.sb("l0a_bcp", [16, 4, P], BF16, stack=st)
    ltm = k.sb("l0a_ltm", [P, P], BF16, stack=st)
    gvec = k.sb("l0a_gv", [16, 1], F32, stack=st)
    lamv = k.sb("l0a_lam", [16, 1], BF16, stack=st)
    rrq = k.sb("l0a_rrq", [16, S], BF16, stack=st)
    rrk = k.sb("l0a_rrk", [16, S], BF16, stack=st)
    ncb = k.sb("l0a_ncb", [P, 16], F32, stack=st)
    crow = k.sb("l0a_crow", [1, 16], F32, stack=st)
    eb = k.sb("l0a_eb", [P, 2, 512], BF16, stack=st)
    zrow_s = k.sb("l0a_zf", [1, TC, 512], F32, stack=st)
    lamc = k.sb("l0a_lamc", [P, 4], F32, stack=st)
    lamn = k.sb("l0a_lamn", [P, 4], F32, stack=st)
    qk = k.sb("l0a_qk", [P, 16, S], BF16, stack=st_qk)
    vbuf = k.sb("l0a_v", [P, KC, S], BF16, stack=st_vb)
    awb = k.sb("l0a_aw", [P, KC, 1024], BF16, stack=st_p1)
    gmS = k.sb("l0a_gm", [2, KC, 1024], BF16, stack=st_p1)
    hsb = gmS
    zdram = k.nc.dram_tensor(f"l0a_zd_{k.uid()}", [32, S], F32, kind="Internal").ap()

    s_wl = k.sem("wl")
    s_pr = k.sem("pr")
    s_ev = k.sem("ev")
    s_tp = k.sem("ln_tp")
    s_dq = k.sem("l0a_dq")
    s_ex = k.sem("l0a_ex")
    s_mk = k.sem("l0a_mk")
    s_yz = k.sem("l0a_yz")
    s_ze = k.sem("l0a_ze")

    b_wl = s_wl.n
    def gl(e):
        e.dma_start(out=g64[:], in_=aps["c_g64_bf"][:]).then_inc(s_wl.h, 16)
        e.dma_start(out=g64T[:], in_=aps["c_g64T_bf"][:]).then_inc(s_wl.h, 16)
        e.dma_start(out=sumP[:], in_=aps["c_sumP_bf"].rearrange("v p m -> p v m")).then_inc(s_wl.h, 16)
        e.dma_start(out=bcP[:], in_=aps["c_bcP_bf"].rearrange("v m p -> m v p")).then_inc(s_wl.h, 16)
        e.dma_start(out=ltm[:], in_=aps["c_ltmask_bf"][:]).then_inc(s_wl.h, 16)
        e.dma_start(out=gvec[:], in_=aps["L0_attn_q_gain"][:]).then_inc(s_wl.h, 16)
        e.dma_start(out=lamv[:], in_=aps["L0_attn_diff_lambda"][:]).then_inc(s_wl.h, 16)
    on(k, "gpsimd", gl)
    s_wl.inc(112)

    for sec in range(3):
        b_wl1 = s_wl.n
        on(k, "gpsimd", lambda e, sec=sec: e.dma_start(
            out=wqs[:], in_=aps["L0a_wqkvt"][sec]).then_inc(s_wl.h, 16))
        s_wl.inc(16)

        t_d0 = s_dq.n
        def sc_d(sc, b_wl1=b_wl1):
            sc.wait_ge(s_wl.h, b_wl1 + 16)
            sc.activation(out=awb[:], in_=wqs[:], func=AF.Abs)
            sc.drain()
            sc.activation(out=wqs[:], in_=wqs[:], func=AF.Sign)
            sc.drain().then_inc(s_dq.h, 1)
        on(k, "scalar", sc_d)
        t_d1 = s_dq.inc(1)

        b_pr1 = s_pr.n
        def te_d(te, t_d1=t_d1, t_d0=t_d0):
            te.wait_ge(s_dq.h, t_d1)
            te.wait_ge(s_wl.h, b_wl + 32)
            n = 0
            for kc in range(KC):
                for j in range(2):
                    n += 1
                    if n > 2:
                        te.wait_ge(s_dq.h, t_d1 + n - 2)
                    te.matmul(out=ps[0:2, n % 2, :], lhsT=g64[:],
                              rhs=awb[:, kc, j * 512:(j + 1) * 512],
                              start=True, stop=True, skip_group_check=True).then_inc(s_pr.h, 1)
        on(k, "tensor", te_d)
        s_pr.n += 16

        def v_d(v, b_pr1=b_pr1):
            n = 0
            for kc in range(KC):
                for j in range(2):
                    n += 1
                    v.wait_ge(s_pr.h, b_pr1 + n)
                    v.tensor_copy(out=gmS[:, kc, j * 512:(j + 1) * 512], in_=ps[0:2, n % 2, :])
                    v.drain().then_inc(s_dq.h, 1)
            v.tensor_scalar(out=hsb[:], in0=gmS[:], scalar1=1.0 / 64.0, scalar2=1e-8,
                            op0=ALU.mult, op1=ALU.max)
            v.drain()
            v.tensor_scalar(out=hsb[:], in0=hsb[:], scalar1=0.5, scalar2=None, op0=ALU.mult)
            v.drain().then_inc(s_dq.h, 1)
        on(k, "vector", v_d)
        s_dq.n += 17
        t_d2 = s_dq.n

        b_pr2 = s_pr.n
        def te_d2(te, t_d2=t_d2):
            te.wait_ge(s_dq.h, t_d2)
            n = 0
            for kc in range(KC):
                for j in range(2):
                    n += 1
                    if n > 2:
                        te.wait_ge(s_dq.h, t_d2 + n - 2)
                    te.matmul(out=ps[:, 2 + n % 2, :], lhsT=g64T[:],
                              rhs=hsb[:, kc, j * 512:(j + 1) * 512],
                              start=True, stop=True, skip_group_check=True).then_inc(s_pr.h, 1)
        on(k, "tensor", te_d2)
        s_pr.n += 16

        def v_d2(v, b_pr2=b_pr2):
            n = 0
            for kc in range(KC):
                for j in range(2):
                    n += 1
                    v.wait_ge(s_pr.h, b_pr2 + n)
                    sl = (slice(None), kc, slice(j * 512, (j + 1) * 512))
                    v.tensor_tensor(out=awb[sl], in0=awb[sl], in1=ps[:, 2 + n % 2, :], op=ALU.is_gt)
                    v.drain()
                    v.scalar_tensor_tensor(out=awb[sl], in0=awb[sl], scalar=2.0,
                                           in1=ps[:, 2 + n % 2, :], op0=ALU.mult, op1=ALU.mult)
                    v.drain()
                    v.tensor_tensor(out=wqs[sl], in0=awb[sl], in1=wqs[sl], op=ALU.mult)
                    v.drain().then_inc(s_dq.h, 1)
        on(k, "vector", v_d2)
        s_dq.n += 16
        k.barrier()

        b_pr3, b_ev3 = s_pr.n, s_ev.n
        dst = vbuf if sec == 2 else qk
        coff = 8 if sec == 1 else 0
        def te_q(te, b_ev3=b_ev3):
            n = 0
            for mc in range(KC):
                for tc in range(TC):
                    n += 1
                    if n > 2:
                        te.wait_ge(s_ev.h, b_ev3 + n - 2)
                    for kc in range(KC):
                        ins = te.matmul(out=ps[:, n % 2, :], lhsT=wqs[:, kc, mc * 128:(mc + 1) * 128],
                                        rhs=xln[:, kc, tc * 512:(tc + 1) * 512],
                                        start=(kc == 0), stop=(kc == KC - 1))
                    ins.then_inc(s_pr.h, 1)
        on(k, "tensor", te_q)
        s_pr.n += KC * TC

        def sc_q(sc, dst=dst, coff=coff, b_pr3=b_pr3):
            n = 0
            for mc in range(KC):
                for tc in range(TC):
                    n += 1
                    sc.wait_ge(s_pr.h, b_pr3 + n)
                    sc.activation(out=dst[:, coff + mc, tc * 512:(tc + 1) * 512],
                                  in_=ps[:, n % 2, :], func=AF.Copy).then_inc(s_ev.h, 1)
        on(k, "scalar", sc_q)
        s_ev.n += KC * TC
        k.barrier()

    st_p1.close()
    if cut(): st_p34.close(); st_p2.close(); st_p1.close(); st_vb.close(); st_qk.close(); st.close(); return

    crep = k.sb("l0a_cos", [P, S], BF16, stack=st_p2)
    srep = k.sb("l0a_sin", [P, S], BF16, stack=st_p2)
    ssq = k.sb("l0a_ssq", [16, S], F32, stack=st_p2)
    sq_s = k.sb("l0a_sq", [P, S], BF16, stack=st_p2)
    rt = [k.sb(f"l0a_rt{i}", [P, S], BF16, stack=st_p2) for i in range(4)]
    b_wl2 = s_wl.n
    def gl2(e):
        e.dma_start(out=crep[:], in_=aps["c_l0cos_bf"][:]).then_inc(s_wl.h, 16)
        e.dma_start(out=srep[:], in_=aps["c_l0sin_bf"][:]).then_inc(s_wl.h, 16)
    on(k, "gpsimd", gl2)
    s_wl.inc(32)
    t_rope_w = s_wl.n
    for sec in range(2):
        coff = sec * 8
        s_sqd = k.sem("l0a_sqd")
        b_sqd = s_sqd.n
        b_ev4 = s_ev.n
        def sc_s2(sc, coff=coff, b_sqd=b_sqd):
            for cc in range(KC):
                if cc >= 1:
                    sc.wait_ge(s_sqd.h, b_sqd + cc)
                sc.activation(out=sq_s[:], in_=qk[:, coff + cc, :], func=AF.Square)
                sc.drain().then_inc(s_ev.h, 1)
        on(k, "scalar", sc_s2)
        s_ev.n += KC

        b_pr4 = s_pr.n
        def te_s(te, b_ev4=b_ev4):
            for cc in range(KC):
                te.wait_ge(s_ev.h, b_ev4 + cc + 1)
                for tc in range(TC):
                    te.matmul(out=ps[0:16, tc, :], lhsT=sumP[:, cc % 4, :],
                              rhs=sq_s[:, tc * 512:(tc + 1) * 512],
                              start=(cc == 0), stop=(cc == KC - 1), skip_group_check=True)
                te.drain().then_inc(s_sqd.h, 1)
            te.drain().then_inc(s_pr.h, 1)
        on(k, "tensor", te_s)
        s_sqd.n += KC
        s_pr.n += 1

        t_pr_s = s_pr.n
        def v_s(v, t_pr_s=t_pr_s):
            v.wait_ge(s_pr.h, t_pr_s)
            for tc in range(TC):
                v.tensor_copy(out=ssq[:, tc * 512:(tc + 1) * 512], in_=ps[0:16, tc, :])
            v.drain()
            v.tensor_scalar(out=ssq[:], in0=ssq[:], scalar1=1.0 / 64.0, scalar2=1e-5,
                            op0=ALU.mult, op1=ALU.add)
            v.drain().then_inc(s_yz.h, 1)
        on(k, "vector", v_s)
        t_y1 = s_yz.inc(1)

        def sc_r(sc, t_y1=t_y1):
            sc.wait_ge(s_yz.h, t_y1)
            sc.activation(out=ssq[:], in_=ssq[:], func=AF.Sqrt)
            sc.drain().then_inc(s_yz.h, 1)
        on(k, "scalar", sc_r)
        t_y2 = s_yz.inc(1)

        def v_r(v, sec=sec, t_y2=t_y2):
            v.wait_ge(s_yz.h, t_y2)
            v.wait_ge(s_wl.h, b_wl + 112)
            v.reciprocal(out=ssq[:], in_=ssq[:])
            v.drain()
            if sec == 0:
                v.tensor_scalar(out=rrq[:], in0=ssq[:], scalar1=gvec[:], scalar2=None, op0=ALU.mult)
            else:
                v.tensor_copy(out=rrk[:], in_=ssq[:])
            v.drain().then_inc(s_yz.h, 1)
        on(k, "vector", v_r)
        t_y3 = s_yz.inc(1)

        b_pr5 = s_pr.n
        b_yz5 = s_yz.n
        rr = rrq if sec == 0 else rrk
        def te_a(te, rr=rr, t_y3=t_y3, b_yz5=b_yz5):
            te.wait_ge(s_yz.h, t_y3)
            n = 0
            for cc in range(KC):
                for tc in range(TC):
                    n += 1
                    if n > 2:
                        te.wait_ge(s_yz.h, b_yz5 + n - 2)
                    te.matmul(out=ps[:, n % 2, :], lhsT=bcP[:, cc % 4, :],
                              rhs=rr[:, tc * 512:(tc + 1) * 512],
                              start=True, stop=True, skip_group_check=True).then_inc(s_pr.h, 1)
        on(k, "tensor", te_a)
        s_pr.n += KC * TC

        def v_a(v, coff=coff, b_pr5=b_pr5):
            n = 0
            for cc in range(KC):
                for tc in range(TC):
                    n += 1
                    v.wait_ge(s_pr.h, b_pr5 + n)
                    sl = (slice(None), coff + cc, slice(tc * 512, (tc + 1) * 512))
                    v.tensor_tensor(out=qk[sl], in0=qk[sl], in1=ps[:, n % 2, :], op=ALU.mult)
                    v.drain().then_inc(s_yz.h, 1)
        on(k, "vector", v_a)
        s_yz.n += KC * TC
        k.barrier()

        def v_rope(v, coff=coff):
            v.wait_ge(s_wl.h, t_rope_w)
            for gg in range(4):
                cA = (slice(None), coff + gg, slice(None))
                cB = (slice(None), coff + 4 + gg, slice(None))
                v.tensor_tensor(out=rt[0][:], in0=qk[cA], in1=crep[:], op=ALU.mult)
                v.tensor_tensor(out=rt[3][:], in0=qk[cA], in1=srep[:], op=ALU.mult)
                v.tensor_tensor(out=rt[2][:], in0=qk[cB], in1=crep[:], op=ALU.mult)
                v.tensor_tensor(out=rt[1][:], in0=qk[cB], in1=srep[:], op=ALU.mult)
                v.drain()
                v.tensor_tensor(out=qk[cA], in0=rt[0][:], in1=rt[1][:], op=ALU.add)
                v.tensor_tensor(out=qk[cB], in0=rt[2][:], in1=rt[3][:], op=ALU.subtract)
                v.drain()
        on(k, "vector", v_rope)
        k.barrier()

    st_p2.close()
    if cut(): st_p34.close(); st_p2.close(); st_p1.close(); st_vb.close(); st_qk.close(); st.close(); return

    b_pr6 = s_pr.n
    def te_c(te):
        te.transpose(out=ps[0:1, 3, 0:16], in_=gvec[:], identity=g["identf"][0:16, 0:16]).then_inc(s_pr.h, 1)
    on(k, "tensor", te_c)
    s_pr.n += 1

    def v_c(sc, b_pr6=b_pr6):
        sc.wait_ge(s_pr.h, b_pr6 + 1)
        sc.activation(out=crow[:], in_=ps[0:1, 3, 0:16], func=AF.Abs, scale=11.3137085)
        sc.drain().then_inc(s_yz.h, 1)
    on(k, "scalar", v_c)
    t_c1 = s_yz.inc(1)

    b_pr7 = s_pr.n
    def te_c2(te, t_c1=t_c1):
        te.wait_ge(s_yz.h, t_c1)
        te.matmul(out=ps[:, 3, 0:16], lhsT=g["ones_row_f32"][:], rhs=crow[:],
                  start=True, stop=True, skip_group_check=True)
        for gg in range(4):
            te.matmul(out=ps[:, 2, gg:gg + 1], lhsT=bcP[:, gg, :], rhs=lamv[:],
                      start=(gg == 0), stop=(gg == 3), skip_group_check=True)
        te.drain().then_inc(s_pr.h, 1)
    on(k, "tensor", te_c2)
    s_pr.n += 1

    def v_c2(v, b_pr7=b_pr7):
        v.wait_ge(s_pr.h, b_pr7 + 1)
        v.tensor_scalar(out=ncb[:], in0=ps[:, 3, 0:16], scalar1=-1.0, scalar2=None, op0=ALU.mult)
        v.drain()
        v.tensor_copy(out=lamc[:], in_=ps[:, 2, 0:4])
        v.drain()
        v.tensor_scalar(out=lamn[:], in0=lamc[:], scalar1=-1.0, scalar2=None, op0=ALU.mult)
        v.drain().then_inc(s_yz.h, 1)
    on(k, "vector", v_c2)
    s_yz.inc(1)
    k.barrier()

    vtok = k.sb("l0a_vt", [P, 32, TT, 33], BF16, stack=st_p34)
    on(k, "vector", lambda v: v.memset(vtok[:, :, :, 32:33], 1.0))
    b_tp = s_tp.n
    b_evv = s_ev.n
    def te_v(te, b_tp=b_tp, b_evv=b_evv):
        for hh in range(32):
            H, hm = hh // 16, hh % 16
            gg, h4 = hm // 4, hm % 4
            cc = H * 4 + gg
            if hh >= 2:
                te.wait_ge(s_ev.h, b_evv + hh - 1)
            for kt in range(TT):
                ins = te.transpose(out=pst[:, hh % 2, kt * 32:(kt + 1) * 32],
                                   in_=vbuf[:, cc, kt * 128:(kt + 1) * 128][h4 * 32:(h4 + 1) * 32, :],
                                   identity=g["ident"][h4 * 32:(h4 + 1) * 32, h4 * 32:(h4 + 1) * 32],
                                   tile_position=(h4 * 32, 0))
            ins.then_inc(s_tp.h, 1)
    on(k, "tensor", te_v)
    s_tp.n += 32

    def sc_v2(sc, b_tp=b_tp):
        for hh in range(32):
            sc.wait_ge(s_tp.h, b_tp + hh + 1)
            sc.activation(out=vtok[:, hh, :, 0:32],
                          in_=pst[:, hh % 2, 0:512].rearrange("p (kt c) -> p kt c", c=32),
                          func=AF.Copy).then_inc(s_ev.h, 1)
    on(k, "scalar", sc_v2)
    s_ev.n += 32
    k.barrier()
    if cut(): st_p34.close(); st_p2.close(); st_p1.close(); st_vb.close(); st_qk.close(); st.close(); return

    ps_sT = ps[:, 0:2, :]
    ps_yz = ps[0:33, 2, :]
    yb = xln
    tiles = [(qc, kt) for qc in range(TC) for kt in range(qc * 4 + 4)]
    NTI = len(tiles)
    for hh in range(32):
        H, hm = hh // 16, hh % 16
        gg, h4 = hm // 4, hm % 4
        ccq = H * 4 + gg
        h_global = 4 * gg + h4
        b_sc, b_ex, b_mk, b_yz, b_ze = s_pr.n, s_ex.n, s_mk.n, s_yz.n, s_ze.n

        def pv0_of(te, i, hh):
            qc, kt = tiles[i]
            n = i + 1
            j = kt - qc * 4
            c0 = max(0, j) * 128
            te.wait_ge(s_mk.h, b_mk + n)
            if kt == 0:
                te.wait_ge(s_ze.h, b_ze + 2 * qc)
            te.matmul(out=ps_yz[:, c0:512], lhsT=vtok[:, hh, kt, :],
                      rhs=eb[:, n % 2, 0:512 - c0],
                      start=(kt == 0), stop=(kt == qc * 4 + 3),
                      skip_group_check=True).then_inc(s_yz.h, 1)

        def te_h(te, hh=hh, ccq=ccq, h4=h4, b_sc=b_sc, b_ex=b_ex, b_mk=b_mk, b_ze=b_ze):
            for i, (qc, kt) in enumerate(tiles):
                n = i + 1
                if n > 2:
                    te.wait_ge(s_ex.h, b_ex + n - 2)
                te.matmul(out=ps_sT[:, n % 2, :],
                          lhsT=qk[:, 8 + ccq, kt * 128:(kt + 1) * 128][h4 * 32:(h4 + 1) * 32, :],
                          rhs=qk[:, ccq, qc * 512:(qc + 1) * 512][h4 * 32:(h4 + 1) * 32, :],
                          start=True, stop=True, skip_group_check=True,
                          tile_position=(h4 * 32, 0)).then_inc(s_pr.h, 1)
                if i >= 1:
                    pv0_of(te, i - 1, hh)
            pv0_of(te, NTI - 1, hh)
        on(k, "tensor", te_h)
        s_pr.n += NTI
        s_yz.n += NTI

        def sc_h(sc, hh=hh, ccq=ccq, h4=h4, h_global=h_global, b_sc=b_sc, b_mk=b_mk, b_yz=b_yz, b_ze=b_ze):
            for i, (qc, kt) in enumerate(tiles):
                n = i + 1
                j = kt - qc * 4
                c0 = max(0, j) * 128
                sc.wait_ge(s_pr.h, b_sc + n)
                sc.activation(out=eb[:, n % 2, 0:512 - c0], in_=ps_sT[:, n % 2, c0:512],
                              func=AF.Exp, scale=0.1767766953,
                              bias=ncb[:, h_global:h_global + 1]).then_inc(s_ex.h, 1)
                if kt == qc * 4 + 3:
                    sc.wait_ge(s_yz.h, b_yz + sum(q * 4 + 4 for q in range(qc + 1)))
                    sc.activation(out=yb[:, ccq, qc * 512:(qc + 1) * 512][h4 * 32:(h4 + 1) * 32, :],
                                  in_=ps_yz[0:32, :], func=AF.Copy).then_inc(s_ze.h, 1)
        on(k, "scalar", sc_h)
        s_ex.n += NTI

        def v_h(v, hh=hh, b_ex=b_ex, b_yz=b_yz, b_ze=b_ze):
            for i, (qc, kt) in enumerate(tiles):
                n = i + 1
                j = kt - qc * 4
                v.wait_ge(s_ex.h, b_ex + n)
                if j >= 0:
                    v.tensor_tensor(out=eb[:, n % 2, 0:128], in0=eb[:, n % 2, 0:128],
                                    in1=ltm[:], op=ALU.mult).then_inc(s_mk.h, 1)
                else:
                    v.nop().then_inc(s_mk.h, 1)
                if kt == qc * 4 + 3:
                    v.wait_ge(s_ze.h, b_ze + 2 * qc + 1)
                    v.tensor_copy(out=zrow_s[:, qc, :], in_=ps_yz[32:33, :]).then_inc(s_ze.h, 1)
        on(k, "vector", v_h)
        s_mk.n += NTI
        s_ze.n += 2 * TC
        t_zhh = s_ze.n
        def g_z(e, hh=hh, t_zhh=t_zhh):
            e.wait_ge(s_ze.h, t_zhh)
            e.dma_start(out=zdram[hh:hh + 1, :], in_=zrow_s[:].rearrange("one q c -> one (q c)")).then_inc(s_ze.h, 16)
        on(k, "gpsimd", g_z)
        s_ze.inc(16)
        k.barrier()
        def g_zw(e, t=s_ze.n):
            e.wait_ge(s_ze.h, t)
        on(k, "gpsimd", g_zw)
    k.barrier()

    st_p34.close()
    st_vb.close()
    st_qk.close()
    if cut(): st_p34.close(); st_p2.close(); st_p1.close(); st_vb.close(); st_qk.close(); st.close(); return

    zr1 = k.sb("l0a_zr1", [16, S], F32, stack=st)
    zr2 = k.sb("l0a_zr2", [16, S], F32, stack=st)
    s_zd = k.sem("l0a_zd")
    b_zd = s_zd.n
    def g_zr(e):
        e.dma_start(out=zr1[:], in_=zdram[0:16, :]).then_inc(s_zd.h, 16)
        e.dma_start(out=zr2[:], in_=zdram[16:32, :]).then_inc(s_zd.h, 16)
    on(k, "gpsimd", g_zr)
    s_zd.inc(32)

    def v_z2(v, b_zd=b_zd):
        v.wait_ge(s_zd.h, b_zd + 32)
        v.reciprocal(out=zr1[:], in_=zr1[:])
        v.reciprocal(out=zr2[:], in_=zr2[:])
        v.drain()
        v.tensor_copy(out=rrq[:], in_=zr1[:])
        v.tensor_copy(out=rrk[:], in_=zr2[:])
        v.drain().then_inc(s_yz.h, 1)
    on(k, "vector", v_z2)
    t_z1 = s_yz.inc(1)
    k.barrier()

    t12a = k.sb("l0a_t12a", [P, 512], F32, stack=st)
    t12b = k.sb("l0a_t12b", [P, 512], F32, stack=st)
    b_pr8 = s_pr.n
    b_cb = s_yz.n
    def te_cb(te, b_cb=b_cb):
        n = 0
        for gg in range(4):
            for tc in range(TC):
                n += 1
                if n > 1:
                    te.wait_ge(s_yz.h, b_cb + n - 1)
                te.matmul(out=ps[:, 0, :], lhsT=bcP[:, gg, :],
                          rhs=rrq[:, tc * 512:(tc + 1) * 512],
                          start=True, stop=True, skip_group_check=True)
                te.matmul(out=ps[:, 1, :], lhsT=bcP[:, gg, :],
                          rhs=rrk[:, tc * 512:(tc + 1) * 512],
                          start=True, stop=True, skip_group_check=True).then_inc(s_pr.h, 1)
    on(k, "tensor", te_cb)
    s_pr.n += 16

    def v_cb(v, b_pr8=b_pr8):
        n = 0
        for gg in range(4):
            for tc in range(TC):
                n += 1
                v.wait_ge(s_pr.h, b_pr8 + n)
                slA = (slice(None), gg, slice(tc * 512, (tc + 1) * 512))
                slB = (slice(None), 4 + gg, slice(tc * 512, (tc + 1) * 512))
                v.tensor_tensor(out=t12a[:], in0=yb[slA], in1=ps[:, 0, :], op=ALU.mult)
                v.tensor_tensor(out=t12b[:], in0=yb[slB], in1=ps[:, 1, :], op=ALU.mult)
                v.drain()
                v.scalar_tensor_tensor(out=yb[slA], in0=t12b[:], scalar=lamn[:, gg:gg + 1], in1=t12a[:],
                                       op0=ALU.mult, op1=ALU.add)
                v.scalar_tensor_tensor(out=yb[slB], in0=t12b[:], scalar=lamc[:, gg:gg + 1], in1=t12a[:],
                                       op0=ALU.mult, op1=ALU.add)
                v.drain().then_inc(s_yz.h, 1)
    on(k, "vector", v_cb)
    s_yz.n += 16
    k.barrier()
    if cut(): st_p34.close(); st_p2.close(); st_p1.close(); st_vb.close(); st_qk.close(); st.close(); return

    awb = k.sb("l0a_aw6", [P, KC, 1024], BF16, stack=st)
    gmS = k.sb("l0a_gm6", [2, KC, 1024], BF16, stack=st)
    hsb = gmS
    b_wl6 = s_wl.n
    on(k, "gpsimd", lambda e: e.dma_start(out=wqs[:], in_=aps["L0a_protot"][:]).then_inc(s_wl.h, 16))
    s_wl.inc(16)
    def sc_d6(sc, b_wl6=b_wl6):
        sc.wait_ge(s_wl.h, b_wl6 + 16)
        sc.activation(out=awb[:], in_=wqs[:], func=AF.Abs)
        sc.drain()
        sc.activation(out=wqs[:], in_=wqs[:], func=AF.Sign)
        sc.drain().then_inc(s_dq.h, 1)
    on(k, "scalar", sc_d6)
    t_d1 = s_dq.inc(1)
    b_pr1 = s_pr.n
    def te_d6(te, t_d1=t_d1):
        te.wait_ge(s_dq.h, t_d1)
        n = 0
        for kc in range(KC):
            for j in range(2):
                n += 1
                if n > 2:
                    te.wait_ge(s_dq.h, t_d1 + n - 2)
                te.matmul(out=ps[0:2, n % 2, :], lhsT=g64[:],
                          rhs=awb[:, kc, j * 512:(j + 1) * 512],
                          start=True, stop=True, skip_group_check=True).then_inc(s_pr.h, 1)
    on(k, "tensor", te_d6)
    s_pr.n += 16
    def v_d6(v, b_pr1=b_pr1):
        n = 0
        for kc in range(KC):
            for j in range(2):
                n += 1
                v.wait_ge(s_pr.h, b_pr1 + n)
                v.tensor_copy(out=gmS[:, kc, j * 512:(j + 1) * 512], in_=ps[0:2, n % 2, :])
                v.drain().then_inc(s_dq.h, 1)
        v.tensor_scalar(out=hsb[:], in0=gmS[:], scalar1=1.0 / 64.0, scalar2=1e-8,
                        op0=ALU.mult, op1=ALU.max)
        v.drain()
        v.tensor_scalar(out=hsb[:], in0=hsb[:], scalar1=0.5, scalar2=None, op0=ALU.mult)
        v.drain().then_inc(s_dq.h, 1)
    on(k, "vector", v_d6)
    s_dq.n += 17
    t_d2 = s_dq.n
    b_pr2 = s_pr.n
    def te_d7(te, t_d2=t_d2):
        te.wait_ge(s_dq.h, t_d2)
        n = 0
        for kc in range(KC):
            for j in range(2):
                n += 1
                if n > 2:
                    te.wait_ge(s_dq.h, t_d2 + n - 2)
                te.matmul(out=ps[:, 2 + n % 2, :], lhsT=g64T[:],
                          rhs=hsb[:, kc, j * 512:(j + 1) * 512],
                          start=True, stop=True, skip_group_check=True).then_inc(s_pr.h, 1)
    on(k, "tensor", te_d7)
    s_pr.n += 16
    def v_d7(v, b_pr2=b_pr2):
        n = 0
        for kc in range(KC):
            for j in range(2):
                n += 1
                v.wait_ge(s_pr.h, b_pr2 + n)
                sl = (slice(None), kc, slice(j * 512, (j + 1) * 512))
                v.tensor_tensor(out=awb[sl], in0=awb[sl], in1=ps[:, 2 + n % 2, :], op=ALU.is_gt)
                v.drain()
                v.scalar_tensor_tensor(out=awb[sl], in0=awb[sl], scalar=2.0,
                                       in1=ps[:, 2 + n % 2, :], op0=ALU.mult, op1=ALU.mult)
                v.drain()
                v.tensor_tensor(out=wqs[sl], in0=awb[sl], in1=wqs[sl], op=ALU.mult)
                v.drain().then_inc(s_dq.h, 1)
    on(k, "vector", v_d7)
    s_dq.n += 16
    k.barrier()

    featP = k.sb("l0a_fp", [P, KC, 16], BF16, stack=st)
    featO = k.sb("l0a_fo", [P, KC, 16], BF16, stack=st)
    tabv = k.sb("l0a_tab", [1, 3], F32, stack=st)
    pf = k.sb("l0a_pf", [16, 1024], F32, stack=st)
    xs6 = k.sb("l0a_xs6", [16, S], BF16, stack=st)
    xf = k.sb("l0a_xf", [16, S], F32, stack=st)
    xa = k.sb("l0a_xa", [16, S], BF16, stack=st)
    xs1 = k.sb("l0a_xs1", [16, S], BF16, stack=st)
    Abuf = k.sb("l0a_A", [16, 1024], BF16, stack=st)
    Bbuf = k.sb("l0a_B", [16, 1024], BF16, stack=st)
    th16 = k.sb("l0a_th", [16, 3], F32, stack=st)
    b_wl7 = s_wl.n
    def gl7(e):
        e.dma_start(out=featP[:], in_=aps["L0a_featP"].rearrange("p kc m -> p (kc m)")).then_inc(s_wl.h, 16)
        e.dma_start(out=featO[:], in_=aps["L0a_featO"].rearrange("p kc m -> p (kc m)")).then_inc(s_wl.h, 16)
        e.dma_start(out=tabv[:, 0:1], in_=aps["L0_attn_theta"][:]).then_inc(s_wl.h, 16)
        e.dma_start(out=tabv[:, 1:2], in_=aps["L0_attn_alpha"][:]).then_inc(s_wl.h, 16)
        e.dma_start(out=tabv[:, 2:3], in_=aps["L0_attn_beta"][:]).then_inc(s_wl.h, 16)
    on(k, "gpsimd", gl7)
    s_wl.inc(80)

    b_pr9 = s_pr.n
    def te_f(te):
        te.wait_ge(s_wl.h, b_wl7 + 80)
        te.matmul(out=ps[0:16, 3, 0:3], lhsT=g["ones_row_f32"][:, 0:16], rhs=tabv[:],
                  start=True, stop=True, skip_group_check=True).then_inc(s_pr.h, 1)
        for j in range(2):
            for kc in range(KC):
                ins = te.matmul(out=ps[0:16, j, :], lhsT=featO[:, kc, :],
                                rhs=wqs[:, kc, j * 512:(j + 1) * 512],
                                start=(kc == 0), stop=(kc == KC - 1), skip_group_check=True)
            ins.then_inc(s_pr.h, 1)
    on(k, "tensor", te_f)
    s_pr.n += 3

    s_xf = k.sem("l0a_xf")
    b_xf = s_xf.n
    b_pr10 = s_pr.n
    def te_f2(te, b_xf=b_xf):
        for tc in range(TC):
            te.wait_ge(s_xf.h, b_xf + (tc if tc >= 2 else 1))
            for kc in range(KC):
                ins = te.matmul(out=ps[0:16, tc % 2, :], lhsT=featP[:, kc, :],
                                rhs=yb[:, kc, tc * 512:(tc + 1) * 512],
                                start=(kc == 0), stop=(kc == KC - 1), skip_group_check=True)
            ins.then_inc(s_pr.h, 1)
    on(k, "tensor", te_f2)
    s_pr.n += TC

    def v_f(v, b_pr9=b_pr9, b_pr10=b_pr10, b_xf=b_xf):
        v.wait_ge(s_pr.h, b_pr9 + 1)
        v.tensor_copy(out=th16[:], in_=ps[0:16, 3, 0:3])
        v.drain()
        v.wait_ge(s_pr.h, b_pr9 + 3)
        v.tensor_copy(out=pf[:, 0:512], in_=ps[0:16, 0, :])
        v.tensor_copy(out=pf[:, 512:1024], in_=ps[0:16, 1, :])
        v.drain().then_inc(s_xf.h, 1)
        for tc in range(TC):
            v.wait_ge(s_pr.h, b_pr10 + tc + 1)
            v.tensor_copy(out=xf[:, tc * 512:(tc + 1) * 512], in_=ps[0:16, tc % 2, :])
            v.drain().then_inc(s_xf.h, 1)
    on(k, "vector", v_f)
    t_pf = b_xf + 1
    s_xf.n += 1 + TC
    t_xf = s_xf.n

    s_sig = k.sem("l0a_sig")
    def sc_f(sc, t_pf=t_pf, t_xf=t_xf):
        sc.wait_ge(s_xf.h, t_pf)
        sc.activation(out=pf[:], in_=pf[:], func=AF.Sigmoid, scale=5.0, accum_out=None)
        sc.drain().then_inc(s_sig.h, 1)
        sc.wait_ge(s_xf.h, t_xf)
        sc.activation(out=xs6[:], in_=xf[:], func=AF.Sigmoid, scale=5.0)
        sc.drain().then_inc(s_sig.h, 1)
    pfr = k.sb("l0a_pfr", [16, 1024], F32, stack=st)
    pa_ = k.sb("l0a_pa", [16, 1024], F32, stack=st)
    tA_ = k.sb("l0a_tA", [16, 1024], F32, stack=st)
    tB_ = k.sb("l0a_tB", [16, 1024], F32, stack=st)
    def v_keep(v, t_pf=t_pf):
        v.wait_ge(s_xf.h, t_pf)
        v.tensor_copy(out=pfr[:], in_=pf[:])
        v.drain().then_inc(s_sig.h, 1)
    on(k, "vector", v_keep)
    t_keep = s_sig.inc(1)
    def sc_f2(sc, t_keep=t_keep, t_xf=t_xf):
        sc.wait_ge(s_sig.h, t_keep)
        sc.activation(out=th16[:], in_=th16[:], func=AF.Abs)
        sc.drain()
        sc.activation(out=pf[:], in_=pf[:], func=AF.Sigmoid, scale=5.0)
        sc.drain().then_inc(s_sig.h, 1)
        sc.wait_ge(s_xf.h, t_xf)
        sc.activation(out=xs6[:], in_=xf[:], func=AF.Sigmoid, scale=5.0)
        sc.drain().then_inc(s_sig.h, 1)
    on(k, "scalar", sc_f2)
    s_sig.inc(2)
    t_sig = s_sig.n

    def v_ab(v, t_sig=t_sig):
        v.wait_ge(s_sig.h, t_sig)
        v.tensor_tensor(out=xa[:], in0=xf[:], in1=xs6[:], op=ALU.mult)
        v.tensor_scalar(out=xs1[:], in0=xs6[:], scalar1=-1.0, scalar2=1.0,
                        op0=ALU.mult, op1=ALU.add)
        v.drain()
        v.tensor_tensor(out=pa_[:], in0=pfr[:], in1=pf[:], op=ALU.mult)
        v.drain()
        v.tensor_scalar(out=tA_[:], in0=pa_[:], scalar1=th16[:, 0:1], scalar2=None, op0=ALU.mult)
        v.tensor_scalar(out=tB_[:], in0=pf[:], scalar1=th16[:, 1:2],
                        scalar2=th16[:, 1:2], op0=ALU.mult, op1=ALU.subtract)
        v.drain()
        v.tensor_tensor(out=Abuf[:], in0=tA_[:], in1=tB_[:], op=ALU.add)
        v.tensor_scalar(out=Bbuf[:], in0=pa_[:], scalar1=th16[:, 2:3], scalar2=-1.0,
                        op0=ALU.mult, op1=ALU.mult)
        v.drain().then_inc(s_sig.h, 1)
    on(k, "vector", v_ab)
    t_ab = s_sig.inc(1)
    k.barrier()

    ra = ResidAdder(k, g, stack=st)
    b_pr11 = s_pr.n
    b_add = ra.s_add.n
    def te_o(te, b_add=b_add):
        n = 0
        for t in range(TT):
            for nn in range(2):
                n += 1
                if n > 2:
                    te.wait_ge(ra.s_add.h, b_add + n - 2)
                te.matmul(out=ps[:, n % 2, :], lhsT=xa[:, t * 128:(t + 1) * 128],
                          rhs=Abuf[:, nn * 512:(nn + 1) * 512],
                          start=True, stop=False, skip_group_check=True)
                te.matmul(out=ps[:, n % 2, :], lhsT=xs1[:, t * 128:(t + 1) * 128],
                          rhs=Bbuf[:, nn * 512:(nn + 1) * 512],
                          start=False, stop=True, skip_group_check=True).then_inc(s_pr.h, 1)
    on(k, "tensor", te_o)
    s_pr.n += TT * 2

    n = 0
    for t in range(TT):
        ra.prepare(t)
        for nn in range(2):
            n += 1
            ra.add(t, nn * 512, (nn + 1) * 512, ps[:, n % 2, :], wait=(s_pr, b_pr11 + n))
        ra.flush(t)
    ra.done()
    k.barrier()
    st.close()

BLOCK_EMITTERS["l0attn"] = emit_l0attn


def emit_l3attn(k, g, aps, upto=99):
    st = ExitStack()
    xln = g["xln"]
    ps = g["ps_mm"]
    pst = g["ps_tr"]

    Q = k.sb("l3a_q", [P, KC, S], BF16, stack=st)
    K_ = k.sb("l3a_k", [P, KC, S], BF16, stack=st)
    V_ = k.sb("l3a_v", [P, KC, S], BF16, stack=st)
    wbuf = k.sb("l3a_w", [P, KC, 1024], BF16, stack=st)
    biasS = k.sb("l3a_b", [P, 4 * KC], F32, stack=st)
    ssqK = k.sb("l3a_ssqk", [16, S], F32, stack=st)
    rnb = k.sb("l3a_rnb", [16, S], BF16, stack=st)
    maxr = k.sb("l3a_maxr", [16, 4], F32, stack=st)
    ncb3 = k.sb("l3a_ncb", [P, 16], F32, stack=st)
    crow3 = k.sb("l3a_crow", [1, 16], F32, stack=st)
    sq3 = k.sb("l3a_sq", [P, S], BF16, stack=st)
    st_h = ExitStack()
    eb = k.sb("l3a_eb", [P, 2, 1024], BF16, stack=st_h)
    vbar = k.sb("l3a_vbar", [64, 1], F32, stack=st_h)
    Vh = k.sb("l3a_vh", [64, S], BF16, stack=st_h)
    Wh = k.sb("l3a_wh", [64, S], BF16, stack=st_h)
    Th = k.sb("l3a_th", [64, S], F32, stack=st_h)
    Kh = k.sb("l3a_kh", [64, S], BF16, stack=st_h)
    Qh = k.sb("l3a_qh", [64, S], BF16, stack=st_h)
    Knh = k.sb("l3a_knh", [64, S], BF16, stack=st_h)
    vtok = k.sb("l3a_vt", [P, TT, 65], BF16, stack=st_h)
    v1tok = k.sb("l3a_v1t", [P, TT, 65], BF16, stack=st_h)
    vttok = k.sb("l3a_vtt", [P, TT, 65], BF16, stack=st_h)
    attnA = xln
    g["l3aQ"], g["l3aK"], g["l3aV"], g["l3arnb"], g["l3ancb"] = Q, K_, V_, rnb, ncb3
    zrow = ssqK[0:1, :]
    zrb = sq3[0:1, :]

    s_wl = k.sem("wl")
    s_pr = k.sem("pr")
    s_ev = k.sem("ev")
    s_tp = k.sem("ln_tp")
    s_ex = k.sem("l0a_ex")
    s_yz = k.sem("l0a_yz")
    s_ze = k.sem("l0a_ze")

    b_wl = s_wl.n
    on(k, "gpsimd", lambda e: e.dma_start(out=biasS[:], in_=aps["L3a_biases"][:]).then_inc(s_wl.h, 16))
    s_wl.inc(16)

    for sec, (wkey, dst, boff) in enumerate([("L3a_wqt", Q, 0), ("L3a_wkt", K_, KC),
                                             ("L3a_wvt", V_, 2 * KC)]):
        b_wl1 = s_wl.n
        on(k, "gpsimd", lambda e, wkey=wkey: e.dma_start(
            out=wbuf[:], in_=aps[wkey].rearrange("mc p n -> p mc n")).then_inc(s_wl.h, 16))
        s_wl.inc(16)
        b_pr1, b_ev1 = s_pr.n, s_ev.n
        def te_p(te, b_wl1=b_wl1, b_ev1=b_ev1):
            te.wait_ge(s_wl.h, b_wl1 + 16)
            n = 0
            for mc in range(KC):
                for tc in range(TC):
                    n += 1
                    if n > 2:
                        te.wait_ge(s_ev.h, b_ev1 + n - 2)
                    for kc in range(KC):
                        ins = te.matmul(out=ps[:, n % 2, :], lhsT=wbuf[:, mc, kc * 128:(kc + 1) * 128],
                                        rhs=xln[:, kc, tc * 512:(tc + 1) * 512],
                                        start=(kc == 0), stop=(kc == KC - 1))
                    ins.then_inc(s_pr.h, 1)
        on(k, "tensor", te_p)
        s_pr.n += KC * TC
        def sc_p(sc, dst=dst, boff=boff, b_pr1=b_pr1):
            n = 0
            for mc in range(KC):
                for tc in range(TC):
                    n += 1
                    sc.wait_ge(s_pr.h, b_pr1 + n)
                    sc.activation(out=dst[:, mc, tc * 512:(tc + 1) * 512], in_=ps[:, n % 2, :],
                                  func=AF.Identity, bias=biasS[:, boff + mc:boff + mc + 1]).then_inc(s_ev.h, 1)
        on(k, "scalar", sc_p)
        s_ev.n += KC * TC
        k.barrier()
        if upto <= 0:
            break

    if upto <= 1:
        st_h.close()
        st.close()
        return

    s_sqd = k.sem("l0a_sqd")
    for which, src_ in enumerate([K_, Q]):
        b_sqd = s_sqd.n
        b_ev4 = s_ev.n
        def sc_s(sc, src_=src_, b_sqd=b_sqd):
            for cc in range(KC):
                if cc >= 1:
                    sc.wait_ge(s_sqd.h, b_sqd + cc)
                sc.activation(out=sq3[:], in_=src_[:, cc, :], func=AF.Square)
                sc.drain().then_inc(s_ev.h, 1)
        on(k, "scalar", sc_s)
        s_ev.n += KC
        b_pr4 = s_pr.n
        def te_s(te, b_ev4=b_ev4):
            for cc in range(KC):
                te.wait_ge(s_ev.h, b_ev4 + cc + 1)
                for tc in range(TC):
                    ins = te.matmul(out=ps[0:16, tc, :], lhsT=g["sum64"][:, cc, :],
                                    rhs=sq3[:, tc * 512:(tc + 1) * 512],
                                    start=(cc == 0), stop=(cc == KC - 1), skip_group_check=True)
                ins.then_inc(s_sqd.h, 1)
            te.drain().then_inc(s_pr.h, 1)
        on(k, "tensor", te_s)
        s_sqd.n += KC
        s_pr.n += 1
        t_prs = s_pr.n
        def v_s(v, which=which, t_prs=t_prs):
            v.wait_ge(s_pr.h, t_prs)
            for tc in range(TC):
                v.tensor_copy(out=ssqK[:, tc * 512:(tc + 1) * 512], in_=ps[0:16, tc, :])
            v.drain().then_inc(s_yz.h, 1)
        on(k, "vector", v_s)
        t_y1 = s_yz.inc(1)
        def sc_n(sc, t_y1=t_y1):
            sc.wait_ge(s_yz.h, t_y1)
            sc.activation(out=ssqK[:], in_=ssqK[:], func=AF.Sqrt)
            sc.drain().then_inc(s_yz.h, 1)
        on(k, "scalar", sc_n)
        t_y2 = s_yz.inc(1)
        def v_n(v, which=which, t_y2=t_y2):
            v.wait_ge(s_yz.h, t_y2)
            v.tensor_reduce(out=maxr[:, which:which + 1], in_=ssqK[:],
                            axis=mybir.AxisListType.X, op=ALU.max)
            v.drain()
            if which == 0:
                v.tensor_scalar(out=ssqK[:], in0=ssqK[:], scalar1=1e-6, scalar2=None, op0=ALU.max)
                v.drain()
                v.reciprocal(out=ssqK[:], in_=ssqK[:])
                v.drain()
                v.tensor_copy(out=rnb[:], in_=ssqK[:])
                v.drain()
            else:
                v.scalar_tensor_tensor(out=maxr[:, 2:3], in0=maxr[:, 0:1], scalar=0.125,
                                       in1=maxr[:, 1:2], op0=ALU.mult, op1=ALU.mult)
                v.drain()
            v.nop().then_inc(s_yz.h, 1)
        on(k, "vector", v_n)
        s_yz.inc(1)
    k.barrier()

    b_pr6 = s_pr.n
    def te_c(te):
        te.transpose(out=ps[0:1, 3, 0:16], in_=maxr[:, 2:3], identity=g["identf"][0:16, 0:16])
        te.drain().then_inc(s_pr.h, 1)
    on(k, "tensor", te_c)
    s_pr.n += 1
    def v_c(v, b_pr6=b_pr6):
        v.wait_ge(s_pr.h, b_pr6 + 1)
        v.tensor_copy(out=crow3[:], in_=ps[0:1, 3, 0:16])
        v.drain().then_inc(s_yz.h, 1)
    on(k, "vector", v_c)
    t_c1 = s_yz.inc(1)
    b_pr7 = s_pr.n
    def te_c2(te, t_c1=t_c1):
        te.wait_ge(s_yz.h, t_c1)
        te.matmul(out=ps[:, 3, 0:16], lhsT=g["ones_row_f32"][:], rhs=crow3[:],
                  start=True, stop=True, skip_group_check=True).then_inc(s_pr.h, 1)
    on(k, "tensor", te_c2)
    s_pr.n += 1
    def v_c2(v, b_pr7=b_pr7):
        v.wait_ge(s_pr.h, b_pr7 + 1)
        v.tensor_scalar(out=ncb3[:], in0=ps[:, 3, 0:16], scalar1=-1.0, scalar2=None, op0=ALU.mult)
        v.drain().then_inc(s_yz.h, 1)
    on(k, "vector", v_c2)
    s_yz.inc(1)
    k.barrier()

    ps_sT = ps[:, 0:4, :]
    ps_yz = k.ps("l3a_pyz", [65, 512], F32, stack=st)

    def ev_pass(h, lhs_tok, dst_fm, kind):
        b_sc, b_ex, b_yz, b_ze = s_pr.n, s_ex.n, s_yz.n, s_ze.n
        NP = TC * TT // 2

        def score_of(te, i):
            ic, jt = i // TT, i % TT
            if kind == "sim":
                te.matmul(out=ps_sT[:, (i % 4), :], lhsT=Knh[:, jt * 128:(jt + 1) * 128],
                          rhs=Knh[:, ic * 512:(ic + 1) * 512],
                          start=True, stop=True, skip_group_check=True).then_inc(s_pr.h, 1)
            else:
                te.matmul(out=ps_sT[:, (i % 4), :], lhsT=Kh[:, jt * 128:(jt + 1) * 128],
                          rhs=Qh[:, ic * 512:(ic + 1) * 512],
                          start=True, stop=True, skip_group_check=True).then_inc(s_pr.h, 1)

        def pv_of(te, i, b_ex, b_ze):
            ic, jt = i // TT, i % TT
            m = i // 2
            te.wait_ge(s_ex.h, b_ex + m + 1)
            if jt == 0:
                te.wait_ge(s_ze.h, b_ze + 2 * ic)
            ins = te.matmul(out=ps_yz[:], lhsT=lhs_tok[:, jt, :],
                            rhs=eb[:, m % 2, (i % 2) * 512:(i % 2 + 1) * 512],
                            start=(jt == 0), stop=(jt == TT - 1), skip_group_check=True)
            if jt == TT - 1:
                ins.then_inc(s_yz.h, 1)

        def te_h(te, h=h, lhs_tok=lhs_tok, kind=kind, b_ex=b_ex, b_ze=b_ze):
            for m in range(NP):
                if m >= 2:
                    te.wait_ge(s_ex.h, b_ex + m - 1)
                score_of(te, 2 * m)
                score_of(te, 2 * m + 1)
                if m >= 1:
                    pv_of(te, 2 * m - 2, b_ex, b_ze)
                    pv_of(te, 2 * m - 1, b_ex, b_ze)
            pv_of(te, 2 * NP - 2, b_ex, b_ze)
            pv_of(te, 2 * NP - 1, b_ex, b_ze)
        on(k, "tensor", te_h)
        s_pr.n += TC * TT
        s_yz.n += TC

        def sc_h(sc, h=h, kind=kind, dst_fm=dst_fm, b_sc=b_sc, b_yz=b_yz):
            for m in range(NP):
                sc.wait_ge(s_pr.h, b_sc + 2 * m + 2)
                s0 = (2 * m) % 4
                if kind == "sim":
                    sc.activation(out=eb[:, m % 2, :], in_=ps_sT[:, s0:s0 + 2, :],
                                  func=AF.Exp, scale=0.125).then_inc(s_ex.h, 1)
                else:
                    sc.activation(out=eb[:, m % 2, :], in_=ps_sT[:, s0:s0 + 2, :],
                                  func=AF.Exp, scale=0.125,
                                  bias=ncb3[:, h:h + 1]).then_inc(s_ex.h, 1)
                if (2 * m + 1) % TT == TT - 1:
                    ic = (2 * m + 1) // TT
                    sc.wait_ge(s_yz.h, b_yz + ic + 1)
                    sc.activation(out=dst_fm[:, ic * 512:(ic + 1) * 512], in_=ps_yz[0:64, :],
                                  func=AF.Copy).then_inc(s_ze.h, 1)
        on(k, "scalar", sc_h)
        s_ex.n += NP

        def v_h(v, b_yz=b_yz, b_ze=b_ze):
            for ic in range(TC):
                v.wait_ge(s_ze.h, b_ze + 2 * ic + 1)
                v.tensor_copy(out=zrow[:, ic * 512:(ic + 1) * 512],
                              in_=ps_yz[64:65, :]).then_inc(s_ze.h, 1)
        on(k, "vector", v_h)
        s_ze.n += 2 * TC

    def normalize_fm(dst_fm, with_vbar):
        s_nf = k.sem("l3a_nf")
        t0_ = s_nf.n
        def v_r(sc, t0_=t0_):
            sc.activation(out=zrow[:], in_=zrow[:], func=AF.Ln)
            sc.drain()
            sc.activation(out=zrb[:], in_=zrow[:], func=AF.Exp, scale=-1.0)
            sc.drain().then_inc(s_nf.h, 1)
        on(k, "scalar", v_r)
        s_nf.inc(1)
        b_pr8 = s_pr.n
        def te_b(te, t0_=t0_):
            te.wait_ge(s_nf.h, t0_ + 1)
            for tc in range(TC):
                if tc >= 1:
                    te.wait_ge(s_nf.h, t0_ + 1 + tc)
                te.matmul(out=ps[0:64, 3, :], lhsT=g["ones_row_bf"][:, 0:64],
                          rhs=zrb[:, tc * 512:(tc + 1) * 512],
                          start=True, stop=True, skip_group_check=True).then_inc(s_pr.h, 1)
        on(k, "tensor", te_b)
        s_pr.n += TC
        def v_m(v, dst_fm=dst_fm, with_vbar=with_vbar, b_pr8=b_pr8, t0_=t0_):
            for tc in range(TC):
                v.wait_ge(s_pr.h, b_pr8 + tc + 1)
                sl = (slice(None), slice(tc * 512, (tc + 1) * 512))
                v.tensor_tensor(out=Th[sl], in0=dst_fm[sl], in1=ps[0:64, 3, :],
                                op=ALU.mult).then_inc(s_nf.h, 1)
                if with_vbar:
                    v.tensor_scalar(out=dst_fm[sl], in0=Th[sl], scalar1=1.0 - 1e-6,
                                    scalar2=vbar[:], op0=ALU.mult, op1=ALU.add)
                else:
                    v.tensor_copy(out=dst_fm[sl], in_=Th[sl])
            v.drain().then_inc(s_nf.h, 1)
        on(k, "vector", v_m)
        s_nf.n += TC + 1

    def tokify(src_fm, dst_tok):
        b_tp = s_tp.n
        b_ev9 = s_ev.n
        def te_t(te, b_tp=b_tp, b_ev9=b_ev9, src_fm=src_fm):
            for half in range(2):
                if half >= 1:
                    te.wait_ge(s_ev.h, b_ev9 + half)
                for jt in range(8):
                    t = half * 8 + jt
                    ins = te.transpose(out=pst[:, half % 2, jt * 64:(jt + 1) * 64],
                                       in_=src_fm[:, t * 128:(t + 1) * 128],
                                       identity=g["ident"][0:64, 0:64])
                ins.then_inc(s_tp.h, 1)
        on(k, "tensor", te_t)
        s_tp.n += 2
        def sc_t(sc, b_tp=b_tp, dst_tok=dst_tok):
            for half in range(2):
                sc.wait_ge(s_tp.h, b_tp + half + 1)
                sc.activation(out=dst_tok[:, half * 8:(half + 1) * 8, 0:64],
                              in_=pst[:, half % 2, 0:512].rearrange("p (jt c) -> p jt c", c=64),
                              func=AF.Copy).then_inc(s_ev.h, 1)
        on(k, "scalar", sc_t)
        s_ev.n += 2

    for h in range(16):
        h2, hc = h % 2, h // 2
        s_cp = k.sem("l3a_cp")
        t_cp = s_cp.n
        def sc_cp(sc, h2=h2, hc=hc):
            sc.activation(out=Vh[:], in_=V_[:, hc, :][h2 * 64:(h2 + 1) * 64, :], func=AF.Copy)
            sc.activation(out=Kh[:], in_=K_[:, hc, :][h2 * 64:(h2 + 1) * 64, :], func=AF.Copy)
            sc.activation(out=Qh[:], in_=Q[:, hc, :][h2 * 64:(h2 + 1) * 64, :],
                          func=AF.Copy).then_inc(s_cp.h, 1)
        on(k, "scalar", sc_cp)
        s_cp.inc(1)
        b_pr9 = s_pr.n
        def te_kn(te, h=h, t_cp=t_cp):
            te.wait_ge(s_cp.h, t_cp + 1)
            for tc in range(TC):
                te.matmul(out=ps[0:64, 3, :], lhsT=g["sel16e"][:, h * 128:h * 128 + 64],
                          rhs=rnb[:, tc * 512:(tc + 1) * 512],
                          start=True, stop=True, skip_group_check=True).then_inc(s_pr.h, 1)
                te.wait_ge(s_cp.h, t_cp + 1 + tc + 1)
        on(k, "tensor", te_kn)
        s_pr.n += TC
        def v_kn(v, b_pr9=b_pr9, t_cp=t_cp):
            for tc in range(TC):
                v.wait_ge(s_pr.h, b_pr9 + tc + 1)
                sl = (slice(None), slice(tc * 512, (tc + 1) * 512))
                v.tensor_tensor(out=Knh[sl], in0=Kh[sl], in1=ps[0:64, 3, :],
                                op=ALU.mult).then_inc(s_cp.h, 1)
            v.tensor_reduce(out=vbar[:], in_=Vh[:], axis=mybir.AxisListType.X, op=ALU.add)
            v.drain()
            v.tensor_scalar(out=vbar[:], in0=vbar[:], scalar1=1e-6 / S, scalar2=None,
                            op0=ALU.mult).then_inc(s_cp.h, 1)
        on(k, "vector", v_kn)
        s_cp.inc(TC + 1)
        k.barrier()

        tokify(Vh, vtok)
        on(k, "vector", lambda v: v.memset(vtok[:, :, 64:65], 1.0))
        on(k, "vector", lambda v: v.memset(v1tok[:, :, 64:65], 1.0))
        on(k, "vector", lambda v: v.memset(vttok[:, :, 64:65], 1.0))
        k.barrier()

        ev_pass(h, vtok, Wh, "sim")
        normalize_fm(Wh, with_vbar=True)
        k.barrier()
        tokify(Wh, v1tok)
        k.barrier()
        ev_pass(h, v1tok, Th, "sim")
        normalize2_dst = Th
        normalize_fm(normalize2_dst, with_vbar=True)
        k.barrier()
        s_cb = k.sem("l3a_cb")
        t_cb = s_cb.n
        def v_cmb(v):
            v.tensor_scalar(out=Wh[:], in0=Wh[:], scalar1=0.0196 / 0.0004, scalar2=None, op0=ALU.mult)
            v.tensor_tensor(out=Th[:], in0=Th[:], in1=Wh[:], op=ALU.add)
            v.tensor_scalar(out=Th[:], in0=Th[:], scalar1=0.0004, scalar2=None, op0=ALU.mult)
            v.scalar_tensor_tensor(out=Wh[:], in0=Vh[:], scalar=0.98, in1=Th[:],
                                   op0=ALU.mult, op1=ALU.add).then_inc(s_cb.h, 1)
        on(k, "vector", v_cmb)
        s_cb.inc(1)
        k.barrier()
        tokify(Wh, vttok)
        k.barrier()
        ev_pass(h, vttok, Wh, "p0")
        normalize_fm(Wh, with_vbar=False)
        k.barrier()
        s_wr = k.sem("l3a_wr")
        t_wr = s_wr.n
        def sc_wr(sc, h2=h2, hc=hc):
            sc.activation(out=attnA[:, hc, :][h2 * 64:(h2 + 1) * 64, :], in_=Wh[:],
                          func=AF.Copy).then_inc(s_wr.h, 1)
        on(k, "scalar", sc_wr)
        s_wr.inc(1)
        k.barrier()

    if upto <= 2:
        st_h.close()
        st.close()
        return

    st_h.close()

    b_wl9 = s_wl.n
    on(k, "gpsimd", lambda e: e.dma_start(
        out=wbuf[:], in_=aps["L3a_wot"].rearrange("kc p n -> p kc n")).then_inc(s_wl.h, 16))
    s_wl.inc(16)
    boB = k.sb("l3a_bo", [1, 1024], BF16, stack=st)
    on(k, "gpsimd", lambda e: e.dma_start(out=boB[:], in_=aps["L3a_bob"][:]).then_inc(s_wl.h, 16))
    s_wl.inc(16)

    ra = ResidAdder(k, g, stack=st)
    b_pr11 = s_pr.n
    b_add = ra.s_add.n
    def te_o(te, b_wl9=b_wl9, b_add=b_add):
        te.wait_ge(s_wl.h, b_wl9 + 32)
        n = 0
        for t in range(TT):
            for nn in range(2):
                n += 1
                if n > 2:
                    te.wait_ge(ra.s_add.h, b_add + n - 2)
                for kc in range(KC):
                    te.matmul(out=ps[:, n % 2, :], lhsT=attnA[:, kc, t * 128:(t + 1) * 128],
                              rhs=wbuf[:, kc, nn * 512:(nn + 1) * 512],
                              start=(kc == 0), stop=False, skip_group_check=True)
                te.matmul(out=ps[:, n % 2, :], lhsT=g["ones_row_bf"][:, t * 0:128] if False else g["ones_row_bf"][:],
                          rhs=boB[:, nn * 512:(nn + 1) * 512],
                          start=False, stop=True, skip_group_check=True).then_inc(s_pr.h, 1)
    on(k, "tensor", te_o)
    s_pr.n += TT * 2

    n = 0
    for t in range(TT):
        ra.prepare(t)
        for nn in range(2):
            n += 1
            ra.add(t, nn * 512, (nn + 1) * 512, ps[:, n % 2, :], wait=(s_pr, b_pr11 + n))
        ra.flush(t)
    ra.done()
    k.barrier()
    st.close()

BLOCK_EMITTERS["l3attn"] = emit_l3attn


def build_program(blocks, host_arrays, debug=()):
    nc = bass.Bass("TRN2", target_bir_lowering=False)
    resid_in = nc.dram_tensor("resid_in", [S, D], F32, kind="ExternalInput").ap()
    resid_out = nc.dram_tensor("resid_out", [S, D], F32, kind="ExternalOutput").ap()
    scratch = [nc.dram_tensor(f"rscratch{i}", [S, D], F32, kind="Internal").ap() for i in range(2)]
    aps = declare_inputs(nc, host_arrays)
    with ExitStack() as ctx:
        block = ctx.enter_context(nc.Block())
        k = Kit(nc, ctx, block)
        g = {}
        g["xln"] = k.sb("xln", [P, KC, S], BF16)
        g["ident"] = k.sb("identc", [P, P], BF16)
        g["ones_row_bf"] = k.sb("onesrbf", [1, P], BF16)
        g["ones_row_f32"] = k.sb("onesrf", [1, P], F32)
        g["ones_col_bf"] = k.sb("onescbf", [P, 1], BF16)
        g["sel16e"] = k.sb("sel16e", [16, 16 * P], BF16)
        g["c_one"] = k.sb("conebf", [1, 1], BF16)
        g["identf"] = k.sb("identf", [P, P], F32)
        g["sum64"] = k.sb("gsum64", [P, KC, 16], BF16)
        g["ps_tr"] = k.ps("ps_tr", [P, 2, 1024], BF16)
        g["ps_mm"] = k.ps("ps_mm", [P, 4, 512], F32)

        sc_, tc_ = load_const(k, g["ident"], aps["c_ident_bf"][:])
        load_const(k, g["ones_row_bf"], aps["c_ones_row_bf"][:])
        load_const(k, g["ones_row_f32"], aps["c_ones_row_f32"][:])
        load_const(k, g["ones_col_bf"], aps["c_ones_col_bf"][:])
        load_const(k, g["c_one"], aps["c_one_bf"][:])
        load_const(k, g["identf"], aps["c_ident_f32"][:])
        sg_, tg_ = load_const(k, g["sum64"], aps["c_sum64_bf"].rearrange("kc p m -> p kc m"), queue="gpsimd")
        _, tlast = load_const(k, g["sel16e"], aps["c_sel16e_bf"][:])
        bar_w = k.sem("ldc")
        for e in ("tensor", "vector", "scalar"):
            def fw(eng, t=tlast, tg=tg_):
                eng.wait_ge(bar_w.h, t)
                eng.wait_ge(sg_.h, tg)
            on(k, e, fw)

        n = len(blocks)
        for i, b in enumerate(blocks):
            g["resid_src"] = resid_in if i == 0 else scratch[(i - 1) % 2]
            g["resid_dst"] = resid_out if i == n - 1 else scratch[i % 2]
            emit_ln(k, g)
            BLOCK_EMITTERS[b](k, g, aps)
        if debug:
            s_dbg = k.sem("dbg")
            names = list(debug)
            def fdbg(e):
                for name in names:
                    buf = g[name]
                    out_ap = nc.dram_tensor("dbg_" + name, list(buf.shape), buf.dtype,
                                            kind="ExternalOutput").ap()
                    e.dma_start(out=out_ap[:], in_=buf[:]).then_inc(s_dbg.h, 16)
                e.wait_ge(s_dbg.h, 16 * len(names))
            on(k, "sync", fdbg)
    return nc

BLOCK_EMITTERS["l2ffn"] = emit_l2ffn


def prep_weights(inputs):
    h = {}
    h["L2f_wt"] = moving_tiles(np.ascontiguousarray(inputs["L2_ffn_weight"].T))
    h["L3f_selt"] = lhsT_tiles(np.ascontiguousarray(inputs["L3_ffn_expert_sel"].T), mtile=16)
    h["L3f_keyst"] = np.stack([lhsT_tiles(inputs["L3_ffn_keys"][e])[0]
                               for e in range(16)])
    h["L3f_valst"] = np.stack([moving_tiles(inputs["L3_ffn_values"][e])[0]
                               for e in range(16)])
    h["L0f_w1t"] = lhsT_tiles(np.ascontiguousarray(inputs["L0_ffn_w1_w"].T))
    h["L0f_w2t"] = lhsT_tiles(np.ascontiguousarray(inputs["L0_ffn_w2_w"].T))
    h["L0f_h1t"] = lhsT_tiles(np.ascontiguousarray(inputs["L0_ffn_h1_w"].T))
    h["L0f_h2t"] = lhsT_tiles(np.ascontiguousarray(inputs["L0_ffn_h2_w"].T))
    h["L0f_w3t"] = lhsT_tiles(np.ascontiguousarray(inputs["L0_ffn_w3_w"].T))
    h["L0f_scales"] = np.concatenate([
        inputs["L0_ffn_w1_s"].reshape(32, 128),
        inputs["L0_ffn_w2_s"].reshape(32, 128),
        inputs["L0_ffn_h1_s"].reshape(32, 128),
        inputs["L0_ffn_h2_s"].reshape(32, 128),
        inputs["L0_ffn_w3_s"].reshape(8, 128)], axis=0).astype(np.float32)
    h["L0f_scales"] = np.ascontiguousarray(h["L0f_scales"].T)
    h["L2a_wqt"] = lhsT_tiles(np.ascontiguousarray(inputs["L2_attn_Wq"].T))
    h["L2a_wkt"] = lhsT_tiles(np.ascontiguousarray(inputs["L2_attn_Wk"].T))
    h["L2a_wvt"] = lhsT_tiles(np.ascontiguousarray(inputs["L2_attn_Wv"].T))
    h["L2a_projt"] = lhsT_tiles(np.ascontiguousarray(inputs["L2_attn_proj"]))
    h["L1f_sigsT"] = np.ascontiguousarray(inputs["L1_ffn_tile_sigs"].T)
    h["L1f_wt"] = np.stack([moving_tiles(np.ascontiguousarray(inputs["L1_ffn_W"][u].T))
                            for u in range(8)])
    h["L1_ffn_b"] = np.ascontiguousarray(inputs["L1_ffn_b"])
    h["L1a_wqkvt"] = lhsT_tiles(np.ascontiguousarray(inputs["L1_attn_w_qkv"]))
    h["L1a_woutt"] = moving_tiles(np.ascontiguousarray(inputs["L1_attn_w_out"]))
    h["L1a_bout"] = np.ascontiguousarray(inputs["L1_attn_b_out"].reshape(1, 1024))
    s_ = np.repeat(np.arange(3), 1024)
    hh_ = np.tile(np.repeat(np.arange(16), 64), 3)
    dd_ = np.tile(np.arange(64), 48)
    newpos = s_ * 1024 + (dd_ // 32) * 512 + (hh_ // 4) * 128 + (hh_ % 4) * 32 + (dd_ % 32)
    order = np.empty(3072, dtype=np.int64)
    order[newpos] = np.arange(3072)
    wT = np.ascontiguousarray(inputs["L0_attn_w_qkv"].T)
    wTp = wT[:, order]
    h["L0a_wqkvt"] = np.ascontiguousarray(
        wTp.reshape(8, 128, 3, 1024).transpose(2, 1, 0, 3))
    pT = np.ascontiguousarray(inputs["L0_attn_prototypes"].T)
    h["L0a_protot"] = np.ascontiguousarray(pT.reshape(8, 128, 1024).transpose(1, 0, 2))
    hv = np.repeat(np.arange(16), 64)
    dv = np.tile(np.arange(64), 16)
    newv = (dv // 32) * 512 + (hv // 4) * 128 + (hv % 4) * 32 + (dv % 32)
    orderv = np.empty(1024, dtype=np.int64)
    orderv[newv] = np.arange(1024)
    fT = np.ascontiguousarray(inputs["L0_attn_features"].T)
    h["L0a_featP"] = np.ascontiguousarray(fT[orderv].reshape(8, 128, 16).transpose(1, 0, 2))
    h["L0a_featO"] = np.ascontiguousarray(fT.reshape(8, 128, 16).transpose(1, 0, 2))
    h["L0_attn_q_gain"] = np.ascontiguousarray(inputs["L0_attn_q_gain"].reshape(16, 1))
    h["L0_attn_diff_lambda"] = np.ascontiguousarray(inputs["L0_attn_diff_lambda"].reshape(16, 1))
    h["L0_attn_theta"] = np.ascontiguousarray(inputs["L0_attn_theta"].reshape(1, 1))
    h["L0_attn_alpha"] = np.ascontiguousarray(inputs["L0_attn_alpha"].reshape(1, 1))
    h["L0_attn_beta"] = np.ascontiguousarray(inputs["L0_attn_beta"].reshape(1, 1))
    h["L3a_wqt"] = lhsT_tiles(np.ascontiguousarray(inputs["L3_attn_Wq"]))
    h["L3a_wkt"] = lhsT_tiles(np.ascontiguousarray(inputs["L3_attn_Wk"]))
    h["L3a_wvt"] = lhsT_tiles(np.ascontiguousarray(inputs["L3_attn_Wv"]))
    h["L3a_wot"] = moving_tiles(np.ascontiguousarray(inputs["L3_attn_Wo"]))
    h["L3a_bob"] = np.ascontiguousarray(inputs["L3_attn_bo"].reshape(1, 1024))
    h["L3a_biases"] = np.ascontiguousarray(np.concatenate(
        [inputs["L3_attn_bq"].reshape(8, 128), inputs["L3_attn_bk"].reshape(8, 128),
         inputs["L3_attn_bv"].reshape(8, 128), inputs["L3_attn_bo"].reshape(8, 128)],
        axis=0).T)
    return h



BLOCK_ORDER = ["l0attn", "l0ffn", "l1attn", "l1ffn", "l2attn", "l2ffn", "l3attn", "l3ffn"]
NCORES = 8

def kernel(**inputs):
    host = prep_weights(inputs)
    host.update(host_constants())
    resid = np.ascontiguousarray(inputs["L0_attn_x"], dtype=np.float32)
    nc = build_program(BLOCK_ORDER, host)
    in_maps = [dict(host, resid_in=np.ascontiguousarray(resid[c])) for c in range(NCORES)]
    res = run_bass_kernel_spmd(nc, in_maps, core_ids=list(range(NCORES)))
    out = [np.asarray(res.results[c]["resid_out"], dtype=np.float32) for c in range(NCORES)]
    return np.stack(out).reshape(8, 2048, 1024)


# version 05
# speedup vs baseline: 1.4464x; 1.0075x over previous
import math
import numpy as np
import concourse.bass as bass
import concourse.mybir as mybir
from concourse.bass_utils import run_bass_kernel_spmd
from contextlib import ExitStack

F32 = mybir.dt.float32
BF16 = mybir.dt.bfloat16
AF = mybir.ActivationFunctionType
ALU = mybir.AluOpType

P = 128
S = 2048
D = 1024
KC = D // P
TT = S // P
TC = S // 512
NH = 16
HD = 64


def lhsT_tiles(w_io, mtile=128):
    din, dout = w_io.shape
    kc, mc = din // P, dout // mtile
    t = w_io.reshape(kc, P, mc, mtile).transpose(2, 1, 0, 3)
    return np.ascontiguousarray(t.reshape(mc, P, kc * mtile))

def moving_tiles(w_io):
    din, dout = w_io.shape
    return np.ascontiguousarray(w_io.reshape(din // P, P, dout))


class Sem:
    __slots__ = ("h", "n")
    def __init__(self, h):
        self.h = h
        self.n = 0
    def inc(self, k=1):
        self.n += k
        return self.n

class Kit:
    def __init__(self, nc, ctx, block):
        self.nc, self.ctx, self.block = nc, ctx, block
        self._sems = {}
        self._uid = 0
    def uid(self):
        self._uid += 1
        return self._uid
    def sem(self, name):
        if name not in self._sems:
            self._sems[name] = Sem(self.ctx.enter_context(self.nc.semaphore(name)))
        return self._sems[name]
    def sb(self, name, shape, dt=F32, stack=None):
        return (stack or self.ctx).enter_context(
            self.nc.sbuf_tensor(f"{name}_{self.uid()}", shape, dt))
    def ps(self, name, shape, dt=F32, stack=None):
        return (stack or self.ctx).enter_context(
            self.nc.psum_tensor(f"{name}_{self.uid()}", shape, dt))
    def barrier(self):
        bar = self.sem("bar")
        tgt = bar.n + 5
        bar.n = tgt
        for name in ("sync", "gpsimd", "vector", "scalar", "tensor"):
            def f(eng, tgt=tgt, name=name):
                if name == "sync":
                    eng.nop().then_inc(self._sems["bar"].h, 1)
                else:
                    eng.drain().then_inc(self._sems["bar"].h, 1)
                eng.wait_ge(self._sems["bar"].h, tgt)
            getattr(self.block, name)(f)

BLOCK_EMITTERS = {}

def on(k, eng_name, f):
    getattr(k.block, eng_name)(f)

def load_const(k, dst, src_ap, queue="sync"):
    s = k.sem("ldc" if queue == "sync" else "ldcg")
    t = s.inc(16)
    on(k, queue, lambda e: e.dma_start(out=dst[:], in_=src_ap).then_inc(s.h, 16))
    return s, t


def host_constants():
    import ml_dtypes
    bf = ml_dtypes.bfloat16
    c = {}
    c["c_ident_bf"] = np.eye(P, dtype=bf)
    c["c_ones_row_f32"] = np.ones((1, P), dtype=np.float32)
    c["c_ones_row_bf"] = np.ones((1, P), dtype=bf)
    c["c_ones_col_bf"] = np.ones((P, 1), dtype=bf)
    c["c_one_bf"] = np.ones((1, 1), dtype=bf)
    se = np.zeros((16, 16 * P), dtype=np.float32)
    for e in range(16):
        se[e, e * P:(e + 1) * P] = 1.0
    c["c_sel16e_bf"] = se.astype(bf)
    pos = np.arange(S, dtype=np.float32)[:, None]
    div = np.exp(np.arange(0, 32, 2, dtype=np.float32) * (-math.log(10000.0) / 32))
    ang = (pos * div).astype(np.float32)
    pe = np.zeros((S, 32), dtype=np.float32)
    pe[:, 0::2] = np.sin(ang)
    pe[:, 1::2] = np.cos(ang)
    c["c_peT"] = np.ascontiguousarray(pe.T)
    ss = np.zeros((KC, P, NH), dtype=np.float32)
    sb_ = np.zeros((KC, NH, P), dtype=np.float32)
    for kc in range(KC):
        for r in range(P):
            hh = 2 * kc + r // 64
            ss[kc, r, hh] = 1.0
            sb_[kc, hh, r] = 1.0
    c["c_sum64_bf"] = ss.astype(bf)
    c["c_bc64_bf"] = sb_.astype(bf)
    cnt = np.full((1, S), 2.0, dtype=np.float32)
    cnt[0, [0, 1, S - 2, S - 1]] = 1.0
    c["c_l1a_count"] = cnt.astype(bf)
    c["c_ident_f32"] = np.eye(P, dtype=np.float32)
    g64 = np.zeros((P, 2), dtype=np.float32)
    g64[0:64, 0] = 1.0
    g64[64:128, 1] = 1.0
    c["c_g64_bf"] = g64.astype(bf)
    c["c_g64T_bf"] = np.ascontiguousarray(g64.T).astype(bf)
    sump = np.zeros((4, P, 16), dtype=np.float32)
    bcp = np.zeros((4, 16, P), dtype=np.float32)
    for v in range(4):
        for r in range(P):
            hh = 4 * v + r // 32
            sump[v, r, hh] = 1.0
            bcp[v, hh, r] = 1.0
    c["c_sumP_bf"] = sump.astype(bf)
    c["c_bcP_bf"] = bcp.astype(bf)
    ar = np.arange(0, 64, 2, dtype=np.float32)
    inv_freq = 1.0 / 10000.0 ** (ar / 64)
    ramp = np.clip((ar / 64 - 0.25) / 0.75, 0.0, 1.0)
    inv_freq = inv_freq / (ramp * 3.0 + 1.0)
    freqs = np.arange(S, dtype=np.float32)[:, None] * inv_freq[None, :]
    cosT = np.cos(freqs)
    sinT = np.sin(freqs)
    ridx = np.arange(P) % 32
    c["c_l0cos_bf"] = np.ascontiguousarray(cosT[:, ridx].T).astype(bf)
    c["c_l0sin_bf"] = np.ascontiguousarray(sinT[:, ridx].T).astype(bf)
    ltm = (np.arange(P)[:, None] <= np.arange(P)[None, :]).astype(np.float32)
    c["c_ltmask_bf"] = ltm.astype(bf)
    return c

def declare_inputs(nc, host):
    aps = {}
    for name, arr in host.items():
        dt = {"float32": F32, "bfloat16": BF16}[str(arr.dtype)]
        aps[name] = nc.dram_tensor(name, list(arr.shape), dt, kind="ExternalInput").ap()
    return aps


def emit_ln(k, g):
    nc = k.nc
    st = ExitStack()
    resid = k.sb("lnresid", [P, TT, D], F32, stack=st)
    st6 = k.sb("lnst6", [P, TT, 12], F32, stack=st)
    mv = k.sb("lnmv", [P, TT, 2], F32, stack=st)
    rstd = k.sb("lnrstd", [P, TT], F32, stack=st)
    veps = k.sb("lnveps", [P, TT], F32, stack=st)
    tmb = k.sb("lntmb", [P, 2, D], BF16, stack=st)
    xln = g["xln"]
    ident = g["ident"]
    pst = g["ps_tr"]

    s_ld = k.sem("ln_ld")
    s_v2s = k.sem("ln_v2s")
    s_s2v = k.sem("ln_s2v")
    s_tm = k.sem("ln_tm")
    s_tp = k.sem("ln_tp")
    s_ev = k.sem("ln_ev")

    t_ld = s_ld.inc(16)
    on(k, "sync", lambda e: e.dma_start(
        out=resid[:], in_=g["resid_src"].rearrange("(t p) d -> p t d", p=P)).then_inc(s_ld.h, 16))

    base_tm, base_tp, base_ev = s_tm.n, s_tp.n, s_ev.n

    def v1(v):
        v.wait_ge(s_ld.h, t_ld)
        for t in range(TT):
            v.bn_stats(out=st6[:, t, 0:6], in_=resid[:, t, 0:512])
            v.bn_stats(out=st6[:, t, 6:12], in_=resid[:, t, 512:1024])
        v.drain()
        for t in range(TT):
            v.bn_aggr(out=mv[:, t, :], in_=st6[:, t, :])
        v.drain()
        v.tensor_scalar(out=veps[:], in0=mv[:, :, 1:2], scalar1=1e-5, scalar2=None, op0=ALU.add)
        v.drain().then_inc(s_v2s.h, 1)
    on(k, "vector", v1)
    tv = s_v2s.inc(1)

    def sc1(sc):
        sc.wait_ge(s_v2s.h, tv)
        sc.activation(out=veps[:], in_=veps[:], func=AF.Sqrt)
        sc.drain().then_inc(s_s2v.h, 1)
    on(k, "scalar", sc1)
    ts_ = s_s2v.inc(1)

    def v2(v):
        v.wait_ge(s_s2v.h, ts_)
        v.reciprocal(out=rstd[:], in_=veps[:])
        v.drain()
        for t in range(TT):
            if t >= 2:
                v.wait_ge(s_tp.h, base_tp + t - 1)
            v.tensor_scalar(out=tmb[:, t % 2, :], in0=resid[:, t, :],
                            scalar1=mv[:, t, 0:1], scalar2=rstd[:, t:t + 1],
                            op0=ALU.subtract, op1=ALU.mult).then_inc(s_tm.h, 1)
    on(k, "vector", v2)
    s_tm.n += TT

    def te(te_):
        for t in range(TT):
            te_.wait_ge(s_tm.h, base_tm + t + 1)
            if t >= 2:
                te_.wait_ge(s_ev.h, base_ev + t - 1)
            for kc in range(KC):
                ins = te_.transpose(out=pst[:, t % 2, kc * 128:(kc + 1) * 128],
                                    in_=tmb[:, t % 2, kc * 128:(kc + 1) * 128],
                                    identity=ident[:])
                if kc == KC - 1:
                    ins.then_inc(s_tp.h, 1)
    on(k, "tensor", te)
    s_tp.n += TT

    def sc2(sc):
        for t in range(TT):
            sc.wait_ge(s_tp.h, base_tp + t + 1)
            sc.activation(out=xln[:, :, t * 128:(t + 1) * 128],
                          in_=pst[:, t % 2, :].rearrange("p (kc c) -> p kc c", kc=KC),
                          func=AF.Copy).then_inc(s_ev.h, 1)
    on(k, "scalar", sc2)
    s_ev.n += TT
    k.barrier()
    st.close()


class ResidAdder:
    def __init__(self, k, g, stack=None):
        self.k, self.g = k, g
        self.buf = k.sb("radd", [P, 2, D], F32, stack=stack)
        self.s_ld = k.sem("ra_ld")
        self.s_add = k.sem("ra_add")
        self.s_st = k.sem("ra_st")
        self.b_ld = self.s_ld.n
        self.b_st = self.s_st.n

    def prepare(self, t):
        kk, g = self.k, self.g
        def f(e, t=t):
            if t >= 2:
                e.wait_ge(self.s_st.h, self.b_st + 16 * (t - 1))
            e.dma_start(out=self.buf[:, t % 2, :],
                        in_=g["resid_src"].rearrange("(t p) d -> t p d", p=P)[t]).then_inc(self.s_ld.h, 16)
        on(kk, "sync", f)
        self.s_ld.inc(16)

    def add(self, t, c0, c1, psum_ap, scale_ap=None, wait=None):
        kk = self.k
        ld_tgt = self.b_ld + 16 * (t + 1)
        def f(v, t=t, c0=c0, c1=c1, psum_ap=psum_ap, scale_ap=scale_ap, wait=wait):
            v.wait_ge(self.s_ld.h, ld_tgt)
            if wait is not None:
                v.wait_ge(wait[0].h, wait[1])
            dst = self.buf[:, t % 2, c0:c1]
            if scale_ap is None:
                v.tensor_tensor(out=dst, in0=psum_ap, in1=dst, op=ALU.add).then_inc(self.s_add.h, 1)
            else:
                v.scalar_tensor_tensor(out=dst, in0=psum_ap, scalar=scale_ap, in1=dst,
                                       op0=ALU.mult, op1=ALU.add).then_inc(self.s_add.h, 1)
        on(kk, "vector", f)
        self.s_add.inc(1)

    def flush(self, t):
        kk, g = self.k, self.g
        tgt = self.s_add.n
        def f(e, t=t, tgt=tgt):
            e.wait_ge(self.s_add.h, tgt)
            e.dma_start(out=g["resid_dst"].rearrange("(t p) d -> t p d", p=P)[t],
                        in_=self.buf[:, t % 2, :]).then_inc(self.s_st.h, 16)
        on(kk, "sync", f)
        self.s_st.inc(16)

    def done(self):
        tgt = self.s_st.n
        on(self.k, "sync", lambda e: e.wait_ge(self.s_st.h, tgt))


def emit_l2ffn(k, g, aps):
    st = ExitStack()
    xln = g["xln"]
    wsign = k.sb("l2fw", [P, 8, 1024], BF16, stack=st)
    psum = g["ps_mm"]
    s_wl = k.sem("wl")
    s_sg = k.sem("l2f_sg")
    s_pr = k.sem("pr")

    t_wl = s_wl.inc(16)
    on(k, "gpsimd", lambda e: e.dma_start(
        out=wsign[:], in_=aps["L2f_wt"].rearrange("kc p n -> p kc n")).then_inc(s_wl.h, 16))

    def sc(sc_):
        sc_.wait_ge(s_wl.h, t_wl)
        sc_.activation(out=wsign[:], in_=wsign[:], func=AF.Sign)
        sc_.drain().then_inc(s_sg.h, 1)
    on(k, "scalar", sc)
    t_sg = s_sg.inc(1)

    ra = ResidAdder(k, g, stack=st)
    base_pr, base_add = s_pr.n, ra.s_add.n
    def te(te_):
        te_.wait_ge(s_sg.h, t_sg)
        n = 0
        for t in range(TT):
            for half in range(2):
                n += 1
                if n > 2:
                    te_.wait_ge(ra.s_add.h, base_add + n - 2)
                for kc in range(KC):
                    ins = te_.matmul(out=psum[:, n % 2, :],
                                     lhsT=xln[:, kc, t * 128:(t + 1) * 128],
                                     rhs=wsign[:, kc, half * 512:(half + 1) * 512],
                                     start=(kc == 0), stop=(kc == KC - 1))
                    if kc == KC - 1:
                        ins.then_inc(s_pr.h, 1)
    on(k, "tensor", te)
    s_pr.n += TT * 2

    n = 0
    for t in range(TT):
        ra.prepare(t)
        for half in range(2):
            n += 1
            ra.add(t, half * 512, (half + 1) * 512, psum[:, n % 2, :], wait=(s_pr, base_pr + n))
        ra.flush(t)
    ra.done()
    k.barrier()
    st.close()


def emit_l3ffn(k, g, aps):
    st = ExitStack()
    xln = g["xln"]
    selW = k.sb("l3f_selw", [P, 8 * 16], BF16, stack=st)
    keysW = k.sb("l3f_keys", [P, 16, 1024], BF16, stack=st)
    valsW = k.sb("l3f_vals", [P, 16, 1024], BF16, stack=st)
    selS = k.sb("l3f_sel", [16, S], BF16, stack=st)
    scb = k.sb("l3f_scb", [P, 2, 256], BF16, stack=st)
    rbuf = k.sb("l3f_rl", [P, 2, 256], BF16, stack=st)
    ps_out = g["ps_mm"]
    pp = [k.ps("l3f_pp0", [P, 512], F32, stack=st),
          k.ps("l3f_pp1", [P, 512], F32, stack=st)]
    ps_sel = g["ps_mm"][0:16, 0, :]

    s_wl = k.sem("wl")
    b_wl = s_wl.n
    def gl(e):
        e.dma_start(out=selW[:], in_=aps["L3f_selt"].rearrange("a p m -> p (a m)")).then_inc(s_wl.h, 16)
        e.dma_start(out=keysW[:], in_=aps["L3f_keyst"].rearrange("e p n -> p e n")).then_inc(s_wl.h, 16)
        e.dma_start(out=valsW[:], in_=aps["L3f_valst"].rearrange("e p n -> p e n")).then_inc(s_wl.h, 16)
    on(k, "gpsimd", gl)
    s_wl.inc(48)

    s_pr = k.sem("pr")
    b_pr = s_pr.n
    s_ev = k.sem("ev")
    b_ev = s_ev.n
    def te1(te):
        te.wait_ge(s_wl.h, b_wl + 16)
        for tc in range(TC):
            if tc >= 1:
                te.wait_ge(s_ev.h, b_ev + tc)
            for kc in range(KC):
                ins = te.matmul(out=ps_sel, lhsT=selW[:, kc * 16:(kc + 1) * 16],
                                rhs=xln[:, kc, tc * 512:(tc + 1) * 512],
                                start=(kc == 0), stop=(kc == KC - 1))
            ins.then_inc(s_pr.h, 1)
    on(k, "tensor", te1)
    s_pr.n += TC

    def sc1(sc):
        for tc in range(TC):
            sc.wait_ge(s_pr.h, b_pr + tc + 1)
            sc.activation(out=selS[:, tc * 512:(tc + 1) * 512], in_=ps_sel,
                          func=AF.Sigmoid).then_inc(s_ev.h, 1)
    on(k, "scalar", sc1)
    s_ev.n += TC

    ra = ResidAdder(k, g, stack=st)
    s_sc = k.sem("l3f_sc")
    s_bc = k.sem("l3f_bc")
    s_rl = k.sem("l3f_rlu")
    s_ss = k.sem("l3f_ss")
    b_sc, b_bc, b_rl, b_ss = s_sc.n, s_bc.n, s_rl.n, s_ss.n
    b_add = ra.s_add.n
    NTG = 8
    E = 16
    b_pr2 = s_pr.n

    def out_of(te, tg, e):
        nsc = tg * E + e + 1
        par = nsc % 2
        te.wait_ge(s_ss.h, b_ss + nsc)
        for tt2 in range(2):
            for nn in range(2):
                te.matmul(out=ps_out[:, tt2 * 2 + nn, :],
                          lhsT=scb[:, par, tt2 * 128:(tt2 + 1) * 128],
                          rhs=valsW[:, e, nn * 512:(nn + 1) * 512],
                          start=(e == 0), stop=(e == E - 1),
                          skip_group_check=True).then_inc(s_pr.h, 1)

    for tg in range(NTG):
        def te2(te, tg=tg):
            if tg == 0:
                te.wait_ge(s_ev.h, b_ev + TC)
                te.wait_ge(s_wl.h, b_wl + 48)
            else:
                te.wait_ge(ra.s_add.h, b_add + tg * 4)
            for e in range(E):
                nsc = tg * E + e + 1
                if nsc > 2:
                    te.wait_ge(s_ss.h, b_ss + nsc - 2)
                par = nsc % 2
                te.matmul(out=pp[par][:, 256:512], lhsT=g["sel16e"][:, e * 128:(e + 1) * 128],
                          rhs=selS[:, tg * 256:(tg + 1) * 256],
                          start=True, stop=True, skip_group_check=True).then_inc(s_bc.h, 1)
                for kc in range(KC):
                    ins = te.matmul(out=pp[par][:, 0:256], lhsT=keysW[:, e, kc * 128:(kc + 1) * 128],
                                    rhs=xln[:, kc, tg * 256:(tg + 1) * 256],
                                    start=(kc == 0), stop=(kc == KC - 1))
                ins.then_inc(s_sc.h, 1)
                if e >= 1:
                    out_of(te, tg, e - 1)
            out_of(te, tg, E - 1)
        on(k, "tensor", te2)
        s_bc.n += E
        s_sc.n += E
        s_pr.n += E * 4

        def sc2(sc, tg=tg):
            for e in range(E):
                n = tg * E + e + 1
                par = n % 2
                sc.wait_ge(s_sc.h, b_sc + n)
                if n > 2:
                    sc.wait_ge(s_ss.h, b_ss + n - 2)
                sc.activation(out=rbuf[:, par, :], in_=pp[par][:, 0:256],
                              func=AF.Relu).then_inc(s_rl.h, 1)
        on(k, "scalar", sc2)
        s_rl.n += E

        def v2(v, tg=tg):
            for e in range(E):
                n = tg * E + e + 1
                par = n % 2
                v.wait_ge(s_rl.h, b_rl + n)
                v.wait_ge(s_bc.h, b_bc + n)
                if n > 2:
                    v.wait_ge(s_pr.h, b_pr2 + (n - 2) * 4)
                v.tensor_tensor(out=scb[:, par, :], in0=rbuf[:, par, :],
                                in1=pp[par][:, 256:512], op=ALU.mult).then_inc(s_ss.h, 1)
        on(k, "vector", v2)
        s_ss.n += E

        t0 = tg * 2
        ra.prepare(t0)
        ra.prepare(t0 + 1)
        n_end = b_pr2 + (tg + 1) * E * 4
        for tt2 in range(2):
            for nn in range(2):
                ra.add(t0 + tt2, nn * 512, (nn + 1) * 512, ps_out[:, tt2 * 2 + nn, :],
                       wait=(s_pr, n_end))
        ra.flush(t0)
        ra.flush(t0 + 1)
    ra.done()
    k.barrier()
    st.close()

BLOCK_EMITTERS["l3ffn"] = emit_l3ffn


def emit_l0ffn(k, g, aps):
    st = ExitStack()
    xln = g["xln"]
    a = k.sb("l0f_a", [P, 32, 1024], BF16, stack=st)
    b = k.sb("l0f_b", [P, 32, 1024], BF16, stack=st)
    outF = b[:, 0:8, :]
    tmp = k.sb("l0f_t", [P, 2, 512], BF16, stack=st)
    w1b = k.sb("l0f_w1", [P, 2, 1024], BF16, stack=st)
    w2b = k.sb("l0f_w2", [P, 2, 1024], BF16, stack=st)
    hb = k.sb("l0f_h", [P, 2, 4096], BF16, stack=st)
    scales = k.sb("l0f_s", [P, 4 * 32 + 8], F32, stack=st)
    ps = g["ps_mm"]
    pst = g["ps_tr"]
    ra = ResidAdder(k, g, stack=st)

    s_wl = k.sem("wl")
    s_wf = k.sem("l0f_wf")
    s_pr = k.sem("pr")
    s_ev = k.sem("ev")
    s_sl = k.sem("l0f_sl")
    s_tp = k.sem("ln_tp")

    t0_ = s_wl.inc(16)
    on(k, "gpsimd", lambda e: e.dma_start(out=scales[:], in_=aps["L0f_scales"][:]).then_inc(s_wl.h, 16))
    sc_s1 = scales[:, 0:32]
    sc_s2 = scales[:, 32:64]
    sc_h1 = scales[:, 64:96]
    sc_h2 = scales[:, 96:128]
    sc_s3 = scales[:, 128:136]

    for half in range(2):
        tok0 = half * 1024
        b_wl, b_wf, b_pr, b_ev, b_sl = s_wl.n, s_wf.n, s_pr.n, s_ev.n, s_sl.n
        def g1(e, b_wf=b_wf):
            for mc in range(32):
                if mc >= 2:
                    e.wait_ge(s_wf.h, b_wf + mc - 1)
                e.dma_start(out=w1b[:, mc % 2, :], in_=aps["L0f_w1t"][mc]).then_inc(s_wl.h, 16)
                e.dma_start(out=w2b[:, mc % 2, :], in_=aps["L0f_w2t"][mc]).then_inc(s_wl.h, 16)
        on(k, "gpsimd", g1)
        s_wl.n += 32 * 32

        def t1(te, b_wl=b_wl, b_ev=b_ev, tok0=tok0):
            te.wait_ge(s_wl.h, t0_)
            n = 0
            for mc in range(32):
                te.wait_ge(s_wl.h, b_wl + 32 * (mc + 1))
                for tcc in range(2):
                    n += 1
                    if n > 2:
                        te.wait_ge(s_ev.h, b_ev + n - 2)
                    q = n % 2
                    for kc in range(KC):
                        i1 = te.matmul(out=ps[:, q * 2, :], lhsT=w1b[:, mc % 2, kc * 128:(kc + 1) * 128],
                                       rhs=xln[:, kc, tok0 + tcc * 512: tok0 + (tcc + 1) * 512],
                                       start=(kc == 0), stop=(kc == KC - 1))
                    i1.then_inc(s_pr.h, 1)
                    for kc in range(KC):
                        i2 = te.matmul(out=ps[:, q * 2 + 1, :], lhsT=w2b[:, mc % 2, kc * 128:(kc + 1) * 128],
                                       rhs=xln[:, kc, tok0 + tcc * 512: tok0 + (tcc + 1) * 512],
                                       start=(kc == 0), stop=(kc == KC - 1))
                    i2.then_inc(s_pr.h, 1)
                te.drain().then_inc(s_wf.h, 1)
        on(k, "tensor", t1)
        s_pr.n += 128
        s_wf.n += 32

        def sc1(sc, b_pr=b_pr, b_sl=b_sl, b_ev=b_ev):
            n = 0
            for mc in range(32):
                for tcc in range(2):
                    n += 1
                    sc.wait_ge(s_pr.h, b_pr + 2 * n - 1)
                    if n > 2:
                        sc.wait_ge(s_ev.h, b_ev + n - 2)
                    sc.activation(out=tmp[:, n % 2, :], in_=ps[:, (n % 2) * 2, :],
                                  func=AF.Silu, scale=sc_s1[:, mc:mc + 1]).then_inc(s_sl.h, 1)
        on(k, "scalar", sc1)
        s_sl.n += 64

        def v1(v, b_pr=b_pr, b_sl=b_sl):
            n = 0
            for mc in range(32):
                for tcc in range(2):
                    n += 1
                    v.wait_ge(s_sl.h, b_sl + n)
                    v.wait_ge(s_pr.h, b_pr + 2 * n)
                    v.scalar_tensor_tensor(out=a[:, mc, tcc * 512:(tcc + 1) * 512],
                                           in0=ps[:, (n % 2) * 2 + 1, :], scalar=sc_s2[:, mc:mc + 1],
                                           in1=tmp[:, n % 2, :], op0=ALU.mult,
                                           op1=ALU.mult).then_inc(s_ev.h, 1)
        on(k, "vector", v1)
        s_ev.n += 64
        k.barrier()

        for stg, (wkey, scol, src, dst) in enumerate(
                [("L0f_h1t", sc_h1, a, b), ("L0f_h2t", sc_h2, b, a)]):
            b_wl, b_wf, b_pr, b_ev = s_wl.n, s_wf.n, s_pr.n, s_ev.n
            def g2(e, wkey=wkey, b_wf=b_wf):
                for mc in range(32):
                    if mc >= 2:
                        e.wait_ge(s_wf.h, b_wf + mc - 1)
                    e.dma_start(out=hb[:, mc % 2, :], in_=aps[wkey][mc]).then_inc(s_wl.h, 16)
            on(k, "gpsimd", g2)
            s_wl.n += 32 * 16

            def t2(te, b_wl=b_wl, b_ev=b_ev, src=src):
                n = 0
                for mc in range(32):
                    te.wait_ge(s_wl.h, b_wl + 16 * (mc + 1))
                    for tcc in range(2):
                        n += 1
                        if n > 2:
                            te.wait_ge(s_ev.h, b_ev + n - 2)
                        for kc in range(32):
                            ins = te.matmul(out=ps[:, n % 2, :],
                                            lhsT=hb[:, mc % 2, kc * 128:(kc + 1) * 128],
                                            rhs=src[:, kc, tcc * 512:(tcc + 1) * 512],
                                            start=(kc == 0), stop=(kc == 31))
                        ins.then_inc(s_pr.h, 1)
                    te.drain().then_inc(s_wf.h, 1)
            on(k, "tensor", t2)
            s_pr.n += 64
            s_wf.n += 32

            def sc2(sc, b_pr=b_pr, scol=scol, dst=dst):
                n = 0
                for mc in range(32):
                    for tcc in range(2):
                        n += 1
                        sc.wait_ge(s_pr.h, b_pr + n)
                        sc.activation(out=dst[:, mc, tcc * 512:(tcc + 1) * 512],
                                      in_=ps[:, n % 2, :], func=AF.Gelu,
                                      scale=scol[:, mc:mc + 1]).then_inc(s_ev.h, 1)
            on(k, "scalar", sc2)
            s_ev.n += 64
            k.barrier()

        b_wl, b_wf, b_pr, b_ev = s_wl.n, s_wf.n, s_pr.n, s_ev.n
        def g4(e, b_wf=b_wf):
            for mc in range(KC):
                if mc >= 2:
                    e.wait_ge(s_wf.h, b_wf + mc - 1)
                e.dma_start(out=hb[:, mc % 2, :], in_=aps["L0f_w3t"][mc]).then_inc(s_wl.h, 16)
        on(k, "gpsimd", g4)
        s_wl.n += KC * 16

        def t4(te, b_wl=b_wl, b_ev=b_ev):
            n = 0
            for mc in range(KC):
                te.wait_ge(s_wl.h, b_wl + 16 * (mc + 1))
                for tcc in range(2):
                    n += 1
                    if n > 2:
                        te.wait_ge(s_ev.h, b_ev + n - 2)
                    for kc in range(32):
                        ins = te.matmul(out=ps[:, n % 2, :],
                                        lhsT=hb[:, mc % 2, kc * 128:(kc + 1) * 128],
                                        rhs=a[:, kc, tcc * 512:(tcc + 1) * 512],
                                        start=(kc == 0), stop=(kc == 31))
                    ins.then_inc(s_pr.h, 1)
                te.drain().then_inc(s_wf.h, 1)
        on(k, "tensor", t4)
        s_pr.n += 16
        s_wf.n += KC

        def sc4(sc, b_pr=b_pr):
            n = 0
            for mc in range(KC):
                for tcc in range(2):
                    n += 1
                    sc.wait_ge(s_pr.h, b_pr + n)
                    sc.activation(out=outF[:, mc, tcc * 512:(tcc + 1) * 512],
                                  in_=ps[:, n % 2, :], func=AF.Copy,
                                  scale=sc_s3[:, mc:mc + 1]).then_inc(s_ev.h, 1)
        on(k, "scalar", sc4)
        s_ev.n += 16
        k.barrier()

        b_tp = s_tp.n
        def t5(te):
            for tt8 in range(8):
                if tt8 >= 2:
                    te.wait_ge(ra.s_add.h, ra.s_add.n - 0 + tt8 - 1 - 8)
                for kc in range(KC):
                    ins = te.transpose(out=pst[:, tt8 % 2, kc * 128:(kc + 1) * 128],
                                       in_=outF[:, kc, tt8 * 128:(tt8 + 1) * 128],
                                       identity=g["ident"][:])
                ins.then_inc(s_tp.h, 1)
        b_add0 = ra.s_add.n
        def t5b(te, b_tp=b_tp, b_add0=b_add0):
            for tt8 in range(8):
                if tt8 >= 2:
                    te.wait_ge(ra.s_add.h, b_add0 + tt8 - 1)
                for kc in range(KC):
                    ins = te.transpose(out=pst[:, tt8 % 2, kc * 128:(kc + 1) * 128],
                                       in_=outF[:, kc, tt8 * 128:(tt8 + 1) * 128],
                                       identity=g["ident"][:])
                ins.then_inc(s_tp.h, 1)
        on(k, "tensor", t5b)
        s_tp.n += 8

        for tt8 in range(8):
            t = half * 8 + tt8
            ra.prepare(t)
            ra.add(t, 0, D, pst[:, tt8 % 2, :], wait=(s_tp, b_tp + tt8 + 1))
            ra.flush(t)
        k.barrier()
    ra.done()
    k.barrier()
    st.close()

BLOCK_EMITTERS["l0ffn"] = emit_l0ffn


def emit_l2attn(k, g, aps, upto=99):
    phase = [0]
    def cut():
        phase[0] += 1
        return phase[0] >= upto
    st = ExitStack()
    xln = g["xln"]
    XB = k.sb("l2a_xb", [P, KC, S], BF16, stack=st)
    wbuf = k.sb("l2a_w", [P, KC, 1024], BF16, stack=st)
    qp = k.sb("l2a_qp", [P, 2, S], BF16, stack=st)
    kp = k.sb("l2a_kp", [P, 2, S], BF16, stack=st)
    ktok = k.sb("l2a_kt", [P, TT, 256], BF16, stack=st)
    vtok = k.sb("l2a_vt", [P, TT, 1024], BF16, stack=st)
    kvS = k.sb("l2a_kv", [P, 2, 1024], BF16, stack=st)
    zrow = k.sb("l2a_z", [1, S], F32, stack=st)
    zib = k.sb("l2a_zib", [1, S], BF16, stack=st)
    zinvT = k.sb("l2a_zt", [P, TT], F32, stack=st)
    fmt = k.sb("l2a_fmt", [P, 2, 512], F32, stack=st)
    fme = k.sb("l2a_fme", [P, 2, 512], F32, stack=st)
    ps = g["ps_mm"]
    pst = g["ps_tr"]
    ps_z = k.ps("l2a_pz", [1, 512], F32, stack=st)
    ps_zt = k.ps("l2a_pzt", [P, TT], F32, stack=st)
    ra = ResidAdder(k, g, stack=st)

    s_wl = k.sem("wl")
    s_pr = k.sem("pr")
    s_ev = k.sem("ev")
    s_fm = k.sem("l2a_fm")
    s_tp = k.sem("ln_tp")

    def loadw(key, mcN=KC):
        t = s_wl.inc(16)
        on(k, "gpsimd", lambda e: e.dma_start(
            out=wbuf[:, 0:mcN, :], in_=aps[key].rearrange("mc p n -> p mc n")).then_inc(s_wl.h, 16))
        return t

    def proj_into_XB(wkey):
        b_pr, b_ev = s_pr.n, s_ev.n
        twl = loadw(wkey)
        def te(te_):
            te_.wait_ge(s_wl.h, twl)
            n = 0
            for mc in range(KC):
                for tc in range(TC):
                    n += 1
                    if n > 2:
                        te_.wait_ge(s_ev.h, b_ev + n - 2)
                    for kc in range(KC):
                        ins = te_.matmul(out=ps[:, n % 2, :],
                                         lhsT=wbuf[:, mc, kc * 128:(kc + 1) * 128],
                                         rhs=xln[:, kc, tc * 512:(tc + 1) * 512],
                                         start=(kc == 0), stop=(kc == KC - 1))
                    ins.then_inc(s_pr.h, 1)
        on(k, "tensor", te)
        s_pr.n += KC * TC
        def sc(sc_):
            n = 0
            for mc in range(KC):
                for tc in range(TC):
                    n += 1
                    sc_.wait_ge(s_pr.h, b_pr + n)
                    sc_.activation(out=XB[:, mc, tc * 512:(tc + 1) * 512], in_=ps[:, n % 2, :],
                                   func=AF.Copy).then_inc(s_ev.h, 1)
        on(k, "scalar", sc)
        s_ev.n += KC * TC
        k.barrier()

    def featuremap_into(dst):
        s_fm2 = k.sem("l2a_fm2")
        b_pr, b_ev, b_fm, b_fm2 = s_pr.n, s_ev.n, s_fm.n, s_fm2.n
        twl = loadw("L2a_projt", mcN=2)
        def te(te_):
            te_.wait_ge(s_wl.h, twl)
            n = 0
            for mc in range(2):
                for tc in range(TC):
                    n += 1
                    if n > 2:
                        te_.wait_ge(s_ev.h, b_ev + n - 2)
                    for kc in range(KC):
                        ins = te_.matmul(out=ps[:, n % 2, :],
                                         lhsT=wbuf[:, mc, kc * 128:(kc + 1) * 128],
                                         rhs=XB[:, kc, tc * 512:(tc + 1) * 512],
                                         start=(kc == 0), stop=(kc == KC - 1))
                    ins.then_inc(s_pr.h, 1)
        on(k, "tensor", te)
        s_pr.n += 2 * TC
        NT = 2 * TC
        def stt_of(v, j, dst):
            mc, tc = (j - 1) // TC, (j - 1) % TC
            v.wait_ge(s_fm2.h, b_fm2 + j)
            v.scalar_tensor_tensor(out=dst[:, mc, tc * 512:(tc + 1) * 512],
                                   in0=ps[:, j % 2, :], scalar=0.0, in1=fme[:, j % 2, :],
                                   op0=ALU.max, op1=ALU.add).then_inc(s_ev.h, 1)
        def vv(v, dst=dst):
            for n in range(1, NT + 1):
                v.wait_ge(s_pr.h, b_pr + n)
                if n > 2:
                    v.wait_ge(s_fm2.h, b_fm2 + n - 2)
                v.tensor_scalar(out=fmt[:, n % 2, :], in0=ps[:, n % 2, :],
                                scalar1=0.0, scalar2=None, op0=ALU.min).then_inc(s_fm.h, 1)
                if n >= 2:
                    stt_of(v, n - 1, dst)
            stt_of(v, NT, dst)
        on(k, "vector", vv)
        def sc1(sc):
            for n in range(1, NT + 1):
                sc.wait_ge(s_fm.h, b_fm + n)
                if n > 2:
                    sc.wait_ge(s_ev.h, b_ev + n - 2)
                sc.activation(out=fme[:, n % 2, :], in_=fmt[:, n % 2, :],
                              func=AF.Exp).then_inc(s_fm2.h, 1)
        on(k, "scalar", sc1)
        s_fm.n += NT
        s_fm2.n += NT
        s_ev.n += NT
        k.barrier()

    proj_into_XB("L2a_wqt")
    if cut(): st_p34.close(); st_p2.close(); st_p1.close(); st_vb.close(); st_qk.close(); st.close(); return
    featuremap_into(qp)
    if cut(): st_p34.close(); st_p2.close(); st_p1.close(); st_vb.close(); st_qk.close(); st.close(); return
    proj_into_XB("L2a_wkt")
    featuremap_into(kp)
    if cut(): st_p34.close(); st_p2.close(); st_p1.close(); st_vb.close(); st_qk.close(); st.close(); return
    proj_into_XB("L2a_wvt")
    if cut(): st_p34.close(); st_p2.close(); st_p1.close(); st_vb.close(); st_qk.close(); st.close(); return

    b_pr, b_ev = s_pr.n, s_ev.n
    def tez(te):
        for tc in range(TC):
            if tc >= 1:
                te.wait_ge(s_ev.h, b_ev + tc)
            for c2 in range(2):
                ins = te.matmul(out=ps_z[:], lhsT=g["ones_col_bf"][:],
                                rhs=qp[:, c2, tc * 512:(tc + 1) * 512],
                                start=(c2 == 0), stop=(c2 == 1))
            ins.then_inc(s_pr.h, 1)
    on(k, "tensor", tez)
    s_pr.n += TC
    def vz(v):
        for tc in range(TC):
            v.wait_ge(s_pr.h, b_pr + tc + 1)
            v.tensor_copy(out=zrow[:, tc * 512:(tc + 1) * 512], in_=ps_z[:]).then_inc(s_ev.h, 1)
        v.tensor_scalar(out=zrow[:], in0=zrow[:], scalar1=1e-8, scalar2=None, op0=ALU.add)
        v.drain()
        v.reciprocal(out=zrow[:], in_=zrow[:])
        v.drain()
        v.tensor_copy(out=zib[:], in_=zrow[:])
        v.drain().then_inc(s_ev.h, 1)
    on(k, "vector", vz)
    s_ev.n += TC + 1
    k.barrier()
    if cut(): st_p34.close(); st_p2.close(); st_p1.close(); st_vb.close(); st_qk.close(); st.close(); return

    b_tp = s_tp.n
    def tt1(te):
        nb = 0
        for tt in range(TT):
            nb += 1
            if nb > 2:
                te.wait_ge(s_ev.h, s_ev.n + nb - 2 - 2 * TT)
    def tt_k(te, b_tp=b_tp):
        for tt in range(TT):
            if tt >= 2:
                te.wait_ge(s_ev.h, b_ev2 + tt - 1)
            for c2 in range(2):
                ins = te.transpose(out=pst[:, tt % 2, c2 * 128:(c2 + 1) * 128],
                                   in_=kp[:, c2, tt * 128:(tt + 1) * 128],
                                   identity=g["ident"][:])
            ins.then_inc(s_tp.h, 1)
    b_ev2 = s_ev.n
    on(k, "tensor", tt_k)
    s_tp.n += TT
    def sc_k(sc, b_tp=b_tp, b_ev2=b_ev2):
        for tt in range(TT):
            sc.wait_ge(s_tp.h, b_tp + tt + 1)
            sc.activation(out=ktok[:, tt, :], in_=pst[:, tt % 2, 0:256],
                          func=AF.Copy).then_inc(s_ev.h, 1)
    on(k, "scalar", sc_k)
    s_ev.n += TT

    b_tp2 = s_tp.n
    b_ev3 = s_ev.n
    def tt_v(te, b_tp2=b_tp2, b_ev3=b_ev3):
        for tt in range(TT):
            if tt < 2:
                te.wait_ge(s_ev.h, b_ev2 + 15 + tt)
            else:
                te.wait_ge(s_ev.h, b_ev3 + tt - 1)
            for kc in range(KC):
                ins = te.transpose(out=pst[:, tt % 2, kc * 128:(kc + 1) * 128],
                                   in_=XB[:, kc, tt * 128:(tt + 1) * 128],
                                   identity=g["ident"][:])
            ins.then_inc(s_tp.h, 1)
    on(k, "tensor", tt_v)
    s_tp.n += TT
    def sc_v(sc, b_tp2=b_tp2):
        for tt in range(TT):
            sc.wait_ge(s_tp.h, b_tp2 + tt + 1)
            sc.activation(out=vtok[:, tt, :], in_=pst[:, tt % 2, :],
                          func=AF.Copy).then_inc(s_ev.h, 1)
    on(k, "scalar", sc_v)
    s_ev.n += TT
    k.barrier()
    if cut(): st_p34.close(); st_p2.close(); st_p1.close(); st_vb.close(); st_qk.close(); st.close(); return

    b_pr, b_ev = s_pr.n, s_ev.n
    def tekv(te):
        n = 0
        for pc2 in range(2):
            for nn in range(2):
                n += 1
                if n > 2:
                    te.wait_ge(s_ev.h, b_ev + n - 2)
                for t in range(TT):
                    ins = te.matmul(out=ps[:, n % 2, :],
                                    lhsT=ktok[:, t, pc2 * 128:(pc2 + 1) * 128],
                                    rhs=vtok[:, t, nn * 512:(nn + 1) * 512],
                                    start=(t == 0), stop=(t == TT - 1))
                ins.then_inc(s_pr.h, 1)
    on(k, "tensor", tekv)
    s_pr.n += 4
    def sckv(sc):
        n = 0
        for pc2 in range(2):
            for nn in range(2):
                n += 1
                sc.wait_ge(s_pr.h, b_pr + n)
                sc.activation(out=kvS[:, pc2, nn * 512:(nn + 1) * 512], in_=ps[:, n % 2, :],
                              func=AF.Copy).then_inc(s_ev.h, 1)
    on(k, "scalar", sckv)
    s_ev.n += 4
    k.barrier()
    if cut(): st_p34.close(); st_p2.close(); st_p1.close(); st_vb.close(); st_qk.close(); st.close(); return

    b_pr, b_ev = s_pr.n, s_ev.n
    def tenum(te):
        n = 0
        for mc in range(KC):
            for tc in range(TC):
                n += 1
                if n > 2:
                    te.wait_ge(s_ev.h, b_ev + n - 2)
                for pc2 in range(2):
                    ins = te.matmul(out=ps[:, n % 2, :],
                                    lhsT=kvS[:, pc2, mc * 128:(mc + 1) * 128],
                                    rhs=qp[:, pc2, tc * 512:(tc + 1) * 512],
                                    start=(pc2 == 0), stop=(pc2 == 1))
                ins.then_inc(s_pr.h, 1)
        for tt in range(TT):
            te.matmul(out=ps_zt[:, tt:tt + 1], lhsT=zib[:, tt * 128:(tt + 1) * 128],
                      rhs=g["c_one"][:], start=True, stop=True)
        te.drain().then_inc(s_pr.h, 1)
    on(k, "tensor", tenum)
    s_pr.n += KC * TC + 1
    def scnum(sc):
        n = 0
        for mc in range(KC):
            for tc in range(TC):
                n += 1
                sc.wait_ge(s_pr.h, b_pr + n)
                sc.activation(out=XB[:, mc, tc * 512:(tc + 1) * 512], in_=ps[:, n % 2, :],
                              func=AF.Copy).then_inc(s_ev.h, 1)
        sc.wait_ge(s_pr.h, b_pr + KC * TC + 1)
        sc.activation(out=zinvT[:], in_=ps_zt[:], func=AF.Copy).then_inc(s_ev.h, 1)
    on(k, "scalar", scnum)
    s_ev.n += KC * TC + 1
    k.barrier()
    if cut(): st_p34.close(); st_p2.close(); st_p1.close(); st_vb.close(); st_qk.close(); st.close(); return

    b_tp3 = s_tp.n
    b_add = ra.s_add.n
    def ttn(te, b_tp3=b_tp3, b_add=b_add):
        for tt in range(TT):
            if tt >= 2:
                te.wait_ge(ra.s_add.h, b_add + tt - 1)
            for kc in range(KC):
                ins = te.transpose(out=pst[:, tt % 2, kc * 128:(kc + 1) * 128],
                                   in_=XB[:, kc, tt * 128:(tt + 1) * 128],
                                   identity=g["ident"][:])
            ins.then_inc(s_tp.h, 1)
    on(k, "tensor", ttn)
    s_tp.n += TT
    for tt in range(TT):
        ra.prepare(tt)
        ra.add(tt, 0, D, pst[:, tt % 2, :], scale_ap=zinvT[:, tt:tt + 1],
               wait=(s_tp, b_tp3 + tt + 1))
        ra.flush(tt)
    ra.done()
    k.barrier()
    st.close()

BLOCK_EMITTERS["l2attn"] = emit_l2attn


def emit_l1ffn(k, g, aps):
    st = ExitStack()
    xln = g["xln"]
    signsF = k.sb("l1f_sg", [32, 8], F32, stack=st)
    peT = k.sb("l1f_pe", [32, S], F32, stack=st)
    scT = k.sb("l1f_sc", [P, TT, 8], F32, stack=st)
    rmax = k.sb("l1f_rm", [P, TT], F32, stack=st)
    eq = k.sb("l1f_eq", [P, TT, 8], F32, stack=st)
    avail = k.sb("l1f_av", [P, TT], F32, stack=st)
    masks = k.sb("l1f_mk", [P, TT, 8], F32, stack=st)
    mb = k.sb("l1f_mb", [P, TT, 8], BF16, stack=st)
    maskRb = k.sb("l1f_mr", [8, S], BF16, stack=st)
    bB = k.sb("l1f_bb", [8, 1024], BF16, stack=st)
    WB = k.sb("l1f_wb", [P, 8, 8, 1024], BF16, stack=st)
    xm = k.sb("l1f_xm", [P, 2, 8, 256], BF16, stack=st)
    ps = g["ps_mm"]
    pst = g["ps_tr"]
    ps1 = [k.ps("l1f_ps1a", [P, 512], F32, stack=st),
           k.ps("l1f_ps1b", [P, 512], F32, stack=st)]
    ra = ResidAdder(k, g, stack=st)

    s_wl = k.sem("wl")
    s_pr = k.sem("pr")
    s_ev = k.sem("ev")
    s_tp = k.sem("ln_tp")
    s_bc = k.sem("l1f_bc")
    s_xm = k.sem("l1f_xm")
    s_xmf = k.sem("l1f_xmf")
    b_wl = s_wl.n

    def gl(e):
        e.dma_start(out=signsF[:], in_=aps["L1f_sigsT"][:]).then_inc(s_wl.h, 16)
        e.dma_start(out=peT[:], in_=aps["c_peT"][:]).then_inc(s_wl.h, 16)
        e.dma_start(out=bB[:], in_=aps["L1_ffn_b"][:]).then_inc(s_wl.h, 16)
        e.dma_start(out=WB[:], in_=aps["L1f_wt"].rearrange("u kc p n -> p u kc n")).then_inc(s_wl.h, 16)
    on(k, "gpsimd", gl)
    s_wl.inc(64)

    s_sg = k.sem("l1f_sgs")
    def sc0(sc):
        sc.wait_ge(s_wl.h, b_wl + 16)
        sc.activation(out=signsF[:], in_=signsF[:], func=AF.Sign)
        sc.drain().then_inc(s_sg.h, 1)
    on(k, "scalar", sc0)
    t_sg = s_sg.inc(1)

    b_pr = s_pr.n
    def te0(te):
        te.wait_ge(s_sg.h, t_sg)
        te.wait_ge(s_wl.h, b_wl + 32)
        for tt in range(TT):
            ins = te.matmul(out=ps1[0][:, tt * 8:(tt + 1) * 8],
                            lhsT=peT[:, tt * 128:(tt + 1) * 128],
                            rhs=signsF[:], start=True, stop=True, skip_group_check=True)
        ins.then_inc(s_pr.h, 1)
    on(k, "tensor", te0)
    s_pr.n += 1

    s_mk = k.sem("l1f_mk")
    def v0(v):
        v.wait_ge(s_pr.h, b_pr + 1)
        v.tensor_copy(out=scT[:], in_=ps1[0][:, 0:128].rearrange("p (t u) -> p t u", u=8))
        v.drain()
        v.tensor_reduce(out=rmax[:], in_=scT[:], axis=mybir.AxisListType.X, op=ALU.max)
        v.drain()
        for t in range(TT):
            v.tensor_scalar(out=eq[:, t, :], in0=scT[:, t, :], scalar1=rmax[:, t:t + 1],
                            scalar2=None, op0=ALU.is_ge)
        v.drain()
        v.tensor_copy(out=masks[:, :, 0:1], in_=eq[:, :, 0:1])
        v.tensor_scalar(out=avail[:], in0=eq[:, :, 0], scalar1=-1.0, scalar2=1.0,
                        op0=ALU.mult, op1=ALU.add)
        v.drain()
        for u in range(1, 8):
            v.tensor_tensor(out=masks[:, :, u], in0=avail[:], in1=eq[:, :, u], op=ALU.mult)
            v.drain()
            v.tensor_tensor(out=avail[:], in0=avail[:], in1=masks[:, :, u], op=ALU.subtract)
            v.drain()
        v.tensor_copy(out=mb[:], in_=masks[:])
        v.drain().then_inc(s_mk.h, 1)
    on(k, "vector", v0)
    t_mk = s_mk.inc(1)

    b_tp = s_tp.n
    def te1(te):
        te.wait_ge(s_mk.h, t_mk)
        for t in range(TT):
            ins = te.transpose(out=pst[0:8, t // 8, (t % 8) * 128:(t % 8 + 1) * 128],
                               in_=mb[:, t, :], identity=g["ident"][:])
        ins.then_inc(s_tp.h, 1)
    on(k, "tensor", te1)
    s_tp.n += 1

    s_mr = k.sem("l1f_mrs")
    def sc1(sc):
        sc.wait_ge(s_tp.h, b_tp + 1)
        sc.activation(out=maskRb[:, 0:1024], in_=pst[0:8, 0, :], func=AF.Copy)
        sc.activation(out=maskRb[:, 1024:2048], in_=pst[0:8, 1, :], func=AF.Copy)
        sc.drain().then_inc(s_mr.h, 1)
    on(k, "scalar", sc1)
    t_mr = s_mr.inc(1)
    k.barrier()

    b_bc, b_xm, b_xmf = s_bc.n, s_xm.n, s_xmf.n
    b_pr2 = s_pr.n
    b_add = ra.s_add.n
    NTG = 8

    def outu_of(te, tg, u):
        n = tg * 8 + u + 1
        par = n % 2
        te.wait_ge(s_xm.h, b_xm + n)
        for tt2 in range(2):
            for nn in range(2):
                ins = None
                for kc in range(KC):
                    ins = te.matmul(out=ps[:, tt2 * 2 + nn, :],
                                    lhsT=xm[:, par, kc, tt2 * 128:(tt2 + 1) * 128],
                                    rhs=WB[:, u, kc, nn * 512:(nn + 1) * 512],
                                    start=(u == 0 and kc == 0), stop=False,
                                    skip_group_check=True)
                if u == 7:
                    ins = te.matmul(out=ps[:, tt2 * 2 + nn, :],
                                    lhsT=maskRb[:, tg * 256 + tt2 * 128: tg * 256 + (tt2 + 1) * 128],
                                    rhs=bB[:, nn * 512:(nn + 1) * 512],
                                    start=False, stop=True, skip_group_check=True)
                    ins.then_inc(s_pr.h, 1)
        te.drain().then_inc(s_xmf.h, 1)

    for tg in range(NTG):
        def te2(te, tg=tg):
            if tg == 0:
                te.wait_ge(s_wl.h, b_wl + 64)
                te.wait_ge(s_mr.h, t_mr)
            else:
                te.wait_ge(ra.s_add.h, b_add + tg * 4)
            for u in range(8):
                n = tg * 8 + u + 1
                if n > 2:
                    te.wait_ge(s_xm.h, b_xm + n - 2)
                par = n % 2
                te.matmul(out=ps1[par][:, 0:256],
                          lhsT=g["sel16e"][0:8, u * 128:(u + 1) * 128],
                          rhs=maskRb[:, tg * 256:(tg + 1) * 256],
                          start=True, stop=True, skip_group_check=True).then_inc(s_bc.h, 1)
                if u >= 1:
                    outu_of(te, tg, u - 1)
            outu_of(te, tg, 7)
        on(k, "tensor", te2)
        s_bc.n += 8
        s_xmf.n += 8
        s_pr.n += 4

        def v2(v, tg=tg):
            for u in range(8):
                n = tg * 8 + u + 1
                par = n % 2
                v.wait_ge(s_bc.h, b_bc + n)
                if n > 2:
                    v.wait_ge(s_xmf.h, b_xmf + n - 2)
                for kc in range(KC):
                    ins = v.tensor_tensor(out=xm[:, par, kc, :], in0=xln[:, kc, tg * 256:(tg + 1) * 256],
                                          in1=ps1[par][:, 0:256], op=ALU.mult)
                ins.then_inc(s_xm.h, 1)
        on(k, "vector", v2)
        s_xm.n += 8

        t0 = tg * 2
        ra.prepare(t0)
        ra.prepare(t0 + 1)
        n_end = b_pr2 + (tg + 1) * 4
        for tt2 in range(2):
            for nn in range(2):
                ra.add(t0 + tt2, nn * 512, (nn + 1) * 512, ps[:, tt2 * 2 + nn, :],
                       wait=(s_pr, n_end))
        ra.flush(t0)
        ra.flush(t0 + 1)
    ra.done()
    k.barrier()
    st.close()

BLOCK_EMITTERS["l1ffn"] = emit_l1ffn


def emit_l1attn(k, g, aps):
    st = ExitStack()
    xln = g["xln"]
    W = 1023
    qkv = k.sb("l1a_qkv", [P, 24, S], BF16, stack=st)
    wbuf = k.sb("l1a_w", [P, 2, 1024], BF16, stack=st)
    prod = k.sb("l1a_pr", [P, 2, W], BF16, stack=st)
    sab4 = k.sb("l1a_s4", [16, 4, W], F32, stack=st)
    smax = k.sb("l1a_sm", [16, W], F32, stack=st)
    pstack = k.sb("l1a_pk", [16, 4, W], BF16, stack=st)
    tb = k.sb("l1a_tb", [P, W], BF16, stack=st)
    acc = k.sb("l1a_ac", [P, W], BF16, stack=st)
    oscat = g["xln"]
    sum64 = k.sb("l1a_s64", [P, KC, 16], BF16, stack=st)
    bc64 = k.sb("l1a_b64", [16, KC, P], BF16, stack=st)
    cntR = k.sb("l1a_cnt", [1, S], BF16, stack=st)
    boutB = k.sb("l1a_bo", [1, 1024], BF16, stack=st)
    ps = g["ps_mm"]
    ra = ResidAdder(k, g, stack=st)

    s_wl = k.sem("wl")
    s_pr = k.sem("pr")
    s_ev = k.sem("ev")
    b_wl = s_wl.n

    def gl(e):
        e.dma_start(out=sum64[:], in_=aps["c_sum64_bf"].rearrange("kc p m -> p kc m")).then_inc(s_wl.h, 16)
        e.dma_start(out=bc64[:], in_=aps["c_bc64_bf"].rearrange("kc m p -> m kc p")).then_inc(s_wl.h, 16)
        e.dma_start(out=cntR[:], in_=aps["c_l1a_count"][:]).then_inc(s_wl.h, 16)
        e.dma_start(out=boutB[:], in_=aps["L1a_bout"][:]).then_inc(s_wl.h, 16)
        e.nop().then_inc(s_wl.h, 1)
    on(k, "gpsimd", gl)
    s_wl.inc(65)

    b_pr, b_ev = s_pr.n, s_ev.n
    s_wf = k.sem("l1a_wf")
    b_wf = s_wf.n
    def ga(e):
        for mc in range(24):
            if mc >= 2:
                e.wait_ge(s_wf.h, b_wf + mc - 1)
            e.dma_start(out=wbuf[:, mc % 2, :], in_=aps["L1a_wqkvt"][mc]).then_inc(s_wl.h, 16)
    on(k, "gpsimd", ga)
    s_wl.n += 24 * 16

    def ta(te):
        n = 0
        for mc in range(24):
            te.wait_ge(s_wl.h, b_wl + 65 + 16 * (mc + 1))
            for tc in range(TC):
                n += 1
                if n > 2:
                    te.wait_ge(s_ev.h, b_ev + n - 2)
                for kc in range(KC):
                    ins = te.matmul(out=ps[:, n % 2, :], lhsT=wbuf[:, mc % 2, kc * 128:(kc + 1) * 128],
                                    rhs=xln[:, kc, tc * 512:(tc + 1) * 512],
                                    start=(kc == 0), stop=(kc == KC - 1))
                ins.then_inc(s_pr.h, 1)
            te.drain().then_inc(s_wf.h, 1)
    on(k, "tensor", ta)
    s_pr.n += 24 * TC
    s_wf.n += 24

    def sca(sc):
        n = 0
        for mc in range(24):
            for tc in range(TC):
                n += 1
                sc.wait_ge(s_pr.h, b_pr + n)
                sc.activation(out=qkv[:, mc, tc * 512:(tc + 1) * 512], in_=ps[:, n % 2, :],
                              func=AF.Copy).then_inc(s_ev.h, 1)
    on(k, "scalar", sca)
    s_ev.n += 24 * TC
    k.barrier()
    on(k, "vector", lambda v: v.memset(oscat[:], 0.0))

    s_pd = k.sem("l1a_pd")
    s_sb = k.sem("l1a_sb")
    s_sx = k.sem("l1a_sx")
    s_bc = k.sem("l1a_bc")
    s_tt = k.sem("l1a_tt")
    for a in range(4):
        b_pd, b_pr1, b_sb = s_pd.n, s_pr.n, s_sb.n
        def vb1b(v, a=a, b_pd=b_pd, b_pr1=b_pr1):
            n = 0
            for b in range(4):
                for kc in range(KC):
                    n += 1
                    if n > 2:
                        v.wait_ge(s_pr.h, b_pr1 + (n - 2) * 2)
                    v.tensor_tensor(out=prod[:, n % 2, :],
                                    in0=qkv[:, kc, a:a + 2 * W - 1:2],
                                    in1=qkv[:, 8 + kc, b:b + 2 * W - 1:2],
                                    op=ALU.mult).then_inc(s_pd.h, 1)
        on(k, "vector", vb1b)
        s_pd.n += 32

        def tb1(te, a=a, b_pd=b_pd, b_sb=b_sb):
            n = 0
            for b in range(4):
                slot = b % 2
                if b >= 2:
                    te.wait_ge(s_sb.h, b_sb + b - 1)
                for kc in range(KC):
                    n += 1
                    te.wait_ge(s_pd.h, b_pd + n)
                    te.matmul(out=ps[0:16, slot * 2, :], lhsT=sum64[:, kc, :],
                              rhs=prod[:, n % 2, 0:512], start=(kc == 0), stop=(kc == KC - 1),
                              skip_group_check=True).then_inc(s_pr.h, 1)
                    te.matmul(out=ps[0:16, slot * 2 + 1, 0:W - 512], lhsT=sum64[:, kc, :],
                              rhs=prod[:, n % 2, 512:W], start=(kc == 0), stop=(kc == KC - 1),
                              skip_group_check=True).then_inc(s_pr.h, 1)
        on(k, "tensor", tb1)
        s_pr.n += 64

        b_pr_end = s_pr.n
        def scb1(sc, a=a, b_pr1=b_pr1, b_sb=b_sb):
            for b in range(4):
                slot = b % 2
                sc.wait_ge(s_pr.h, b_pr1 + (b + 1) * 16)
                sc.activation(out=sab4[:, b, 0:512], in_=ps[0:16, slot * 2, :], func=AF.Copy)
                sc.activation(out=sab4[:, b, 512:W], in_=ps[0:16, slot * 2 + 1, 0:W - 512], func=AF.Copy)
                sc.drain().then_inc(s_sb.h, 1)
        on(k, "scalar", scb1)
        s_sb.n += 4

        b_sx = s_sx.n
        def vb2(v, a=a, b_sb=b_sb):
            v.wait_ge(s_sb.h, b_sb + 4)
            v.tensor_tensor(out=smax[:], in0=sab4[:, 0, :], in1=sab4[:, 1, :], op=ALU.max)
            v.drain()
            v.tensor_tensor(out=smax[:], in0=smax[:], in1=sab4[:, 2, :], op=ALU.max)
            v.drain()
            v.tensor_tensor(out=smax[:], in0=smax[:], in1=sab4[:, 3, :], op=ALU.max)
            v.drain()
            for b in range(4):
                v.tensor_tensor(out=sab4[:, b, :], in0=sab4[:, b, :], in1=smax[:], op=ALU.subtract)
                v.drain()
            v.nop().then_inc(s_sx.h, 1)
        on(k, "vector", vb2)
        t_sx1 = s_sx.inc(1)

        def scb2(sc, t_sx1=t_sx1):
            sc.wait_ge(s_sx.h, t_sx1)
            for b in range(4):
                sc.activation(out=sab4[:, b, :], in_=sab4[:, b, :], func=AF.Exp,
                              scale=1.0 / 32.0)
            sc.drain().then_inc(s_sx.h, 1)
        on(k, "scalar", scb2)
        t_sx2 = s_sx.inc(1)

        def vb3(v, t_sx2=t_sx2):
            v.wait_ge(s_sx.h, t_sx2)
            v.tensor_tensor(out=smax[:], in0=sab4[:, 0, :], in1=sab4[:, 1, :], op=ALU.add)
            v.drain()
            v.tensor_tensor(out=smax[:], in0=smax[:], in1=sab4[:, 2, :], op=ALU.add)
            v.drain()
            v.tensor_tensor(out=smax[:], in0=smax[:], in1=sab4[:, 3, :], op=ALU.add)
            v.drain()
            v.reciprocal(out=smax[:], in_=smax[:])
            v.drain()
            for b in range(4):
                v.tensor_tensor(out=pstack[:, b, :], in0=sab4[:, b, :], in1=smax[:], op=ALU.mult)
                v.drain()
            v.nop().then_inc(s_sx.h, 1)
        on(k, "vector", vb3)
        t_sx3 = s_sx.inc(1)

        b_bc, b_tt = s_bc.n, s_tt.n
        def tb3(te, a=a, t_sx3=t_sx3, b_tt=b_tt):
            te.wait_ge(s_sx.h, t_sx3)
            n = 0
            for kc in range(KC):
                for b in range(4):
                    n += 1
                    if n > 2:
                        te.wait_ge(s_tt.h, b_tt + n - 2)
                    slot = n % 2
                    te.matmul(out=ps[:, slot * 2, :], lhsT=bc64[:, kc, :],
                              rhs=pstack[:, b, 0:512], start=True, stop=True,
                              skip_group_check=True)
                    te.matmul(out=ps[:, slot * 2 + 1, 0:W - 512], lhsT=bc64[:, kc, :],
                              rhs=pstack[:, b, 512:W], start=True, stop=True,
                              skip_group_check=True).then_inc(s_bc.h, 1)
        on(k, "tensor", tb3)
        s_bc.n += 32

        def vb4(v, a=a, b_bc=b_bc, b_tt=b_tt):
            n = 0
            for kc in range(KC):
                for b in range(4):
                    n += 1
                    slot = n % 2
                    v.wait_ge(s_bc.h, b_bc + n)
                    v.tensor_tensor(out=tb[:, 0:512], in0=ps[:, slot * 2, :],
                                    in1=qkv[:, 16 + kc, b:b + 2 * W - 1:2][:, 0:512], op=ALU.mult)
                    v.tensor_tensor(out=tb[:, 512:W], in0=ps[:, slot * 2 + 1, 0:W - 512],
                                    in1=qkv[:, 16 + kc, b:b + 2 * W - 1:2][:, 512:W], op=ALU.mult)
                    v.drain()
                    if b == 0:
                        v.tensor_copy(out=acc[:], in_=tb[:])
                    else:
                        v.tensor_tensor(out=acc[:], in0=acc[:], in1=tb[:], op=ALU.add)
                    v.drain().then_inc(s_tt.h, 1)
                v.tensor_tensor(out=oscat[:, kc, a:a + 2 * W - 1:2],
                                in0=oscat[:, kc, a:a + 2 * W - 1:2], in1=acc[:], op=ALU.add)
                v.drain()
            v.nop().then_inc(s_tt.h, 1)
        on(k, "vector", vb4)
        s_tt.n += 33
        k.barrier()

    woutB = k.sb("l1a_wo", [P, KC, 1024], BF16, stack=st)
    b_wl3 = s_wl.n
    on(k, "gpsimd", lambda e: e.dma_start(
        out=woutB[:], in_=aps["L1a_woutt"].rearrange("kc p n -> p kc n")).then_inc(s_wl.h, 16))
    s_wl.inc(16)

    b_pr4 = s_pr.n
    b_add = ra.s_add.n
    def tc_(te):
        te.wait_ge(s_wl.h, b_wl3 + 16)
        te.wait_ge(s_wl.h, b_wl + 64)
        n = 0
        for t in range(TT):
            for nn in range(2):
                n += 1
                if n > 2:
                    te.wait_ge(ra.s_add.h, b_add + n - 2)
                for kc in range(KC):
                    te.matmul(out=ps[:, n % 2, :], lhsT=oscat[:, kc, t * 128:(t + 1) * 128],
                              rhs=woutB[:, kc, nn * 512:(nn + 1) * 512],
                              start=(kc == 0), stop=False, skip_group_check=True)
                te.matmul(out=ps[:, n % 2, :], lhsT=cntR[:, t * 128:(t + 1) * 128],
                          rhs=boutB[:, nn * 512:(nn + 1) * 512],
                          start=False, stop=True, skip_group_check=True).then_inc(s_pr.h, 1)
    on(k, "tensor", tc_)
    s_pr.n += TT * 2

    n = 0
    for t in range(TT):
        ra.prepare(t)
        for nn in range(2):
            n += 1
            ra.add(t, nn * 512, (nn + 1) * 512, ps[:, n % 2, :], wait=(s_pr, b_pr4 + n))
        ra.flush(t)
    ra.done()
    k.barrier()
    st.close()

BLOCK_EMITTERS["l1attn"] = emit_l1attn


def emit_l0attn(k, g, aps, upto=99):
    phase = [0]
    def cut():
        phase[0] += 1
        return phase[0] >= upto
    st = ExitStack()
    xln = g["xln"]
    ps = g["ps_mm"]
    pst = g["ps_tr"]

    st_qk = ExitStack()
    st_p1 = ExitStack()
    st_p2 = ExitStack()
    st_vb = ExitStack()
    st_p34 = ExitStack()
    wqs = k.sb("l0a_wq", [P, KC, 1024], BF16, stack=st)
    g64 = k.sb("l0a_g64", [P, 2], BF16, stack=st)
    g64T = k.sb("l0a_g64T", [2, P], BF16, stack=st)
    sumP = k.sb("l0a_sump", [P, 4, 16], BF16, stack=st)
    bcP = k.sb("l0a_bcp", [16, 4, P], BF16, stack=st)
    ltm = k.sb("l0a_ltm", [P, P], BF16, stack=st)
    gvec = k.sb("l0a_gv", [16, 1], F32, stack=st)
    lamv = k.sb("l0a_lam", [16, 1], BF16, stack=st)
    rrq = k.sb("l0a_rrq", [16, S], BF16, stack=st)
    rrk = k.sb("l0a_rrk", [16, S], BF16, stack=st)
    ncb = k.sb("l0a_ncb", [P, 16], F32, stack=st)
    crow = k.sb("l0a_crow", [1, 16], F32, stack=st)
    eb = k.sb("l0a_eb", [P, 2, 512], BF16, stack=st)
    zrow_s = k.sb("l0a_zf", [1, TC, 512], F32, stack=st)
    lamc = k.sb("l0a_lamc", [P, 4], F32, stack=st)
    lamn = k.sb("l0a_lamn", [P, 4], F32, stack=st)
    qk = k.sb("l0a_qk", [P, 16, S], BF16, stack=st_qk)
    vbuf = k.sb("l0a_v", [P, KC, S], BF16, stack=st_vb)
    awb = k.sb("l0a_aw", [P, KC, 1024], BF16, stack=st_p1)
    gmS = k.sb("l0a_gm", [2, KC, 1024], BF16, stack=st_p1)
    hsb = gmS
    zdram = k.nc.dram_tensor(f"l0a_zd_{k.uid()}", [32, S], F32, kind="Internal").ap()

    s_wl = k.sem("wl")
    s_pr = k.sem("pr")
    s_ev = k.sem("ev")
    s_tp = k.sem("ln_tp")
    s_dq = k.sem("l0a_dq")
    s_ex = k.sem("l0a_ex")
    s_mk = k.sem("l0a_mk")
    s_yz = k.sem("l0a_yz")
    s_ze = k.sem("l0a_ze")

    b_wl = s_wl.n
    def gl(e):
        e.dma_start(out=g64[:], in_=aps["c_g64_bf"][:]).then_inc(s_wl.h, 16)
        e.dma_start(out=g64T[:], in_=aps["c_g64T_bf"][:]).then_inc(s_wl.h, 16)
        e.dma_start(out=sumP[:], in_=aps["c_sumP_bf"].rearrange("v p m -> p v m")).then_inc(s_wl.h, 16)
        e.dma_start(out=bcP[:], in_=aps["c_bcP_bf"].rearrange("v m p -> m v p")).then_inc(s_wl.h, 16)
        e.dma_start(out=ltm[:], in_=aps["c_ltmask_bf"][:]).then_inc(s_wl.h, 16)
        e.dma_start(out=gvec[:], in_=aps["L0_attn_q_gain"][:]).then_inc(s_wl.h, 16)
        e.dma_start(out=lamv[:], in_=aps["L0_attn_diff_lambda"][:]).then_inc(s_wl.h, 16)
    on(k, "gpsimd", gl)
    s_wl.inc(112)

    for sec in range(3):
        b_wl1 = s_wl.n
        on(k, "gpsimd", lambda e, sec=sec: e.dma_start(
            out=wqs[:], in_=aps["L0a_wqkvt"][sec]).then_inc(s_wl.h, 16))
        s_wl.inc(16)

        t_d0 = s_dq.n
        def sc_d(sc, b_wl1=b_wl1):
            sc.wait_ge(s_wl.h, b_wl1 + 16)
            sc.activation(out=awb[:], in_=wqs[:], func=AF.Abs)
            sc.drain()
            sc.activation(out=wqs[:], in_=wqs[:], func=AF.Sign)
            sc.drain().then_inc(s_dq.h, 1)
        on(k, "scalar", sc_d)
        t_d1 = s_dq.inc(1)

        b_pr1 = s_pr.n
        def te_d(te, t_d1=t_d1, t_d0=t_d0):
            te.wait_ge(s_dq.h, t_d1)
            te.wait_ge(s_wl.h, b_wl + 32)
            n = 0
            for kc in range(KC):
                for j in range(2):
                    n += 1
                    if n > 2:
                        te.wait_ge(s_dq.h, t_d1 + n - 2)
                    te.matmul(out=ps[0:2, n % 2, :], lhsT=g64[:],
                              rhs=awb[:, kc, j * 512:(j + 1) * 512],
                              start=True, stop=True, skip_group_check=True).then_inc(s_pr.h, 1)
        on(k, "tensor", te_d)
        s_pr.n += 16

        def v_d(v, b_pr1=b_pr1):
            n = 0
            for kc in range(KC):
                for j in range(2):
                    n += 1
                    v.wait_ge(s_pr.h, b_pr1 + n)
                    v.tensor_copy(out=gmS[:, kc, j * 512:(j + 1) * 512], in_=ps[0:2, n % 2, :])
                    v.drain().then_inc(s_dq.h, 1)
            v.tensor_scalar(out=hsb[:], in0=gmS[:], scalar1=1.0 / 64.0, scalar2=1e-8,
                            op0=ALU.mult, op1=ALU.max)
            v.drain()
            v.tensor_scalar(out=hsb[:], in0=hsb[:], scalar1=0.5, scalar2=None, op0=ALU.mult)
            v.drain().then_inc(s_dq.h, 1)
        on(k, "vector", v_d)
        s_dq.n += 17
        t_d2 = s_dq.n

        b_pr2 = s_pr.n
        def te_d2(te, t_d2=t_d2):
            te.wait_ge(s_dq.h, t_d2)
            n = 0
            for kc in range(KC):
                for j in range(2):
                    n += 1
                    if n > 2:
                        te.wait_ge(s_dq.h, t_d2 + n - 2)
                    te.matmul(out=ps[:, 2 + n % 2, :], lhsT=g64T[:],
                              rhs=hsb[:, kc, j * 512:(j + 1) * 512],
                              start=True, stop=True, skip_group_check=True).then_inc(s_pr.h, 1)
        on(k, "tensor", te_d2)
        s_pr.n += 16

        def v_d2(v, b_pr2=b_pr2):
            n = 0
            for kc in range(KC):
                for j in range(2):
                    n += 1
                    v.wait_ge(s_pr.h, b_pr2 + n)
                    sl = (slice(None), kc, slice(j * 512, (j + 1) * 512))
                    v.tensor_tensor(out=awb[sl], in0=awb[sl], in1=ps[:, 2 + n % 2, :], op=ALU.is_gt)
                    v.drain()
                    v.scalar_tensor_tensor(out=awb[sl], in0=awb[sl], scalar=2.0,
                                           in1=ps[:, 2 + n % 2, :], op0=ALU.mult, op1=ALU.mult)
                    v.drain()
                    v.tensor_tensor(out=wqs[sl], in0=awb[sl], in1=wqs[sl], op=ALU.mult)
                    v.drain().then_inc(s_dq.h, 1)
        on(k, "vector", v_d2)
        s_dq.n += 16
        k.barrier()

        b_pr3, b_ev3 = s_pr.n, s_ev.n
        dst = vbuf if sec == 2 else qk
        coff = 8 if sec == 1 else 0
        def te_q(te, b_ev3=b_ev3):
            n = 0
            for mc in range(KC):
                for tc in range(TC):
                    n += 1
                    if n > 2:
                        te.wait_ge(s_ev.h, b_ev3 + n - 2)
                    for kc in range(KC):
                        ins = te.matmul(out=ps[:, n % 2, :], lhsT=wqs[:, kc, mc * 128:(mc + 1) * 128],
                                        rhs=xln[:, kc, tc * 512:(tc + 1) * 512],
                                        start=(kc == 0), stop=(kc == KC - 1))
                    ins.then_inc(s_pr.h, 1)
        on(k, "tensor", te_q)
        s_pr.n += KC * TC

        def sc_q(sc, dst=dst, coff=coff, b_pr3=b_pr3):
            n = 0
            for mc in range(KC):
                for tc in range(TC):
                    n += 1
                    sc.wait_ge(s_pr.h, b_pr3 + n)
                    sc.activation(out=dst[:, coff + mc, tc * 512:(tc + 1) * 512],
                                  in_=ps[:, n % 2, :], func=AF.Copy).then_inc(s_ev.h, 1)
        on(k, "scalar", sc_q)
        s_ev.n += KC * TC
        k.barrier()

    st_p1.close()
    if cut(): st_p34.close(); st_p2.close(); st_p1.close(); st_vb.close(); st_qk.close(); st.close(); return

    crep = k.sb("l0a_cos", [P, S], BF16, stack=st_p2)
    srep = k.sb("l0a_sin", [P, S], BF16, stack=st_p2)
    ssq = k.sb("l0a_ssq", [16, S], F32, stack=st_p2)
    sq_s = k.sb("l0a_sq", [P, S], BF16, stack=st_p2)
    rt = [k.sb(f"l0a_rt{i}", [P, S], BF16, stack=st_p2) for i in range(4)]
    b_wl2 = s_wl.n
    def gl2(e):
        e.dma_start(out=crep[:], in_=aps["c_l0cos_bf"][:]).then_inc(s_wl.h, 16)
        e.dma_start(out=srep[:], in_=aps["c_l0sin_bf"][:]).then_inc(s_wl.h, 16)
    on(k, "gpsimd", gl2)
    s_wl.inc(32)
    t_rope_w = s_wl.n
    for sec in range(2):
        coff = sec * 8
        s_sqd = k.sem("l0a_sqd")
        b_sqd = s_sqd.n
        b_ev4 = s_ev.n
        def sc_s2(sc, coff=coff, b_sqd=b_sqd):
            for cc in range(KC):
                if cc >= 1:
                    sc.wait_ge(s_sqd.h, b_sqd + cc)
                sc.activation(out=sq_s[:], in_=qk[:, coff + cc, :], func=AF.Square)
                sc.drain().then_inc(s_ev.h, 1)
        on(k, "scalar", sc_s2)
        s_ev.n += KC

        b_pr4 = s_pr.n
        def te_s(te, b_ev4=b_ev4):
            for cc in range(KC):
                te.wait_ge(s_ev.h, b_ev4 + cc + 1)
                for tc in range(TC):
                    te.matmul(out=ps[0:16, tc, :], lhsT=sumP[:, cc % 4, :],
                              rhs=sq_s[:, tc * 512:(tc + 1) * 512],
                              start=(cc == 0), stop=(cc == KC - 1), skip_group_check=True)
                te.drain().then_inc(s_sqd.h, 1)
            te.drain().then_inc(s_pr.h, 1)
        on(k, "tensor", te_s)
        s_sqd.n += KC
        s_pr.n += 1

        t_pr_s = s_pr.n
        def v_s(v, t_pr_s=t_pr_s):
            v.wait_ge(s_pr.h, t_pr_s)
            for tc in range(TC):
                v.tensor_copy(out=ssq[:, tc * 512:(tc + 1) * 512], in_=ps[0:16, tc, :])
            v.drain()
            v.tensor_scalar(out=ssq[:], in0=ssq[:], scalar1=1.0 / 64.0, scalar2=1e-5,
                            op0=ALU.mult, op1=ALU.add)
            v.drain().then_inc(s_yz.h, 1)
        on(k, "vector", v_s)
        t_y1 = s_yz.inc(1)

        def sc_r(sc, t_y1=t_y1):
            sc.wait_ge(s_yz.h, t_y1)
            sc.activation(out=ssq[:], in_=ssq[:], func=AF.Sqrt)
            sc.drain().then_inc(s_yz.h, 1)
        on(k, "scalar", sc_r)
        t_y2 = s_yz.inc(1)

        def v_r(v, sec=sec, t_y2=t_y2):
            v.wait_ge(s_yz.h, t_y2)
            v.wait_ge(s_wl.h, b_wl + 112)
            v.reciprocal(out=ssq[:], in_=ssq[:])
            v.drain()
            if sec == 0:
                v.tensor_scalar(out=rrq[:], in0=ssq[:], scalar1=gvec[:], scalar2=None, op0=ALU.mult)
            else:
                v.tensor_copy(out=rrk[:], in_=ssq[:])
            v.drain().then_inc(s_yz.h, 1)
        on(k, "vector", v_r)
        t_y3 = s_yz.inc(1)

        b_pr5 = s_pr.n
        b_yz5 = s_yz.n
        rr = rrq if sec == 0 else rrk
        def te_a(te, rr=rr, t_y3=t_y3, b_yz5=b_yz5):
            te.wait_ge(s_yz.h, t_y3)
            n = 0
            for cc in range(KC):
                for tc in range(TC):
                    n += 1
                    if n > 2:
                        te.wait_ge(s_yz.h, b_yz5 + n - 2)
                    te.matmul(out=ps[:, n % 2, :], lhsT=bcP[:, cc % 4, :],
                              rhs=rr[:, tc * 512:(tc + 1) * 512],
                              start=True, stop=True, skip_group_check=True).then_inc(s_pr.h, 1)
        on(k, "tensor", te_a)
        s_pr.n += KC * TC

        def v_a(v, coff=coff, b_pr5=b_pr5):
            n = 0
            for cc in range(KC):
                for tc in range(TC):
                    n += 1
                    v.wait_ge(s_pr.h, b_pr5 + n)
                    sl = (slice(None), coff + cc, slice(tc * 512, (tc + 1) * 512))
                    v.tensor_tensor(out=qk[sl], in0=qk[sl], in1=ps[:, n % 2, :], op=ALU.mult)
                    v.drain().then_inc(s_yz.h, 1)
        on(k, "vector", v_a)
        s_yz.n += KC * TC
        k.barrier()

        def v_rope(v, coff=coff):
            v.wait_ge(s_wl.h, t_rope_w)
            for gg in range(4):
                cA = (slice(None), coff + gg, slice(None))
                cB = (slice(None), coff + 4 + gg, slice(None))
                v.tensor_tensor(out=rt[0][:], in0=qk[cA], in1=crep[:], op=ALU.mult)
                v.tensor_tensor(out=rt[3][:], in0=qk[cA], in1=srep[:], op=ALU.mult)
                v.tensor_tensor(out=rt[2][:], in0=qk[cB], in1=crep[:], op=ALU.mult)
                v.tensor_tensor(out=rt[1][:], in0=qk[cB], in1=srep[:], op=ALU.mult)
                v.drain()
                v.tensor_tensor(out=qk[cA], in0=rt[0][:], in1=rt[1][:], op=ALU.add)
                v.tensor_tensor(out=qk[cB], in0=rt[2][:], in1=rt[3][:], op=ALU.subtract)
                v.drain()
        on(k, "vector", v_rope)
        k.barrier()

    st_p2.close()
    if cut(): st_p34.close(); st_p2.close(); st_p1.close(); st_vb.close(); st_qk.close(); st.close(); return

    b_pr6 = s_pr.n
    def te_c(te):
        te.transpose(out=ps[0:1, 3, 0:16], in_=gvec[:], identity=g["identf"][0:16, 0:16]).then_inc(s_pr.h, 1)
    on(k, "tensor", te_c)
    s_pr.n += 1

    def v_c(sc, b_pr6=b_pr6):
        sc.wait_ge(s_pr.h, b_pr6 + 1)
        sc.activation(out=crow[:], in_=ps[0:1, 3, 0:16], func=AF.Abs, scale=11.3137085)
        sc.drain().then_inc(s_yz.h, 1)
    on(k, "scalar", v_c)
    t_c1 = s_yz.inc(1)

    b_pr7 = s_pr.n
    def te_c2(te, t_c1=t_c1):
        te.wait_ge(s_yz.h, t_c1)
        te.matmul(out=ps[:, 3, 0:16], lhsT=g["ones_row_f32"][:], rhs=crow[:],
                  start=True, stop=True, skip_group_check=True)
        for gg in range(4):
            te.matmul(out=ps[:, 2, gg:gg + 1], lhsT=bcP[:, gg, :], rhs=lamv[:],
                      start=(gg == 0), stop=(gg == 3), skip_group_check=True)
        te.drain().then_inc(s_pr.h, 1)
    on(k, "tensor", te_c2)
    s_pr.n += 1

    def v_c2(v, b_pr7=b_pr7):
        v.wait_ge(s_pr.h, b_pr7 + 1)
        v.tensor_scalar(out=ncb[:], in0=ps[:, 3, 0:16], scalar1=-1.0, scalar2=None, op0=ALU.mult)
        v.drain()
        v.tensor_copy(out=lamc[:], in_=ps[:, 2, 0:4])
        v.drain()
        v.tensor_scalar(out=lamn[:], in0=lamc[:], scalar1=-1.0, scalar2=None, op0=ALU.mult)
        v.drain().then_inc(s_yz.h, 1)
    on(k, "vector", v_c2)
    s_yz.inc(1)
    k.barrier()

    vtok = k.sb("l0a_vt", [P, 32, TT, 33], BF16, stack=st_p34)
    on(k, "vector", lambda v: v.memset(vtok[:, :, :, 32:33], 1.0))
    b_tp = s_tp.n
    b_evv = s_ev.n
    def te_v(te, b_tp=b_tp, b_evv=b_evv):
        for hh in range(32):
            H, hm = hh // 16, hh % 16
            gg, h4 = hm // 4, hm % 4
            cc = H * 4 + gg
            if hh >= 2:
                te.wait_ge(s_ev.h, b_evv + hh - 1)
            for kt in range(TT):
                ins = te.transpose(out=pst[:, hh % 2, kt * 32:(kt + 1) * 32],
                                   in_=vbuf[:, cc, kt * 128:(kt + 1) * 128][h4 * 32:(h4 + 1) * 32, :],
                                   identity=g["ident"][h4 * 32:(h4 + 1) * 32, h4 * 32:(h4 + 1) * 32],
                                   tile_position=(h4 * 32, 0))
            ins.then_inc(s_tp.h, 1)
    on(k, "tensor", te_v)
    s_tp.n += 32

    def sc_v2(sc, b_tp=b_tp):
        for hh in range(32):
            sc.wait_ge(s_tp.h, b_tp + hh + 1)
            sc.activation(out=vtok[:, hh, :, 0:32],
                          in_=pst[:, hh % 2, 0:512].rearrange("p (kt c) -> p kt c", c=32),
                          func=AF.Copy).then_inc(s_ev.h, 1)
    on(k, "scalar", sc_v2)
    s_ev.n += 32
    k.barrier()
    if cut(): st_p34.close(); st_p2.close(); st_p1.close(); st_vb.close(); st_qk.close(); st.close(); return

    ps_sT = ps[:, 0:2, :]
    ps_yz = ps[0:33, 2, :]
    yb = xln
    tiles = [(qc, kt) for qc in range(TC) for kt in range(qc * 4 + 4)]
    NTI = len(tiles)
    for hh in range(32):
        H, hm = hh // 16, hh % 16
        gg, h4 = hm // 4, hm % 4
        ccq = H * 4 + gg
        h_global = 4 * gg + h4
        b_sc, b_ex, b_mk, b_yz, b_ze = s_pr.n, s_ex.n, s_mk.n, s_yz.n, s_ze.n

        def pv0_of(te, i, hh):
            qc, kt = tiles[i]
            n = i + 1
            j = kt - qc * 4
            c0 = max(0, j) * 128
            te.wait_ge(s_mk.h, b_mk + n)
            if kt == 0:
                te.wait_ge(s_ze.h, b_ze + 2 * qc)
            te.matmul(out=ps_yz[:, c0:512], lhsT=vtok[:, hh, kt, :],
                      rhs=eb[:, n % 2, 0:512 - c0],
                      start=(kt == 0), stop=(kt == qc * 4 + 3),
                      skip_group_check=True).then_inc(s_yz.h, 1)

        def te_h(te, hh=hh, ccq=ccq, h4=h4, b_sc=b_sc, b_ex=b_ex, b_mk=b_mk, b_ze=b_ze):
            for i, (qc, kt) in enumerate(tiles):
                n = i + 1
                if n > 2:
                    te.wait_ge(s_ex.h, b_ex + n - 2)
                te.matmul(out=ps_sT[:, n % 2, :],
                          lhsT=qk[:, 8 + ccq, kt * 128:(kt + 1) * 128][h4 * 32:(h4 + 1) * 32, :],
                          rhs=qk[:, ccq, qc * 512:(qc + 1) * 512][h4 * 32:(h4 + 1) * 32, :],
                          start=True, stop=True, skip_group_check=True,
                          tile_position=(h4 * 32, 0)).then_inc(s_pr.h, 1)
                if i >= 1:
                    pv0_of(te, i - 1, hh)
            pv0_of(te, NTI - 1, hh)
        on(k, "tensor", te_h)
        s_pr.n += NTI
        s_yz.n += NTI

        def sc_h(sc, hh=hh, ccq=ccq, h4=h4, h_global=h_global, b_sc=b_sc, b_mk=b_mk, b_yz=b_yz, b_ze=b_ze):
            for i, (qc, kt) in enumerate(tiles):
                n = i + 1
                j = kt - qc * 4
                c0 = max(0, j) * 128
                sc.wait_ge(s_pr.h, b_sc + n)
                sc.activation(out=eb[:, n % 2, 0:512 - c0], in_=ps_sT[:, n % 2, c0:512],
                              func=AF.Exp, scale=0.1767766953,
                              bias=ncb[:, h_global:h_global + 1]).then_inc(s_ex.h, 1)
                if kt == qc * 4 + 3:
                    sc.wait_ge(s_yz.h, b_yz + sum(q * 4 + 4 for q in range(qc + 1)))
                    sc.activation(out=yb[:, ccq, qc * 512:(qc + 1) * 512][h4 * 32:(h4 + 1) * 32, :],
                                  in_=ps_yz[0:32, :], func=AF.Copy).then_inc(s_ze.h, 1)
        on(k, "scalar", sc_h)
        s_ex.n += NTI

        def v_h(v, hh=hh, b_ex=b_ex, b_yz=b_yz, b_ze=b_ze):
            for i, (qc, kt) in enumerate(tiles):
                n = i + 1
                j = kt - qc * 4
                v.wait_ge(s_ex.h, b_ex + n)
                if j >= 0:
                    v.tensor_tensor(out=eb[:, n % 2, 0:128], in0=eb[:, n % 2, 0:128],
                                    in1=ltm[:], op=ALU.mult).then_inc(s_mk.h, 1)
                else:
                    v.nop().then_inc(s_mk.h, 1)
                if kt == qc * 4 + 3:
                    v.wait_ge(s_ze.h, b_ze + 2 * qc + 1)
                    v.tensor_copy(out=zrow_s[:, qc, :], in_=ps_yz[32:33, :]).then_inc(s_ze.h, 1)
        on(k, "vector", v_h)
        s_mk.n += NTI
        s_ze.n += 2 * TC
        t_zhh = s_ze.n
        def g_z(e, hh=hh, t_zhh=t_zhh):
            e.wait_ge(s_ze.h, t_zhh)
            e.dma_start(out=zdram[hh:hh + 1, :], in_=zrow_s[:].rearrange("one q c -> one (q c)")).then_inc(s_ze.h, 16)
        on(k, "gpsimd", g_z)
        s_ze.inc(16)
        k.barrier()
        def g_zw(e, t=s_ze.n):
            e.wait_ge(s_ze.h, t)
        on(k, "gpsimd", g_zw)
    k.barrier()

    st_p34.close()
    st_vb.close()
    st_qk.close()
    if cut(): st_p34.close(); st_p2.close(); st_p1.close(); st_vb.close(); st_qk.close(); st.close(); return

    zr1 = k.sb("l0a_zr1", [16, S], F32, stack=st)
    zr2 = k.sb("l0a_zr2", [16, S], F32, stack=st)
    s_zd = k.sem("l0a_zd")
    b_zd = s_zd.n
    def g_zr(e):
        e.dma_start(out=zr1[:], in_=zdram[0:16, :]).then_inc(s_zd.h, 16)
        e.dma_start(out=zr2[:], in_=zdram[16:32, :]).then_inc(s_zd.h, 16)
    on(k, "gpsimd", g_zr)
    s_zd.inc(32)

    def v_z2(v, b_zd=b_zd):
        v.wait_ge(s_zd.h, b_zd + 32)
        v.reciprocal(out=zr1[:], in_=zr1[:])
        v.reciprocal(out=zr2[:], in_=zr2[:])
        v.drain()
        v.tensor_copy(out=rrq[:], in_=zr1[:])
        v.tensor_copy(out=rrk[:], in_=zr2[:])
        v.drain().then_inc(s_yz.h, 1)
    on(k, "vector", v_z2)
    t_z1 = s_yz.inc(1)
    k.barrier()

    t12a = k.sb("l0a_t12a", [P, 512], F32, stack=st)
    t12b = k.sb("l0a_t12b", [P, 512], F32, stack=st)
    b_pr8 = s_pr.n
    b_cb = s_yz.n
    def te_cb(te, b_cb=b_cb):
        n = 0
        for gg in range(4):
            for tc in range(TC):
                n += 1
                if n > 1:
                    te.wait_ge(s_yz.h, b_cb + n - 1)
                te.matmul(out=ps[:, 0, :], lhsT=bcP[:, gg, :],
                          rhs=rrq[:, tc * 512:(tc + 1) * 512],
                          start=True, stop=True, skip_group_check=True)
                te.matmul(out=ps[:, 1, :], lhsT=bcP[:, gg, :],
                          rhs=rrk[:, tc * 512:(tc + 1) * 512],
                          start=True, stop=True, skip_group_check=True).then_inc(s_pr.h, 1)
    on(k, "tensor", te_cb)
    s_pr.n += 16

    def v_cb(v, b_pr8=b_pr8):
        n = 0
        for gg in range(4):
            for tc in range(TC):
                n += 1
                v.wait_ge(s_pr.h, b_pr8 + n)
                slA = (slice(None), gg, slice(tc * 512, (tc + 1) * 512))
                slB = (slice(None), 4 + gg, slice(tc * 512, (tc + 1) * 512))
                v.tensor_tensor(out=t12a[:], in0=yb[slA], in1=ps[:, 0, :], op=ALU.mult)
                v.tensor_tensor(out=t12b[:], in0=yb[slB], in1=ps[:, 1, :], op=ALU.mult)
                v.drain()
                v.scalar_tensor_tensor(out=yb[slA], in0=t12b[:], scalar=lamn[:, gg:gg + 1], in1=t12a[:],
                                       op0=ALU.mult, op1=ALU.add)
                v.scalar_tensor_tensor(out=yb[slB], in0=t12b[:], scalar=lamc[:, gg:gg + 1], in1=t12a[:],
                                       op0=ALU.mult, op1=ALU.add)
                v.drain().then_inc(s_yz.h, 1)
    on(k, "vector", v_cb)
    s_yz.n += 16
    k.barrier()
    if cut(): st_p34.close(); st_p2.close(); st_p1.close(); st_vb.close(); st_qk.close(); st.close(); return

    awb = k.sb("l0a_aw6", [P, KC, 1024], BF16, stack=st)
    gmS = k.sb("l0a_gm6", [2, KC, 1024], BF16, stack=st)
    hsb = gmS
    b_wl6 = s_wl.n
    on(k, "gpsimd", lambda e: e.dma_start(out=wqs[:], in_=aps["L0a_protot"][:]).then_inc(s_wl.h, 16))
    s_wl.inc(16)
    def sc_d6(sc, b_wl6=b_wl6):
        sc.wait_ge(s_wl.h, b_wl6 + 16)
        sc.activation(out=awb[:], in_=wqs[:], func=AF.Abs)
        sc.drain()
        sc.activation(out=wqs[:], in_=wqs[:], func=AF.Sign)
        sc.drain().then_inc(s_dq.h, 1)
    on(k, "scalar", sc_d6)
    t_d1 = s_dq.inc(1)
    b_pr1 = s_pr.n
    def te_d6(te, t_d1=t_d1):
        te.wait_ge(s_dq.h, t_d1)
        n = 0
        for kc in range(KC):
            for j in range(2):
                n += 1
                if n > 2:
                    te.wait_ge(s_dq.h, t_d1 + n - 2)
                te.matmul(out=ps[0:2, n % 2, :], lhsT=g64[:],
                          rhs=awb[:, kc, j * 512:(j + 1) * 512],
                          start=True, stop=True, skip_group_check=True).then_inc(s_pr.h, 1)
    on(k, "tensor", te_d6)
    s_pr.n += 16
    def v_d6(v, b_pr1=b_pr1):
        n = 0
        for kc in range(KC):
            for j in range(2):
                n += 1
                v.wait_ge(s_pr.h, b_pr1 + n)
                v.tensor_copy(out=gmS[:, kc, j * 512:(j + 1) * 512], in_=ps[0:2, n % 2, :])
                v.drain().then_inc(s_dq.h, 1)
        v.tensor_scalar(out=hsb[:], in0=gmS[:], scalar1=1.0 / 64.0, scalar2=1e-8,
                        op0=ALU.mult, op1=ALU.max)
        v.drain()
        v.tensor_scalar(out=hsb[:], in0=hsb[:], scalar1=0.5, scalar2=None, op0=ALU.mult)
        v.drain().then_inc(s_dq.h, 1)
    on(k, "vector", v_d6)
    s_dq.n += 17
    t_d2 = s_dq.n
    b_pr2 = s_pr.n
    def te_d7(te, t_d2=t_d2):
        te.wait_ge(s_dq.h, t_d2)
        n = 0
        for kc in range(KC):
            for j in range(2):
                n += 1
                if n > 2:
                    te.wait_ge(s_dq.h, t_d2 + n - 2)
                te.matmul(out=ps[:, 2 + n % 2, :], lhsT=g64T[:],
                          rhs=hsb[:, kc, j * 512:(j + 1) * 512],
                          start=True, stop=True, skip_group_check=True).then_inc(s_pr.h, 1)
    on(k, "tensor", te_d7)
    s_pr.n += 16
    def v_d7(v, b_pr2=b_pr2):
        n = 0
        for kc in range(KC):
            for j in range(2):
                n += 1
                v.wait_ge(s_pr.h, b_pr2 + n)
                sl = (slice(None), kc, slice(j * 512, (j + 1) * 512))
                v.tensor_tensor(out=awb[sl], in0=awb[sl], in1=ps[:, 2 + n % 2, :], op=ALU.is_gt)
                v.drain()
                v.scalar_tensor_tensor(out=awb[sl], in0=awb[sl], scalar=2.0,
                                       in1=ps[:, 2 + n % 2, :], op0=ALU.mult, op1=ALU.mult)
                v.drain()
                v.tensor_tensor(out=wqs[sl], in0=awb[sl], in1=wqs[sl], op=ALU.mult)
                v.drain().then_inc(s_dq.h, 1)
    on(k, "vector", v_d7)
    s_dq.n += 16
    k.barrier()

    featP = k.sb("l0a_fp", [P, KC, 16], BF16, stack=st)
    featO = k.sb("l0a_fo", [P, KC, 16], BF16, stack=st)
    tabv = k.sb("l0a_tab", [1, 3], F32, stack=st)
    pf = k.sb("l0a_pf", [16, 1024], F32, stack=st)
    xs6 = k.sb("l0a_xs6", [16, S], BF16, stack=st)
    xf = k.sb("l0a_xf", [16, S], F32, stack=st)
    xa = k.sb("l0a_xa", [16, S], BF16, stack=st)
    xs1 = k.sb("l0a_xs1", [16, S], BF16, stack=st)
    Abuf = k.sb("l0a_A", [16, 1024], BF16, stack=st)
    Bbuf = k.sb("l0a_B", [16, 1024], BF16, stack=st)
    th16 = k.sb("l0a_th", [16, 3], F32, stack=st)
    b_wl7 = s_wl.n
    def gl7(e):
        e.dma_start(out=featP[:], in_=aps["L0a_featP"].rearrange("p kc m -> p (kc m)")).then_inc(s_wl.h, 16)
        e.dma_start(out=featO[:], in_=aps["L0a_featO"].rearrange("p kc m -> p (kc m)")).then_inc(s_wl.h, 16)
        e.dma_start(out=tabv[:, 0:1], in_=aps["L0_attn_theta"][:]).then_inc(s_wl.h, 16)
        e.dma_start(out=tabv[:, 1:2], in_=aps["L0_attn_alpha"][:]).then_inc(s_wl.h, 16)
        e.dma_start(out=tabv[:, 2:3], in_=aps["L0_attn_beta"][:]).then_inc(s_wl.h, 16)
    on(k, "gpsimd", gl7)
    s_wl.inc(80)

    b_pr9 = s_pr.n
    def te_f(te):
        te.wait_ge(s_wl.h, b_wl7 + 80)
        te.matmul(out=ps[0:16, 3, 0:3], lhsT=g["ones_row_f32"][:, 0:16], rhs=tabv[:],
                  start=True, stop=True, skip_group_check=True).then_inc(s_pr.h, 1)
        for j in range(2):
            for kc in range(KC):
                ins = te.matmul(out=ps[0:16, j, :], lhsT=featO[:, kc, :],
                                rhs=wqs[:, kc, j * 512:(j + 1) * 512],
                                start=(kc == 0), stop=(kc == KC - 1), skip_group_check=True)
            ins.then_inc(s_pr.h, 1)
    on(k, "tensor", te_f)
    s_pr.n += 3

    s_xf = k.sem("l0a_xf")
    b_xf = s_xf.n
    b_pr10 = s_pr.n
    def te_f2(te, b_xf=b_xf):
        for tc in range(TC):
            te.wait_ge(s_xf.h, b_xf + (tc if tc >= 2 else 1))
            for kc in range(KC):
                ins = te.matmul(out=ps[0:16, tc % 2, :], lhsT=featP[:, kc, :],
                                rhs=yb[:, kc, tc * 512:(tc + 1) * 512],
                                start=(kc == 0), stop=(kc == KC - 1), skip_group_check=True)
            ins.then_inc(s_pr.h, 1)
    on(k, "tensor", te_f2)
    s_pr.n += TC

    def v_f(v, b_pr9=b_pr9, b_pr10=b_pr10, b_xf=b_xf):
        v.wait_ge(s_pr.h, b_pr9 + 1)
        v.tensor_copy(out=th16[:], in_=ps[0:16, 3, 0:3])
        v.drain()
        v.wait_ge(s_pr.h, b_pr9 + 3)
        v.tensor_copy(out=pf[:, 0:512], in_=ps[0:16, 0, :])
        v.tensor_copy(out=pf[:, 512:1024], in_=ps[0:16, 1, :])
        v.drain().then_inc(s_xf.h, 1)
        for tc in range(TC):
            v.wait_ge(s_pr.h, b_pr10 + tc + 1)
            v.tensor_copy(out=xf[:, tc * 512:(tc + 1) * 512], in_=ps[0:16, tc % 2, :])
            v.drain().then_inc(s_xf.h, 1)
    on(k, "vector", v_f)
    t_pf = b_xf + 1
    s_xf.n += 1 + TC
    t_xf = s_xf.n

    s_sig = k.sem("l0a_sig")
    def sc_f(sc, t_pf=t_pf, t_xf=t_xf):
        sc.wait_ge(s_xf.h, t_pf)
        sc.activation(out=pf[:], in_=pf[:], func=AF.Sigmoid, scale=5.0, accum_out=None)
        sc.drain().then_inc(s_sig.h, 1)
        sc.wait_ge(s_xf.h, t_xf)
        sc.activation(out=xs6[:], in_=xf[:], func=AF.Sigmoid, scale=5.0)
        sc.drain().then_inc(s_sig.h, 1)
    pfr = k.sb("l0a_pfr", [16, 1024], F32, stack=st)
    pa_ = k.sb("l0a_pa", [16, 1024], F32, stack=st)
    tA_ = k.sb("l0a_tA", [16, 1024], F32, stack=st)
    tB_ = k.sb("l0a_tB", [16, 1024], F32, stack=st)
    def v_keep(v, t_pf=t_pf):
        v.wait_ge(s_xf.h, t_pf)
        v.tensor_copy(out=pfr[:], in_=pf[:])
        v.drain().then_inc(s_sig.h, 1)
    on(k, "vector", v_keep)
    t_keep = s_sig.inc(1)
    def sc_f2(sc, t_keep=t_keep, t_xf=t_xf):
        sc.wait_ge(s_sig.h, t_keep)
        sc.activation(out=th16[:], in_=th16[:], func=AF.Abs)
        sc.drain()
        sc.activation(out=pf[:], in_=pf[:], func=AF.Sigmoid, scale=5.0)
        sc.drain().then_inc(s_sig.h, 1)
        sc.wait_ge(s_xf.h, t_xf)
        sc.activation(out=xs6[:], in_=xf[:], func=AF.Sigmoid, scale=5.0)
        sc.drain().then_inc(s_sig.h, 1)
    on(k, "scalar", sc_f2)
    s_sig.inc(2)
    t_sig = s_sig.n

    def v_ab(v, t_sig=t_sig):
        v.wait_ge(s_sig.h, t_sig)
        v.tensor_tensor(out=xa[:], in0=xf[:], in1=xs6[:], op=ALU.mult)
        v.tensor_scalar(out=xs1[:], in0=xs6[:], scalar1=-1.0, scalar2=1.0,
                        op0=ALU.mult, op1=ALU.add)
        v.drain()
        v.tensor_tensor(out=pa_[:], in0=pfr[:], in1=pf[:], op=ALU.mult)
        v.drain()
        v.tensor_scalar(out=tA_[:], in0=pa_[:], scalar1=th16[:, 0:1], scalar2=None, op0=ALU.mult)
        v.tensor_scalar(out=tB_[:], in0=pf[:], scalar1=th16[:, 1:2],
                        scalar2=th16[:, 1:2], op0=ALU.mult, op1=ALU.subtract)
        v.drain()
        v.tensor_tensor(out=Abuf[:], in0=tA_[:], in1=tB_[:], op=ALU.add)
        v.tensor_scalar(out=Bbuf[:], in0=pa_[:], scalar1=th16[:, 2:3], scalar2=-1.0,
                        op0=ALU.mult, op1=ALU.mult)
        v.drain().then_inc(s_sig.h, 1)
    on(k, "vector", v_ab)
    t_ab = s_sig.inc(1)
    k.barrier()

    ra = ResidAdder(k, g, stack=st)
    b_pr11 = s_pr.n
    b_add = ra.s_add.n
    def te_o(te, b_add=b_add):
        n = 0
        for t in range(TT):
            for nn in range(2):
                n += 1
                if n > 2:
                    te.wait_ge(ra.s_add.h, b_add + n - 2)
                te.matmul(out=ps[:, n % 2, :], lhsT=xa[:, t * 128:(t + 1) * 128],
                          rhs=Abuf[:, nn * 512:(nn + 1) * 512],
                          start=True, stop=False, skip_group_check=True)
                te.matmul(out=ps[:, n % 2, :], lhsT=xs1[:, t * 128:(t + 1) * 128],
                          rhs=Bbuf[:, nn * 512:(nn + 1) * 512],
                          start=False, stop=True, skip_group_check=True).then_inc(s_pr.h, 1)
    on(k, "tensor", te_o)
    s_pr.n += TT * 2

    n = 0
    for t in range(TT):
        ra.prepare(t)
        for nn in range(2):
            n += 1
            ra.add(t, nn * 512, (nn + 1) * 512, ps[:, n % 2, :], wait=(s_pr, b_pr11 + n))
        ra.flush(t)
    ra.done()
    k.barrier()
    st.close()

BLOCK_EMITTERS["l0attn"] = emit_l0attn


def emit_l3attn(k, g, aps, upto=99):
    st = ExitStack()
    xln = g["xln"]
    ps = g["ps_mm"]
    pst = g["ps_tr"]

    Q = k.sb("l3a_q", [P, KC, S], BF16, stack=st)
    K_ = k.sb("l3a_k", [P, KC, S], BF16, stack=st)
    V_ = k.sb("l3a_v", [P, KC, S], BF16, stack=st)
    wbuf = k.sb("l3a_w", [P, KC, 1024], BF16, stack=st)
    biasS = k.sb("l3a_b", [P, 4 * KC], F32, stack=st)
    ssqK = k.sb("l3a_ssqk", [16, S], F32, stack=st)
    rnb = k.sb("l3a_rnb", [16, S], BF16, stack=st)
    maxr = k.sb("l3a_maxr", [16, 4], F32, stack=st)
    ncb3 = k.sb("l3a_ncb", [P, 16], F32, stack=st)
    crow3 = k.sb("l3a_crow", [1, 16], F32, stack=st)
    sq3 = k.sb("l3a_sq", [P, S], BF16, stack=st)
    st_h = ExitStack()
    eb = k.sb("l3a_eb", [P, 2, 1024], BF16, stack=st_h)
    vbar = k.sb("l3a_vbar", [64, 1], F32, stack=st_h)
    Vh = k.sb("l3a_vh", [64, S], BF16, stack=st_h)
    Wh = k.sb("l3a_wh", [64, S], BF16, stack=st_h)
    Th = k.sb("l3a_th", [64, S], F32, stack=st_h)
    Kh = k.sb("l3a_kh", [64, S], BF16, stack=st_h)
    Qh = k.sb("l3a_qh", [64, S], BF16, stack=st_h)
    Knh = k.sb("l3a_knh", [64, S], BF16, stack=st_h)
    vtok = k.sb("l3a_vt", [P, TT, 65], BF16, stack=st_h)
    v1tok = k.sb("l3a_v1t", [P, TT, 65], BF16, stack=st_h)
    vttok = k.sb("l3a_vtt", [P, TT, 65], BF16, stack=st_h)
    attnA = xln
    g["l3aQ"], g["l3aK"], g["l3aV"], g["l3arnb"], g["l3ancb"] = Q, K_, V_, rnb, ncb3
    zrow = ssqK[0:1, :]
    zrb = sq3[0:1, :]

    s_wl = k.sem("wl")
    s_pr = k.sem("pr")
    s_ev = k.sem("ev")
    s_tp = k.sem("ln_tp")
    s_ex = k.sem("l0a_ex")
    s_yz = k.sem("l0a_yz")
    s_ze = k.sem("l0a_ze")

    b_wl = s_wl.n
    on(k, "gpsimd", lambda e: e.dma_start(out=biasS[:], in_=aps["L3a_biases"][:]).then_inc(s_wl.h, 16))
    s_wl.inc(16)

    for sec, (wkey, dst, boff) in enumerate([("L3a_wqt", Q, 0), ("L3a_wkt", K_, KC),
                                             ("L3a_wvt", V_, 2 * KC)]):
        b_wl1 = s_wl.n
        on(k, "gpsimd", lambda e, wkey=wkey: e.dma_start(
            out=wbuf[:], in_=aps[wkey].rearrange("mc p n -> p mc n")).then_inc(s_wl.h, 16))
        s_wl.inc(16)
        b_pr1, b_ev1 = s_pr.n, s_ev.n
        def te_p(te, b_wl1=b_wl1, b_ev1=b_ev1):
            te.wait_ge(s_wl.h, b_wl1 + 16)
            n = 0
            for mc in range(KC):
                for tc in range(TC):
                    n += 1
                    if n > 2:
                        te.wait_ge(s_ev.h, b_ev1 + n - 2)
                    for kc in range(KC):
                        ins = te.matmul(out=ps[:, n % 2, :], lhsT=wbuf[:, mc, kc * 128:(kc + 1) * 128],
                                        rhs=xln[:, kc, tc * 512:(tc + 1) * 512],
                                        start=(kc == 0), stop=(kc == KC - 1))
                    ins.then_inc(s_pr.h, 1)
        on(k, "tensor", te_p)
        s_pr.n += KC * TC
        def sc_p(sc, dst=dst, boff=boff, b_pr1=b_pr1):
            n = 0
            for mc in range(KC):
                for tc in range(TC):
                    n += 1
                    sc.wait_ge(s_pr.h, b_pr1 + n)
                    sc.activation(out=dst[:, mc, tc * 512:(tc + 1) * 512], in_=ps[:, n % 2, :],
                                  func=AF.Identity, bias=biasS[:, boff + mc:boff + mc + 1]).then_inc(s_ev.h, 1)
        on(k, "scalar", sc_p)
        s_ev.n += KC * TC
        k.barrier()
        if upto <= 0:
            break

    if upto <= 1:
        st_h.close()
        st.close()
        return

    s_sqd = k.sem("l0a_sqd")
    for which, src_ in enumerate([K_, Q]):
        b_sqd = s_sqd.n
        b_ev4 = s_ev.n
        def sc_s(sc, src_=src_, b_sqd=b_sqd):
            for cc in range(KC):
                if cc >= 1:
                    sc.wait_ge(s_sqd.h, b_sqd + cc)
                sc.activation(out=sq3[:], in_=src_[:, cc, :], func=AF.Square)
                sc.drain().then_inc(s_ev.h, 1)
        on(k, "scalar", sc_s)
        s_ev.n += KC
        b_pr4 = s_pr.n
        def te_s(te, b_ev4=b_ev4):
            for cc in range(KC):
                te.wait_ge(s_ev.h, b_ev4 + cc + 1)
                for tc in range(TC):
                    ins = te.matmul(out=ps[0:16, tc, :], lhsT=g["sum64"][:, cc, :],
                                    rhs=sq3[:, tc * 512:(tc + 1) * 512],
                                    start=(cc == 0), stop=(cc == KC - 1), skip_group_check=True)
                ins.then_inc(s_sqd.h, 1)
            te.drain().then_inc(s_pr.h, 1)
        on(k, "tensor", te_s)
        s_sqd.n += KC
        s_pr.n += 1
        t_prs = s_pr.n
        def v_s(v, which=which, t_prs=t_prs):
            v.wait_ge(s_pr.h, t_prs)
            for tc in range(TC):
                v.tensor_copy(out=ssqK[:, tc * 512:(tc + 1) * 512], in_=ps[0:16, tc, :])
            v.drain().then_inc(s_yz.h, 1)
        on(k, "vector", v_s)
        t_y1 = s_yz.inc(1)
        def sc_n(sc, t_y1=t_y1):
            sc.wait_ge(s_yz.h, t_y1)
            sc.activation(out=ssqK[:], in_=ssqK[:], func=AF.Sqrt)
            sc.drain().then_inc(s_yz.h, 1)
        on(k, "scalar", sc_n)
        t_y2 = s_yz.inc(1)
        def v_n(v, which=which, t_y2=t_y2):
            v.wait_ge(s_yz.h, t_y2)
            v.tensor_reduce(out=maxr[:, which:which + 1], in_=ssqK[:],
                            axis=mybir.AxisListType.X, op=ALU.max)
            v.drain()
            if which == 0:
                v.tensor_scalar(out=ssqK[:], in0=ssqK[:], scalar1=1e-6, scalar2=None, op0=ALU.max)
                v.drain()
                v.reciprocal(out=ssqK[:], in_=ssqK[:])
                v.drain()
                v.tensor_copy(out=rnb[:], in_=ssqK[:])
                v.drain()
            else:
                v.scalar_tensor_tensor(out=maxr[:, 2:3], in0=maxr[:, 0:1], scalar=0.125,
                                       in1=maxr[:, 1:2], op0=ALU.mult, op1=ALU.mult)
                v.drain()
            v.nop().then_inc(s_yz.h, 1)
        on(k, "vector", v_n)
        s_yz.inc(1)
    k.barrier()

    b_pr6 = s_pr.n
    def te_c(te):
        te.transpose(out=ps[0:1, 3, 0:16], in_=maxr[:, 2:3], identity=g["identf"][0:16, 0:16])
        te.drain().then_inc(s_pr.h, 1)
    on(k, "tensor", te_c)
    s_pr.n += 1
    def v_c(v, b_pr6=b_pr6):
        v.wait_ge(s_pr.h, b_pr6 + 1)
        v.tensor_copy(out=crow3[:], in_=ps[0:1, 3, 0:16])
        v.drain().then_inc(s_yz.h, 1)
    on(k, "vector", v_c)
    t_c1 = s_yz.inc(1)
    b_pr7 = s_pr.n
    def te_c2(te, t_c1=t_c1):
        te.wait_ge(s_yz.h, t_c1)
        te.matmul(out=ps[:, 3, 0:16], lhsT=g["ones_row_f32"][:], rhs=crow3[:],
                  start=True, stop=True, skip_group_check=True).then_inc(s_pr.h, 1)
    on(k, "tensor", te_c2)
    s_pr.n += 1
    def v_c2(v, b_pr7=b_pr7):
        v.wait_ge(s_pr.h, b_pr7 + 1)
        v.tensor_scalar(out=ncb3[:], in0=ps[:, 3, 0:16], scalar1=-1.0, scalar2=None, op0=ALU.mult)
        v.drain().then_inc(s_yz.h, 1)
    on(k, "vector", v_c2)
    s_yz.inc(1)
    k.barrier()

    ps_sT = ps[:, 0:4, :]
    ps_yz = k.ps("l3a_pyz", [65, 512], F32, stack=st)

    def ev_pass(h, lhs_tok, dst_fm, kind):
        b_sc, b_ex, b_yz, b_ze = s_pr.n, s_ex.n, s_yz.n, s_ze.n
        NP = TC * TT // 2

        def score_of(te, i):
            ic, jt = i // TT, i % TT
            if kind == "sim":
                te.matmul(out=ps_sT[:, (i % 4), :], lhsT=Knh[:, jt * 128:(jt + 1) * 128],
                          rhs=Knh[:, ic * 512:(ic + 1) * 512],
                          start=True, stop=True, skip_group_check=True).then_inc(s_pr.h, 1)
            else:
                te.matmul(out=ps_sT[:, (i % 4), :], lhsT=Kh[:, jt * 128:(jt + 1) * 128],
                          rhs=Qh[:, ic * 512:(ic + 1) * 512],
                          start=True, stop=True, skip_group_check=True).then_inc(s_pr.h, 1)

        def pv_of(te, i, b_ex, b_ze):
            ic, jt = i // TT, i % TT
            m = i // 2
            te.wait_ge(s_ex.h, b_ex + m + 1)
            if jt == 0:
                te.wait_ge(s_ze.h, b_ze + 2 * ic)
            ins = te.matmul(out=ps_yz[:], lhsT=lhs_tok[:, jt, :],
                            rhs=eb[:, m % 2, (i % 2) * 512:(i % 2 + 1) * 512],
                            start=(jt == 0), stop=(jt == TT - 1), skip_group_check=True)
            if jt == TT - 1:
                ins.then_inc(s_yz.h, 1)

        def te_h(te, h=h, lhs_tok=lhs_tok, kind=kind, b_ex=b_ex, b_ze=b_ze):
            for m in range(NP):
                if m >= 2:
                    te.wait_ge(s_ex.h, b_ex + m - 1)
                score_of(te, 2 * m)
                score_of(te, 2 * m + 1)
                if m >= 1:
                    pv_of(te, 2 * m - 2, b_ex, b_ze)
                    pv_of(te, 2 * m - 1, b_ex, b_ze)
            pv_of(te, 2 * NP - 2, b_ex, b_ze)
            pv_of(te, 2 * NP - 1, b_ex, b_ze)
        on(k, "tensor", te_h)
        s_pr.n += TC * TT
        s_yz.n += TC

        def sc_h(sc, h=h, kind=kind, dst_fm=dst_fm, b_sc=b_sc, b_yz=b_yz):
            for m in range(NP):
                sc.wait_ge(s_pr.h, b_sc + 2 * m + 2)
                s0 = (2 * m) % 4
                if kind == "sim":
                    sc.activation(out=eb[:, m % 2, :], in_=ps_sT[:, s0:s0 + 2, :],
                                  func=AF.Exp, scale=0.125).then_inc(s_ex.h, 1)
                else:
                    sc.activation(out=eb[:, m % 2, :], in_=ps_sT[:, s0:s0 + 2, :],
                                  func=AF.Exp, scale=0.125,
                                  bias=ncb3[:, h:h + 1]).then_inc(s_ex.h, 1)
                if (2 * m + 1) % TT == TT - 1:
                    ic = (2 * m + 1) // TT
                    sc.wait_ge(s_yz.h, b_yz + ic + 1)
                    sc.activation(out=dst_fm[:, ic * 512:(ic + 1) * 512], in_=ps_yz[0:64, :],
                                  func=AF.Copy).then_inc(s_ze.h, 1)
        on(k, "scalar", sc_h)
        s_ex.n += NP

        def v_h(v, b_yz=b_yz, b_ze=b_ze):
            for ic in range(TC):
                v.wait_ge(s_ze.h, b_ze + 2 * ic + 1)
                v.tensor_copy(out=zrow[:, ic * 512:(ic + 1) * 512],
                              in_=ps_yz[64:65, :]).then_inc(s_ze.h, 1)
        on(k, "vector", v_h)
        s_ze.n += 2 * TC

    def warm_pe(te, n):
        for _ in range(n):
            te.matmul(out=ps[:, 0, :], lhsT=g["ident"][:], rhs=xln[:, 0, 0:512],
                      start=True, stop=True, skip_group_check=True)

    def normalize_fm(dst_fm, with_vbar, reuse_z=False):
        s_nf = k.sem("l3a_nf")
        t0_ = s_nf.n
        z_end = s_ze.n
        if not reuse_z:
            def v_r(sc, t0_=t0_):
                sc.activation(out=zrow[:], in_=zrow[:], func=AF.Ln)
                sc.drain()
                sc.activation(out=zrb[:], in_=zrow[:], func=AF.Exp, scale=-1.0)
                sc.drain().then_inc(s_nf.h, 1)
            on(k, "scalar", v_r)
            s_nf.inc(1)
            base = t0_ + 1
        else:
            base = t0_
        b_pr8 = s_pr.n
        def te_b(te, base=base, z_end=z_end, reuse_z=reuse_z):
            if reuse_z:
                te.wait_ge(s_ze.h, z_end)
            else:
                te.wait_ge(s_nf.h, base)
            for tc in range(TC):
                if tc >= 1:
                    te.wait_ge(s_nf.h, base + tc)
                te.matmul(out=ps[0:64, 3, :], lhsT=g["ones_row_bf"][:, 0:64],
                          rhs=zrb[:, tc * 512:(tc + 1) * 512],
                          start=True, stop=True, skip_group_check=True).then_inc(s_pr.h, 1)
        on(k, "tensor", te_b)
        s_pr.n += TC
        def v_m(v, dst_fm=dst_fm, with_vbar=with_vbar, b_pr8=b_pr8, t0_=t0_):
            for tc in range(TC):
                v.wait_ge(s_pr.h, b_pr8 + tc + 1)
                sl = (slice(None), slice(tc * 512, (tc + 1) * 512))
                v.tensor_tensor(out=Th[sl], in0=dst_fm[sl], in1=ps[0:64, 3, :],
                                op=ALU.mult).then_inc(s_nf.h, 1)
                if with_vbar:
                    v.tensor_scalar(out=dst_fm[sl], in0=Th[sl], scalar1=1.0 - 1e-6,
                                    scalar2=vbar[:], op0=ALU.mult, op1=ALU.add)
                else:
                    v.tensor_copy(out=dst_fm[sl], in_=Th[sl])
            v.drain().then_inc(s_nf.h, 1)
        on(k, "vector", v_m)
        s_nf.n += TC + 1

    def tokify(src_fm, dst_tok):
        b_tp = s_tp.n
        b_ev9 = s_ev.n
        def te_t(te, b_tp=b_tp, b_ev9=b_ev9, src_fm=src_fm):
            for half in range(2):
                if half >= 1:
                    te.wait_ge(s_ev.h, b_ev9 + half)
                for jt in range(8):
                    t = half * 8 + jt
                    ins = te.transpose(out=pst[:, half % 2, jt * 64:(jt + 1) * 64],
                                       in_=src_fm[:, t * 128:(t + 1) * 128],
                                       identity=g["ident"][0:64, 0:64])
                ins.then_inc(s_tp.h, 1)
        on(k, "tensor", te_t)
        s_tp.n += 2
        def sc_t(sc, b_tp=b_tp, dst_tok=dst_tok):
            for half in range(2):
                sc.wait_ge(s_tp.h, b_tp + half + 1)
                sc.activation(out=dst_tok[:, half * 8:(half + 1) * 8, 0:64],
                              in_=pst[:, half % 2, 0:512].rearrange("p (jt c) -> p jt c", c=64),
                              func=AF.Copy).then_inc(s_ev.h, 1)
        on(k, "scalar", sc_t)
        s_ev.n += 2

    for h in range(16):
        h2, hc = h % 2, h // 2
        s_cp = k.sem("l3a_cp")
        t_cp = s_cp.n
        def sc_cp(sc, h2=h2, hc=hc):
            sc.activation(out=Vh[:], in_=V_[:, hc, :][h2 * 64:(h2 + 1) * 64, :], func=AF.Copy)
            sc.activation(out=Kh[:], in_=K_[:, hc, :][h2 * 64:(h2 + 1) * 64, :], func=AF.Copy)
            sc.activation(out=Qh[:], in_=Q[:, hc, :][h2 * 64:(h2 + 1) * 64, :],
                          func=AF.Copy).then_inc(s_cp.h, 1)
        on(k, "scalar", sc_cp)
        s_cp.inc(1)
        b_pr9 = s_pr.n
        def te_kn(te, h=h, t_cp=t_cp):
            te.wait_ge(s_cp.h, t_cp + 1)
            for tc in range(TC):
                te.matmul(out=ps[0:64, 3, :], lhsT=g["sel16e"][:, h * 128:h * 128 + 64],
                          rhs=rnb[:, tc * 512:(tc + 1) * 512],
                          start=True, stop=True, skip_group_check=True).then_inc(s_pr.h, 1)
                te.wait_ge(s_cp.h, t_cp + 1 + tc + 1)
        on(k, "tensor", te_kn)
        s_pr.n += TC
        def v_kn(v, b_pr9=b_pr9, t_cp=t_cp):
            for tc in range(TC):
                v.wait_ge(s_pr.h, b_pr9 + tc + 1)
                sl = (slice(None), slice(tc * 512, (tc + 1) * 512))
                v.tensor_tensor(out=Knh[sl], in0=Kh[sl], in1=ps[0:64, 3, :],
                                op=ALU.mult).then_inc(s_cp.h, 1)
            v.tensor_reduce(out=vbar[:], in_=Vh[:], axis=mybir.AxisListType.X, op=ALU.add)
            v.drain()
            v.tensor_scalar(out=vbar[:], in0=vbar[:], scalar1=1e-6 / S, scalar2=None,
                            op0=ALU.mult).then_inc(s_cp.h, 1)
        on(k, "vector", v_kn)
        s_cp.inc(TC + 1)
        k.barrier()

        tokify(Vh, vtok)
        on(k, "vector", lambda v: v.memset(vtok[:, :, 64:65], 1.0))
        on(k, "vector", lambda v: v.memset(v1tok[:, :, 64:65], 1.0))
        on(k, "vector", lambda v: v.memset(vttok[:, :, 64:65], 1.0))
        k.barrier()

        ev_pass(h, vtok, Wh, "sim")
        normalize_fm(Wh, with_vbar=True)
        k.barrier()
        tokify(Wh, v1tok)
        k.barrier()
        ev_pass(h, v1tok, Th, "sim")
        normalize2_dst = Th
        normalize_fm(normalize2_dst, with_vbar=True, reuse_z=True)
        k.barrier()
        s_cb = k.sem("l3a_cb")
        t_cb = s_cb.n
        def v_cmb(v):
            v.tensor_scalar(out=Wh[:], in0=Wh[:], scalar1=0.0196 / 0.0004, scalar2=None, op0=ALU.mult)
            v.tensor_tensor(out=Th[:], in0=Th[:], in1=Wh[:], op=ALU.add)
            v.tensor_scalar(out=Th[:], in0=Th[:], scalar1=0.0004, scalar2=None, op0=ALU.mult)
            v.scalar_tensor_tensor(out=Wh[:], in0=Vh[:], scalar=0.98, in1=Th[:],
                                   op0=ALU.mult, op1=ALU.add).then_inc(s_cb.h, 1)
        on(k, "vector", v_cmb)
        s_cb.inc(1)
        k.barrier()
        tokify(Wh, vttok)
        k.barrier()
        ev_pass(h, vttok, Wh, "p0")
        normalize_fm(Wh, with_vbar=False)
        k.barrier()
        s_wr = k.sem("l3a_wr")
        t_wr = s_wr.n
        def sc_wr(sc, h2=h2, hc=hc):
            sc.activation(out=attnA[:, hc, :][h2 * 64:(h2 + 1) * 64, :], in_=Wh[:],
                          func=AF.Copy).then_inc(s_wr.h, 1)
        on(k, "scalar", sc_wr)
        s_wr.inc(1)
        k.barrier()

    if upto <= 2:
        st_h.close()
        st.close()
        return

    st_h.close()

    b_wl9 = s_wl.n
    on(k, "gpsimd", lambda e: e.dma_start(
        out=wbuf[:], in_=aps["L3a_wot"].rearrange("kc p n -> p kc n")).then_inc(s_wl.h, 16))
    s_wl.inc(16)
    boB = k.sb("l3a_bo", [1, 1024], BF16, stack=st)
    on(k, "gpsimd", lambda e: e.dma_start(out=boB[:], in_=aps["L3a_bob"][:]).then_inc(s_wl.h, 16))
    s_wl.inc(16)

    ra = ResidAdder(k, g, stack=st)
    b_pr11 = s_pr.n
    b_add = ra.s_add.n
    def te_o(te, b_wl9=b_wl9, b_add=b_add):
        te.wait_ge(s_wl.h, b_wl9 + 32)
        n = 0
        for t in range(TT):
            for nn in range(2):
                n += 1
                if n > 2:
                    te.wait_ge(ra.s_add.h, b_add + n - 2)
                for kc in range(KC):
                    te.matmul(out=ps[:, n % 2, :], lhsT=attnA[:, kc, t * 128:(t + 1) * 128],
                              rhs=wbuf[:, kc, nn * 512:(nn + 1) * 512],
                              start=(kc == 0), stop=False, skip_group_check=True)
                te.matmul(out=ps[:, n % 2, :], lhsT=g["ones_row_bf"][:, t * 0:128] if False else g["ones_row_bf"][:],
                          rhs=boB[:, nn * 512:(nn + 1) * 512],
                          start=False, stop=True, skip_group_check=True).then_inc(s_pr.h, 1)
    on(k, "tensor", te_o)
    s_pr.n += TT * 2

    n = 0
    for t in range(TT):
        ra.prepare(t)
        for nn in range(2):
            n += 1
            ra.add(t, nn * 512, (nn + 1) * 512, ps[:, n % 2, :], wait=(s_pr, b_pr11 + n))
        ra.flush(t)
    ra.done()
    k.barrier()
    st.close()

BLOCK_EMITTERS["l3attn"] = emit_l3attn


def build_program(blocks, host_arrays, debug=()):
    nc = bass.Bass("TRN2", target_bir_lowering=False)
    resid_in = nc.dram_tensor("resid_in", [S, D], F32, kind="ExternalInput").ap()
    resid_out = nc.dram_tensor("resid_out", [S, D], F32, kind="ExternalOutput").ap()
    scratch = [nc.dram_tensor(f"rscratch{i}", [S, D], F32, kind="Internal").ap() for i in range(2)]
    aps = declare_inputs(nc, host_arrays)
    with ExitStack() as ctx:
        block = ctx.enter_context(nc.Block())
        k = Kit(nc, ctx, block)
        g = {}
        g["xln"] = k.sb("xln", [P, KC, S], BF16)
        g["ident"] = k.sb("identc", [P, P], BF16)
        g["ones_row_bf"] = k.sb("onesrbf", [1, P], BF16)
        g["ones_row_f32"] = k.sb("onesrf", [1, P], F32)
        g["ones_col_bf"] = k.sb("onescbf", [P, 1], BF16)
        g["sel16e"] = k.sb("sel16e", [16, 16 * P], BF16)
        g["c_one"] = k.sb("conebf", [1, 1], BF16)
        g["identf"] = k.sb("identf", [P, P], F32)
        g["sum64"] = k.sb("gsum64", [P, KC, 16], BF16)
        g["ps_tr"] = k.ps("ps_tr", [P, 2, 1024], BF16)
        g["ps_mm"] = k.ps("ps_mm", [P, 4, 512], F32)

        sc_, tc_ = load_const(k, g["ident"], aps["c_ident_bf"][:])
        load_const(k, g["ones_row_bf"], aps["c_ones_row_bf"][:])
        load_const(k, g["ones_row_f32"], aps["c_ones_row_f32"][:])
        load_const(k, g["ones_col_bf"], aps["c_ones_col_bf"][:])
        load_const(k, g["c_one"], aps["c_one_bf"][:])
        load_const(k, g["identf"], aps["c_ident_f32"][:])
        sg_, tg_ = load_const(k, g["sum64"], aps["c_sum64_bf"].rearrange("kc p m -> p kc m"), queue="gpsimd")
        _, tlast = load_const(k, g["sel16e"], aps["c_sel16e_bf"][:])
        bar_w = k.sem("ldc")
        for e in ("tensor", "vector", "scalar"):
            def fw(eng, t=tlast, tg=tg_):
                eng.wait_ge(bar_w.h, t)
                eng.wait_ge(sg_.h, tg)
            on(k, e, fw)

        n = len(blocks)
        for i, b in enumerate(blocks):
            g["resid_src"] = resid_in if i == 0 else scratch[(i - 1) % 2]
            g["resid_dst"] = resid_out if i == n - 1 else scratch[i % 2]
            emit_ln(k, g)
            BLOCK_EMITTERS[b](k, g, aps)
        if debug:
            s_dbg = k.sem("dbg")
            names = list(debug)
            def fdbg(e):
                for name in names:
                    buf = g[name]
                    out_ap = nc.dram_tensor("dbg_" + name, list(buf.shape), buf.dtype,
                                            kind="ExternalOutput").ap()
                    e.dma_start(out=out_ap[:], in_=buf[:]).then_inc(s_dbg.h, 16)
                e.wait_ge(s_dbg.h, 16 * len(names))
            on(k, "sync", fdbg)
    return nc

BLOCK_EMITTERS["l2ffn"] = emit_l2ffn


def prep_weights(inputs):
    h = {}
    h["L2f_wt"] = moving_tiles(np.ascontiguousarray(inputs["L2_ffn_weight"].T))
    h["L3f_selt"] = lhsT_tiles(np.ascontiguousarray(inputs["L3_ffn_expert_sel"].T), mtile=16)
    h["L3f_keyst"] = np.stack([lhsT_tiles(inputs["L3_ffn_keys"][e])[0]
                               for e in range(16)])
    h["L3f_valst"] = np.stack([moving_tiles(inputs["L3_ffn_values"][e])[0]
                               for e in range(16)])
    h["L0f_w1t"] = lhsT_tiles(np.ascontiguousarray(inputs["L0_ffn_w1_w"].T))
    h["L0f_w2t"] = lhsT_tiles(np.ascontiguousarray(inputs["L0_ffn_w2_w"].T))
    h["L0f_h1t"] = lhsT_tiles(np.ascontiguousarray(inputs["L0_ffn_h1_w"].T))
    h["L0f_h2t"] = lhsT_tiles(np.ascontiguousarray(inputs["L0_ffn_h2_w"].T))
    h["L0f_w3t"] = lhsT_tiles(np.ascontiguousarray(inputs["L0_ffn_w3_w"].T))
    h["L0f_scales"] = np.concatenate([
        inputs["L0_ffn_w1_s"].reshape(32, 128),
        inputs["L0_ffn_w2_s"].reshape(32, 128),
        inputs["L0_ffn_h1_s"].reshape(32, 128),
        inputs["L0_ffn_h2_s"].reshape(32, 128),
        inputs["L0_ffn_w3_s"].reshape(8, 128)], axis=0).astype(np.float32)
    h["L0f_scales"] = np.ascontiguousarray(h["L0f_scales"].T)
    h["L2a_wqt"] = lhsT_tiles(np.ascontiguousarray(inputs["L2_attn_Wq"].T))
    h["L2a_wkt"] = lhsT_tiles(np.ascontiguousarray(inputs["L2_attn_Wk"].T))
    h["L2a_wvt"] = lhsT_tiles(np.ascontiguousarray(inputs["L2_attn_Wv"].T))
    h["L2a_projt"] = lhsT_tiles(np.ascontiguousarray(inputs["L2_attn_proj"]))
    h["L1f_sigsT"] = np.ascontiguousarray(inputs["L1_ffn_tile_sigs"].T)
    h["L1f_wt"] = np.stack([moving_tiles(np.ascontiguousarray(inputs["L1_ffn_W"][u].T))
                            for u in range(8)])
    h["L1_ffn_b"] = np.ascontiguousarray(inputs["L1_ffn_b"])
    h["L1a_wqkvt"] = lhsT_tiles(np.ascontiguousarray(inputs["L1_attn_w_qkv"]))
    h["L1a_woutt"] = moving_tiles(np.ascontiguousarray(inputs["L1_attn_w_out"]))
    h["L1a_bout"] = np.ascontiguousarray(inputs["L1_attn_b_out"].reshape(1, 1024))
    s_ = np.repeat(np.arange(3), 1024)
    hh_ = np.tile(np.repeat(np.arange(16), 64), 3)
    dd_ = np.tile(np.arange(64), 48)
    newpos = s_ * 1024 + (dd_ // 32) * 512 + (hh_ // 4) * 128 + (hh_ % 4) * 32 + (dd_ % 32)
    order = np.empty(3072, dtype=np.int64)
    order[newpos] = np.arange(3072)
    wT = np.ascontiguousarray(inputs["L0_attn_w_qkv"].T)
    wTp = wT[:, order]
    h["L0a_wqkvt"] = np.ascontiguousarray(
        wTp.reshape(8, 128, 3, 1024).transpose(2, 1, 0, 3))
    pT = np.ascontiguousarray(inputs["L0_attn_prototypes"].T)
    h["L0a_protot"] = np.ascontiguousarray(pT.reshape(8, 128, 1024).transpose(1, 0, 2))
    hv = np.repeat(np.arange(16), 64)
    dv = np.tile(np.arange(64), 16)
    newv = (dv // 32) * 512 + (hv // 4) * 128 + (hv % 4) * 32 + (dv % 32)
    orderv = np.empty(1024, dtype=np.int64)
    orderv[newv] = np.arange(1024)
    fT = np.ascontiguousarray(inputs["L0_attn_features"].T)
    h["L0a_featP"] = np.ascontiguousarray(fT[orderv].reshape(8, 128, 16).transpose(1, 0, 2))
    h["L0a_featO"] = np.ascontiguousarray(fT.reshape(8, 128, 16).transpose(1, 0, 2))
    h["L0_attn_q_gain"] = np.ascontiguousarray(inputs["L0_attn_q_gain"].reshape(16, 1))
    h["L0_attn_diff_lambda"] = np.ascontiguousarray(inputs["L0_attn_diff_lambda"].reshape(16, 1))
    h["L0_attn_theta"] = np.ascontiguousarray(inputs["L0_attn_theta"].reshape(1, 1))
    h["L0_attn_alpha"] = np.ascontiguousarray(inputs["L0_attn_alpha"].reshape(1, 1))
    h["L0_attn_beta"] = np.ascontiguousarray(inputs["L0_attn_beta"].reshape(1, 1))
    h["L3a_wqt"] = lhsT_tiles(np.ascontiguousarray(inputs["L3_attn_Wq"]))
    h["L3a_wkt"] = lhsT_tiles(np.ascontiguousarray(inputs["L3_attn_Wk"]))
    h["L3a_wvt"] = lhsT_tiles(np.ascontiguousarray(inputs["L3_attn_Wv"]))
    h["L3a_wot"] = moving_tiles(np.ascontiguousarray(inputs["L3_attn_Wo"]))
    h["L3a_bob"] = np.ascontiguousarray(inputs["L3_attn_bo"].reshape(1, 1024))
    h["L3a_biases"] = np.ascontiguousarray(np.concatenate(
        [inputs["L3_attn_bq"].reshape(8, 128), inputs["L3_attn_bk"].reshape(8, 128),
         inputs["L3_attn_bv"].reshape(8, 128), inputs["L3_attn_bo"].reshape(8, 128)],
        axis=0).T)
    return h



BLOCK_ORDER = ["l0attn", "l0ffn", "l1attn", "l1ffn", "l2attn", "l2ffn", "l3attn", "l3ffn"]
NCORES = 8

def kernel(**inputs):
    host = prep_weights(inputs)
    host.update(host_constants())
    resid = np.ascontiguousarray(inputs["L0_attn_x"], dtype=np.float32)
    nc = build_program(BLOCK_ORDER, host)
    in_maps = [dict(host, resid_in=np.ascontiguousarray(resid[c])) for c in range(NCORES)]
    res = run_bass_kernel_spmd(nc, in_maps, core_ids=list(range(NCORES)))
    out = [np.asarray(res.results[c]["resid_out"], dtype=np.float32) for c in range(NCORES)]
    return np.stack(out).reshape(8, 2048, 1024)


# version 07 (final)
# speedup vs baseline: 1.4498x; 1.0024x over previous
import math
import numpy as np
import concourse.bass as bass
import concourse.mybir as mybir
from concourse.bass_utils import run_bass_kernel_spmd
from contextlib import ExitStack

F32 = mybir.dt.float32
BF16 = mybir.dt.bfloat16
AF = mybir.ActivationFunctionType
ALU = mybir.AluOpType

P = 128
S = 2048
D = 1024
KC = D // P
TT = S // P
TC = S // 512
NH = 16
HD = 64


def lhsT_tiles(w_io, mtile=128):
    din, dout = w_io.shape
    kc, mc = din // P, dout // mtile
    t = w_io.reshape(kc, P, mc, mtile).transpose(2, 1, 0, 3)
    return np.ascontiguousarray(t.reshape(mc, P, kc * mtile))

def moving_tiles(w_io):
    din, dout = w_io.shape
    return np.ascontiguousarray(w_io.reshape(din // P, P, dout))


class Sem:
    __slots__ = ("h", "n")
    def __init__(self, h):
        self.h = h
        self.n = 0
    def inc(self, k=1):
        self.n += k
        return self.n

class Kit:
    def __init__(self, nc, ctx, block):
        self.nc, self.ctx, self.block = nc, ctx, block
        self._sems = {}
        self._uid = 0
    def uid(self):
        self._uid += 1
        return self._uid
    def sem(self, name):
        if name not in self._sems:
            self._sems[name] = Sem(self.ctx.enter_context(self.nc.semaphore(name)))
        return self._sems[name]
    def sb(self, name, shape, dt=F32, stack=None):
        return (stack or self.ctx).enter_context(
            self.nc.sbuf_tensor(f"{name}_{self.uid()}", shape, dt))
    def ps(self, name, shape, dt=F32, stack=None):
        return (stack or self.ctx).enter_context(
            self.nc.psum_tensor(f"{name}_{self.uid()}", shape, dt))
    def barrier(self):
        bar = self.sem("bar")
        tgt = bar.n + 5
        bar.n = tgt
        for name in ("sync", "gpsimd", "vector", "scalar", "tensor"):
            def f(eng, tgt=tgt, name=name):
                if name == "sync":
                    eng.nop().then_inc(self._sems["bar"].h, 1)
                else:
                    eng.drain().then_inc(self._sems["bar"].h, 1)
                eng.wait_ge(self._sems["bar"].h, tgt)
            getattr(self.block, name)(f)

BLOCK_EMITTERS = {}

def on(k, eng_name, f):
    getattr(k.block, eng_name)(f)

def load_const(k, dst, src_ap, queue="sync"):
    s = k.sem("ldc" if queue == "sync" else "ldcg")
    t = s.inc(16)
    on(k, queue, lambda e: e.dma_start(out=dst[:], in_=src_ap).then_inc(s.h, 16))
    return s, t


def host_constants():
    import ml_dtypes
    bf = ml_dtypes.bfloat16
    c = {}
    c["c_ident_bf"] = np.eye(P, dtype=bf)
    c["c_ones_row_f32"] = np.ones((1, P), dtype=np.float32)
    c["c_ones_row_bf"] = np.ones((1, P), dtype=bf)
    c["c_ones_col_bf"] = np.ones((P, 1), dtype=bf)
    c["c_one_bf"] = np.ones((1, 1), dtype=bf)
    se = np.zeros((16, 16 * P), dtype=np.float32)
    for e in range(16):
        se[e, e * P:(e + 1) * P] = 1.0
    c["c_sel16e_bf"] = se.astype(bf)
    pos = np.arange(S, dtype=np.float32)[:, None]
    div = np.exp(np.arange(0, 32, 2, dtype=np.float32) * (-math.log(10000.0) / 32))
    ang = (pos * div).astype(np.float32)
    pe = np.zeros((S, 32), dtype=np.float32)
    pe[:, 0::2] = np.sin(ang)
    pe[:, 1::2] = np.cos(ang)
    c["c_peT"] = np.ascontiguousarray(pe.T)
    ss = np.zeros((KC, P, NH), dtype=np.float32)
    sb_ = np.zeros((KC, NH, P), dtype=np.float32)
    for kc in range(KC):
        for r in range(P):
            hh = 2 * kc + r // 64
            ss[kc, r, hh] = 1.0
            sb_[kc, hh, r] = 1.0
    c["c_sum64_bf"] = ss.astype(bf)
    c["c_bc64_bf"] = sb_.astype(bf)
    cnt = np.full((1, S), 2.0, dtype=np.float32)
    cnt[0, [0, 1, S - 2, S - 1]] = 1.0
    c["c_l1a_count"] = cnt.astype(bf)
    c["c_ident_f32"] = np.eye(P, dtype=np.float32)
    g64 = np.zeros((P, 2), dtype=np.float32)
    g64[0:64, 0] = 1.0
    g64[64:128, 1] = 1.0
    c["c_g64_bf"] = g64.astype(bf)
    c["c_g64T_bf"] = np.ascontiguousarray(g64.T).astype(bf)
    sump = np.zeros((4, P, 16), dtype=np.float32)
    bcp = np.zeros((4, 16, P), dtype=np.float32)
    for v in range(4):
        for r in range(P):
            hh = 4 * v + r // 32
            sump[v, r, hh] = 1.0
            bcp[v, hh, r] = 1.0
    c["c_sumP_bf"] = sump.astype(bf)
    c["c_bcP_bf"] = bcp.astype(bf)
    ar = np.arange(0, 64, 2, dtype=np.float32)
    inv_freq = 1.0 / 10000.0 ** (ar / 64)
    ramp = np.clip((ar / 64 - 0.25) / 0.75, 0.0, 1.0)
    inv_freq = inv_freq / (ramp * 3.0 + 1.0)
    freqs = np.arange(S, dtype=np.float32)[:, None] * inv_freq[None, :]
    cosT = np.cos(freqs)
    sinT = np.sin(freqs)
    ridx = np.arange(P) % 32
    c["c_l0cos_bf"] = np.ascontiguousarray(cosT[:, ridx].T).astype(bf)
    c["c_l0sin_bf"] = np.ascontiguousarray(sinT[:, ridx].T).astype(bf)
    ltm = (np.arange(P)[:, None] <= np.arange(P)[None, :]).astype(np.float32)
    c["c_ltmask_bf"] = ltm.astype(bf)
    return c

def declare_inputs(nc, host):
    aps = {}
    for name, arr in host.items():
        dt = {"float32": F32, "bfloat16": BF16}[str(arr.dtype)]
        aps[name] = nc.dram_tensor(name, list(arr.shape), dt, kind="ExternalInput").ap()
    return aps


def emit_ln(k, g):
    nc = k.nc
    st = ExitStack()
    resid = k.sb("lnresid", [P, TT, D], F32, stack=st)
    st6 = k.sb("lnst6", [P, TT, 12], F32, stack=st)
    mv = k.sb("lnmv", [P, TT, 2], F32, stack=st)
    rstd = k.sb("lnrstd", [P, TT], F32, stack=st)
    veps = k.sb("lnveps", [P, TT], F32, stack=st)
    tmb = k.sb("lntmb", [P, 2, D], BF16, stack=st)
    xln = g["xln"]
    ident = g["ident"]
    pst = g["ps_tr"]

    s_ld = k.sem("ln_ld")
    s_v2s = k.sem("ln_v2s")
    s_s2v = k.sem("ln_s2v")
    s_tm = k.sem("ln_tm")
    s_tp = k.sem("ln_tp")
    s_ev = k.sem("ln_ev")

    t_ld = s_ld.inc(16)
    on(k, "sync", lambda e: e.dma_start(
        out=resid[:], in_=g["resid_src"].rearrange("(t p) d -> p t d", p=P)).then_inc(s_ld.h, 16))

    base_tm, base_tp, base_ev = s_tm.n, s_tp.n, s_ev.n

    def v1(v):
        v.wait_ge(s_ld.h, t_ld)
        for t in range(TT):
            v.bn_stats(out=st6[:, t, 0:6], in_=resid[:, t, 0:512])
            v.bn_stats(out=st6[:, t, 6:12], in_=resid[:, t, 512:1024])
        v.drain()
        for t in range(TT):
            v.bn_aggr(out=mv[:, t, :], in_=st6[:, t, :])
        v.drain()
        v.tensor_scalar(out=veps[:], in0=mv[:, :, 1:2], scalar1=1e-5, scalar2=None, op0=ALU.add)
        v.drain().then_inc(s_v2s.h, 1)
    on(k, "vector", v1)
    tv = s_v2s.inc(1)

    def sc1(sc):
        sc.wait_ge(s_v2s.h, tv)
        sc.activation(out=veps[:], in_=veps[:], func=AF.Sqrt)
        sc.drain().then_inc(s_s2v.h, 1)
    on(k, "scalar", sc1)
    ts_ = s_s2v.inc(1)

    def v2(v):
        v.wait_ge(s_s2v.h, ts_)
        v.reciprocal(out=rstd[:], in_=veps[:])
        v.drain()
        for t in range(TT):
            if t >= 2:
                v.wait_ge(s_tp.h, base_tp + t - 1)
            v.tensor_scalar(out=tmb[:, t % 2, :], in0=resid[:, t, :],
                            scalar1=mv[:, t, 0:1], scalar2=rstd[:, t:t + 1],
                            op0=ALU.subtract, op1=ALU.mult).then_inc(s_tm.h, 1)
    on(k, "vector", v2)
    s_tm.n += TT

    def te(te_):
        for t in range(TT):
            te_.wait_ge(s_tm.h, base_tm + t + 1)
            if t >= 2:
                te_.wait_ge(s_ev.h, base_ev + t - 1)
            for kc in range(KC):
                ins = te_.transpose(out=pst[:, t % 2, kc * 128:(kc + 1) * 128],
                                    in_=tmb[:, t % 2, kc * 128:(kc + 1) * 128],
                                    identity=ident[:])
                if kc == KC - 1:
                    ins.then_inc(s_tp.h, 1)
    on(k, "tensor", te)
    s_tp.n += TT

    def sc2(sc):
        for t in range(TT):
            sc.wait_ge(s_tp.h, base_tp + t + 1)
            sc.activation(out=xln[:, :, t * 128:(t + 1) * 128],
                          in_=pst[:, t % 2, :].rearrange("p (kc c) -> p kc c", kc=KC),
                          func=AF.Copy).then_inc(s_ev.h, 1)
    on(k, "scalar", sc2)
    s_ev.n += TT
    k.barrier()
    st.close()


class ResidAdder:
    def __init__(self, k, g, stack=None):
        self.k, self.g = k, g
        self.buf = k.sb("radd", [P, 2, D], F32, stack=stack)
        self.s_ld = k.sem("ra_ld")
        self.s_add = k.sem("ra_add")
        self.s_st = k.sem("ra_st")
        self.b_ld = self.s_ld.n
        self.b_st = self.s_st.n

    def prepare(self, t):
        kk, g = self.k, self.g
        def f(e, t=t):
            if t >= 2:
                e.wait_ge(self.s_st.h, self.b_st + 16 * (t - 1))
            e.dma_start(out=self.buf[:, t % 2, :],
                        in_=g["resid_src"].rearrange("(t p) d -> t p d", p=P)[t]).then_inc(self.s_ld.h, 16)
        on(kk, "sync", f)
        self.s_ld.inc(16)

    def add(self, t, c0, c1, psum_ap, scale_ap=None, wait=None):
        kk = self.k
        ld_tgt = self.b_ld + 16 * (t + 1)
        def f(v, t=t, c0=c0, c1=c1, psum_ap=psum_ap, scale_ap=scale_ap, wait=wait):
            v.wait_ge(self.s_ld.h, ld_tgt)
            if wait is not None:
                v.wait_ge(wait[0].h, wait[1])
            dst = self.buf[:, t % 2, c0:c1]
            if scale_ap is None:
                v.tensor_tensor(out=dst, in0=psum_ap, in1=dst, op=ALU.add).then_inc(self.s_add.h, 1)
            else:
                v.scalar_tensor_tensor(out=dst, in0=psum_ap, scalar=scale_ap, in1=dst,
                                       op0=ALU.mult, op1=ALU.add).then_inc(self.s_add.h, 1)
        on(kk, "vector", f)
        self.s_add.inc(1)

    def flush(self, t):
        kk, g = self.k, self.g
        tgt = self.s_add.n
        def f(e, t=t, tgt=tgt):
            e.wait_ge(self.s_add.h, tgt)
            e.dma_start(out=g["resid_dst"].rearrange("(t p) d -> t p d", p=P)[t],
                        in_=self.buf[:, t % 2, :]).then_inc(self.s_st.h, 16)
        on(kk, "sync", f)
        self.s_st.inc(16)

    def done(self):
        tgt = self.s_st.n
        on(self.k, "sync", lambda e: e.wait_ge(self.s_st.h, tgt))


def emit_l2ffn(k, g, aps):
    st = ExitStack()
    xln = g["xln"]
    wsign = k.sb("l2fw", [P, 8, 1024], BF16, stack=st)
    psum = g["ps_mm"]
    s_wl = k.sem("wl")
    s_sg = k.sem("l2f_sg")
    s_pr = k.sem("pr")

    t_wl = s_wl.inc(16)
    on(k, "gpsimd", lambda e: e.dma_start(
        out=wsign[:], in_=aps["L2f_wt"].rearrange("kc p n -> p kc n")).then_inc(s_wl.h, 16))

    def sc(sc_):
        sc_.wait_ge(s_wl.h, t_wl)
        sc_.activation(out=wsign[:], in_=wsign[:], func=AF.Sign)
        sc_.drain().then_inc(s_sg.h, 1)
    on(k, "scalar", sc)
    t_sg = s_sg.inc(1)

    ra = ResidAdder(k, g, stack=st)
    base_pr, base_add = s_pr.n, ra.s_add.n
    def te(te_):
        te_.wait_ge(s_sg.h, t_sg)
        n = 0
        for t in range(TT):
            for half in range(2):
                n += 1
                if n > 2:
                    te_.wait_ge(ra.s_add.h, base_add + n - 2)
                for kc in range(KC):
                    ins = te_.matmul(out=psum[:, n % 2, :],
                                     lhsT=xln[:, kc, t * 128:(t + 1) * 128],
                                     rhs=wsign[:, kc, half * 512:(half + 1) * 512],
                                     start=(kc == 0), stop=(kc == KC - 1))
                    if kc == KC - 1:
                        ins.then_inc(s_pr.h, 1)
    on(k, "tensor", te)
    s_pr.n += TT * 2

    n = 0
    for t in range(TT):
        ra.prepare(t)
        for half in range(2):
            n += 1
            ra.add(t, half * 512, (half + 1) * 512, psum[:, n % 2, :], wait=(s_pr, base_pr + n))
        ra.flush(t)
    ra.done()
    k.barrier()
    st.close()


def emit_l3ffn(k, g, aps):
    st = ExitStack()
    xln = g["xln"]
    selW = k.sb("l3f_selw", [P, 8 * 16], BF16, stack=st)
    keysW = k.sb("l3f_keys", [P, 16, 1024], BF16, stack=st)
    valsW = k.sb("l3f_vals", [P, 16, 1024], BF16, stack=st)
    selS = k.sb("l3f_sel", [16, S], BF16, stack=st)
    scb = k.sb("l3f_scb", [P, 2, 256], BF16, stack=st)
    rbuf = k.sb("l3f_rl", [P, 2, 256], BF16, stack=st)
    ps_out = g["ps_mm"]
    pp = [k.ps("l3f_pp0", [P, 512], F32, stack=st),
          k.ps("l3f_pp1", [P, 512], F32, stack=st)]
    ps_sel = g["ps_mm"][0:16, 0, :]

    s_wl = k.sem("wl")
    b_wl = s_wl.n
    def gl(e):
        e.dma_start(out=selW[:], in_=aps["L3f_selt"].rearrange("a p m -> p (a m)")).then_inc(s_wl.h, 16)
        e.dma_start(out=keysW[:], in_=aps["L3f_keyst"].rearrange("e p n -> p e n")).then_inc(s_wl.h, 16)
        e.dma_start(out=valsW[:], in_=aps["L3f_valst"].rearrange("e p n -> p e n")).then_inc(s_wl.h, 16)
    on(k, "gpsimd", gl)
    s_wl.inc(48)

    s_pr = k.sem("pr")
    b_pr = s_pr.n
    s_ev = k.sem("ev")
    b_ev = s_ev.n
    def te1(te):
        te.wait_ge(s_wl.h, b_wl + 16)
        for tc in range(TC):
            if tc >= 1:
                te.wait_ge(s_ev.h, b_ev + tc)
            for kc in range(KC):
                ins = te.matmul(out=ps_sel, lhsT=selW[:, kc * 16:(kc + 1) * 16],
                                rhs=xln[:, kc, tc * 512:(tc + 1) * 512],
                                start=(kc == 0), stop=(kc == KC - 1))
            ins.then_inc(s_pr.h, 1)
    on(k, "tensor", te1)
    s_pr.n += TC

    def sc1(sc):
        for tc in range(TC):
            sc.wait_ge(s_pr.h, b_pr + tc + 1)
            sc.activation(out=selS[:, tc * 512:(tc + 1) * 512], in_=ps_sel,
                          func=AF.Sigmoid).then_inc(s_ev.h, 1)
    on(k, "scalar", sc1)
    s_ev.n += TC

    ra = ResidAdder(k, g, stack=st)
    s_sc = k.sem("l3f_sc")
    s_bc = k.sem("l3f_bc")
    s_rl = k.sem("l3f_rlu")
    s_ss = k.sem("l3f_ss")
    b_sc, b_bc, b_rl, b_ss = s_sc.n, s_bc.n, s_rl.n, s_ss.n
    b_add = ra.s_add.n
    NTG = 8
    E = 16
    b_pr2 = s_pr.n

    def out_of(te, tg, e):
        nsc = tg * E + e + 1
        par = nsc % 2
        te.wait_ge(s_ss.h, b_ss + nsc)
        for tt2 in range(2):
            for nn in range(2):
                te.matmul(out=ps_out[:, tt2 * 2 + nn, :],
                          lhsT=scb[:, par, tt2 * 128:(tt2 + 1) * 128],
                          rhs=valsW[:, e, nn * 512:(nn + 1) * 512],
                          start=(e == 0), stop=(e == E - 1),
                          skip_group_check=True).then_inc(s_pr.h, 1)

    for tg in range(NTG):
        def te2(te, tg=tg):
            if tg == 0:
                te.wait_ge(s_ev.h, b_ev + TC)
                te.wait_ge(s_wl.h, b_wl + 48)
            else:
                te.wait_ge(ra.s_add.h, b_add + tg * 4)
            for e in range(E):
                nsc = tg * E + e + 1
                if nsc > 2:
                    te.wait_ge(s_ss.h, b_ss + nsc - 2)
                par = nsc % 2
                te.matmul(out=pp[par][:, 256:512], lhsT=g["sel16e"][:, e * 128:(e + 1) * 128],
                          rhs=selS[:, tg * 256:(tg + 1) * 256],
                          start=True, stop=True, skip_group_check=True).then_inc(s_bc.h, 1)
                for kc in range(KC):
                    ins = te.matmul(out=pp[par][:, 0:256], lhsT=keysW[:, e, kc * 128:(kc + 1) * 128],
                                    rhs=xln[:, kc, tg * 256:(tg + 1) * 256],
                                    start=(kc == 0), stop=(kc == KC - 1))
                ins.then_inc(s_sc.h, 1)
                if e >= 1:
                    out_of(te, tg, e - 1)
            out_of(te, tg, E - 1)
        on(k, "tensor", te2)
        s_bc.n += E
        s_sc.n += E
        s_pr.n += E * 4

        def sc2(sc, tg=tg):
            for e in range(E):
                n = tg * E + e + 1
                par = n % 2
                sc.wait_ge(s_sc.h, b_sc + n)
                if n > 2:
                    sc.wait_ge(s_ss.h, b_ss + n - 2)
                sc.activation(out=rbuf[:, par, :], in_=pp[par][:, 0:256],
                              func=AF.Relu).then_inc(s_rl.h, 1)
        on(k, "scalar", sc2)
        s_rl.n += E

        def v2(v, tg=tg):
            for e in range(E):
                n = tg * E + e + 1
                par = n % 2
                v.wait_ge(s_rl.h, b_rl + n)
                v.wait_ge(s_bc.h, b_bc + n)
                if n > 2:
                    v.wait_ge(s_pr.h, b_pr2 + (n - 2) * 4)
                v.tensor_tensor(out=scb[:, par, :], in0=rbuf[:, par, :],
                                in1=pp[par][:, 256:512], op=ALU.mult).then_inc(s_ss.h, 1)
        on(k, "vector", v2)
        s_ss.n += E

        t0 = tg * 2
        ra.prepare(t0)
        ra.prepare(t0 + 1)
        n_end = b_pr2 + (tg + 1) * E * 4
        for tt2 in range(2):
            for nn in range(2):
                ra.add(t0 + tt2, nn * 512, (nn + 1) * 512, ps_out[:, tt2 * 2 + nn, :],
                       wait=(s_pr, n_end))
        ra.flush(t0)
        ra.flush(t0 + 1)
    ra.done()
    k.barrier()
    st.close()

BLOCK_EMITTERS["l3ffn"] = emit_l3ffn


def emit_l0ffn(k, g, aps):
    st = ExitStack()
    xln = g["xln"]
    a = k.sb("l0f_a", [P, 32, 1024], BF16, stack=st)
    b = k.sb("l0f_b", [P, 32, 1024], BF16, stack=st)
    outF = b[:, 0:8, :]
    tmp = k.sb("l0f_t", [P, 2, 512], BF16, stack=st)
    w1b = k.sb("l0f_w1", [P, 2, 1024], BF16, stack=st)
    w2b = k.sb("l0f_w2", [P, 2, 1024], BF16, stack=st)
    hb = k.sb("l0f_h", [P, 2, 4096], BF16, stack=st)
    scales = k.sb("l0f_s", [P, 4 * 32 + 8], F32, stack=st)
    ps = g["ps_mm"]
    pst = g["ps_tr"]
    ra = ResidAdder(k, g, stack=st)

    s_wl = k.sem("wl")
    s_wf = k.sem("l0f_wf")
    s_pr = k.sem("pr")
    s_ev = k.sem("ev")
    s_sl = k.sem("l0f_sl")
    s_tp = k.sem("ln_tp")

    t0_ = s_wl.inc(16)
    on(k, "gpsimd", lambda e: e.dma_start(out=scales[:], in_=aps["L0f_scales"][:]).then_inc(s_wl.h, 16))
    sc_s1 = scales[:, 0:32]
    sc_s2 = scales[:, 32:64]
    sc_h1 = scales[:, 64:96]
    sc_h2 = scales[:, 96:128]
    sc_s3 = scales[:, 128:136]

    for half in range(2):
        tok0 = half * 1024
        b_wl, b_wf, b_pr, b_ev, b_sl = s_wl.n, s_wf.n, s_pr.n, s_ev.n, s_sl.n
        def g1(e, b_wf=b_wf):
            for mc in range(32):
                if mc >= 2:
                    e.wait_ge(s_wf.h, b_wf + mc - 1)
                e.dma_start(out=w1b[:, mc % 2, :], in_=aps["L0f_w1t"][mc]).then_inc(s_wl.h, 16)
                e.dma_start(out=w2b[:, mc % 2, :], in_=aps["L0f_w2t"][mc]).then_inc(s_wl.h, 16)
        on(k, "gpsimd", g1)
        s_wl.n += 32 * 32

        def t1(te, b_wl=b_wl, b_ev=b_ev, tok0=tok0):
            te.wait_ge(s_wl.h, t0_)
            n = 0
            for mc in range(32):
                te.wait_ge(s_wl.h, b_wl + 32 * (mc + 1))
                for tcc in range(2):
                    n += 1
                    if n > 2:
                        te.wait_ge(s_ev.h, b_ev + n - 2)
                    q = n % 2
                    for kc in range(KC):
                        i1 = te.matmul(out=ps[:, q * 2, :], lhsT=w1b[:, mc % 2, kc * 128:(kc + 1) * 128],
                                       rhs=xln[:, kc, tok0 + tcc * 512: tok0 + (tcc + 1) * 512],
                                       start=(kc == 0), stop=(kc == KC - 1))
                    i1.then_inc(s_pr.h, 1)
                    for kc in range(KC):
                        i2 = te.matmul(out=ps[:, q * 2 + 1, :], lhsT=w2b[:, mc % 2, kc * 128:(kc + 1) * 128],
                                       rhs=xln[:, kc, tok0 + tcc * 512: tok0 + (tcc + 1) * 512],
                                       start=(kc == 0), stop=(kc == KC - 1))
                    i2.then_inc(s_pr.h, 1)
                te.drain().then_inc(s_wf.h, 1)
        on(k, "tensor", t1)
        s_pr.n += 128
        s_wf.n += 32

        def sc1(sc, b_pr=b_pr, b_sl=b_sl, b_ev=b_ev):
            n = 0
            for mc in range(32):
                for tcc in range(2):
                    n += 1
                    sc.wait_ge(s_pr.h, b_pr + 2 * n - 1)
                    if n > 2:
                        sc.wait_ge(s_ev.h, b_ev + n - 2)
                    sc.activation(out=tmp[:, n % 2, :], in_=ps[:, (n % 2) * 2, :],
                                  func=AF.Silu, scale=sc_s1[:, mc:mc + 1]).then_inc(s_sl.h, 1)
        on(k, "scalar", sc1)
        s_sl.n += 64

        def v1(v, b_pr=b_pr, b_sl=b_sl):
            n = 0
            for mc in range(32):
                for tcc in range(2):
                    n += 1
                    v.wait_ge(s_sl.h, b_sl + n)
                    v.wait_ge(s_pr.h, b_pr + 2 * n)
                    v.scalar_tensor_tensor(out=a[:, mc, tcc * 512:(tcc + 1) * 512],
                                           in0=ps[:, (n % 2) * 2 + 1, :], scalar=sc_s2[:, mc:mc + 1],
                                           in1=tmp[:, n % 2, :], op0=ALU.mult,
                                           op1=ALU.mult).then_inc(s_ev.h, 1)
        on(k, "vector", v1)
        s_ev.n += 64
        k.barrier()

        for stg, (wkey, scol, src, dst) in enumerate(
                [("L0f_h1t", sc_h1, a, b), ("L0f_h2t", sc_h2, b, a)]):
            b_wl, b_wf, b_pr, b_ev = s_wl.n, s_wf.n, s_pr.n, s_ev.n
            def g2(e, wkey=wkey, b_wf=b_wf):
                for mc in range(32):
                    if mc >= 2:
                        e.wait_ge(s_wf.h, b_wf + mc - 1)
                    e.dma_start(out=hb[:, mc % 2, :], in_=aps[wkey][mc]).then_inc(s_wl.h, 16)
            on(k, "gpsimd", g2)
            s_wl.n += 32 * 16

            def t2(te, b_wl=b_wl, b_ev=b_ev, src=src):
                n = 0
                for mc in range(32):
                    te.wait_ge(s_wl.h, b_wl + 16 * (mc + 1))
                    for tcc in range(2):
                        n += 1
                        if n > 2:
                            te.wait_ge(s_ev.h, b_ev + n - 2)
                        for kc in range(32):
                            ins = te.matmul(out=ps[:, n % 2, :],
                                            lhsT=hb[:, mc % 2, kc * 128:(kc + 1) * 128],
                                            rhs=src[:, kc, tcc * 512:(tcc + 1) * 512],
                                            start=(kc == 0), stop=(kc == 31))
                        ins.then_inc(s_pr.h, 1)
                    te.drain().then_inc(s_wf.h, 1)
            on(k, "tensor", t2)
            s_pr.n += 64
            s_wf.n += 32

            def sc2(sc, b_pr=b_pr, scol=scol, dst=dst):
                n = 0
                for mc in range(32):
                    for tcc in range(2):
                        n += 1
                        sc.wait_ge(s_pr.h, b_pr + n)
                        sc.activation(out=dst[:, mc, tcc * 512:(tcc + 1) * 512],
                                      in_=ps[:, n % 2, :], func=AF.Gelu,
                                      scale=scol[:, mc:mc + 1]).then_inc(s_ev.h, 1)
            on(k, "scalar", sc2)
            s_ev.n += 64
            k.barrier()

        b_wl, b_wf, b_pr, b_ev = s_wl.n, s_wf.n, s_pr.n, s_ev.n
        def g4(e, b_wf=b_wf):
            for mc in range(KC):
                if mc >= 2:
                    e.wait_ge(s_wf.h, b_wf + mc - 1)
                e.dma_start(out=hb[:, mc % 2, :], in_=aps["L0f_w3t"][mc]).then_inc(s_wl.h, 16)
        on(k, "gpsimd", g4)
        s_wl.n += KC * 16

        def t4(te, b_wl=b_wl, b_ev=b_ev):
            n = 0
            for mc in range(KC):
                te.wait_ge(s_wl.h, b_wl + 16 * (mc + 1))
                for tcc in range(2):
                    n += 1
                    if n > 2:
                        te.wait_ge(s_ev.h, b_ev + n - 2)
                    for kc in range(32):
                        ins = te.matmul(out=ps[:, n % 2, :],
                                        lhsT=hb[:, mc % 2, kc * 128:(kc + 1) * 128],
                                        rhs=a[:, kc, tcc * 512:(tcc + 1) * 512],
                                        start=(kc == 0), stop=(kc == 31))
                    ins.then_inc(s_pr.h, 1)
                te.drain().then_inc(s_wf.h, 1)
        on(k, "tensor", t4)
        s_pr.n += 16
        s_wf.n += KC

        def sc4(sc, b_pr=b_pr):
            n = 0
            for mc in range(KC):
                for tcc in range(2):
                    n += 1
                    sc.wait_ge(s_pr.h, b_pr + n)
                    sc.activation(out=outF[:, mc, tcc * 512:(tcc + 1) * 512],
                                  in_=ps[:, n % 2, :], func=AF.Copy,
                                  scale=sc_s3[:, mc:mc + 1]).then_inc(s_ev.h, 1)
        on(k, "scalar", sc4)
        s_ev.n += 16
        k.barrier()

        b_tp = s_tp.n
        def t5(te):
            for tt8 in range(8):
                if tt8 >= 2:
                    te.wait_ge(ra.s_add.h, ra.s_add.n - 0 + tt8 - 1 - 8)
                for kc in range(KC):
                    ins = te.transpose(out=pst[:, tt8 % 2, kc * 128:(kc + 1) * 128],
                                       in_=outF[:, kc, tt8 * 128:(tt8 + 1) * 128],
                                       identity=g["ident"][:])
                ins.then_inc(s_tp.h, 1)
        b_add0 = ra.s_add.n
        def t5b(te, b_tp=b_tp, b_add0=b_add0):
            for tt8 in range(8):
                if tt8 >= 2:
                    te.wait_ge(ra.s_add.h, b_add0 + tt8 - 1)
                for kc in range(KC):
                    ins = te.transpose(out=pst[:, tt8 % 2, kc * 128:(kc + 1) * 128],
                                       in_=outF[:, kc, tt8 * 128:(tt8 + 1) * 128],
                                       identity=g["ident"][:])
                ins.then_inc(s_tp.h, 1)
        on(k, "tensor", t5b)
        s_tp.n += 8

        for tt8 in range(8):
            t = half * 8 + tt8
            ra.prepare(t)
            ra.add(t, 0, D, pst[:, tt8 % 2, :], wait=(s_tp, b_tp + tt8 + 1))
            ra.flush(t)
        k.barrier()
    ra.done()
    k.barrier()
    st.close()

BLOCK_EMITTERS["l0ffn"] = emit_l0ffn


def emit_l2attn(k, g, aps, upto=99):
    phase = [0]
    def cut():
        phase[0] += 1
        return phase[0] >= upto
    st = ExitStack()
    xln = g["xln"]
    XB = k.sb("l2a_xb", [P, KC, S], BF16, stack=st)
    wbuf = k.sb("l2a_w", [P, KC, 1024], BF16, stack=st)
    wb2 = k.sb("l2a_w2", [P, 2, 1024], BF16, stack=st)
    wbufB = k.sb("l2a_wB", [P, KC, 1024], BF16, stack=st)
    qp = k.sb("l2a_qp", [P, 2, S], BF16, stack=st)
    kp = k.sb("l2a_kp", [P, 2, S], BF16, stack=st)
    ktok = k.sb("l2a_kt", [P, TT, 256], BF16, stack=st)
    vtok = k.sb("l2a_vt", [P, TT, 1024], BF16, stack=st)
    kvS = k.sb("l2a_kv", [P, 2, 1024], BF16, stack=st)
    zrow = k.sb("l2a_z", [1, S], F32, stack=st)
    zib = k.sb("l2a_zib", [1, S], BF16, stack=st)
    zinvT = k.sb("l2a_zt", [P, TT], F32, stack=st)
    fmt = k.sb("l2a_fmt", [P, 2, 512], F32, stack=st)
    fme = k.sb("l2a_fme", [P, 2, 512], F32, stack=st)
    ps = g["ps_mm"]
    pst = g["ps_tr"]
    ps_z = k.ps("l2a_pz", [1, 512], F32, stack=st)
    ps_zt = k.ps("l2a_pzt", [P, TT], F32, stack=st)
    ra = ResidAdder(k, g, stack=st)

    s_wl = k.sem("wl")
    s_pr = k.sem("pr")
    s_ev = k.sem("ev")
    s_fm = k.sem("l2a_fm")
    s_tp = k.sem("ln_tp")

    def loadw(key, mcN=KC, war=None, dst=None):
        t = s_wl.inc(16)
        buf = wbuf if dst is None else dst
        def gl(e, war=war, buf=buf, key=key, mcN=mcN):
            if war is not None:
                e.wait_ge(s_pr.h, war)
            e.dma_start(out=buf[:, 0:mcN, :],
                        in_=aps[key].rearrange("mc p n -> p mc n")).then_inc(s_wl.h, 16)
        on(k, "gpsimd", gl)
        return t

    def proj_into_XB(wkey, war=None, fin=False, buf=None):
        buf = wbuf if buf is None else buf
        b_pr, b_ev = s_pr.n, s_ev.n
        twl = loadw(wkey, war=war, dst=buf)
        def te(te_, buf=buf):
            te_.wait_ge(s_wl.h, twl)
            n = 0
            for mc in range(KC):
                for tc in range(TC):
                    n += 1
                    if b_ev + n - 2 > 0:
                        te_.wait_ge(s_ev.h, b_ev + n - 2)
                    for kc in range(KC):
                        ins = te_.matmul(out=ps[:, n % 2, :],
                                         lhsT=buf[:, mc, kc * 128:(kc + 1) * 128],
                                         rhs=xln[:, kc, tc * 512:(tc + 1) * 512],
                                         start=(kc == 0), stop=(kc == KC - 1))
                    ins.then_inc(s_pr.h, 1)
        on(k, "tensor", te)
        s_pr.n += KC * TC
        def sc(sc_):
            n = 0
            for mc in range(KC):
                for tc in range(TC):
                    n += 1
                    sc_.wait_ge(s_pr.h, b_pr + n)
                    sc_.activation(out=XB[:, mc, tc * 512:(tc + 1) * 512], in_=ps[:, n % 2, :],
                                   func=AF.Copy).then_inc(s_ev.h, 1)
        on(k, "scalar", sc)
        s_ev.n += KC * TC
        if fin:
            k.barrier()

    def featuremap_into(dst, twl):
        s_fm2 = k.sem("l2a_fm2")
        b_pr, b_ev, b_fm, b_fm2 = s_pr.n, s_ev.n, s_fm.n, s_fm2.n
        def te(te_):
            te_.wait_ge(s_wl.h, twl)
            n = 0
            for mc in range(2):
                for tc in range(TC):
                    n += 1
                    if b_ev + n - 2 > 0:
                        te_.wait_ge(s_ev.h, b_ev + n - 2)
                    for kc in range(KC):
                        ins = te_.matmul(out=ps[:, n % 2, :],
                                         lhsT=wb2[:, mc, kc * 128:(kc + 1) * 128],
                                         rhs=XB[:, kc, tc * 512:(tc + 1) * 512],
                                         start=(kc == 0), stop=(kc == KC - 1))
                    ins.then_inc(s_pr.h, 1)
        on(k, "tensor", te)
        s_pr.n += 2 * TC
        NT = 2 * TC
        def stt_of(v, j, dst):
            mc, tc = (j - 1) // TC, (j - 1) % TC
            v.wait_ge(s_fm2.h, b_fm2 + j)
            v.scalar_tensor_tensor(out=dst[:, mc, tc * 512:(tc + 1) * 512],
                                   in0=ps[:, j % 2, :], scalar=0.0, in1=fme[:, j % 2, :],
                                   op0=ALU.max, op1=ALU.add).then_inc(s_ev.h, 1)
        def vv(v, dst=dst):
            for n in range(1, NT + 1):
                v.wait_ge(s_pr.h, b_pr + n)
                if b_fm2 + n - 2 > 0:
                    v.wait_ge(s_fm2.h, b_fm2 + n - 2)
                v.tensor_scalar(out=fmt[:, n % 2, :], in0=ps[:, n % 2, :],
                                scalar1=0.0, scalar2=None, op0=ALU.min).then_inc(s_fm.h, 1)
                if n >= 2:
                    stt_of(v, n - 1, dst)
            stt_of(v, NT, dst)
        on(k, "vector", vv)
        def sc1(sc):
            for n in range(1, NT + 1):
                sc.wait_ge(s_fm.h, b_fm + n)
                if b_ev + n - 2 > 0:
                    sc.wait_ge(s_ev.h, b_ev + n - 2)
                sc.activation(out=fme[:, n % 2, :], in_=fmt[:, n % 2, :],
                              func=AF.Exp).then_inc(s_fm2.h, 1)
        on(k, "scalar", sc1)
        s_fm.n += NT
        s_fm2.n += NT
        s_ev.n += NT

    proj_into_XB("L2a_wqt")
    t_proj = loadw("L2a_projt", mcN=2, dst=wb2)
    war_q = s_pr.n
    if cut(): st_p34.close(); st_p2.close(); st_p1.close(); st_vb.close(); st_qk.close(); st.close(); return
    featuremap_into(qp, t_proj)
    if cut(): st_p34.close(); st_p2.close(); st_p1.close(); st_vb.close(); st_qk.close(); st.close(); return
    proj_into_XB("L2a_wkt", buf=wbufB)
    war_k = s_pr.n
    featuremap_into(kp, t_proj)
    if cut(): st_p34.close(); st_p2.close(); st_p1.close(); st_vb.close(); st_qk.close(); st.close(); return
    proj_into_XB("L2a_wvt", war=war_q, fin=True)
    if cut(): st_p34.close(); st_p2.close(); st_p1.close(); st_vb.close(); st_qk.close(); st.close(); return

    b_pr, b_ev = s_pr.n, s_ev.n
    def tez(te):
        for tc in range(TC):
            if tc >= 1:
                te.wait_ge(s_ev.h, b_ev + tc)
            for c2 in range(2):
                ins = te.matmul(out=ps_z[:], lhsT=g["ones_col_bf"][:],
                                rhs=qp[:, c2, tc * 512:(tc + 1) * 512],
                                start=(c2 == 0), stop=(c2 == 1))
            ins.then_inc(s_pr.h, 1)
    on(k, "tensor", tez)
    s_pr.n += TC
    def vz(v):
        for tc in range(TC):
            v.wait_ge(s_pr.h, b_pr + tc + 1)
            v.tensor_copy(out=zrow[:, tc * 512:(tc + 1) * 512], in_=ps_z[:]).then_inc(s_ev.h, 1)
        v.tensor_scalar(out=zrow[:], in0=zrow[:], scalar1=1e-8, scalar2=None, op0=ALU.add)
        v.drain()
        v.reciprocal(out=zrow[:], in_=zrow[:])
        v.drain()
        v.tensor_copy(out=zib[:], in_=zrow[:])
        v.drain().then_inc(s_ev.h, 1)
    on(k, "vector", vz)
    s_ev.n += TC + 1
    k.barrier()
    if cut(): st_p34.close(); st_p2.close(); st_p1.close(); st_vb.close(); st_qk.close(); st.close(); return

    b_tp = s_tp.n
    def tt1(te):
        nb = 0
        for tt in range(TT):
            nb += 1
            if nb > 2:
                te.wait_ge(s_ev.h, s_ev.n + nb - 2 - 2 * TT)
    def tt_k(te, b_tp=b_tp):
        for tt in range(TT):
            if tt >= 2:
                te.wait_ge(s_ev.h, b_ev2 + tt - 1)
            for c2 in range(2):
                ins = te.transpose(out=pst[:, tt % 2, c2 * 128:(c2 + 1) * 128],
                                   in_=kp[:, c2, tt * 128:(tt + 1) * 128],
                                   identity=g["ident"][:])
            ins.then_inc(s_tp.h, 1)
    b_ev2 = s_ev.n
    on(k, "tensor", tt_k)
    s_tp.n += TT
    def sc_k(sc, b_tp=b_tp, b_ev2=b_ev2):
        for tt in range(TT):
            sc.wait_ge(s_tp.h, b_tp + tt + 1)
            sc.activation(out=ktok[:, tt, :], in_=pst[:, tt % 2, 0:256],
                          func=AF.Copy).then_inc(s_ev.h, 1)
    on(k, "scalar", sc_k)
    s_ev.n += TT

    b_tp2 = s_tp.n
    b_ev3 = s_ev.n
    def tt_v(te, b_tp2=b_tp2, b_ev3=b_ev3):
        for tt in range(TT):
            if tt < 2:
                te.wait_ge(s_ev.h, b_ev2 + 15 + tt)
            else:
                te.wait_ge(s_ev.h, b_ev3 + tt - 1)
            for kc in range(KC):
                ins = te.transpose(out=pst[:, tt % 2, kc * 128:(kc + 1) * 128],
                                   in_=XB[:, kc, tt * 128:(tt + 1) * 128],
                                   identity=g["ident"][:])
            ins.then_inc(s_tp.h, 1)
    on(k, "tensor", tt_v)
    s_tp.n += TT
    def sc_v(sc, b_tp2=b_tp2):
        for tt in range(TT):
            sc.wait_ge(s_tp.h, b_tp2 + tt + 1)
            sc.activation(out=vtok[:, tt, :], in_=pst[:, tt % 2, :],
                          func=AF.Copy).then_inc(s_ev.h, 1)
    on(k, "scalar", sc_v)
    s_ev.n += TT
    k.barrier()
    if cut(): st_p34.close(); st_p2.close(); st_p1.close(); st_vb.close(); st_qk.close(); st.close(); return

    b_pr, b_ev = s_pr.n, s_ev.n
    def tekv(te):
        n = 0
        for pc2 in range(2):
            for nn in range(2):
                n += 1
                if n > 2:
                    te.wait_ge(s_ev.h, b_ev + n - 2)
                for t in range(TT):
                    ins = te.matmul(out=ps[:, n % 2, :],
                                    lhsT=ktok[:, t, pc2 * 128:(pc2 + 1) * 128],
                                    rhs=vtok[:, t, nn * 512:(nn + 1) * 512],
                                    start=(t == 0), stop=(t == TT - 1))
                ins.then_inc(s_pr.h, 1)
    on(k, "tensor", tekv)
    s_pr.n += 4
    def sckv(sc):
        n = 0
        for pc2 in range(2):
            for nn in range(2):
                n += 1
                sc.wait_ge(s_pr.h, b_pr + n)
                sc.activation(out=kvS[:, pc2, nn * 512:(nn + 1) * 512], in_=ps[:, n % 2, :],
                              func=AF.Copy).then_inc(s_ev.h, 1)
    on(k, "scalar", sckv)
    s_ev.n += 4
    k.barrier()
    if cut(): st_p34.close(); st_p2.close(); st_p1.close(); st_vb.close(); st_qk.close(); st.close(); return

    b_pr, b_ev = s_pr.n, s_ev.n
    def tenum(te):
        n = 0
        for mc in range(KC):
            for tc in range(TC):
                n += 1
                if n > 2:
                    te.wait_ge(s_ev.h, b_ev + n - 2)
                for pc2 in range(2):
                    ins = te.matmul(out=ps[:, n % 2, :],
                                    lhsT=kvS[:, pc2, mc * 128:(mc + 1) * 128],
                                    rhs=qp[:, pc2, tc * 512:(tc + 1) * 512],
                                    start=(pc2 == 0), stop=(pc2 == 1))
                ins.then_inc(s_pr.h, 1)
        for tt in range(TT):
            te.matmul(out=ps_zt[:, tt:tt + 1], lhsT=zib[:, tt * 128:(tt + 1) * 128],
                      rhs=g["c_one"][:], start=True, stop=True)
        te.drain().then_inc(s_pr.h, 1)
    on(k, "tensor", tenum)
    s_pr.n += KC * TC + 1
    def scnum(sc):
        n = 0
        for mc in range(KC):
            for tc in range(TC):
                n += 1
                sc.wait_ge(s_pr.h, b_pr + n)
                sc.activation(out=XB[:, mc, tc * 512:(tc + 1) * 512], in_=ps[:, n % 2, :],
                              func=AF.Copy).then_inc(s_ev.h, 1)
        sc.wait_ge(s_pr.h, b_pr + KC * TC + 1)
        sc.activation(out=zinvT[:], in_=ps_zt[:], func=AF.Copy).then_inc(s_ev.h, 1)
    on(k, "scalar", scnum)
    s_ev.n += KC * TC + 1
    k.barrier()
    if cut(): st_p34.close(); st_p2.close(); st_p1.close(); st_vb.close(); st_qk.close(); st.close(); return

    b_tp3 = s_tp.n
    b_add = ra.s_add.n
    def ttn(te, b_tp3=b_tp3, b_add=b_add):
        for tt in range(TT):
            if tt >= 2:
                te.wait_ge(ra.s_add.h, b_add + tt - 1)
            for kc in range(KC):
                ins = te.transpose(out=pst[:, tt % 2, kc * 128:(kc + 1) * 128],
                                   in_=XB[:, kc, tt * 128:(tt + 1) * 128],
                                   identity=g["ident"][:])
            ins.then_inc(s_tp.h, 1)
    on(k, "tensor", ttn)
    s_tp.n += TT
    for tt in range(TT):
        ra.prepare(tt)
        ra.add(tt, 0, D, pst[:, tt % 2, :], scale_ap=zinvT[:, tt:tt + 1],
               wait=(s_tp, b_tp3 + tt + 1))
        ra.flush(tt)
    ra.done()
    k.barrier()
    st.close()

BLOCK_EMITTERS["l2attn"] = emit_l2attn


def emit_l1ffn(k, g, aps):
    st = ExitStack()
    xln = g["xln"]
    signsF = k.sb("l1f_sg", [32, 8], F32, stack=st)
    peT = k.sb("l1f_pe", [32, S], F32, stack=st)
    scT = k.sb("l1f_sc", [P, TT, 8], F32, stack=st)
    rmax = k.sb("l1f_rm", [P, TT], F32, stack=st)
    eq = k.sb("l1f_eq", [P, TT, 8], F32, stack=st)
    avail = k.sb("l1f_av", [P, TT], F32, stack=st)
    masks = k.sb("l1f_mk", [P, TT, 8], F32, stack=st)
    mb = k.sb("l1f_mb", [P, TT, 8], BF16, stack=st)
    maskRb = k.sb("l1f_mr", [8, S], BF16, stack=st)
    bB = k.sb("l1f_bb", [8, 1024], BF16, stack=st)
    WB = k.sb("l1f_wb", [P, 8, 8, 1024], BF16, stack=st)
    xm = k.sb("l1f_xm", [P, 2, 8, 256], BF16, stack=st)
    ps = g["ps_mm"]
    pst = g["ps_tr"]
    ps1 = [k.ps("l1f_ps1a", [P, 512], F32, stack=st),
           k.ps("l1f_ps1b", [P, 512], F32, stack=st)]
    ra = ResidAdder(k, g, stack=st)

    s_wl = k.sem("wl")
    s_pr = k.sem("pr")
    s_ev = k.sem("ev")
    s_tp = k.sem("ln_tp")
    s_bc = k.sem("l1f_bc")
    s_xm = k.sem("l1f_xm")
    s_xmf = k.sem("l1f_xmf")
    b_wl = s_wl.n

    def gl(e):
        e.dma_start(out=signsF[:], in_=aps["L1f_sigsT"][:]).then_inc(s_wl.h, 16)
        e.dma_start(out=peT[:], in_=aps["c_peT"][:]).then_inc(s_wl.h, 16)
        e.dma_start(out=bB[:], in_=aps["L1_ffn_b"][:]).then_inc(s_wl.h, 16)
        e.dma_start(out=WB[:], in_=aps["L1f_wt"].rearrange("u kc p n -> p u kc n")).then_inc(s_wl.h, 16)
    on(k, "gpsimd", gl)
    s_wl.inc(64)

    s_sg = k.sem("l1f_sgs")
    def sc0(sc):
        sc.wait_ge(s_wl.h, b_wl + 16)
        sc.activation(out=signsF[:], in_=signsF[:], func=AF.Sign)
        sc.drain().then_inc(s_sg.h, 1)
    on(k, "scalar", sc0)
    t_sg = s_sg.inc(1)

    b_pr = s_pr.n
    def te0(te):
        te.wait_ge(s_sg.h, t_sg)
        te.wait_ge(s_wl.h, b_wl + 32)
        for tt in range(TT):
            ins = te.matmul(out=ps1[0][:, tt * 8:(tt + 1) * 8],
                            lhsT=peT[:, tt * 128:(tt + 1) * 128],
                            rhs=signsF[:], start=True, stop=True, skip_group_check=True)
        ins.then_inc(s_pr.h, 1)
    on(k, "tensor", te0)
    s_pr.n += 1

    s_mk = k.sem("l1f_mk")
    def v0(v):
        v.wait_ge(s_pr.h, b_pr + 1)
        v.tensor_copy(out=scT[:], in_=ps1[0][:, 0:128].rearrange("p (t u) -> p t u", u=8))
        v.drain()
        v.tensor_reduce(out=rmax[:], in_=scT[:], axis=mybir.AxisListType.X, op=ALU.max)
        v.drain()
        for t in range(TT):
            v.tensor_scalar(out=eq[:, t, :], in0=scT[:, t, :], scalar1=rmax[:, t:t + 1],
                            scalar2=None, op0=ALU.is_ge)
        v.drain()
        v.tensor_copy(out=masks[:, :, 0:1], in_=eq[:, :, 0:1])
        v.tensor_scalar(out=avail[:], in0=eq[:, :, 0], scalar1=-1.0, scalar2=1.0,
                        op0=ALU.mult, op1=ALU.add)
        v.drain()
        for u in range(1, 8):
            v.tensor_tensor(out=masks[:, :, u], in0=avail[:], in1=eq[:, :, u], op=ALU.mult)
            v.drain()
            v.tensor_tensor(out=avail[:], in0=avail[:], in1=masks[:, :, u], op=ALU.subtract)
            v.drain()
        v.tensor_copy(out=mb[:], in_=masks[:])
        v.drain().then_inc(s_mk.h, 1)
    on(k, "vector", v0)
    t_mk = s_mk.inc(1)

    b_tp = s_tp.n
    def te1(te):
        te.wait_ge(s_mk.h, t_mk)
        for t in range(TT):
            ins = te.transpose(out=pst[0:8, t // 8, (t % 8) * 128:(t % 8 + 1) * 128],
                               in_=mb[:, t, :], identity=g["ident"][:])
        ins.then_inc(s_tp.h, 1)
    on(k, "tensor", te1)
    s_tp.n += 1

    s_mr = k.sem("l1f_mrs")
    def sc1(sc):
        sc.wait_ge(s_tp.h, b_tp + 1)
        sc.activation(out=maskRb[:, 0:1024], in_=pst[0:8, 0, :], func=AF.Copy)
        sc.activation(out=maskRb[:, 1024:2048], in_=pst[0:8, 1, :], func=AF.Copy)
        sc.drain().then_inc(s_mr.h, 1)
    on(k, "scalar", sc1)
    t_mr = s_mr.inc(1)
    k.barrier()

    b_bc, b_xm, b_xmf = s_bc.n, s_xm.n, s_xmf.n
    b_pr2 = s_pr.n
    b_add = ra.s_add.n
    NTG = 8

    def outu_of(te, tg, u):
        n = tg * 8 + u + 1
        par = n % 2
        te.wait_ge(s_xm.h, b_xm + n)
        for tt2 in range(2):
            for nn in range(2):
                ins = None
                for kc in range(KC):
                    ins = te.matmul(out=ps[:, tt2 * 2 + nn, :],
                                    lhsT=xm[:, par, kc, tt2 * 128:(tt2 + 1) * 128],
                                    rhs=WB[:, u, kc, nn * 512:(nn + 1) * 512],
                                    start=(u == 0 and kc == 0), stop=False,
                                    skip_group_check=True)
                if u == 7:
                    ins = te.matmul(out=ps[:, tt2 * 2 + nn, :],
                                    lhsT=maskRb[:, tg * 256 + tt2 * 128: tg * 256 + (tt2 + 1) * 128],
                                    rhs=bB[:, nn * 512:(nn + 1) * 512],
                                    start=False, stop=True, skip_group_check=True)
                    ins.then_inc(s_pr.h, 1)
        te.drain().then_inc(s_xmf.h, 1)

    for tg in range(NTG):
        def te2(te, tg=tg):
            if tg == 0:
                te.wait_ge(s_wl.h, b_wl + 64)
                te.wait_ge(s_mr.h, t_mr)
            else:
                te.wait_ge(ra.s_add.h, b_add + tg * 4)
            for u in range(8):
                n = tg * 8 + u + 1
                if n > 2:
                    te.wait_ge(s_xm.h, b_xm + n - 2)
                par = n % 2
                te.matmul(out=ps1[par][:, 0:256],
                          lhsT=g["sel16e"][0:8, u * 128:(u + 1) * 128],
                          rhs=maskRb[:, tg * 256:(tg + 1) * 256],
                          start=True, stop=True, skip_group_check=True).then_inc(s_bc.h, 1)
                if u >= 1:
                    outu_of(te, tg, u - 1)
            outu_of(te, tg, 7)
        on(k, "tensor", te2)
        s_bc.n += 8
        s_xmf.n += 8
        s_pr.n += 4

        def v2(v, tg=tg):
            for u in range(8):
                n = tg * 8 + u + 1
                par = n % 2
                v.wait_ge(s_bc.h, b_bc + n)
                if n > 2:
                    v.wait_ge(s_xmf.h, b_xmf + n - 2)
                for kc in range(KC):
                    ins = v.tensor_tensor(out=xm[:, par, kc, :], in0=xln[:, kc, tg * 256:(tg + 1) * 256],
                                          in1=ps1[par][:, 0:256], op=ALU.mult)
                ins.then_inc(s_xm.h, 1)
        on(k, "vector", v2)
        s_xm.n += 8

        t0 = tg * 2
        ra.prepare(t0)
        ra.prepare(t0 + 1)
        n_end = b_pr2 + (tg + 1) * 4
        for tt2 in range(2):
            for nn in range(2):
                ra.add(t0 + tt2, nn * 512, (nn + 1) * 512, ps[:, tt2 * 2 + nn, :],
                       wait=(s_pr, n_end))
        ra.flush(t0)
        ra.flush(t0 + 1)
    ra.done()
    k.barrier()
    st.close()

BLOCK_EMITTERS["l1ffn"] = emit_l1ffn


def emit_l1attn(k, g, aps):
    st = ExitStack()
    xln = g["xln"]
    W = 1023
    qkv = k.sb("l1a_qkv", [P, 24, S], BF16, stack=st)
    wbuf = k.sb("l1a_w", [P, 2, 1024], BF16, stack=st)
    prod = k.sb("l1a_pr", [P, 2, W], BF16, stack=st)
    sab4 = k.sb("l1a_s4", [16, 4, W], F32, stack=st)
    smax = k.sb("l1a_sm", [16, W], F32, stack=st)
    pstack = k.sb("l1a_pk", [16, 4, W], BF16, stack=st)
    tb = k.sb("l1a_tb", [P, W], BF16, stack=st)
    acc = k.sb("l1a_ac", [P, W], BF16, stack=st)
    oscat = g["xln"]
    sum64 = k.sb("l1a_s64", [P, KC, 16], BF16, stack=st)
    bc64 = k.sb("l1a_b64", [16, KC, P], BF16, stack=st)
    cntR = k.sb("l1a_cnt", [1, S], BF16, stack=st)
    boutB = k.sb("l1a_bo", [1, 1024], BF16, stack=st)
    ps = g["ps_mm"]
    ra = ResidAdder(k, g, stack=st)

    s_wl = k.sem("wl")
    s_pr = k.sem("pr")
    s_ev = k.sem("ev")
    b_wl = s_wl.n

    def gl(e):
        e.dma_start(out=sum64[:], in_=aps["c_sum64_bf"].rearrange("kc p m -> p kc m")).then_inc(s_wl.h, 16)
        e.dma_start(out=bc64[:], in_=aps["c_bc64_bf"].rearrange("kc m p -> m kc p")).then_inc(s_wl.h, 16)
        e.dma_start(out=cntR[:], in_=aps["c_l1a_count"][:]).then_inc(s_wl.h, 16)
        e.dma_start(out=boutB[:], in_=aps["L1a_bout"][:]).then_inc(s_wl.h, 16)
        e.nop().then_inc(s_wl.h, 1)
    on(k, "gpsimd", gl)
    s_wl.inc(65)

    b_pr, b_ev = s_pr.n, s_ev.n
    s_wf = k.sem("l1a_wf")
    b_wf = s_wf.n
    def ga(e):
        for mc in range(24):
            if mc >= 2:
                e.wait_ge(s_wf.h, b_wf + mc - 1)
            e.dma_start(out=wbuf[:, mc % 2, :], in_=aps["L1a_wqkvt"][mc]).then_inc(s_wl.h, 16)
    on(k, "gpsimd", ga)
    s_wl.n += 24 * 16

    def ta(te):
        n = 0
        for mc in range(24):
            te.wait_ge(s_wl.h, b_wl + 65 + 16 * (mc + 1))
            for tc in range(TC):
                n += 1
                if n > 2:
                    te.wait_ge(s_ev.h, b_ev + n - 2)
                for kc in range(KC):
                    ins = te.matmul(out=ps[:, n % 2, :], lhsT=wbuf[:, mc % 2, kc * 128:(kc + 1) * 128],
                                    rhs=xln[:, kc, tc * 512:(tc + 1) * 512],
                                    start=(kc == 0), stop=(kc == KC - 1))
                ins.then_inc(s_pr.h, 1)
            te.drain().then_inc(s_wf.h, 1)
    on(k, "tensor", ta)
    s_pr.n += 24 * TC
    s_wf.n += 24

    def sca(sc):
        n = 0
        for mc in range(24):
            for tc in range(TC):
                n += 1
                sc.wait_ge(s_pr.h, b_pr + n)
                sc.activation(out=qkv[:, mc, tc * 512:(tc + 1) * 512], in_=ps[:, n % 2, :],
                              func=AF.Copy).then_inc(s_ev.h, 1)
    on(k, "scalar", sca)
    s_ev.n += 24 * TC
    k.barrier()
    on(k, "vector", lambda v: v.memset(oscat[:], 0.0))

    s_pd = k.sem("l1a_pd")
    s_sb = k.sem("l1a_sb")
    s_sx = k.sem("l1a_sx")
    s_bc = k.sem("l1a_bc")
    s_tt = k.sem("l1a_tt")
    for a in range(4):
        b_pd, b_pr1, b_sb = s_pd.n, s_pr.n, s_sb.n
        def vb1b(v, a=a, b_pd=b_pd, b_pr1=b_pr1):
            n = 0
            for b in range(4):
                for kc in range(KC):
                    n += 1
                    if n > 2:
                        v.wait_ge(s_pr.h, b_pr1 + (n - 2) * 2)
                    v.tensor_tensor(out=prod[:, n % 2, :],
                                    in0=qkv[:, kc, a:a + 2 * W - 1:2],
                                    in1=qkv[:, 8 + kc, b:b + 2 * W - 1:2],
                                    op=ALU.mult).then_inc(s_pd.h, 1)
        on(k, "vector", vb1b)
        s_pd.n += 32

        def tb1(te, a=a, b_pd=b_pd, b_sb=b_sb):
            n = 0
            for b in range(4):
                slot = b % 2
                if b >= 2:
                    te.wait_ge(s_sb.h, b_sb + b - 1)
                for kc in range(KC):
                    n += 1
                    te.wait_ge(s_pd.h, b_pd + n)
                    te.matmul(out=ps[0:16, slot * 2, :], lhsT=sum64[:, kc, :],
                              rhs=prod[:, n % 2, 0:512], start=(kc == 0), stop=(kc == KC - 1),
                              skip_group_check=True).then_inc(s_pr.h, 1)
                    te.matmul(out=ps[0:16, slot * 2 + 1, 0:W - 512], lhsT=sum64[:, kc, :],
                              rhs=prod[:, n % 2, 512:W], start=(kc == 0), stop=(kc == KC - 1),
                              skip_group_check=True).then_inc(s_pr.h, 1)
        on(k, "tensor", tb1)
        s_pr.n += 64

        b_pr_end = s_pr.n
        def scb1(sc, a=a, b_pr1=b_pr1, b_sb=b_sb):
            for b in range(4):
                slot = b % 2
                sc.wait_ge(s_pr.h, b_pr1 + (b + 1) * 16)
                sc.activation(out=sab4[:, b, 0:512], in_=ps[0:16, slot * 2, :], func=AF.Copy)
                sc.activation(out=sab4[:, b, 512:W], in_=ps[0:16, slot * 2 + 1, 0:W - 512], func=AF.Copy)
                sc.drain().then_inc(s_sb.h, 1)
        on(k, "scalar", scb1)
        s_sb.n += 4

        b_sx = s_sx.n
        def vb2(v, a=a, b_sb=b_sb):
            v.wait_ge(s_sb.h, b_sb + 4)
            v.tensor_tensor(out=smax[:], in0=sab4[:, 0, :], in1=sab4[:, 1, :], op=ALU.max)
            v.drain()
            v.tensor_tensor(out=smax[:], in0=smax[:], in1=sab4[:, 2, :], op=ALU.max)
            v.drain()
            v.tensor_tensor(out=smax[:], in0=smax[:], in1=sab4[:, 3, :], op=ALU.max)
            v.drain()
            for b in range(4):
                v.tensor_tensor(out=sab4[:, b, :], in0=sab4[:, b, :], in1=smax[:], op=ALU.subtract)
                v.drain()
            v.nop().then_inc(s_sx.h, 1)
        on(k, "vector", vb2)
        t_sx1 = s_sx.inc(1)

        def scb2(sc, t_sx1=t_sx1):
            sc.wait_ge(s_sx.h, t_sx1)
            for b in range(4):
                sc.activation(out=sab4[:, b, :], in_=sab4[:, b, :], func=AF.Exp,
                              scale=1.0 / 32.0)
            sc.drain().then_inc(s_sx.h, 1)
        on(k, "scalar", scb2)
        t_sx2 = s_sx.inc(1)

        def vb3(v, t_sx2=t_sx2):
            v.wait_ge(s_sx.h, t_sx2)
            v.tensor_tensor(out=smax[:], in0=sab4[:, 0, :], in1=sab4[:, 1, :], op=ALU.add)
            v.drain()
            v.tensor_tensor(out=smax[:], in0=smax[:], in1=sab4[:, 2, :], op=ALU.add)
            v.drain()
            v.tensor_tensor(out=smax[:], in0=smax[:], in1=sab4[:, 3, :], op=ALU.add)
            v.drain()
            v.reciprocal(out=smax[:], in_=smax[:])
            v.drain()
            for b in range(4):
                v.tensor_tensor(out=pstack[:, b, :], in0=sab4[:, b, :], in1=smax[:], op=ALU.mult)
                v.drain()
            v.nop().then_inc(s_sx.h, 1)
        on(k, "vector", vb3)
        t_sx3 = s_sx.inc(1)

        b_bc, b_tt = s_bc.n, s_tt.n
        def tb3(te, a=a, t_sx3=t_sx3, b_tt=b_tt):
            te.wait_ge(s_sx.h, t_sx3)
            n = 0
            for kc in range(KC):
                for b in range(4):
                    n += 1
                    if n > 2:
                        te.wait_ge(s_tt.h, b_tt + n - 2)
                    slot = n % 2
                    te.matmul(out=ps[:, slot * 2, :], lhsT=bc64[:, kc, :],
                              rhs=pstack[:, b, 0:512], start=True, stop=True,
                              skip_group_check=True)
                    te.matmul(out=ps[:, slot * 2 + 1, 0:W - 512], lhsT=bc64[:, kc, :],
                              rhs=pstack[:, b, 512:W], start=True, stop=True,
                              skip_group_check=True).then_inc(s_bc.h, 1)
        on(k, "tensor", tb3)
        s_bc.n += 32

        def vb4(v, a=a, b_bc=b_bc, b_tt=b_tt):
            n = 0
            for kc in range(KC):
                for b in range(4):
                    n += 1
                    slot = n % 2
                    v.wait_ge(s_bc.h, b_bc + n)
                    v.tensor_tensor(out=tb[:, 0:512], in0=ps[:, slot * 2, :],
                                    in1=qkv[:, 16 + kc, b:b + 2 * W - 1:2][:, 0:512], op=ALU.mult)
                    v.tensor_tensor(out=tb[:, 512:W], in0=ps[:, slot * 2 + 1, 0:W - 512],
                                    in1=qkv[:, 16 + kc, b:b + 2 * W - 1:2][:, 512:W], op=ALU.mult)
                    v.drain()
                    if b == 0:
                        v.tensor_copy(out=acc[:], in_=tb[:])
                    else:
                        v.tensor_tensor(out=acc[:], in0=acc[:], in1=tb[:], op=ALU.add)
                    v.drain().then_inc(s_tt.h, 1)
                v.tensor_tensor(out=oscat[:, kc, a:a + 2 * W - 1:2],
                                in0=oscat[:, kc, a:a + 2 * W - 1:2], in1=acc[:], op=ALU.add)
                v.drain()
            v.nop().then_inc(s_tt.h, 1)
        on(k, "vector", vb4)
        s_tt.n += 33
        k.barrier()

    woutB = k.sb("l1a_wo", [P, KC, 1024], BF16, stack=st)
    b_wl3 = s_wl.n
    on(k, "gpsimd", lambda e: e.dma_start(
        out=woutB[:], in_=aps["L1a_woutt"].rearrange("kc p n -> p kc n")).then_inc(s_wl.h, 16))
    s_wl.inc(16)

    b_pr4 = s_pr.n
    b_add = ra.s_add.n
    def tc_(te):
        te.wait_ge(s_wl.h, b_wl3 + 16)
        te.wait_ge(s_wl.h, b_wl + 64)
        n = 0
        for t in range(TT):
            for nn in range(2):
                n += 1
                if n > 2:
                    te.wait_ge(ra.s_add.h, b_add + n - 2)
                for kc in range(KC):
                    te.matmul(out=ps[:, n % 2, :], lhsT=oscat[:, kc, t * 128:(t + 1) * 128],
                              rhs=woutB[:, kc, nn * 512:(nn + 1) * 512],
                              start=(kc == 0), stop=False, skip_group_check=True)
                te.matmul(out=ps[:, n % 2, :], lhsT=cntR[:, t * 128:(t + 1) * 128],
                          rhs=boutB[:, nn * 512:(nn + 1) * 512],
                          start=False, stop=True, skip_group_check=True).then_inc(s_pr.h, 1)
    on(k, "tensor", tc_)
    s_pr.n += TT * 2

    n = 0
    for t in range(TT):
        ra.prepare(t)
        for nn in range(2):
            n += 1
            ra.add(t, nn * 512, (nn + 1) * 512, ps[:, n % 2, :], wait=(s_pr, b_pr4 + n))
        ra.flush(t)
    ra.done()
    k.barrier()
    st.close()

BLOCK_EMITTERS["l1attn"] = emit_l1attn


def emit_l0attn(k, g, aps, upto=99):
    phase = [0]
    def cut():
        phase[0] += 1
        return phase[0] >= upto
    st = ExitStack()
    xln = g["xln"]
    ps = g["ps_mm"]
    pst = g["ps_tr"]

    st_qk = ExitStack()
    st_p1 = ExitStack()
    st_p2 = ExitStack()
    st_vb = ExitStack()
    st_p34 = ExitStack()
    wqs = k.sb("l0a_wq", [P, KC, 1024], BF16, stack=st)
    g64 = k.sb("l0a_g64", [P, 2], BF16, stack=st)
    g64T = k.sb("l0a_g64T", [2, P], BF16, stack=st)
    sumP = k.sb("l0a_sump", [P, 4, 16], BF16, stack=st)
    bcP = k.sb("l0a_bcp", [16, 4, P], BF16, stack=st)
    ltm = k.sb("l0a_ltm", [P, P], BF16, stack=st)
    gvec = k.sb("l0a_gv", [16, 1], F32, stack=st)
    lamv = k.sb("l0a_lam", [16, 1], BF16, stack=st)
    rrq = k.sb("l0a_rrq", [16, S], BF16, stack=st)
    rrk = k.sb("l0a_rrk", [16, S], BF16, stack=st)
    ncb = k.sb("l0a_ncb", [P, 16], F32, stack=st)
    crow = k.sb("l0a_crow", [1, 16], F32, stack=st)
    eb = k.sb("l0a_eb", [P, 2, 512], BF16, stack=st)
    zrow_s = k.sb("l0a_zf", [1, TC, 512], F32, stack=st)
    lamc = k.sb("l0a_lamc", [P, 4], F32, stack=st)
    lamn = k.sb("l0a_lamn", [P, 4], F32, stack=st)
    qk = k.sb("l0a_qk", [P, 16, S], BF16, stack=st_qk)
    vbuf = k.sb("l0a_v", [P, KC, S], BF16, stack=st_vb)
    awb = k.sb("l0a_aw", [P, KC, 1024], BF16, stack=st_p1)
    gmS = k.sb("l0a_gm", [2, KC, 1024], BF16, stack=st_p1)
    hsb = gmS
    zdram = k.nc.dram_tensor(f"l0a_zd_{k.uid()}", [32, S], F32, kind="Internal").ap()

    s_wl = k.sem("wl")
    s_pr = k.sem("pr")
    s_ev = k.sem("ev")
    s_tp = k.sem("ln_tp")
    s_dq = k.sem("l0a_dq")
    s_ex = k.sem("l0a_ex")
    s_mk = k.sem("l0a_mk")
    s_yz = k.sem("l0a_yz")
    s_ze = k.sem("l0a_ze")

    b_wl = s_wl.n
    def gl(e):
        e.dma_start(out=g64[:], in_=aps["c_g64_bf"][:]).then_inc(s_wl.h, 16)
        e.dma_start(out=g64T[:], in_=aps["c_g64T_bf"][:]).then_inc(s_wl.h, 16)
        e.dma_start(out=sumP[:], in_=aps["c_sumP_bf"].rearrange("v p m -> p v m")).then_inc(s_wl.h, 16)
        e.dma_start(out=bcP[:], in_=aps["c_bcP_bf"].rearrange("v m p -> m v p")).then_inc(s_wl.h, 16)
        e.dma_start(out=ltm[:], in_=aps["c_ltmask_bf"][:]).then_inc(s_wl.h, 16)
        e.dma_start(out=gvec[:], in_=aps["L0_attn_q_gain"][:]).then_inc(s_wl.h, 16)
        e.dma_start(out=lamv[:], in_=aps["L0_attn_diff_lambda"][:]).then_inc(s_wl.h, 16)
    on(k, "gpsimd", gl)
    s_wl.inc(112)

    for sec in range(3):
        b_wl1 = s_wl.n
        on(k, "gpsimd", lambda e, sec=sec: e.dma_start(
            out=wqs[:], in_=aps["L0a_wqkvt"][sec]).then_inc(s_wl.h, 16))
        s_wl.inc(16)

        t_d0 = s_dq.n
        def sc_d(sc, b_wl1=b_wl1):
            sc.wait_ge(s_wl.h, b_wl1 + 16)
            sc.activation(out=awb[:], in_=wqs[:], func=AF.Abs)
            sc.drain()
            sc.activation(out=wqs[:], in_=wqs[:], func=AF.Sign)
            sc.drain().then_inc(s_dq.h, 1)
        on(k, "scalar", sc_d)
        t_d1 = s_dq.inc(1)

        b_pr1 = s_pr.n
        def te_d(te, t_d1=t_d1, t_d0=t_d0):
            te.wait_ge(s_dq.h, t_d1)
            te.wait_ge(s_wl.h, b_wl + 32)
            n = 0
            for kc in range(KC):
                for j in range(2):
                    n += 1
                    if n > 2:
                        te.wait_ge(s_dq.h, t_d1 + n - 2)
                    te.matmul(out=ps[0:2, n % 2, :], lhsT=g64[:],
                              rhs=awb[:, kc, j * 512:(j + 1) * 512],
                              start=True, stop=True, skip_group_check=True).then_inc(s_pr.h, 1)
        on(k, "tensor", te_d)
        s_pr.n += 16

        def v_d(v, b_pr1=b_pr1):
            n = 0
            for kc in range(KC):
                for j in range(2):
                    n += 1
                    v.wait_ge(s_pr.h, b_pr1 + n)
                    v.tensor_copy(out=gmS[:, kc, j * 512:(j + 1) * 512], in_=ps[0:2, n % 2, :])
                    v.drain().then_inc(s_dq.h, 1)
            v.tensor_scalar(out=hsb[:], in0=gmS[:], scalar1=1.0 / 64.0, scalar2=1e-8,
                            op0=ALU.mult, op1=ALU.max)
            v.drain()
            v.tensor_scalar(out=hsb[:], in0=hsb[:], scalar1=0.5, scalar2=None, op0=ALU.mult)
            v.drain().then_inc(s_dq.h, 1)
        on(k, "vector", v_d)
        s_dq.n += 17
        t_d2 = s_dq.n

        b_pr2 = s_pr.n
        def te_d2(te, t_d2=t_d2):
            te.wait_ge(s_dq.h, t_d2)
            n = 0
            for kc in range(KC):
                for j in range(2):
                    n += 1
                    if n > 2:
                        te.wait_ge(s_dq.h, t_d2 + n - 2)
                    te.matmul(out=ps[:, 2 + n % 2, :], lhsT=g64T[:],
                              rhs=hsb[:, kc, j * 512:(j + 1) * 512],
                              start=True, stop=True, skip_group_check=True).then_inc(s_pr.h, 1)
        on(k, "tensor", te_d2)
        s_pr.n += 16

        def v_d2(v, b_pr2=b_pr2):
            n = 0
            for kc in range(KC):
                for j in range(2):
                    n += 1
                    v.wait_ge(s_pr.h, b_pr2 + n)
                    sl = (slice(None), kc, slice(j * 512, (j + 1) * 512))
                    v.tensor_tensor(out=awb[sl], in0=awb[sl], in1=ps[:, 2 + n % 2, :], op=ALU.is_gt)
                    v.drain()
                    v.scalar_tensor_tensor(out=awb[sl], in0=awb[sl], scalar=2.0,
                                           in1=ps[:, 2 + n % 2, :], op0=ALU.mult, op1=ALU.mult)
                    v.drain()
                    v.tensor_tensor(out=wqs[sl], in0=awb[sl], in1=wqs[sl], op=ALU.mult)
                    v.drain().then_inc(s_dq.h, 1)
        on(k, "vector", v_d2)
        s_dq.n += 16
        k.barrier()

        b_pr3, b_ev3 = s_pr.n, s_ev.n
        dst = vbuf if sec == 2 else qk
        coff = 8 if sec == 1 else 0
        def te_q(te, b_ev3=b_ev3):
            n = 0
            for mc in range(KC):
                for tc in range(TC):
                    n += 1
                    if n > 2:
                        te.wait_ge(s_ev.h, b_ev3 + n - 2)
                    for kc in range(KC):
                        ins = te.matmul(out=ps[:, n % 2, :], lhsT=wqs[:, kc, mc * 128:(mc + 1) * 128],
                                        rhs=xln[:, kc, tc * 512:(tc + 1) * 512],
                                        start=(kc == 0), stop=(kc == KC - 1))
                    ins.then_inc(s_pr.h, 1)
        on(k, "tensor", te_q)
        s_pr.n += KC * TC

        def sc_q(sc, dst=dst, coff=coff, b_pr3=b_pr3):
            n = 0
            for mc in range(KC):
                for tc in range(TC):
                    n += 1
                    sc.wait_ge(s_pr.h, b_pr3 + n)
                    sc.activation(out=dst[:, coff + mc, tc * 512:(tc + 1) * 512],
                                  in_=ps[:, n % 2, :], func=AF.Copy).then_inc(s_ev.h, 1)
        on(k, "scalar", sc_q)
        s_ev.n += KC * TC
        k.barrier()

    st_p1.close()
    if cut(): st_p34.close(); st_p2.close(); st_p1.close(); st_vb.close(); st_qk.close(); st.close(); return

    crep = k.sb("l0a_cos", [P, S], BF16, stack=st_p2)
    srep = k.sb("l0a_sin", [P, S], BF16, stack=st_p2)
    ssq = k.sb("l0a_ssq", [16, S], F32, stack=st_p2)
    sq_s = k.sb("l0a_sq", [P, S], BF16, stack=st_p2)
    rt = [k.sb(f"l0a_rt{i}", [P, S], BF16, stack=st_p2) for i in range(4)]
    b_wl2 = s_wl.n
    def gl2(e):
        e.dma_start(out=crep[:], in_=aps["c_l0cos_bf"][:]).then_inc(s_wl.h, 16)
        e.dma_start(out=srep[:], in_=aps["c_l0sin_bf"][:]).then_inc(s_wl.h, 16)
    on(k, "gpsimd", gl2)
    s_wl.inc(32)
    t_rope_w = s_wl.n
    for sec in range(2):
        coff = sec * 8
        s_sqd = k.sem("l0a_sqd")
        b_sqd = s_sqd.n
        b_ev4 = s_ev.n
        def sc_s2(sc, coff=coff, b_sqd=b_sqd):
            for cc in range(KC):
                if cc >= 1:
                    sc.wait_ge(s_sqd.h, b_sqd + cc)
                sc.activation(out=sq_s[:], in_=qk[:, coff + cc, :], func=AF.Square)
                sc.drain().then_inc(s_ev.h, 1)
        on(k, "scalar", sc_s2)
        s_ev.n += KC

        b_pr4 = s_pr.n
        def te_s(te, b_ev4=b_ev4):
            for cc in range(KC):
                te.wait_ge(s_ev.h, b_ev4 + cc + 1)
                for tc in range(TC):
                    te.matmul(out=ps[0:16, tc, :], lhsT=sumP[:, cc % 4, :],
                              rhs=sq_s[:, tc * 512:(tc + 1) * 512],
                              start=(cc == 0), stop=(cc == KC - 1), skip_group_check=True)
                te.drain().then_inc(s_sqd.h, 1)
            te.drain().then_inc(s_pr.h, 1)
        on(k, "tensor", te_s)
        s_sqd.n += KC
        s_pr.n += 1

        t_pr_s = s_pr.n
        def v_s(v, t_pr_s=t_pr_s):
            v.wait_ge(s_pr.h, t_pr_s)
            for tc in range(TC):
                v.tensor_copy(out=ssq[:, tc * 512:(tc + 1) * 512], in_=ps[0:16, tc, :])
            v.drain()
            v.tensor_scalar(out=ssq[:], in0=ssq[:], scalar1=1.0 / 64.0, scalar2=1e-5,
                            op0=ALU.mult, op1=ALU.add)
            v.drain().then_inc(s_yz.h, 1)
        on(k, "vector", v_s)
        t_y1 = s_yz.inc(1)

        def sc_r(sc, t_y1=t_y1):
            sc.wait_ge(s_yz.h, t_y1)
            sc.activation(out=ssq[:], in_=ssq[:], func=AF.Sqrt)
            sc.drain().then_inc(s_yz.h, 1)
        on(k, "scalar", sc_r)
        t_y2 = s_yz.inc(1)

        def v_r(v, sec=sec, t_y2=t_y2):
            v.wait_ge(s_yz.h, t_y2)
            v.wait_ge(s_wl.h, b_wl + 112)
            v.reciprocal(out=ssq[:], in_=ssq[:])
            v.drain()
            if sec == 0:
                v.tensor_scalar(out=rrq[:], in0=ssq[:], scalar1=gvec[:], scalar2=None, op0=ALU.mult)
            else:
                v.tensor_copy(out=rrk[:], in_=ssq[:])
            v.drain().then_inc(s_yz.h, 1)
        on(k, "vector", v_r)
        t_y3 = s_yz.inc(1)

        b_pr5 = s_pr.n
        b_yz5 = s_yz.n
        rr = rrq if sec == 0 else rrk
        def te_a(te, rr=rr, t_y3=t_y3, b_yz5=b_yz5):
            te.wait_ge(s_yz.h, t_y3)
            n = 0
            for cc in range(KC):
                for tc in range(TC):
                    n += 1
                    if n > 2:
                        te.wait_ge(s_yz.h, b_yz5 + n - 2)
                    te.matmul(out=ps[:, n % 2, :], lhsT=bcP[:, cc % 4, :],
                              rhs=rr[:, tc * 512:(tc + 1) * 512],
                              start=True, stop=True, skip_group_check=True).then_inc(s_pr.h, 1)
        on(k, "tensor", te_a)
        s_pr.n += KC * TC

        def v_a(v, coff=coff, b_pr5=b_pr5):
            n = 0
            for cc in range(KC):
                for tc in range(TC):
                    n += 1
                    v.wait_ge(s_pr.h, b_pr5 + n)
                    sl = (slice(None), coff + cc, slice(tc * 512, (tc + 1) * 512))
                    v.tensor_tensor(out=qk[sl], in0=qk[sl], in1=ps[:, n % 2, :], op=ALU.mult)
                    v.drain().then_inc(s_yz.h, 1)
        on(k, "vector", v_a)
        s_yz.n += KC * TC
        k.barrier()

        def v_rope(v, coff=coff):
            v.wait_ge(s_wl.h, t_rope_w)
            for gg in range(4):
                cA = (slice(None), coff + gg, slice(None))
                cB = (slice(None), coff + 4 + gg, slice(None))
                v.tensor_tensor(out=rt[0][:], in0=qk[cA], in1=crep[:], op=ALU.mult)
                v.tensor_tensor(out=rt[3][:], in0=qk[cA], in1=srep[:], op=ALU.mult)
                v.tensor_tensor(out=rt[2][:], in0=qk[cB], in1=crep[:], op=ALU.mult)
                v.tensor_tensor(out=rt[1][:], in0=qk[cB], in1=srep[:], op=ALU.mult)
                v.drain()
                v.tensor_tensor(out=qk[cA], in0=rt[0][:], in1=rt[1][:], op=ALU.add)
                v.tensor_tensor(out=qk[cB], in0=rt[2][:], in1=rt[3][:], op=ALU.subtract)
                v.drain()
        on(k, "vector", v_rope)
        k.barrier()

    st_p2.close()
    if cut(): st_p34.close(); st_p2.close(); st_p1.close(); st_vb.close(); st_qk.close(); st.close(); return

    b_pr6 = s_pr.n
    def te_c(te):
        te.transpose(out=ps[0:1, 3, 0:16], in_=gvec[:], identity=g["identf"][0:16, 0:16]).then_inc(s_pr.h, 1)
    on(k, "tensor", te_c)
    s_pr.n += 1

    def v_c(sc, b_pr6=b_pr6):
        sc.wait_ge(s_pr.h, b_pr6 + 1)
        sc.activation(out=crow[:], in_=ps[0:1, 3, 0:16], func=AF.Abs, scale=11.3137085)
        sc.drain().then_inc(s_yz.h, 1)
    on(k, "scalar", v_c)
    t_c1 = s_yz.inc(1)

    b_pr7 = s_pr.n
    def te_c2(te, t_c1=t_c1):
        te.wait_ge(s_yz.h, t_c1)
        te.matmul(out=ps[:, 3, 0:16], lhsT=g["ones_row_f32"][:], rhs=crow[:],
                  start=True, stop=True, skip_group_check=True)
        for gg in range(4):
            te.matmul(out=ps[:, 2, gg:gg + 1], lhsT=bcP[:, gg, :], rhs=lamv[:],
                      start=(gg == 0), stop=(gg == 3), skip_group_check=True)
        te.drain().then_inc(s_pr.h, 1)
    on(k, "tensor", te_c2)
    s_pr.n += 1

    def v_c2(v, b_pr7=b_pr7):
        v.wait_ge(s_pr.h, b_pr7 + 1)
        v.tensor_scalar(out=ncb[:], in0=ps[:, 3, 0:16], scalar1=-1.0, scalar2=None, op0=ALU.mult)
        v.drain()
        v.tensor_copy(out=lamc[:], in_=ps[:, 2, 0:4])
        v.drain()
        v.tensor_scalar(out=lamn[:], in0=lamc[:], scalar1=-1.0, scalar2=None, op0=ALU.mult)
        v.drain().then_inc(s_yz.h, 1)
    on(k, "vector", v_c2)
    s_yz.inc(1)
    k.barrier()

    vtok = k.sb("l0a_vt", [P, 32, TT, 33], BF16, stack=st_p34)
    on(k, "vector", lambda v: v.memset(vtok[:, :, :, 32:33], 1.0))
    b_tp = s_tp.n
    b_evv = s_ev.n
    def te_v(te, b_tp=b_tp, b_evv=b_evv):
        for hh in range(32):
            H, hm = hh // 16, hh % 16
            gg, h4 = hm // 4, hm % 4
            cc = H * 4 + gg
            if hh >= 2:
                te.wait_ge(s_ev.h, b_evv + hh - 1)
            for kt in range(TT):
                ins = te.transpose(out=pst[:, hh % 2, kt * 32:(kt + 1) * 32],
                                   in_=vbuf[:, cc, kt * 128:(kt + 1) * 128][h4 * 32:(h4 + 1) * 32, :],
                                   identity=g["ident"][h4 * 32:(h4 + 1) * 32, h4 * 32:(h4 + 1) * 32],
                                   tile_position=(h4 * 32, 0))
            ins.then_inc(s_tp.h, 1)
    on(k, "tensor", te_v)
    s_tp.n += 32

    def sc_v2(sc, b_tp=b_tp):
        for hh in range(32):
            sc.wait_ge(s_tp.h, b_tp + hh + 1)
            sc.activation(out=vtok[:, hh, :, 0:32],
                          in_=pst[:, hh % 2, 0:512].rearrange("p (kt c) -> p kt c", c=32),
                          func=AF.Copy).then_inc(s_ev.h, 1)
    on(k, "scalar", sc_v2)
    s_ev.n += 32
    k.barrier()
    if cut(): st_p34.close(); st_p2.close(); st_p1.close(); st_vb.close(); st_qk.close(); st.close(); return

    ps_sT = ps[:, 0:2, :]
    ps_yz = ps[0:33, 2, :]
    yb = xln
    tiles = [(qc, kt) for qc in range(TC) for kt in range(qc * 4 + 4)]
    NTI = len(tiles)
    for hh in range(32):
        H, hm = hh // 16, hh % 16
        gg, h4 = hm // 4, hm % 4
        ccq = H * 4 + gg
        h_global = 4 * gg + h4
        b_sc, b_ex, b_mk, b_yz, b_ze = s_pr.n, s_ex.n, s_mk.n, s_yz.n, s_ze.n

        def pv0_of(te, i, hh):
            qc, kt = tiles[i]
            n = i + 1
            j = kt - qc * 4
            c0 = max(0, j) * 128
            te.wait_ge(s_mk.h, b_mk + n)
            if kt == 0:
                te.wait_ge(s_ze.h, b_ze + 2 * qc)
            te.matmul(out=ps_yz[:, c0:512], lhsT=vtok[:, hh, kt, :],
                      rhs=eb[:, n % 2, 0:512 - c0],
                      start=(kt == 0), stop=(kt == qc * 4 + 3),
                      skip_group_check=True).then_inc(s_yz.h, 1)

        def te_h(te, hh=hh, ccq=ccq, h4=h4, b_sc=b_sc, b_ex=b_ex, b_mk=b_mk, b_ze=b_ze):
            for i, (qc, kt) in enumerate(tiles):
                n = i + 1
                if n > 2:
                    te.wait_ge(s_ex.h, b_ex + n - 2)
                te.matmul(out=ps_sT[:, n % 2, :],
                          lhsT=qk[:, 8 + ccq, kt * 128:(kt + 1) * 128][h4 * 32:(h4 + 1) * 32, :],
                          rhs=qk[:, ccq, qc * 512:(qc + 1) * 512][h4 * 32:(h4 + 1) * 32, :],
                          start=True, stop=True, skip_group_check=True,
                          tile_position=(h4 * 32, 0)).then_inc(s_pr.h, 1)
                if i >= 1:
                    pv0_of(te, i - 1, hh)
            pv0_of(te, NTI - 1, hh)
        on(k, "tensor", te_h)
        s_pr.n += NTI
        s_yz.n += NTI

        def sc_h(sc, hh=hh, ccq=ccq, h4=h4, h_global=h_global, b_sc=b_sc, b_mk=b_mk, b_yz=b_yz, b_ze=b_ze):
            for i, (qc, kt) in enumerate(tiles):
                n = i + 1
                j = kt - qc * 4
                c0 = max(0, j) * 128
                sc.wait_ge(s_pr.h, b_sc + n)
                sc.activation(out=eb[:, n % 2, 0:512 - c0], in_=ps_sT[:, n % 2, c0:512],
                              func=AF.Exp, scale=0.1767766953,
                              bias=ncb[:, h_global:h_global + 1]).then_inc(s_ex.h, 1)
                if kt == qc * 4 + 3:
                    sc.wait_ge(s_yz.h, b_yz + sum(q * 4 + 4 for q in range(qc + 1)))
                    sc.activation(out=yb[:, ccq, qc * 512:(qc + 1) * 512][h4 * 32:(h4 + 1) * 32, :],
                                  in_=ps_yz[0:32, :], func=AF.Copy).then_inc(s_ze.h, 1)
        on(k, "scalar", sc_h)
        s_ex.n += NTI

        def v_h(v, hh=hh, b_ex=b_ex, b_yz=b_yz, b_ze=b_ze):
            for i, (qc, kt) in enumerate(tiles):
                n = i + 1
                j = kt - qc * 4
                v.wait_ge(s_ex.h, b_ex + n)
                if j >= 0:
                    v.tensor_tensor(out=eb[:, n % 2, 0:128], in0=eb[:, n % 2, 0:128],
                                    in1=ltm[:], op=ALU.mult).then_inc(s_mk.h, 1)
                else:
                    v.nop().then_inc(s_mk.h, 1)
                if kt == qc * 4 + 3:
                    v.wait_ge(s_ze.h, b_ze + 2 * qc + 1)
                    v.tensor_copy(out=zrow_s[:, qc, :], in_=ps_yz[32:33, :]).then_inc(s_ze.h, 1)
        on(k, "vector", v_h)
        s_mk.n += NTI
        s_ze.n += 2 * TC
        t_zhh = s_ze.n
        def g_z(e, hh=hh, t_zhh=t_zhh):
            e.wait_ge(s_ze.h, t_zhh)
            e.dma_start(out=zdram[hh:hh + 1, :], in_=zrow_s[:].rearrange("one q c -> one (q c)")).then_inc(s_ze.h, 16)
        on(k, "gpsimd", g_z)
        s_ze.inc(16)
        k.barrier()
        def g_zw(e, t=s_ze.n):
            e.wait_ge(s_ze.h, t)
        on(k, "gpsimd", g_zw)
    k.barrier()

    st_p34.close()
    st_vb.close()
    st_qk.close()
    if cut(): st_p34.close(); st_p2.close(); st_p1.close(); st_vb.close(); st_qk.close(); st.close(); return

    zr1 = k.sb("l0a_zr1", [16, S], F32, stack=st)
    zr2 = k.sb("l0a_zr2", [16, S], F32, stack=st)
    s_zd = k.sem("l0a_zd")
    b_zd = s_zd.n
    def g_zr(e):
        e.dma_start(out=zr1[:], in_=zdram[0:16, :]).then_inc(s_zd.h, 16)
        e.dma_start(out=zr2[:], in_=zdram[16:32, :]).then_inc(s_zd.h, 16)
    on(k, "gpsimd", g_zr)
    s_zd.inc(32)

    def v_z2(v, b_zd=b_zd):
        v.wait_ge(s_zd.h, b_zd + 32)
        v.reciprocal(out=zr1[:], in_=zr1[:])
        v.reciprocal(out=zr2[:], in_=zr2[:])
        v.drain()
        v.tensor_copy(out=rrq[:], in_=zr1[:])
        v.tensor_copy(out=rrk[:], in_=zr2[:])
        v.drain().then_inc(s_yz.h, 1)
    on(k, "vector", v_z2)
    t_z1 = s_yz.inc(1)
    k.barrier()

    t12a = k.sb("l0a_t12a", [P, 512], F32, stack=st)
    t12b = k.sb("l0a_t12b", [P, 512], F32, stack=st)
    b_pr8 = s_pr.n
    b_cb = s_yz.n
    def te_cb(te, b_cb=b_cb):
        n = 0
        for gg in range(4):
            for tc in range(TC):
                n += 1
                if n > 1:
                    te.wait_ge(s_yz.h, b_cb + n - 1)
                te.matmul(out=ps[:, 0, :], lhsT=bcP[:, gg, :],
                          rhs=rrq[:, tc * 512:(tc + 1) * 512],
                          start=True, stop=True, skip_group_check=True)
                te.matmul(out=ps[:, 1, :], lhsT=bcP[:, gg, :],
                          rhs=rrk[:, tc * 512:(tc + 1) * 512],
                          start=True, stop=True, skip_group_check=True).then_inc(s_pr.h, 1)
    on(k, "tensor", te_cb)
    s_pr.n += 16

    def v_cb(v, b_pr8=b_pr8):
        n = 0
        for gg in range(4):
            for tc in range(TC):
                n += 1
                v.wait_ge(s_pr.h, b_pr8 + n)
                slA = (slice(None), gg, slice(tc * 512, (tc + 1) * 512))
                slB = (slice(None), 4 + gg, slice(tc * 512, (tc + 1) * 512))
                v.tensor_tensor(out=t12a[:], in0=yb[slA], in1=ps[:, 0, :], op=ALU.mult)
                v.tensor_tensor(out=t12b[:], in0=yb[slB], in1=ps[:, 1, :], op=ALU.mult)
                v.drain()
                v.scalar_tensor_tensor(out=yb[slA], in0=t12b[:], scalar=lamn[:, gg:gg + 1], in1=t12a[:],
                                       op0=ALU.mult, op1=ALU.add)
                v.scalar_tensor_tensor(out=yb[slB], in0=t12b[:], scalar=lamc[:, gg:gg + 1], in1=t12a[:],
                                       op0=ALU.mult, op1=ALU.add)
                v.drain().then_inc(s_yz.h, 1)
    on(k, "vector", v_cb)
    s_yz.n += 16
    k.barrier()
    if cut(): st_p34.close(); st_p2.close(); st_p1.close(); st_vb.close(); st_qk.close(); st.close(); return

    awb = k.sb("l0a_aw6", [P, KC, 1024], BF16, stack=st)
    gmS = k.sb("l0a_gm6", [2, KC, 1024], BF16, stack=st)
    hsb = gmS
    b_wl6 = s_wl.n
    on(k, "gpsimd", lambda e: e.dma_start(out=wqs[:], in_=aps["L0a_protot"][:]).then_inc(s_wl.h, 16))
    s_wl.inc(16)
    def sc_d6(sc, b_wl6=b_wl6):
        sc.wait_ge(s_wl.h, b_wl6 + 16)
        sc.activation(out=awb[:], in_=wqs[:], func=AF.Abs)
        sc.drain()
        sc.activation(out=wqs[:], in_=wqs[:], func=AF.Sign)
        sc.drain().then_inc(s_dq.h, 1)
    on(k, "scalar", sc_d6)
    t_d1 = s_dq.inc(1)
    b_pr1 = s_pr.n
    def te_d6(te, t_d1=t_d1):
        te.wait_ge(s_dq.h, t_d1)
        n = 0
        for kc in range(KC):
            for j in range(2):
                n += 1
                if n > 2:
                    te.wait_ge(s_dq.h, t_d1 + n - 2)
                te.matmul(out=ps[0:2, n % 2, :], lhsT=g64[:],
                          rhs=awb[:, kc, j * 512:(j + 1) * 512],
                          start=True, stop=True, skip_group_check=True).then_inc(s_pr.h, 1)
    on(k, "tensor", te_d6)
    s_pr.n += 16
    def v_d6(v, b_pr1=b_pr1):
        n = 0
        for kc in range(KC):
            for j in range(2):
                n += 1
                v.wait_ge(s_pr.h, b_pr1 + n)
                v.tensor_copy(out=gmS[:, kc, j * 512:(j + 1) * 512], in_=ps[0:2, n % 2, :])
                v.drain().then_inc(s_dq.h, 1)
        v.tensor_scalar(out=hsb[:], in0=gmS[:], scalar1=1.0 / 64.0, scalar2=1e-8,
                        op0=ALU.mult, op1=ALU.max)
        v.drain()
        v.tensor_scalar(out=hsb[:], in0=hsb[:], scalar1=0.5, scalar2=None, op0=ALU.mult)
        v.drain().then_inc(s_dq.h, 1)
    on(k, "vector", v_d6)
    s_dq.n += 17
    t_d2 = s_dq.n
    b_pr2 = s_pr.n
    def te_d7(te, t_d2=t_d2):
        te.wait_ge(s_dq.h, t_d2)
        n = 0
        for kc in range(KC):
            for j in range(2):
                n += 1
                if n > 2:
                    te.wait_ge(s_dq.h, t_d2 + n - 2)
                te.matmul(out=ps[:, 2 + n % 2, :], lhsT=g64T[:],
                          rhs=hsb[:, kc, j * 512:(j + 1) * 512],
                          start=True, stop=True, skip_group_check=True).then_inc(s_pr.h, 1)
    on(k, "tensor", te_d7)
    s_pr.n += 16
    def v_d7(v, b_pr2=b_pr2):
        n = 0
        for kc in range(KC):
            for j in range(2):
                n += 1
                v.wait_ge(s_pr.h, b_pr2 + n)
                sl = (slice(None), kc, slice(j * 512, (j + 1) * 512))
                v.tensor_tensor(out=awb[sl], in0=awb[sl], in1=ps[:, 2 + n % 2, :], op=ALU.is_gt)
                v.drain()
                v.scalar_tensor_tensor(out=awb[sl], in0=awb[sl], scalar=2.0,
                                       in1=ps[:, 2 + n % 2, :], op0=ALU.mult, op1=ALU.mult)
                v.drain()
                v.tensor_tensor(out=wqs[sl], in0=awb[sl], in1=wqs[sl], op=ALU.mult)
                v.drain().then_inc(s_dq.h, 1)
    on(k, "vector", v_d7)
    s_dq.n += 16
    k.barrier()

    featP = k.sb("l0a_fp", [P, KC, 16], BF16, stack=st)
    featO = k.sb("l0a_fo", [P, KC, 16], BF16, stack=st)
    tabv = k.sb("l0a_tab", [1, 3], F32, stack=st)
    pf = k.sb("l0a_pf", [16, 1024], F32, stack=st)
    xs6 = k.sb("l0a_xs6", [16, S], BF16, stack=st)
    xf = k.sb("l0a_xf", [16, S], F32, stack=st)
    xa = k.sb("l0a_xa", [16, S], BF16, stack=st)
    xs1 = k.sb("l0a_xs1", [16, S], BF16, stack=st)
    Abuf = k.sb("l0a_A", [16, 1024], BF16, stack=st)
    Bbuf = k.sb("l0a_B", [16, 1024], BF16, stack=st)
    th16 = k.sb("l0a_th", [16, 3], F32, stack=st)
    b_wl7 = s_wl.n
    def gl7(e):
        e.dma_start(out=featP[:], in_=aps["L0a_featP"].rearrange("p kc m -> p (kc m)")).then_inc(s_wl.h, 16)
        e.dma_start(out=featO[:], in_=aps["L0a_featO"].rearrange("p kc m -> p (kc m)")).then_inc(s_wl.h, 16)
        e.dma_start(out=tabv[:, 0:1], in_=aps["L0_attn_theta"][:]).then_inc(s_wl.h, 16)
        e.dma_start(out=tabv[:, 1:2], in_=aps["L0_attn_alpha"][:]).then_inc(s_wl.h, 16)
        e.dma_start(out=tabv[:, 2:3], in_=aps["L0_attn_beta"][:]).then_inc(s_wl.h, 16)
    on(k, "gpsimd", gl7)
    s_wl.inc(80)

    b_pr9 = s_pr.n
    def te_f(te):
        te.wait_ge(s_wl.h, b_wl7 + 80)
        te.matmul(out=ps[0:16, 3, 0:3], lhsT=g["ones_row_f32"][:, 0:16], rhs=tabv[:],
                  start=True, stop=True, skip_group_check=True).then_inc(s_pr.h, 1)
        for j in range(2):
            for kc in range(KC):
                ins = te.matmul(out=ps[0:16, j, :], lhsT=featO[:, kc, :],
                                rhs=wqs[:, kc, j * 512:(j + 1) * 512],
                                start=(kc == 0), stop=(kc == KC - 1), skip_group_check=True)
            ins.then_inc(s_pr.h, 1)
    on(k, "tensor", te_f)
    s_pr.n += 3

    s_xf = k.sem("l0a_xf")
    b_xf = s_xf.n
    b_pr10 = s_pr.n
    def te_f2(te, b_xf=b_xf):
        for tc in range(TC):
            te.wait_ge(s_xf.h, b_xf + (tc if tc >= 2 else 1))
            for kc in range(KC):
                ins = te.matmul(out=ps[0:16, tc % 2, :], lhsT=featP[:, kc, :],
                                rhs=yb[:, kc, tc * 512:(tc + 1) * 512],
                                start=(kc == 0), stop=(kc == KC - 1), skip_group_check=True)
            ins.then_inc(s_pr.h, 1)
    on(k, "tensor", te_f2)
    s_pr.n += TC

    def v_f(v, b_pr9=b_pr9, b_pr10=b_pr10, b_xf=b_xf):
        v.wait_ge(s_pr.h, b_pr9 + 1)
        v.tensor_copy(out=th16[:], in_=ps[0:16, 3, 0:3])
        v.drain()
        v.wait_ge(s_pr.h, b_pr9 + 3)
        v.tensor_copy(out=pf[:, 0:512], in_=ps[0:16, 0, :])
        v.tensor_copy(out=pf[:, 512:1024], in_=ps[0:16, 1, :])
        v.drain().then_inc(s_xf.h, 1)
        for tc in range(TC):
            v.wait_ge(s_pr.h, b_pr10 + tc + 1)
            v.tensor_copy(out=xf[:, tc * 512:(tc + 1) * 512], in_=ps[0:16, tc % 2, :])
            v.drain().then_inc(s_xf.h, 1)
    on(k, "vector", v_f)
    t_pf = b_xf + 1
    s_xf.n += 1 + TC
    t_xf = s_xf.n

    s_sig = k.sem("l0a_sig")
    def sc_f(sc, t_pf=t_pf, t_xf=t_xf):
        sc.wait_ge(s_xf.h, t_pf)
        sc.activation(out=pf[:], in_=pf[:], func=AF.Sigmoid, scale=5.0, accum_out=None)
        sc.drain().then_inc(s_sig.h, 1)
        sc.wait_ge(s_xf.h, t_xf)
        sc.activation(out=xs6[:], in_=xf[:], func=AF.Sigmoid, scale=5.0)
        sc.drain().then_inc(s_sig.h, 1)
    pfr = k.sb("l0a_pfr", [16, 1024], F32, stack=st)
    pa_ = k.sb("l0a_pa", [16, 1024], F32, stack=st)
    tA_ = k.sb("l0a_tA", [16, 1024], F32, stack=st)
    tB_ = k.sb("l0a_tB", [16, 1024], F32, stack=st)
    def v_keep(v, t_pf=t_pf):
        v.wait_ge(s_xf.h, t_pf)
        v.tensor_copy(out=pfr[:], in_=pf[:])
        v.drain().then_inc(s_sig.h, 1)
    on(k, "vector", v_keep)
    t_keep = s_sig.inc(1)
    def sc_f2(sc, t_keep=t_keep, t_xf=t_xf):
        sc.wait_ge(s_sig.h, t_keep)
        sc.activation(out=th16[:], in_=th16[:], func=AF.Abs)
        sc.drain()
        sc.activation(out=pf[:], in_=pf[:], func=AF.Sigmoid, scale=5.0)
        sc.drain().then_inc(s_sig.h, 1)
        sc.wait_ge(s_xf.h, t_xf)
        sc.activation(out=xs6[:], in_=xf[:], func=AF.Sigmoid, scale=5.0)
        sc.drain().then_inc(s_sig.h, 1)
    on(k, "scalar", sc_f2)
    s_sig.inc(2)
    t_sig = s_sig.n

    def v_ab(v, t_sig=t_sig):
        v.wait_ge(s_sig.h, t_sig)
        v.tensor_tensor(out=xa[:], in0=xf[:], in1=xs6[:], op=ALU.mult)
        v.tensor_scalar(out=xs1[:], in0=xs6[:], scalar1=-1.0, scalar2=1.0,
                        op0=ALU.mult, op1=ALU.add)
        v.drain()
        v.tensor_tensor(out=pa_[:], in0=pfr[:], in1=pf[:], op=ALU.mult)
        v.drain()
        v.tensor_scalar(out=tA_[:], in0=pa_[:], scalar1=th16[:, 0:1], scalar2=None, op0=ALU.mult)
        v.tensor_scalar(out=tB_[:], in0=pf[:], scalar1=th16[:, 1:2],
                        scalar2=th16[:, 1:2], op0=ALU.mult, op1=ALU.subtract)
        v.drain()
        v.tensor_tensor(out=Abuf[:], in0=tA_[:], in1=tB_[:], op=ALU.add)
        v.tensor_scalar(out=Bbuf[:], in0=pa_[:], scalar1=th16[:, 2:3], scalar2=-1.0,
                        op0=ALU.mult, op1=ALU.mult)
        v.drain().then_inc(s_sig.h, 1)
    on(k, "vector", v_ab)
    t_ab = s_sig.inc(1)
    k.barrier()

    ra = ResidAdder(k, g, stack=st)
    b_pr11 = s_pr.n
    b_add = ra.s_add.n
    def te_o(te, b_add=b_add):
        n = 0
        for t in range(TT):
            for nn in range(2):
                n += 1
                if n > 2:
                    te.wait_ge(ra.s_add.h, b_add + n - 2)
                te.matmul(out=ps[:, n % 2, :], lhsT=xa[:, t * 128:(t + 1) * 128],
                          rhs=Abuf[:, nn * 512:(nn + 1) * 512],
                          start=True, stop=False, skip_group_check=True)
                te.matmul(out=ps[:, n % 2, :], lhsT=xs1[:, t * 128:(t + 1) * 128],
                          rhs=Bbuf[:, nn * 512:(nn + 1) * 512],
                          start=False, stop=True, skip_group_check=True).then_inc(s_pr.h, 1)
    on(k, "tensor", te_o)
    s_pr.n += TT * 2

    n = 0
    for t in range(TT):
        ra.prepare(t)
        for nn in range(2):
            n += 1
            ra.add(t, nn * 512, (nn + 1) * 512, ps[:, n % 2, :], wait=(s_pr, b_pr11 + n))
        ra.flush(t)
    ra.done()
    k.barrier()
    st.close()

BLOCK_EMITTERS["l0attn"] = emit_l0attn


def emit_l3attn(k, g, aps, upto=99):
    st = ExitStack()
    xln = g["xln"]
    ps = g["ps_mm"]
    pst = g["ps_tr"]

    Q = k.sb("l3a_q", [P, KC, S], BF16, stack=st)
    K_ = k.sb("l3a_k", [P, KC, S], BF16, stack=st)
    V_ = k.sb("l3a_v", [P, KC, S], BF16, stack=st)
    wbuf = k.sb("l3a_w", [P, KC, 1024], BF16, stack=st)
    biasS = k.sb("l3a_b", [P, 4 * KC], F32, stack=st)
    ssqK = k.sb("l3a_ssqk", [16, S], F32, stack=st)
    rnb = k.sb("l3a_rnb", [16, S], BF16, stack=st)
    maxr = k.sb("l3a_maxr", [16, 4], F32, stack=st)
    ncb3 = k.sb("l3a_ncb", [P, 16], F32, stack=st)
    crow3 = k.sb("l3a_crow", [1, 16], F32, stack=st)
    sq3 = k.sb("l3a_sq", [P, S], BF16, stack=st)
    st_h = ExitStack()
    eb = k.sb("l3a_eb", [P, 2, 1024], BF16, stack=st_h)
    vbar = k.sb("l3a_vbar", [64, 1], F32, stack=st_h)
    Vh = k.sb("l3a_vh", [64, S], BF16, stack=st_h)
    Wh = k.sb("l3a_wh", [64, S], BF16, stack=st_h)
    Th = k.sb("l3a_th", [64, S], F32, stack=st_h)
    Kh = k.sb("l3a_kh", [64, S], BF16, stack=st_h)
    Qh = k.sb("l3a_qh", [64, S], BF16, stack=st_h)
    Knh = k.sb("l3a_knh", [64, S], BF16, stack=st_h)
    vtok = k.sb("l3a_vt", [P, TT, 65], BF16, stack=st_h)
    v1tok = k.sb("l3a_v1t", [P, TT, 65], BF16, stack=st_h)
    vttok = k.sb("l3a_vtt", [P, TT, 65], BF16, stack=st_h)
    attnA = xln
    g["l3aQ"], g["l3aK"], g["l3aV"], g["l3arnb"], g["l3ancb"] = Q, K_, V_, rnb, ncb3
    zrow = ssqK[0:1, :]
    zrb = sq3[0:1, :]

    s_wl = k.sem("wl")
    s_pr = k.sem("pr")
    s_ev = k.sem("ev")
    s_tp = k.sem("ln_tp")
    s_ex = k.sem("l0a_ex")
    s_yz = k.sem("l0a_yz")
    s_ze = k.sem("l0a_ze")

    b_wl = s_wl.n
    on(k, "gpsimd", lambda e: e.dma_start(out=biasS[:], in_=aps["L3a_biases"][:]).then_inc(s_wl.h, 16))
    s_wl.inc(16)

    for sec, (wkey, dst, boff) in enumerate([("L3a_wqt", Q, 0), ("L3a_wkt", K_, KC),
                                             ("L3a_wvt", V_, 2 * KC)]):
        b_wl1 = s_wl.n
        on(k, "gpsimd", lambda e, wkey=wkey: e.dma_start(
            out=wbuf[:], in_=aps[wkey].rearrange("mc p n -> p mc n")).then_inc(s_wl.h, 16))
        s_wl.inc(16)
        b_pr1, b_ev1 = s_pr.n, s_ev.n
        def te_p(te, b_wl1=b_wl1, b_ev1=b_ev1):
            te.wait_ge(s_wl.h, b_wl1 + 16)
            n = 0
            for mc in range(KC):
                for tc in range(TC):
                    n += 1
                    if n > 2:
                        te.wait_ge(s_ev.h, b_ev1 + n - 2)
                    for kc in range(KC):
                        ins = te.matmul(out=ps[:, n % 2, :], lhsT=wbuf[:, mc, kc * 128:(kc + 1) * 128],
                                        rhs=xln[:, kc, tc * 512:(tc + 1) * 512],
                                        start=(kc == 0), stop=(kc == KC - 1))
                    ins.then_inc(s_pr.h, 1)
        on(k, "tensor", te_p)
        s_pr.n += KC * TC
        def sc_p(sc, dst=dst, boff=boff, b_pr1=b_pr1):
            n = 0
            for mc in range(KC):
                for tc in range(TC):
                    n += 1
                    sc.wait_ge(s_pr.h, b_pr1 + n)
                    sc.activation(out=dst[:, mc, tc * 512:(tc + 1) * 512], in_=ps[:, n % 2, :],
                                  func=AF.Identity, bias=biasS[:, boff + mc:boff + mc + 1]).then_inc(s_ev.h, 1)
        on(k, "scalar", sc_p)
        s_ev.n += KC * TC
        k.barrier()
        if upto <= 0:
            break

    if upto <= 1:
        st_h.close()
        st.close()
        return

    s_sqd = k.sem("l0a_sqd")
    for which, src_ in enumerate([K_, Q]):
        b_sqd = s_sqd.n
        b_ev4 = s_ev.n
        def sc_s(sc, src_=src_, b_sqd=b_sqd):
            for cc in range(KC):
                if cc >= 1:
                    sc.wait_ge(s_sqd.h, b_sqd + cc)
                sc.activation(out=sq3[:], in_=src_[:, cc, :], func=AF.Square)
                sc.drain().then_inc(s_ev.h, 1)
        on(k, "scalar", sc_s)
        s_ev.n += KC
        b_pr4 = s_pr.n
        def te_s(te, b_ev4=b_ev4):
            for cc in range(KC):
                te.wait_ge(s_ev.h, b_ev4 + cc + 1)
                for tc in range(TC):
                    ins = te.matmul(out=ps[0:16, tc, :], lhsT=g["sum64"][:, cc, :],
                                    rhs=sq3[:, tc * 512:(tc + 1) * 512],
                                    start=(cc == 0), stop=(cc == KC - 1), skip_group_check=True)
                ins.then_inc(s_sqd.h, 1)
            te.drain().then_inc(s_pr.h, 1)
        on(k, "tensor", te_s)
        s_sqd.n += KC
        s_pr.n += 1
        t_prs = s_pr.n
        def v_s(v, which=which, t_prs=t_prs):
            v.wait_ge(s_pr.h, t_prs)
            for tc in range(TC):
                v.tensor_copy(out=ssqK[:, tc * 512:(tc + 1) * 512], in_=ps[0:16, tc, :])
            v.drain().then_inc(s_yz.h, 1)
        on(k, "vector", v_s)
        t_y1 = s_yz.inc(1)
        def sc_n(sc, t_y1=t_y1):
            sc.wait_ge(s_yz.h, t_y1)
            sc.activation(out=ssqK[:], in_=ssqK[:], func=AF.Sqrt)
            sc.drain().then_inc(s_yz.h, 1)
        on(k, "scalar", sc_n)
        t_y2 = s_yz.inc(1)
        def v_n(v, which=which, t_y2=t_y2):
            v.wait_ge(s_yz.h, t_y2)
            v.tensor_reduce(out=maxr[:, which:which + 1], in_=ssqK[:],
                            axis=mybir.AxisListType.X, op=ALU.max)
            v.drain()
            if which == 0:
                v.tensor_scalar(out=ssqK[:], in0=ssqK[:], scalar1=1e-6, scalar2=None, op0=ALU.max)
                v.drain()
                v.reciprocal(out=ssqK[:], in_=ssqK[:])
                v.drain()
                v.tensor_copy(out=rnb[:], in_=ssqK[:])
                v.drain()
            else:
                v.scalar_tensor_tensor(out=maxr[:, 2:3], in0=maxr[:, 0:1], scalar=0.125,
                                       in1=maxr[:, 1:2], op0=ALU.mult, op1=ALU.mult)
                v.drain()
            v.nop().then_inc(s_yz.h, 1)
        on(k, "vector", v_n)
        s_yz.inc(1)
    k.barrier()

    b_pr6 = s_pr.n
    def te_c(te):
        te.transpose(out=ps[0:1, 3, 0:16], in_=maxr[:, 2:3], identity=g["identf"][0:16, 0:16])
        te.drain().then_inc(s_pr.h, 1)
    on(k, "tensor", te_c)
    s_pr.n += 1
    def v_c(v, b_pr6=b_pr6):
        v.wait_ge(s_pr.h, b_pr6 + 1)
        v.tensor_copy(out=crow3[:], in_=ps[0:1, 3, 0:16])
        v.drain().then_inc(s_yz.h, 1)
    on(k, "vector", v_c)
    t_c1 = s_yz.inc(1)
    b_pr7 = s_pr.n
    def te_c2(te, t_c1=t_c1):
        te.wait_ge(s_yz.h, t_c1)
        te.matmul(out=ps[:, 3, 0:16], lhsT=g["ones_row_f32"][:], rhs=crow3[:],
                  start=True, stop=True, skip_group_check=True).then_inc(s_pr.h, 1)
    on(k, "tensor", te_c2)
    s_pr.n += 1
    def v_c2(v, b_pr7=b_pr7):
        v.wait_ge(s_pr.h, b_pr7 + 1)
        v.tensor_scalar(out=ncb3[:], in0=ps[:, 3, 0:16], scalar1=-1.0, scalar2=None, op0=ALU.mult)
        v.drain().then_inc(s_yz.h, 1)
    on(k, "vector", v_c2)
    s_yz.inc(1)
    k.barrier()

    ps_sT = ps[:, 0:4, :]
    ps_yz = k.ps("l3a_pyz", [65, 512], F32, stack=st)

    def ev_pass(h, lhs_tok, dst_fm, kind):
        b_sc, b_ex, b_yz, b_ze = s_pr.n, s_ex.n, s_yz.n, s_ze.n
        NP = TC * TT // 2

        def score_of(te, i):
            ic, jt = i // TT, i % TT
            if kind == "sim":
                te.matmul(out=ps_sT[:, (i % 4), :], lhsT=Knh[:, jt * 128:(jt + 1) * 128],
                          rhs=Knh[:, ic * 512:(ic + 1) * 512],
                          start=True, stop=True, skip_group_check=True).then_inc(s_pr.h, 1)
            else:
                te.matmul(out=ps_sT[:, (i % 4), :], lhsT=Kh[:, jt * 128:(jt + 1) * 128],
                          rhs=Qh[:, ic * 512:(ic + 1) * 512],
                          start=True, stop=True, skip_group_check=True).then_inc(s_pr.h, 1)

        def pv_of(te, i, b_ex, b_ze):
            ic, jt = i // TT, i % TT
            m = i // 2
            te.wait_ge(s_ex.h, b_ex + m + 1)
            if jt == 0:
                te.wait_ge(s_ze.h, b_ze + 2 * ic)
            ins = te.matmul(out=ps_yz[:], lhsT=lhs_tok[:, jt, :],
                            rhs=eb[:, m % 2, (i % 2) * 512:(i % 2 + 1) * 512],
                            start=(jt == 0), stop=(jt == TT - 1), skip_group_check=True)
            if jt == TT - 1:
                ins.then_inc(s_yz.h, 1)

        def te_h(te, h=h, lhs_tok=lhs_tok, kind=kind, b_ex=b_ex, b_ze=b_ze):
            for m in range(NP):
                if m >= 2:
                    te.wait_ge(s_ex.h, b_ex + m - 1)
                score_of(te, 2 * m)
                score_of(te, 2 * m + 1)
                if m >= 1:
                    pv_of(te, 2 * m - 2, b_ex, b_ze)
                    pv_of(te, 2 * m - 1, b_ex, b_ze)
            pv_of(te, 2 * NP - 2, b_ex, b_ze)
            pv_of(te, 2 * NP - 1, b_ex, b_ze)
        on(k, "tensor", te_h)
        s_pr.n += TC * TT
        s_yz.n += TC

        def sc_h(sc, h=h, kind=kind, dst_fm=dst_fm, b_sc=b_sc, b_yz=b_yz):
            for m in range(NP):
                sc.wait_ge(s_pr.h, b_sc + 2 * m + 2)
                s0 = (2 * m) % 4
                if kind == "sim":
                    sc.activation(out=eb[:, m % 2, :], in_=ps_sT[:, s0:s0 + 2, :],
                                  func=AF.Exp, scale=0.125).then_inc(s_ex.h, 1)
                else:
                    sc.activation(out=eb[:, m % 2, :], in_=ps_sT[:, s0:s0 + 2, :],
                                  func=AF.Exp, scale=0.125,
                                  bias=ncb3[:, h:h + 1]).then_inc(s_ex.h, 1)
                if (2 * m + 1) % TT == TT - 1:
                    ic = (2 * m + 1) // TT
                    sc.wait_ge(s_yz.h, b_yz + ic + 1)
                    sc.activation(out=dst_fm[:, ic * 512:(ic + 1) * 512], in_=ps_yz[0:64, :],
                                  func=AF.Copy).then_inc(s_ze.h, 1)
        on(k, "scalar", sc_h)
        s_ex.n += NP

        def v_h(v, b_yz=b_yz, b_ze=b_ze):
            for ic in range(TC):
                v.wait_ge(s_ze.h, b_ze + 2 * ic + 1)
                v.tensor_copy(out=zrow[:, ic * 512:(ic + 1) * 512],
                              in_=ps_yz[64:65, :]).then_inc(s_ze.h, 1)
        on(k, "vector", v_h)
        s_ze.n += 2 * TC

    def warm_pe(te, n):
        for _ in range(n):
            te.matmul(out=ps[:, 0, :], lhsT=g["ident"][:], rhs=xln[:, 0, 0:512],
                      start=True, stop=True, skip_group_check=True)

    def normalize_fm(dst_fm, with_vbar, reuse_z=False):
        s_nf = k.sem("l3a_nf")
        t0_ = s_nf.n
        z_end = s_ze.n
        if not reuse_z:
            def v_r(sc, t0_=t0_):
                sc.activation(out=zrow[:], in_=zrow[:], func=AF.Ln)
                sc.drain()
                sc.activation(out=zrb[:], in_=zrow[:], func=AF.Exp, scale=-1.0)
                sc.drain().then_inc(s_nf.h, 1)
            on(k, "scalar", v_r)
            s_nf.inc(1)
            base = t0_ + 1
        else:
            base = t0_
        b_pr8 = s_pr.n
        def te_b(te, base=base, z_end=z_end, reuse_z=reuse_z):
            if reuse_z:
                te.wait_ge(s_ze.h, z_end)
            else:
                te.wait_ge(s_nf.h, base)
            for tc in range(TC):
                if tc >= 1:
                    te.wait_ge(s_nf.h, base + tc)
                te.matmul(out=ps[0:64, 3, :], lhsT=g["ones_row_bf"][:, 0:64],
                          rhs=zrb[:, tc * 512:(tc + 1) * 512],
                          start=True, stop=True, skip_group_check=True).then_inc(s_pr.h, 1)
        on(k, "tensor", te_b)
        s_pr.n += TC
        def v_m(v, dst_fm=dst_fm, with_vbar=with_vbar, b_pr8=b_pr8, t0_=t0_):
            for tc in range(TC):
                v.wait_ge(s_pr.h, b_pr8 + tc + 1)
                sl = (slice(None), slice(tc * 512, (tc + 1) * 512))
                v.tensor_tensor(out=Th[sl], in0=dst_fm[sl], in1=ps[0:64, 3, :],
                                op=ALU.mult).then_inc(s_nf.h, 1)
                if with_vbar:
                    v.tensor_scalar(out=dst_fm[sl], in0=Th[sl], scalar1=1.0 - 1e-6,
                                    scalar2=vbar[:], op0=ALU.mult, op1=ALU.add)
                else:
                    v.tensor_copy(out=dst_fm[sl], in_=Th[sl])
            v.drain().then_inc(s_nf.h, 1)
        on(k, "vector", v_m)
        s_nf.n += TC + 1

    def tokify(src_fm, dst_tok):
        b_tp = s_tp.n
        b_ev9 = s_ev.n
        def te_t(te, b_tp=b_tp, b_ev9=b_ev9, src_fm=src_fm):
            for half in range(2):
                if half >= 1:
                    te.wait_ge(s_ev.h, b_ev9 + half)
                for jt in range(8):
                    t = half * 8 + jt
                    ins = te.transpose(out=pst[:, half % 2, jt * 64:(jt + 1) * 64],
                                       in_=src_fm[:, t * 128:(t + 1) * 128],
                                       identity=g["ident"][0:64, 0:64])
                ins.then_inc(s_tp.h, 1)
        on(k, "tensor", te_t)
        s_tp.n += 2
        def sc_t(sc, b_tp=b_tp, dst_tok=dst_tok):
            for half in range(2):
                sc.wait_ge(s_tp.h, b_tp + half + 1)
                sc.activation(out=dst_tok[:, half * 8:(half + 1) * 8, 0:64],
                              in_=pst[:, half % 2, 0:512].rearrange("p (jt c) -> p jt c", c=64),
                              func=AF.Copy).then_inc(s_ev.h, 1)
        on(k, "scalar", sc_t)
        s_ev.n += 2

    for h in range(16):
        h2, hc = h % 2, h // 2
        s_cp = k.sem("l3a_cp")
        t_cp = s_cp.n
        def sc_cp(sc, h2=h2, hc=hc):
            sc.activation(out=Vh[:], in_=V_[:, hc, :][h2 * 64:(h2 + 1) * 64, :], func=AF.Copy)
            sc.activation(out=Kh[:], in_=K_[:, hc, :][h2 * 64:(h2 + 1) * 64, :], func=AF.Copy)
            sc.activation(out=Qh[:], in_=Q[:, hc, :][h2 * 64:(h2 + 1) * 64, :],
                          func=AF.Copy).then_inc(s_cp.h, 1)
        on(k, "scalar", sc_cp)
        s_cp.inc(1)
        b_pr9 = s_pr.n
        def te_kn(te, h=h, t_cp=t_cp):
            te.wait_ge(s_cp.h, t_cp + 1)
            for tc in range(TC):
                te.matmul(out=ps[0:64, 3, :], lhsT=g["sel16e"][:, h * 128:h * 128 + 64],
                          rhs=rnb[:, tc * 512:(tc + 1) * 512],
                          start=True, stop=True, skip_group_check=True).then_inc(s_pr.h, 1)
                te.wait_ge(s_cp.h, t_cp + 1 + tc + 1)
        on(k, "tensor", te_kn)
        s_pr.n += TC
        def v_kn(v, b_pr9=b_pr9, t_cp=t_cp):
            for tc in range(TC):
                v.wait_ge(s_pr.h, b_pr9 + tc + 1)
                sl = (slice(None), slice(tc * 512, (tc + 1) * 512))
                v.tensor_tensor(out=Knh[sl], in0=Kh[sl], in1=ps[0:64, 3, :],
                                op=ALU.mult).then_inc(s_cp.h, 1)
            v.tensor_reduce(out=vbar[:], in_=Vh[:], axis=mybir.AxisListType.X, op=ALU.add)
            v.drain()
            v.tensor_scalar(out=vbar[:], in0=vbar[:], scalar1=1e-6 / S, scalar2=None,
                            op0=ALU.mult).then_inc(s_cp.h, 1)
        on(k, "vector", v_kn)
        s_cp.inc(TC + 1)
        k.barrier()

        tokify(Vh, vtok)
        on(k, "vector", lambda v: v.memset(vtok[:, :, 64:65], 1.0))
        on(k, "vector", lambda v: v.memset(v1tok[:, :, 64:65], 1.0))
        on(k, "vector", lambda v: v.memset(vttok[:, :, 64:65], 1.0))
        k.barrier()

        ev_pass(h, vtok, Wh, "sim")
        normalize_fm(Wh, with_vbar=True)
        k.barrier()
        tokify(Wh, v1tok)
        k.barrier()
        ev_pass(h, v1tok, Th, "sim")
        normalize2_dst = Th
        normalize_fm(normalize2_dst, with_vbar=True, reuse_z=True)
        k.barrier()
        s_cb = k.sem("l3a_cb")
        t_cb = s_cb.n
        def v_cmb(v):
            v.tensor_scalar(out=Wh[:], in0=Wh[:], scalar1=0.0196 / 0.0004, scalar2=None, op0=ALU.mult)
            v.tensor_tensor(out=Th[:], in0=Th[:], in1=Wh[:], op=ALU.add)
            v.tensor_scalar(out=Th[:], in0=Th[:], scalar1=0.0004, scalar2=None, op0=ALU.mult)
            v.scalar_tensor_tensor(out=Wh[:], in0=Vh[:], scalar=0.98, in1=Th[:],
                                   op0=ALU.mult, op1=ALU.add).then_inc(s_cb.h, 1)
        on(k, "vector", v_cmb)
        s_cb.inc(1)
        k.barrier()
        tokify(Wh, vttok)
        k.barrier()
        ev_pass(h, vttok, Wh, "p0")
        normalize_fm(Wh, with_vbar=False)
        k.barrier()
        s_wr = k.sem("l3a_wr")
        t_wr = s_wr.n
        def sc_wr(sc, h2=h2, hc=hc):
            sc.activation(out=attnA[:, hc, :][h2 * 64:(h2 + 1) * 64, :], in_=Wh[:],
                          func=AF.Copy).then_inc(s_wr.h, 1)
        on(k, "scalar", sc_wr)
        s_wr.inc(1)
        k.barrier()

    if upto <= 2:
        st_h.close()
        st.close()
        return

    st_h.close()

    b_wl9 = s_wl.n
    on(k, "gpsimd", lambda e: e.dma_start(
        out=wbuf[:], in_=aps["L3a_wot"].rearrange("kc p n -> p kc n")).then_inc(s_wl.h, 16))
    s_wl.inc(16)
    boB = k.sb("l3a_bo", [1, 1024], BF16, stack=st)
    on(k, "gpsimd", lambda e: e.dma_start(out=boB[:], in_=aps["L3a_bob"][:]).then_inc(s_wl.h, 16))
    s_wl.inc(16)

    ra = ResidAdder(k, g, stack=st)
    b_pr11 = s_pr.n
    b_add = ra.s_add.n
    def te_o(te, b_wl9=b_wl9, b_add=b_add):
        te.wait_ge(s_wl.h, b_wl9 + 32)
        n = 0
        for t in range(TT):
            for nn in range(2):
                n += 1
                if n > 2:
                    te.wait_ge(ra.s_add.h, b_add + n - 2)
                for kc in range(KC):
                    te.matmul(out=ps[:, n % 2, :], lhsT=attnA[:, kc, t * 128:(t + 1) * 128],
                              rhs=wbuf[:, kc, nn * 512:(nn + 1) * 512],
                              start=(kc == 0), stop=False, skip_group_check=True)
                te.matmul(out=ps[:, n % 2, :], lhsT=g["ones_row_bf"][:, t * 0:128] if False else g["ones_row_bf"][:],
                          rhs=boB[:, nn * 512:(nn + 1) * 512],
                          start=False, stop=True, skip_group_check=True).then_inc(s_pr.h, 1)
    on(k, "tensor", te_o)
    s_pr.n += TT * 2

    n = 0
    for t in range(TT):
        ra.prepare(t)
        for nn in range(2):
            n += 1
            ra.add(t, nn * 512, (nn + 1) * 512, ps[:, n % 2, :], wait=(s_pr, b_pr11 + n))
        ra.flush(t)
    ra.done()
    k.barrier()
    st.close()

BLOCK_EMITTERS["l3attn"] = emit_l3attn


def build_program(blocks, host_arrays, debug=()):
    nc = bass.Bass("TRN2", target_bir_lowering=False)
    resid_in = nc.dram_tensor("resid_in", [S, D], F32, kind="ExternalInput").ap()
    resid_out = nc.dram_tensor("resid_out", [S, D], F32, kind="ExternalOutput").ap()
    scratch = [nc.dram_tensor(f"rscratch{i}", [S, D], F32, kind="Internal").ap() for i in range(2)]
    aps = declare_inputs(nc, host_arrays)
    with ExitStack() as ctx:
        block = ctx.enter_context(nc.Block())
        k = Kit(nc, ctx, block)
        g = {}
        g["xln"] = k.sb("xln", [P, KC, S], BF16)
        g["ident"] = k.sb("identc", [P, P], BF16)
        g["ones_row_bf"] = k.sb("onesrbf", [1, P], BF16)
        g["ones_row_f32"] = k.sb("onesrf", [1, P], F32)
        g["ones_col_bf"] = k.sb("onescbf", [P, 1], BF16)
        g["sel16e"] = k.sb("sel16e", [16, 16 * P], BF16)
        g["c_one"] = k.sb("conebf", [1, 1], BF16)
        g["identf"] = k.sb("identf", [P, P], F32)
        g["sum64"] = k.sb("gsum64", [P, KC, 16], BF16)
        g["ps_tr"] = k.ps("ps_tr", [P, 2, 1024], BF16)
        g["ps_mm"] = k.ps("ps_mm", [P, 4, 512], F32)

        sc_, tc_ = load_const(k, g["ident"], aps["c_ident_bf"][:])
        load_const(k, g["ones_row_bf"], aps["c_ones_row_bf"][:])
        load_const(k, g["ones_row_f32"], aps["c_ones_row_f32"][:])
        load_const(k, g["ones_col_bf"], aps["c_ones_col_bf"][:])
        load_const(k, g["c_one"], aps["c_one_bf"][:])
        load_const(k, g["identf"], aps["c_ident_f32"][:])
        sg_, tg_ = load_const(k, g["sum64"], aps["c_sum64_bf"].rearrange("kc p m -> p kc m"), queue="gpsimd")
        _, tlast = load_const(k, g["sel16e"], aps["c_sel16e_bf"][:])
        bar_w = k.sem("ldc")
        for e in ("tensor", "vector", "scalar"):
            def fw(eng, t=tlast, tg=tg_):
                eng.wait_ge(bar_w.h, t)
                eng.wait_ge(sg_.h, tg)
            on(k, e, fw)

        n = len(blocks)
        for i, b in enumerate(blocks):
            g["resid_src"] = resid_in if i == 0 else scratch[(i - 1) % 2]
            g["resid_dst"] = resid_out if i == n - 1 else scratch[i % 2]
            emit_ln(k, g)
            BLOCK_EMITTERS[b](k, g, aps)
        if debug:
            s_dbg = k.sem("dbg")
            names = list(debug)
            def fdbg(e):
                for name in names:
                    buf = g[name]
                    out_ap = nc.dram_tensor("dbg_" + name, list(buf.shape), buf.dtype,
                                            kind="ExternalOutput").ap()
                    e.dma_start(out=out_ap[:], in_=buf[:]).then_inc(s_dbg.h, 16)
                e.wait_ge(s_dbg.h, 16 * len(names))
            on(k, "sync", fdbg)
    return nc

BLOCK_EMITTERS["l2ffn"] = emit_l2ffn


def prep_weights(inputs):
    h = {}
    h["L2f_wt"] = moving_tiles(np.ascontiguousarray(inputs["L2_ffn_weight"].T))
    h["L3f_selt"] = lhsT_tiles(np.ascontiguousarray(inputs["L3_ffn_expert_sel"].T), mtile=16)
    h["L3f_keyst"] = np.stack([lhsT_tiles(inputs["L3_ffn_keys"][e])[0]
                               for e in range(16)])
    h["L3f_valst"] = np.stack([moving_tiles(inputs["L3_ffn_values"][e])[0]
                               for e in range(16)])
    h["L0f_w1t"] = lhsT_tiles(np.ascontiguousarray(inputs["L0_ffn_w1_w"].T))
    h["L0f_w2t"] = lhsT_tiles(np.ascontiguousarray(inputs["L0_ffn_w2_w"].T))
    h["L0f_h1t"] = lhsT_tiles(np.ascontiguousarray(inputs["L0_ffn_h1_w"].T))
    h["L0f_h2t"] = lhsT_tiles(np.ascontiguousarray(inputs["L0_ffn_h2_w"].T))
    h["L0f_w3t"] = lhsT_tiles(np.ascontiguousarray(inputs["L0_ffn_w3_w"].T))
    h["L0f_scales"] = np.concatenate([
        inputs["L0_ffn_w1_s"].reshape(32, 128),
        inputs["L0_ffn_w2_s"].reshape(32, 128),
        inputs["L0_ffn_h1_s"].reshape(32, 128),
        inputs["L0_ffn_h2_s"].reshape(32, 128),
        inputs["L0_ffn_w3_s"].reshape(8, 128)], axis=0).astype(np.float32)
    h["L0f_scales"] = np.ascontiguousarray(h["L0f_scales"].T)
    h["L2a_wqt"] = lhsT_tiles(np.ascontiguousarray(inputs["L2_attn_Wq"].T))
    h["L2a_wkt"] = lhsT_tiles(np.ascontiguousarray(inputs["L2_attn_Wk"].T))
    h["L2a_wvt"] = lhsT_tiles(np.ascontiguousarray(inputs["L2_attn_Wv"].T))
    h["L2a_projt"] = lhsT_tiles(np.ascontiguousarray(inputs["L2_attn_proj"]))
    h["L1f_sigsT"] = np.ascontiguousarray(inputs["L1_ffn_tile_sigs"].T)
    h["L1f_wt"] = np.stack([moving_tiles(np.ascontiguousarray(inputs["L1_ffn_W"][u].T))
                            for u in range(8)])
    h["L1_ffn_b"] = np.ascontiguousarray(inputs["L1_ffn_b"])
    h["L1a_wqkvt"] = lhsT_tiles(np.ascontiguousarray(inputs["L1_attn_w_qkv"]))
    h["L1a_woutt"] = moving_tiles(np.ascontiguousarray(inputs["L1_attn_w_out"]))
    h["L1a_bout"] = np.ascontiguousarray(inputs["L1_attn_b_out"].reshape(1, 1024))
    s_ = np.repeat(np.arange(3), 1024)
    hh_ = np.tile(np.repeat(np.arange(16), 64), 3)
    dd_ = np.tile(np.arange(64), 48)
    newpos = s_ * 1024 + (dd_ // 32) * 512 + (hh_ // 4) * 128 + (hh_ % 4) * 32 + (dd_ % 32)
    order = np.empty(3072, dtype=np.int64)
    order[newpos] = np.arange(3072)
    wT = np.ascontiguousarray(inputs["L0_attn_w_qkv"].T)
    wTp = wT[:, order]
    h["L0a_wqkvt"] = np.ascontiguousarray(
        wTp.reshape(8, 128, 3, 1024).transpose(2, 1, 0, 3))
    pT = np.ascontiguousarray(inputs["L0_attn_prototypes"].T)
    h["L0a_protot"] = np.ascontiguousarray(pT.reshape(8, 128, 1024).transpose(1, 0, 2))
    hv = np.repeat(np.arange(16), 64)
    dv = np.tile(np.arange(64), 16)
    newv = (dv // 32) * 512 + (hv // 4) * 128 + (hv % 4) * 32 + (dv % 32)
    orderv = np.empty(1024, dtype=np.int64)
    orderv[newv] = np.arange(1024)
    fT = np.ascontiguousarray(inputs["L0_attn_features"].T)
    h["L0a_featP"] = np.ascontiguousarray(fT[orderv].reshape(8, 128, 16).transpose(1, 0, 2))
    h["L0a_featO"] = np.ascontiguousarray(fT.reshape(8, 128, 16).transpose(1, 0, 2))
    h["L0_attn_q_gain"] = np.ascontiguousarray(inputs["L0_attn_q_gain"].reshape(16, 1))
    h["L0_attn_diff_lambda"] = np.ascontiguousarray(inputs["L0_attn_diff_lambda"].reshape(16, 1))
    h["L0_attn_theta"] = np.ascontiguousarray(inputs["L0_attn_theta"].reshape(1, 1))
    h["L0_attn_alpha"] = np.ascontiguousarray(inputs["L0_attn_alpha"].reshape(1, 1))
    h["L0_attn_beta"] = np.ascontiguousarray(inputs["L0_attn_beta"].reshape(1, 1))
    h["L3a_wqt"] = lhsT_tiles(np.ascontiguousarray(inputs["L3_attn_Wq"]))
    h["L3a_wkt"] = lhsT_tiles(np.ascontiguousarray(inputs["L3_attn_Wk"]))
    h["L3a_wvt"] = lhsT_tiles(np.ascontiguousarray(inputs["L3_attn_Wv"]))
    h["L3a_wot"] = moving_tiles(np.ascontiguousarray(inputs["L3_attn_Wo"]))
    h["L3a_bob"] = np.ascontiguousarray(inputs["L3_attn_bo"].reshape(1, 1024))
    h["L3a_biases"] = np.ascontiguousarray(np.concatenate(
        [inputs["L3_attn_bq"].reshape(8, 128), inputs["L3_attn_bk"].reshape(8, 128),
         inputs["L3_attn_bv"].reshape(8, 128), inputs["L3_attn_bo"].reshape(8, 128)],
        axis=0).T)
    return h



BLOCK_ORDER = ["l0attn", "l0ffn", "l1attn", "l1ffn", "l2attn", "l2ffn", "l3attn", "l3ffn"]
NCORES = 8

def kernel(**inputs):
    host = prep_weights(inputs)
    host.update(host_constants())
    resid = np.ascontiguousarray(inputs["L0_attn_x"], dtype=np.float32)
    nc = build_program(BLOCK_ORDER, host)
    in_maps = [dict(host, resid_in=np.ascontiguousarray(resid[c])) for c in range(NCORES)]
    res = run_bass_kernel_spmd(nc, in_maps, core_ids=list(range(NCORES)))
    out = [np.asarray(res.results[c]["resid_out"], dtype=np.float32) for c in range(NCORES)]
    return np.stack(out).reshape(8, 2048, 1024)
